# version 14
# baseline (speedup 1.0000x reference)
import os
import numpy as np
import ml_dtypes
import concourse.bass as bass
import concourse.mybir as mybir
from concourse.bass_utils import run_bass_kernel_spmd

F32 = mybir.dt.float32
BF16 = mybir.dt.bfloat16
AF = mybir.ActivationFunctionType
ALU = mybir.AluOpType
AX = mybir.AxisListType
BF16NP = ml_dtypes.bfloat16

D = 1024
NCORE = 8
CENG = ("sync", "tensor", "vector", "scalar", "gpsimd")


class Em:
    def __init__(self, nc):
        self.nc = nc
        self.ops = []
        self.last_w = {}
        self.readers = {}

    @staticmethod
    def _k(h):
        if isinstance(h, str):
            return h
        name = getattr(h, "name", None)
        if name is None:
            name = h.tensor.name
        return name

    def op(self, eng, fn, reads=(), writes=(), dma=False, writes_accum=()):
        if not hasattr(self, "accum"):
            self.accum = {}
        i = len(self.ops)
        deps = set()
        for h in reads:
            k = self._k(h)
            j = self.last_w.get(k)
            if j is not None:
                deps.add(j)
            deps.update(self.accum.get(k, ()))
        for h in writes:
            k = self._k(h)
            j = self.last_w.get(k)
            if j is not None:
                deps.add(j)
            deps.update(self.readers.get(k, ()))
            deps.update(self.accum.get(k, ()))
        for h in writes_accum:
            k = self._k(h)
            j = self.last_w.get(k)
            if j is not None:
                deps.add(j)
            deps.update(self.readers.get(k, ()))
        o = {"i": i, "eng": eng, "fn": fn, "dma": bool(dma), "deps": deps,
             "inc": bool(dma), "bar": False}
        self.ops.append(o)
        for h in reads:
            self.readers.setdefault(self._k(h), set()).add(i)
        for h in writes:
            k = self._k(h)
            self.last_w[k] = i
            self.readers[k] = set()
            self.accum[k] = set()
        for h in writes_accum:
            self.accum.setdefault(self._k(h), set()).add(i)
        return o

    def barrier(self):
        self.ops.append({"i": len(self.ops), "eng": None, "fn": None,
                         "dma": False, "deps": set(), "inc": False, "bar": True})
        self.last_w = {}
        self.readers = {}
        self.accum = {}

    def dma(self, eng, out, in_, reads=(), writes=(), writes_accum=(), **kw):
        return self.op(eng,
                       lambda e, out=out, in_=in_, kw=kw: e.dma_start(out=out, in_=in_, **kw),
                       reads=reads, writes=writes, writes_accum=writes_accum, dma=True)

    def build(self):
        NQ = 8
        ops = self.ops
        if not (ops and ops[-1]["bar"]):
            self.barrier()
            ops = self.ops
        for o in ops:
            if o["bar"]:
                continue
            for d in o["deps"]:
                p = ops[d]
                if (p["eng"] != o["eng"]) or p["dma"]:
                    p["inc"] = True
        lastc = {e: None for e in CENG}
        for o in ops:
            if o["bar"]:
                for e in CENG:
                    if lastc[e] is not None:
                        ops[lastc[e]]["inc"] = True
                continue
            if not o["dma"]:
                lastc[o["eng"]] = o["i"]
        ccnt = {e: 0 for e in CENG}
        dcnt = {}
        dseq = {e: 0 for e in CENG}
        for o in ops:
            if o["bar"]:
                o["floor"] = (dict(ccnt), dict(dcnt))
                continue
            if o["dma"]:
                slot = dseq[o["eng"]] % NQ
                dseq[o["eng"]] += 1
                key = (o["eng"], slot)
                o["qkey"] = key
                o["prev"] = dcnt.get(key, 0) * 16
                dcnt[key] = dcnt.get(key, 0) + 1
                o["cval"] = dcnt[key] * 16
            elif o["inc"]:
                ccnt[o["eng"]] += 1
                o["cval"] = ccnt[o["eng"]]

        nc = self.nc
        csem = {e: nc.alloc_semaphore(f"c_{e}") for e in CENG}
        qsem = {(e, s): nc.alloc_semaphore(f"q_{e}{s}") for e in CENG for s in range(NQ)}

        def emit_engine(e_obj, ename):
            wm_c = {x: 0 for x in CENG}
            wm_d = {}
            drained_upto = -1
            last_own = -1
            pend_floor = None

            def wait_d(key, v):
                if v > wm_d.get(key, 0):
                    e_obj.wait_ge(qsem[key], v)
                    wm_d[key] = v

            def do_floor():
                nonlocal drained_upto, pend_floor
                fc, fd = pend_floor
                for x in CENG:
                    if fc[x] > wm_c[x]:
                        e_obj.wait_ge(csem[x], fc[x])
                        wm_c[x] = fc[x]
                for key, n in fd.items():
                    wait_d(key, n * 16)
                drained_upto = last_own
                pend_floor = None

            for o in ops:
                if o["bar"]:
                    pend_floor = o["floor"]
                    continue
                if o["eng"] != ename:
                    continue
                if pend_floor is not None:
                    do_floor()
                need_drain = False
                for d in sorted(o["deps"]):
                    p = ops[d]
                    if p["dma"]:
                        wait_d(p["qkey"], p["cval"])
                    elif p["eng"] == ename:
                        if ename != "tensor" and p["i"] > drained_upto:
                            need_drain = True
                    else:
                        v = p["cval"]
                        if v > wm_c[p["eng"]]:
                            e_obj.wait_ge(csem[p["eng"]], v)
                            wm_c[p["eng"]] = v
                if o["dma"] and o["prev"] > 0:
                    wait_d(o["qkey"], o["prev"])
                if need_drain:
                    e_obj.drain()
                    drained_upto = last_own
                try:
                    inst = o["fn"](e_obj)
                except Exception:
                    import traceback as _tb
                    print("FAILING OP idx", o["i"], "eng", o["eng"])
                    _tb.print_exc()
                    raise
                if o["dma"]:
                    inst.then_inc(qsem[o["qkey"]], 16)
                elif o["inc"]:
                    inst.then_inc(csem[o["eng"]], 1)
                last_own = o["i"]
            if pend_floor is not None:
                do_floor()

        with nc.Block() as block:
            for ename in CENG:
                getattr(block, ename)(lambda e, ename=ename: emit_engine(e, ename))


def tt(em, eng, out, in0, in1, op, wkey=None):
    em.op(eng, lambda e, out=out, in0=in0, in1=in1, op=op: e.tensor_tensor(out, in0, in1, op),
          reads=[in0, in1], writes=[wkey if wkey is not None else out])


def ts(em, eng, out, in0, s1, s2, op0, op1=None, wkey=None):
    def fn(e, out=out, in0=in0, s1=s1, s2=s2, op0=op0, op1=op1):
        if op1 is None:
            return e.tensor_single_scalar(out, in0, s1, op0)
        return e.tensor_scalar(out, in0, s1, s2, op0, op1)
    rd = [in0]
    for s in (s1, s2):
        if hasattr(s, "tensor"):
            rd.append(s)
    em.op(eng, fn, reads=rd, writes=[wkey if wkey is not None else out])


def stt(em, eng, out, in0, scalar, in1, op0, op1, wkey=None):
    rd = [in0, in1]
    if hasattr(scalar, "tensor"):
        rd.append(scalar)
    em.op(eng, lambda e, out=out, in0=in0, scalar=scalar, in1=in1, op0=op0, op1=op1:
          e.scalar_tensor_tensor(out, in0, scalar, in1, op0, op1),
          reads=rd, writes=[wkey if wkey is not None else out])


def act(em, out, in_, func, bias=0.0, scale=1.0, wkey=None):
    rd = [in_]
    if hasattr(bias, "tensor"):
        rd.append(bias)
    em.op("scalar", lambda e, out=out, in_=in_, func=func, bias=bias, scale=scale:
          e.activation(out, in_, func, bias=bias, scale=scale),
          reads=rd, writes=[wkey if wkey is not None else out])


def mm(em, out, lhsT, rhs, start=True, stop=True):
    em.op("tensor", lambda e, out=out, lhsT=lhsT, rhs=rhs, start=start, stop=stop:
          e.matmul(out, lhsT, rhs, start=start, stop=stop),
          reads=[lhsT, rhs], writes=[out])


def transp(em, out, in_, ident):
    em.op("tensor", lambda e, out=out, in_=in_, ident=ident: e.transpose(out, in_, ident),
          reads=[in_, ident], writes=[out])


def copy(em, eng, out, in_, wkey=None):
    em.op(eng, lambda e, out=out, in_=in_: e.tensor_copy(out, in_),
          reads=[in_], writes=[wkey if wkey is not None else out])


def scopy(em, out, in_, wkey=None):
    em.op("scalar", lambda e, out=out, in_=in_: e.copy(out, in_),
          reads=[in_], writes=[wkey if wkey is not None else out])


def memset(em, eng, ap_, val, wkey):
    em.op(eng, lambda e, ap_=ap_, val=val: e.memset(ap_, val), writes=[wkey])


def reduce(em, out, in_, op, eng="vector", wkey=None):
    em.op(eng, lambda e, out=out, in_=in_, op=op: e.tensor_reduce(out, in_, AX.X, op),
          reads=[in_], writes=[wkey if wkey is not None else out])


def recip(em, out, in_, wkey=None):
    em.op("vector", lambda e, out=out, in_=in_: e.reciprocal(out, in_),
          reads=[in_], writes=[wkey if wkey is not None else out])


def scan(em, out, d0, d1, init, op0, op1):
    em.op("vector", lambda e, out=out, d0=d0, d1=d1, init=init, op0=op0, op1=op1:
          e.tensor_tensor_scan(out, d0, d1, init, op0, op1),
          reads=[d0, d1], writes=[out])


class Ctx:
    pass


def setup_common(nc, em, ctx, E, T):
    ctx.T = T
    eye_d = nc.inline_tensor(np.eye(128).astype(BF16NP), "eye_bf_d")
    ctx.eye = E(nc.sbuf_tensor("eye", [128, 128], BF16))
    em.dma("sync", out=ctx.eye[:], in_=eye_d.ap(), writes=[ctx.eye])
    ones_d = nc.inline_tensor(np.ones((1, 128), np.float32), "ones_f32_d")
    ctx.ones_f = E(nc.sbuf_tensor("ones_f", [1, 128], F32))
    em.dma("sync", out=ctx.ones_f[:], in_=ones_d.ap(), writes=[ctx.ones_f])
    onesb_d = nc.inline_tensor(np.ones((128, 1), BF16NP), "ones_col_bf_d")
    ctx.ones_col_bf = E(nc.sbuf_tensor("ones_col_bf", [128, 1], BF16))
    em.dma("sync", out=ctx.ones_col_bf[:], in_=onesb_d.ap(), writes=[ctx.ones_col_bf])
    ones1b_d = nc.inline_tensor(np.ones((1, 128), BF16NP), "ones_row_bf_d")
    ctx.ones_row_bf = E(nc.sbuf_tensor("ones_row_bf", [1, 128], BF16))
    em.dma("sync", out=ctx.ones_row_bf[:], in_=ones1b_d.ap(), writes=[ctx.ones_row_bf])
    ctx.zeros = E(nc.sbuf_tensor("zeros", [128, 512], F32))
    memset(em, "vector", ctx.zeros[:], 0.0, wkey=ctx.zeros)
    ctx.eps5 = E(nc.sbuf_tensor("eps5", [128, 1], F32))
    memset(em, "vector", ctx.eps5[:], 1e-5, wkey=ctx.eps5)
    ctx.eps6 = E(nc.sbuf_tensor("eps6", [128, 1], F32))
    memset(em, "vector", ctx.eps6[:], 1e-6, wkey=ctx.eps6)


def alloc_stream(nc, E, T):
    return [E(nc.sbuf_tensor(f"X{j}", [128, T], F32)) for j in range(8)]


def load_stream(em, ctx, x_fm_ap):
    for j in range(8):
        em.dma("sync", out=ctx.X[j][:], in_=x_fm_ap[128 * j:128 * (j + 1), :],
               writes=[ctx.X[j]])


def store_stream(em, ctx, y_fm_ap):
    for j in range(8):
        em.dma("gpsimd", out=y_fm_ap[128 * j:128 * (j + 1), :], in_=ctx.X[j][:],
               reads=[ctx.X[j]], writes=["OUT_y%d" % j])


def cast_weight(nc, em, ctx, E, name, dram_ap, nrow, ncol, stage=None):
    chunks = []
    for c in range(nrow // 128):
        tile = E(nc.sbuf_tensor(f"{name}_{c}", [128, ncol], BF16))
        em.dma("gpsimd", out=tile[:], in_=dram_ap[128 * c:128 * (c + 1), :],
               reads=[tile], writes=[tile])
        chunks.append(tile)
    return chunks


def dbg(em, ctx, name, tile, rows, cols):
    if not getattr(ctx, "debug", False) or name in ctx.dbg_done:
        return
    ctx.dbg_done.add(name)
    d = em.nc.dram_tensor("dbg_" + name, [rows, cols], F32, kind="ExternalOutput").ap()
    t = ctx.dbg_stage
    copy(em, "vector", t[0:rows, 0:cols], tile, wkey=t)
    em.dma("gpsimd", out=d, in_=t[0:rows, 0:cols], reads=[t], writes=["DBG" + name])


def build_phase1(nc, em, ctx, E, inp):
    T = ctx.T
    TT_ = 256
    L = 64
    NT = T // TT_
    NCH = TT_ // L
    NSUB = TT_ // 128

    P = [E(nc.psum_tensor(f"p1P{i}", [128, 512], F32)) for i in range(6)]
    P7 = E(nc.psum_tensor("p1P7", [128, 1024], BF16))

    tm_w1 = cast_weight(nc, em, ctx, E, "tmw1", inp["tm_w1"], 1024, 160)
    td_w1 = cast_weight(nc, em, ctx, E, "tdw1", inp["td_w1"], 1024, 64)
    tm_w2 = []
    for f in range(5):
        tile = E(nc.sbuf_tensor(f"tmw2_{f}", [32, 1024], BF16))
        em.dma("gpsimd", out=tile[:], in_=inp["tm_w2"][f], reads=[tile], writes=[tile])
        tm_w2.append(tile)
    td_w2 = E(nc.sbuf_tensor("tdw2", [64, 1024], BF16))
    em.dma("gpsimd", out=td_w2[:], in_=inp["td_w2"], reads=[td_w2], writes=[td_w2])

    pv = E(nc.sbuf_tensor("p1pv", [128, 80], F32))
    em.dma("sync", out=pv[:], in_=inp["pvec"], writes=[pv])

    def pvcol(idx, j, sl=None):
        if sl is not None:
            return pv[sl, idx * 8 + j:idx * 8 + j + 1]
        return pv[:, idx * 8 + j:idx * 8 + j + 1]
    IX, IW, IK, IV, IR, IG, ITD, ITF, ILW, ILB = range(10)

    lnrow_st = E(nc.sbuf_tensor("p1lnr", [1, 1024], BF16))
    ones164 = E(nc.sbuf_tensor("ones164", [1, 64], BF16))
    memset(em, "vector", ones164[:], 1.0, wkey=ones164)
    lnw_b = E(nc.sbuf_tensor("lnw_b", [64, 1024], BF16))
    lnb_b = E(nc.sbuf_tensor("lnb_b", [64, 1024], BF16))
    for dst, nm in ((lnw_b, "ln_w"), (lnb_b, "ln_b")):
        em.dma("gpsimd", out=lnrow_st[0:1, :], in_=inp[nm][None, :],
               reads=[lnrow_st], writes=[lnrow_st])
        mm(em, P[0][0:64, 0:512], ones164[:], lnrow_st[0:1, 0:512])
        mm(em, P[1][0:64, 0:512], ones164[:], lnrow_st[0:1, 512:1024])
        copy(em, "vector", dst[:, 0:512], P[0][0:64, 0:512], wkey=dst)
        copy(em, "vector", dst[:, 512:1024], P[1][0:64, 0:512], wkey=dst)

    mu_d = nc.inline_tensor(np.triu(np.ones((L, L), np.float32), 1).astype(BF16NP),
                            "maskU64")
    maskU = E(nc.sbuf_tensor("maskU", [L, L], BF16))
    em.dma("sync", out=maskU[:], in_=mu_d.ap(), writes=[maskU])

    em.barrier()

    xt = [E(nc.sbuf_tensor(f"p1x{j}", [128, TT_], F32)) for j in range(8)]
    xn_bf = [E(nc.sbuf_tensor(f"xn{j}", [128, TT_], BF16)) for j in range(8)]
    sx_bf = [E(nc.sbuf_tensor(f"sx{j}", [128, TT_], BF16)) for j in range(8)]
    u_p = [E(nc.sbuf_tensor(f"up{j}", [128, TT_], BF16)) for j in range(8)]
    u_g = [E(nc.sbuf_tensor(f"ug{j}", [128, TT_], BF16)) for j in range(8)]
    xn_last = E(nc.sbuf_tensor("xnlast", [128, 8], BF16))
    memset(em, "vector", xn_last[:], 0.0, wkey=xn_last)
    sq2 = [E(nc.sbuf_tensor(f"p1sq{i}", [128, TT_], BF16)) for i in range(2)]
    tmpf = [E(nc.sbuf_tensor(f"p1tmp{i}", [128, TT_], F32)) for i in range(2)]
    lnrow = [E(nc.sbuf_tensor(f"p1row{i}", [1, TT_], F32)) for i in range(4)]
    tm_f = [E(nc.sbuf_tensor(f"tmf{f}", [32, TT_], BF16)) for f in range(5)]
    tdh = E(nc.sbuf_tensor("tdh", [64, TT_], BF16))
    exp_w = E(nc.sbuf_tensor("expw", [128, TT_], F32))
    negew = E(nc.sbuf_tensor("negew", [128, TT_], F32))
    c_rel = [E(nc.sbuf_tensor(f"crel{j}", [128, TT_], F32)) for j in range(8)]
    E1 = [E(nc.sbuf_tensor(f"E1_{j}", [128, TT_], BF16)) for j in range(8)]
    E2 = [E(nc.sbuf_tensor(f"E2_{j}", [128, TT_], BF16)) for j in range(8)]
    E4h = [E(nc.sbuf_tensor(f"E4h{h}", [64, NCH], F32)) for h in range(16)]
    r_t = [E(nc.sbuf_tensor(f"rt{h}", [64, TT_], BF16)) for h in range(16)]
    r_raw = [E(nc.sbuf_tensor(f"rr{h}", [64, TT_], BF16)) for h in range(16)]
    k_t = [E(nc.sbuf_tensor(f"kt{h}", [64, TT_], BF16)) for h in range(16)]
    ku = [E(nc.sbuf_tensor(f"kuv{h}", [64, TT_], BF16)) for h in range(16)]
    kh_tmp = [E(nc.sbuf_tensor(f"khtmp{i}", [64, 64], BF16)) for i in range(2)]
    khT = [E(nc.sbuf_tensor(f"khT{c}", [64, 1024], BF16)) for c in range(NCH)]
    v_tm = [E(nc.sbuf_tensor(f"vtm{c}", [64, 1024], BF16)) for c in range(NCH)]
    g_tm = [E(nc.sbuf_tensor(f"gtm{c}", [64, 1024], BF16)) for c in range(NCH)]
    S_f = [E(nc.sbuf_tensor(f"Sf{h}", [64, 64], F32)) for h in range(16)]
    S_b = [E(nc.sbuf_tensor(f"Sb{h}", [64, 64], BF16)) for h in range(16)]
    for h in range(16):
        memset(em, "vector", S_f[h][:], 0.0, wkey=S_f[h])
        memset(em, "vector", S_b[h][:], 0.0, wkey=S_b[h])
    At = [E(nc.sbuf_tensor(f"At{i}", [64, 64], BF16)) for i in range(2)]
    At2 = [E(nc.sbuf_tensor(f"At2{i}", [64, 64], F32)) for i in range(2)]
    y_sb = E(nc.sbuf_tensor("ysb", [64, 1024], F32))
    gn_sq = E(nc.sbuf_tensor("gnsq", [64, 1024], F32))
    gn_mu = E(nc.sbuf_tensor("gnmu", [64, 16], F32))
    gn_s2 = E(nc.sbuf_tensor("gns2", [64, 16], F32))
    gn_rs = E(nc.sbuf_tensor("gnrs", [64, 16], F32))
    y2 = [E(nc.sbuf_tensor(f"y2_{c}", [64, 1024], BF16)) for c in range(NCH)]
    y_fm = [E(nc.sbuf_tensor(f"yfm{j}", [128, TT_], BF16)) for j in range(8)]
    wbuf = [E(nc.sbuf_tensor(f"p1wb{i}", [128, 256], BF16)) for i in range(4)]
    wbig = [E(nc.sbuf_tensor(f"p1wB{i}", [128, 512], BF16)) for i in range(4)]

    def proj_std(wname, consumer):
        for pj in range(4):
            for kk in range(8):
                wb = wbuf[kk % 4]
                em.dma("gpsimd", out=wb[:],
                       in_=inp[wname][128 * kk:128 * (kk + 1), 256 * pj:256 * (pj + 1)],
                       reads=[wb], writes=[wb])
                mm(em, P[2][:, 0:TT_], wb[:, 0:128], u_p[kk][:],
                   start=(kk == 0), stop=(kk == 7))
                mm(em, P[3][:, 0:TT_], wb[:, 128:256], u_p[kk][:],
                   start=(kk == 0), stop=(kk == 7))
            consumer(2 * pj, P[2])
            consumer(2 * pj + 1, P[3])

    def proj_flip(wname, src, consumer):
        for nh in range(2):
            for kk in range(8):
                wb = wbig[kk % 4]
                em.dma("gpsimd", out=wb[:],
                       in_=inp[wname][128 * kk:128 * (kk + 1), 512 * nh:512 * (nh + 1)],
                       reads=[wb], writes=[wb])
                for s in range(NSUB):
                    mm(em, P[2 + s][:, 0:512], src[kk][:, 128 * s:128 * (s + 1)], wb[:],
                       start=(kk == 0), stop=(kk == 7))
            for s in range(NSUB):
                consumer(s, nh, P[2 + s])

    for it in range(NT):
        t0 = it * TT_
        if ctx.fused:
            for j in range(8):
                copy(em, "vector", xt[j][:], ctx.X[j][:, t0:t0 + TT_], wkey=xt[j])
        else:
            for j in range(8):
                em.dma("sync", out=xt[j][:], in_=ctx.x_in[128 * j:128 * (j + 1), t0:t0 + TT_],
                       reads=[xt[j]], writes=[xt[j]])
        for j in range(8):
            copy(em, "vector", xn_bf[j][:], xt[j][:], wkey=xn_bf[j])
            act(em, sq2[j % 2][:], xn_bf[j][:], AF.Square, wkey=sq2[j % 2])
            mm(em, P[4][0:1, 0:TT_], ctx.ones_col_bf[:], xn_bf[j][:],
               start=(j == 0), stop=(j == 7))
            mm(em, P[5][0:1, 0:TT_], ctx.ones_col_bf[:], sq2[j % 2][:],
               start=(j == 0), stop=(j == 7))
        mean, msq, var, rstd = lnrow
        ts(em, "vector", mean[:], P[4][0:1, 0:TT_], 1.0 / 1024, None, ALU.mult)
        ts(em, "vector", msq[:], P[5][0:1, 0:TT_], 1.0 / 1024, None, ALU.mult)
        tt(em, "vector", var[:], mean[:], mean[:], ALU.mult)
        tt(em, "vector", var[:], msq[:], var[:], ALU.subtract)
        act(em, rstd[:], var[:], AF.Sqrt, bias=ctx.eps5[0:1, 0:1], wkey=rstd)
        recip(em, rstd[:], rstd[:])
        mm(em, P[0][:, 0:TT_], ctx.ones_f[:], mean[:])
        mm(em, P[1][:, 0:TT_], ctx.ones_f[:], rstd[:])
        for j in range(8):
            tf = tmpf[j % 2]
            tt(em, "vector", tf[:], xt[j][:], P[0][:, 0:TT_], ALU.subtract, wkey=tf)
            tt(em, "vector", xn_bf[j][:], tf[:], P[1][:, 0:TT_], ALU.mult)
        dbg(em, ctx, "xn0", xn_bf[0][:], 128, TT_)
        for j in range(8):
            tt(em, "vector", sx_bf[j][:, 1:TT_], xn_bf[j][:, 0:TT_ - 1],
               xn_bf[j][:, 1:TT_], ALU.subtract, wkey=sx_bf[j])
            tt(em, "vector", sx_bf[j][:, 0:1], xn_last[:, j:j + 1],
               xn_bf[j][:, 0:1], ALU.subtract, wkey=sx_bf[j])
            copy(em, "vector", xn_last[:, j:j + 1], xn_bf[j][:, TT_ - 1:TT_],
                 wkey=xn_last)
        for j in range(8):
            stt(em, "vector", u_p[j][:], sx_bf[j][:], pvcol(IX, j), xn_bf[j][:],
                ALU.mult, ALU.add)
        for j in range(8):
            mm(em, P[2][0:96, 0:TT_], tm_w1[j][:, 0:96], u_p[j][:],
               start=(j == 0), stop=(j == 7))
            mm(em, P[3][0:64, 0:TT_], tm_w1[j][:, 96:160], u_p[j][:],
               start=(j == 0), stop=(j == 7))
        for f in range(3):
            act(em, tm_f[f][:], P[2][32 * f:32 * (f + 1), 0:TT_], AF.Tanh,
                wkey=tm_f[f])
        for f in range(2):
            act(em, tm_f[3 + f][:], P[3][32 * f:32 * (f + 1), 0:TT_], AF.Tanh,
                wkey=tm_f[3 + f])

        def emit_path(f_idx, maa_idx, dest):
            for j in range(8):
                pm = P[j % 2]
                mm(em, pm[:, 0:TT_], tm_w2[f_idx][:, 128 * j:128 * (j + 1)],
                   tm_f[f_idx][:])
                tf = tmpf[j % 2]
                ts(em, "vector", tf[:], pm[:, 0:TT_], pvcol(maa_idx, j), None,
                   ALU.add, wkey=tf)
                tt(em, "vector", tf[:], tf[:], sx_bf[j][:], ALU.mult, wkey=tf)
                tt(em, "vector", dest[j][:], tf[:], xn_bf[j][:], ALU.add)

        dbg(em, ctx, "sx0", sx_bf[0][:], 128, TT_)
        dbg(em, ctx, "tmf0", tm_f[0][:], 32, TT_)
        dbg(em, ctx, "tmf4", tm_f[4][:], 32, TT_)
        emit_path(0, IW, u_p)
        dbg(em, ctx, "uw0", u_p[0][:], 128, TT_)
        for j in range(8):
            mm(em, P[4][0:64, 0:TT_], td_w1[j][:, 0:64], u_p[j][:],
               start=(j == 0), stop=(j == 7))
        act(em, tdh[:], P[4][0:64, 0:TT_], AF.Tanh, wkey=tdh)
        for j in range(8):
            pm = P[j % 2]
            mm(em, pm[:, 0:TT_], td_w2[:, 128 * j:128 * (j + 1)], tdh[:])
            act(em, exp_w[:], pm[:, 0:TT_], AF.Exp, bias=pvcol(ITD, j), wkey=exp_w)
            ts(em, "vector", negew[:], exp_w[:], -1.0, None, ALU.mult, wkey=negew)
            scan(em, c_rel[j][:], negew[:], ctx.zeros[:, 0:TT_], 0.0,
                 ALU.add, ALU.add)
            for c in range(NCH - 1, 0, -1):
                cs = c * L
                ts(em, "vector", c_rel[j][:, cs:cs + L], c_rel[j][:, cs:cs + L],
                   c_rel[j][:, cs - 1:cs], None, ALU.subtract, wkey=c_rel[j])
            tf = tmpf[j % 2]
            tt(em, "vector", tf[:], c_rel[j][:], exp_w[:], ALU.add, wkey=tf)
            act(em, E1[j][:], tf[:], AF.Exp, wkey=E1[j])
            ts(em, "vector", tf[:], c_rel[j][:], -1.0, 70.0, ALU.mult, ALU.min,
               wkey=tf)
            act(em, E2[j][:], tf[:], AF.Exp, wkey=E2[j])
            em.op("scalar", lambda e, j=j: e.activation(
                E4h[2 * j][:, 0:NCH], c_rel[j][0:64, L - 1:TT_:L], AF.Exp),
                reads=[c_rel[j]], writes=[E4h[2 * j]])
            em.op("scalar", lambda e, j=j: e.activation(
                E4h[2 * j + 1][:, 0:NCH], c_rel[j][64:128, L - 1:TT_:L], AF.Exp),
                reads=[c_rel[j]], writes=[E4h[2 * j + 1]])

        dbg(em, ctx, "expw7", exp_w[:], 128, TT_)
        dbg(em, ctx, "crel0", c_rel[0][:], 128, TT_)
        dbg(em, ctx, "E1_0", E1[0][:], 128, TT_)
        dbg(em, ctx, "E2_0", E2[0][:], 128, TT_)
        dbg(em, ctx, "E4_0", E4h[0][:], 64, NCH)
        emit_path(1, IK, u_p)

        def k_consumer(j, pp):
            for half in range(2):
                h = 2 * j + half
                sl = slice(64 * half, 64 * (half + 1))
                tt(em, "vector", k_t[h][:], pp[sl, 0:TT_], E1[j][sl, :] if False
                   else E2[j][sl, :], ALU.mult)
                ts(em, "vector", ku[h][:], pp[sl, 0:TT_], pvcol(ITF, j, sl), None,
                   ALU.mult)
        proj_std("Wk", k_consumer)
        for c in range(NCH):
            for h in range(16):
                j, half = h // 2, h % 2
                kt_ = kh_tmp[h % 2]
                ts(em, "vector", kt_[:], k_t[h][:, c * L:(c + 1) * L],
                   E4h[h][:, c:c + 1], None, ALU.mult, wkey=kt_)
                transp(em, P7[0:64, 64 * h:64 * (h + 1)], kt_[:], ctx.eye[0:64, 0:64])
            copy(em, "vector", khT[c][:], P7[0:64, 0:1024], wkey=khT[c])

        dbg(em, ctx, "kt0", k_t[0][:], 64, TT_)
        dbg(em, ctx, "ku0", ku[0][:], 64, TT_)
        dbg(em, ctx, "khT0", khT[0][:], 64, 1024)
        emit_path(2, IV, u_p)

        def v_consumer(s, nh, pp):
            for half in range(2):
                c = 2 * s + half
                copy(em, "vector", v_tm[c][:, 512 * nh:512 * (nh + 1)],
                     pp[64 * half:64 * (half + 1), 0:512], wkey=v_tm[c])
        proj_flip("Wv", u_p, v_consumer)

        dbg(em, ctx, "vtm0", v_tm[0][:], 64, 1024)
        emit_path(3, IR, u_p)

        def r_consumer(j, pp):
            for half in range(2):
                h = 2 * j + half
                sl = slice(64 * half, 64 * (half + 1))
                tt(em, "vector", r_t[h][:], pp[sl, 0:TT_], E1[j][sl, :], ALU.mult)
                copy(em, "vector", r_raw[h][:], pp[sl, 0:TT_])
        proj_std("Wr", r_consumer)

        dbg(em, ctx, "rt0", r_t[0][:], 64, TT_)
        dbg(em, ctx, "rr0", r_raw[0][:], 64, TT_)
        emit_path(4, IG, u_g)

        def g_consumer(s, nh, pp):
            for half in range(2):
                c = 2 * s + half
                act(em, g_tm[c][:, 512 * nh:512 * (nh + 1)],
                    pp[64 * half:64 * (half + 1), 0:512], AF.Silu, wkey=g_tm[c])
        proj_flip("Wg", u_g, g_consumer)

        pA, pA2, pY, pS = P[0], P[1], P[2], P[3]
        for c in range(NCH):
            cs = c * L
            for h in range(16):
                j, half = h // 2, h % 2
                ia = h % 2
                mm(em, pA[0:64, 0:64], k_t[h][:, cs:cs + L], r_t[h][:, cs:cs + L])
                mm(em, pA2[0:64, 0:64], ku[h][:, cs:cs + L], r_raw[h][:, cs:cs + L])
                tt(em, "vector", At[ia][:], pA[0:64, 0:64], maskU[:], ALU.mult)
                tt(em, "vector", At2[ia][:], pA2[0:64, 0:64], ctx.eye[0:64, 0:64],
                   ALU.mult)
                tt(em, "vector", At[ia][:], At[ia][:], At2[ia][:], ALU.add)
                mm(em, pY[0:64, 0:64], At[ia][:], v_tm[c][:, 64 * h:64 * (h + 1)],
                   start=True, stop=False)
                mm(em, pY[0:64, 0:64], r_t[h][:, cs:cs + L], S_b[h][:],
                   start=False, stop=True)
                act(em, y_sb[:, 64 * h:64 * (h + 1)], pY[0:64, 0:64], AF.Identity,
                    scale=0.125, wkey=y_sb)
                mm(em, pS[0:64, 0:64], khT[c][:, 64 * h:64 * (h + 1)],
                   v_tm[c][:, 64 * h:64 * (h + 1)])
                stt(em, "vector", S_f[h][:], S_f[h][:],
                    E4h[h][:, c:c + 1], pS[0:64, 0:64],
                    ALU.mult, ALU.add)
                copy(em, "vector", S_b[h][:], S_f[h][:])
            act(em, gn_sq[:], y_sb[:], AF.Square, wkey=gn_sq)
            for h in range(16):
                reduce(em, gn_mu[:, h:h + 1], y_sb[:, 64 * h:64 * (h + 1)], ALU.add,
                       wkey=gn_mu)
                reduce(em, gn_s2[:, h:h + 1], gn_sq[:, 64 * h:64 * (h + 1)], ALU.add,
                       wkey=gn_s2)
            ts(em, "vector", gn_mu[:], gn_mu[:], 1.0 / 64, None, ALU.mult)
            ts(em, "vector", gn_s2[:], gn_s2[:], 1.0 / 64, None, ALU.mult)
            tt(em, "vector", gn_rs[:], gn_mu[:], gn_mu[:], ALU.mult)
            tt(em, "vector", gn_rs[:], gn_s2[:], gn_rs[:], ALU.subtract)
            act(em, gn_rs[:], gn_rs[:], AF.Sqrt, bias=ctx.eps5[0:64, 0:1], wkey=gn_rs)
            recip(em, gn_rs[:], gn_rs[:])
            for h in range(16):
                ts(em, "vector", y_sb[:, 64 * h:64 * (h + 1)],
                   y_sb[:, 64 * h:64 * (h + 1)], gn_mu[:, h:h + 1],
                   gn_rs[:, h:h + 1], ALU.subtract, ALU.mult, wkey=y_sb)
            if c == 0:
                dbg(em, ctx, "ysb0", y_sb[:], 64, 1024)
            tt(em, "vector", y2[c][:], y_sb[:], lnw_b[:], ALU.mult)
            tt(em, "vector", y2[c][:], y2[c][:], lnb_b[:], ALU.add)
            tt(em, "vector", y2[c][:], y2[c][:], g_tm[c][:], ALU.mult)

        dbg(em, ctx, "gtm0", g_tm[0][:], 64, 1024)
        dbg(em, ctx, "y2_0", y2[0][:], 64, 1024)
        for j in range(8):
            for c in range(NCH):
                transp(em, P7[0:128, 64 * c:64 * (c + 1)],
                       y2[c][:, 128 * j:128 * (j + 1)], ctx.eye[0:64, 0:64])
            copy(em, "vector", y_fm[j][:], P7[0:128, 0:TT_], wkey=y_fm[j])
        for pj in range(4):
            for kk in range(8):
                wb = wbuf[kk % 4]
                em.dma("gpsimd", out=wb[:],
                       in_=inp["Wo"][128 * kk:128 * (kk + 1), 256 * pj:256 * (pj + 1)],
                       reads=[wb], writes=[wb])
                mm(em, P[4][:, 0:TT_], wb[:, 0:128], y_fm[kk][:],
                   start=(kk == 0), stop=(kk == 7))
                mm(em, P[5][:, 0:TT_], wb[:, 128:256], y_fm[kk][:],
                   start=(kk == 0), stop=(kk == 7))
            for half in range(2):
                j = 2 * pj + half
                tt(em, "vector", xt[j][:], P[4 + half][:, 0:TT_], xt[j][:], ALU.add,
                   wkey=xt[j])
        if ctx.fused:
            for j in range(8):
                copy(em, "vector", ctx.X[j][:, t0:t0 + TT_], xt[j][:], wkey=ctx.X[j])
        else:
            for j in range(8):
                em.dma("gpsimd", out=ctx.y_out[128 * j:128 * (j + 1), t0:t0 + TT_],
                       in_=xt[j][:], reads=[xt[j]], writes=[f"OUTY{j}"])

    em.barrier()


def build_phase2(nc, em, ctx, E, inp):
    T = ctx.T
    NTW = T // 512
    NE = int(os.environ.get('MOENE', '160'))

    pg = [E(nc.psum_tensor(f"p2g{i}", [128, 512], F32)) for i in range(4)]
    pz = [E(nc.psum_tensor(f"p2z{i}", [128, 512], F32)) for i in range(2)]
    pxtra = E(nc.psum_tensor("p2xtra", [128, 512], F32))
    prow = pxtra
    pbc = pxtra


    eye32_d = nc.inline_tensor(np.eye(128).astype(np.float32), "eye_f32_d")
    eye_f = E(nc.sbuf_tensor("eye_f", [128, 128], F32))
    em.dma("sync", out=eye_f[:], in_=eye32_d.ap(), writes=[eye_f])

    rw = []
    for c in range(8):
        t = E(nc.sbuf_tensor(f"p2rw{c}", [128, 160], F32))
        em.dma("sync", out=t[:], in_=inp["router_wt"][128 * c:128 * (c + 1), :],
               writes=[t])
        rw.append(t)
    ebias = E(nc.sbuf_tensor("p2eb", [128, 2], F32))
    em.dma("sync", out=ebias[:], in_=inp["ebias"], writes=[ebias])

    xn_bf = [E(nc.sbuf_tensor(f"p2xn{j}", [128, T], BF16)) for j in range(8)]
    _sp = getattr(ctx, "moe_sparse", False)
    RNK = [E(nc.sbuf_tensor(f"p2rnk{i}", [128, 160], BF16)) for i in range(T // 128)] if _sp else None
    CTM = [E(nc.sbuf_tensor(f"p2ctm{i}", [128, 160], BF16)) for i in range(T // 128)] if _sp else None
    comb_lo = E(nc.sbuf_tensor("p2clo", [128, T], BF16))
    comb_hi = E(nc.sbuf_tensor("p2chi", [32, T], BF16))
    mcol = E(nc.sbuf_tensor("p2mcol", [128, 1], F32))
    tsum = E(nc.sbuf_tensor("p2tsum", [128, 1], F32))
    ctm = E(nc.sbuf_tensor("p2ctm", [128, 160], BF16))
    sq2 = [E(nc.sbuf_tensor(f"p2sq{i}", [128, 512], BF16)) for i in range(2)]
    tmpf = [E(nc.sbuf_tensor(f"p2tmp{i}", [128, 512], F32)) for i in range(2)]
    lnrow = [E(nc.sbuf_tensor(f"p2row{i}", [1, 512], F32)) for i in range(4)]
    h_bf = [E(nc.sbuf_tensor(f"p2h{i}", [128, 512], BF16)) for i in range(4)]
    xnf = [E(nc.sbuf_tensor(f"p2xnf{i}", [128, 512], F32)) for i in range(2)]
    rowb = E(nc.sbuf_tensor("p2rowb", [1, 512], BF16))
    NSET = 1 if getattr(ctx, "moe_sparse", False) else 2
    wg_bf = [[E(nc.sbuf_tensor(f"p2wg{b}_{c}", [128, 512], BF16)) for c in range(8)]
             for b in range(NSET)]
    wd_bf = [[E(nc.sbuf_tensor(f"p2wd{b}_{c}", [128, 1024], BF16)) for c in range(4)]
             for b in range(NSET)]
    from contextlib import ExitStack as _ES
    _pre = _ES()
    E2 = _pre.enter_context
    pbf = E2(nc.psum_tensor("p2bf", [128, 1024], BF16))
    probs_lo = E2(nc.sbuf_tensor("p2plo", [128, T], F32))
    probs_hi = E2(nc.sbuf_tensor("p2phi", [32, T], F32))
    ptm = [E2(nc.sbuf_tensor(f"p2ptm{i}", [128, 160], F32)) for i in range(T // 128)]
    work = E2(nc.sbuf_tensor("p2work", [128, 160], F32))
    sel = E2(nc.sbuf_tensor("p2sel", [128, 160], F32))
    msk = E2(nc.sbuf_tensor("p2msk", [128, 160], F32))

    for w in range(NTW):
        t0 = w * 512
        for j in range(8):
            copy(em, "vector", xn_bf[j][:, t0:t0 + 512], ctx.X[j][:, t0:t0 + 512],
                 wkey=xn_bf[j])
            act(em, sq2[j % 2][:], xn_bf[j][:, t0:t0 + 512], AF.Square,
                wkey=sq2[j % 2])
            mm(em, pg[0][0:1, 0:512], ctx.ones_col_bf[:], xn_bf[j][:, t0:t0 + 512],
               start=(j == 0), stop=(j == 7))
            mm(em, pg[1][0:1, 0:512], ctx.ones_col_bf[:], sq2[j % 2][:],
               start=(j == 0), stop=(j == 7))
        mean, msq, var, rstd = lnrow
        ts(em, "vector", mean[:], pg[0][0:1, 0:512], 1.0 / 1024, None, ALU.mult)
        ts(em, "vector", msq[:], pg[1][0:1, 0:512], 1.0 / 1024, None, ALU.mult)
        tt(em, "vector", var[:], mean[:], mean[:], ALU.mult)
        tt(em, "vector", var[:], msq[:], var[:], ALU.subtract)
        act(em, rstd[:], var[:], AF.Sqrt, bias=ctx.eps5[0:1, 0:1], wkey=rstd)
        recip(em, rstd[:], rstd[:])
        mm(em, pg[2][:, 0:512], ctx.ones_f[:], mean[:])
        mm(em, pg[3][:, 0:512], ctx.ones_f[:], rstd[:])
        for j in range(8):
            tf = tmpf[j % 2]
            xf = xnf[j % 2]
            tt(em, "vector", tf[:], ctx.X[j][:, t0:t0 + 512], pg[2][:, 0:512],
               ALU.subtract, wkey=tf)
            tt(em, "vector", xf[:], tf[:], pg[3][:, 0:512], ALU.mult, wkey=xf)
            copy(em, "vector", xn_bf[j][:, t0:t0 + 512], xf[:], wkey=xn_bf[j])
            mm(em, pg[0][:, 0:512], rw[j][:, 0:128], xf[:],
               start=(j == 0), stop=(j == 7))
            mm(em, pg[1][0:32, 0:512], rw[j][:, 128:160], xf[:],
               start=(j == 0), stop=(j == 7))
        act(em, probs_lo[:, t0:t0 + 512], pg[0][:, 0:512], AF.Sigmoid,
            bias=ebias[:, 0:1], wkey=probs_lo)
        act(em, probs_hi[:, t0:t0 + 512], pg[1][0:32, 0:512], AF.Sigmoid,
            bias=ebias[0:32, 1:2], wkey=probs_hi)

    for i in range(T // 128):
        c0 = i * 128
        transp(em, pg[0][:, 0:128], probs_lo[:, c0:c0 + 128], eye_f[:])
        transp(em, pg[0][:, 128:160], probs_hi[:, c0:c0 + 128], eye_f[0:32, 0:32])
        copy(em, "vector", ptm[i][:, 0:128], pg[0][0:128, 0:128], wkey=ptm[i])
        copy(em, "vector", ptm[i][:, 128:160], pg[0][0:128, 128:160], wkey=ptm[i])
    for i in range(T // 128):
        copy(em, "vector", work[:], ptm[i][:], wkey=work)
        memset(em, "vector", sel[:], 0.0, wkey=sel)
        if _sp:
            memset(em, "vector", RNK[i][:], 0.0, wkey=RNK[i])
        for it8 in range(8):
            reduce(em, mcol[:], work[:], ALU.max, wkey=mcol)
            ts(em, "vector", msk[:], work[:], mcol[:], None, ALU.is_ge, wkey=msk)
            tt(em, "vector", sel[:], sel[:], msk[:], ALU.add)
            if _sp:
                stt(em, "vector", RNK[i][:], msk[:], float(it8 + 1), RNK[i][:],
                    ALU.mult, ALU.add, wkey=RNK[i])
            tt(em, "vector", msk[:], msk[:], work[:], ALU.mult)
            tt(em, "vector", work[:], work[:], msk[:], ALU.subtract)
        tt(em, "vector", sel[:], sel[:], ptm[i][:], ALU.mult)
        reduce(em, tsum[:], sel[:], ALU.add, wkey=tsum)
        ts(em, "vector", tsum[:], tsum[:], 1e-9, None, ALU.add, wkey=tsum)
        recip(em, tsum[:], tsum[:])
        if _sp:
            ts(em, "vector", CTM[i][:], sel[:], tsum[:], 2.5, ALU.mult, ALU.mult,
               wkey=CTM[i])
            copy(em, "vector", ctm[:], CTM[i][:], wkey=ctm)
        else:
            ts(em, "vector", ctm[:], sel[:], tsum[:], 2.5, ALU.mult, ALU.mult,
               wkey=ctm)
        transp(em, pbf[0:128, 0:128], ctm[:, 0:128], ctx.eye[:])
        transp(em, pbf[0:32, 128:256], ctm[:, 128:160], ctx.eye[:])
        copy(em, "vector", comb_lo[:, i * 128:(i + 1) * 128], pbf[0:128, 0:128],
             wkey=comb_lo)
        copy(em, "vector", comb_hi[:, i * 128:(i + 1) * 128], pbf[0:32, 128:256],
             wkey=comb_hi)

    _pre.close()
    em.barrier()
    pg_x = E(nc.psum_tensor("p2gx", [128, 512], F32))
    cwsb = [E(nc.sbuf_tensor(f"p2cw{w}", [128, 512], BF16)) for w in range(NTW)]
    h16 = [E(nc.sbuf_tensor(f"p2h16_{i}", [128, 512], BF16)) for i in range(4 * NTW)]
    for c in range(8):
        em.dma("gpsimd", out=wg_bf[0][c][:],
               in_=inp["wgs_t"][128 * c:128 * (c + 1), :],
               reads=[wg_bf[0][c]], writes=[wg_bf[0][c]])
    for c in range(4):
        em.dma("gpsimd", out=wd_bf[0][c][:],
               in_=inp["wds_t"][128 * c:128 * (c + 1), :],
               reads=[wd_bf[0][c]], writes=[wd_bf[0][c]])
    for w in range(NTW):
        t0 = w * 512
        for mi in range(4):
            for kk in range(8):
                mm(em, pg[mi][:, 0:512], wg_bf[0][kk][:, 128 * mi:128 * (mi + 1)],
                   xn_bf[kk][:, t0:t0 + 512], start=(kk == 0), stop=(kk == 7))
            act(em, h_bf[mi][:], pg[mi][:, 0:512], AF.Silu, wkey=h_bf[mi])
        for j in range(8):
            for kk in range(4):
                mm(em, pz[j % 2][:, 0:512], wd_bf[0][kk][:, 128 * j:128 * (j + 1)],
                   h_bf[kk][:], start=(kk == 0), stop=(kk == 3))
            tt(em, "vector", ctx.X[j][:, t0:t0 + 512], pz[j % 2][:, 0:512],
               ctx.X[j][:, t0:t0 + 512], ALU.add, wkey=ctx.X[j])

    if getattr(ctx, "moe_sparse", False):
        _moe_sparse(nc, em, ctx, E, inp, xn_bf, CTM, RNK, pg, pz, prow, pbf, eye_f,
                    tmpf, lnrow)
        em.barrier()
        return

    def load_expert(e, b):
        for c in range(8):
            em.dma("gpsimd", out=wg_bf[b][c][:],
                   in_=inp["wg_t"][e, 128 * c:128 * (c + 1), :],
                   reads=[wg_bf[b][c]], writes=[wg_bf[b][c]])
        for c in range(4):
            em.dma("gpsimd", out=wd_bf[b][c][:],
                   in_=inp["wd_t"][e, 128 * c:128 * (c + 1), :],
                   reads=[wd_bf[b][c]], writes=[wd_bf[b][c]])

    NWH = min(NTW, 2)
    NGP = (NTW + NWH - 1) // NWH

    def emit_expert(e, b):
        for w in range(NTW):
            t0 = w * 512
            if e < 128:
                mm(em, prow[0:1, 0:512], ctx.eye[:, e:e + 1], comb_lo[:, t0:t0 + 512])
            else:
                mm(em, prow[0:1, 0:512], ctx.eye[0:32, e - 128:e - 127],
                   comb_hi[:, t0:t0 + 512])
            copy(em, "vector", rowb[:], prow[0:1, 0:512], wkey=rowb)
            mm(em, pbc[:, 0:512], ctx.ones_row_bf[:], rowb[:])
            copy(em, "vector", cwsb[w][:], pbc[:, 0:512], wkey=cwsb[w])
        for wp in range(NGP):
            for kk in range(8):
                for mi in range(4):
                    for wh in range(NWH):
                        w = wp * NWH + wh
                        pgb = pg[mi] if wh == 0 else (pz[mi % 2] if mi < 2 else
                                                      (pxtra if mi == 2 else pg_x))
                        mm(em, pgb[:, 0:512],
                           wg_bf[b][kk][:, 128 * mi:128 * (mi + 1)],
                           xn_bf[kk][:, 512 * w:512 * (w + 1)],
                           start=(kk == 0), stop=(kk == 7))
            for mi in range(4):
                for wh in range(NWH):
                    w = wp * NWH + wh
                    pgb = pg[mi] if wh == 0 else (pz[mi % 2] if mi < 2 else
                                                  (pxtra if mi == 2 else pg_x))
                    act(em, h16[w * 4 + mi][:], pgb[:, 0:512], AF.Silu,
                        wkey=h16[w * 4 + mi])
                    tt(em, "vector", h16[w * 4 + mi][:], h16[w * 4 + mi][:],
                       cwsb[w][:], ALU.mult)
        for j in range(8):
            for kk in range(4):
                for w in range(NTW):
                    pzb = (pg[w] if NTW == 4 else pg[w % len(pg)])
                    mm(em, pzb[:, 0:512], wd_bf[b][kk][:, 128 * j:128 * (j + 1)],
                       h16[w * 4 + kk][:], start=(kk == 0), stop=(kk == 3))
            for w in range(NTW):
                pzb = (pg[w] if NTW == 4 else pg[w % len(pg)])
                tt(em, "vector", ctx.X[j][:, 512 * w:512 * (w + 1)], pzb[:, 0:512],
                   ctx.X[j][:, 512 * w:512 * (w + 1)], ALU.add, wkey=ctx.X[j])

    load_expert(0, 0)
    for e in range(NE):
        b = e % 2
        emit_expert(e, b)
        if e + 1 < NE:
            load_expert(e + 1, 1 - b)
def _moe_sparse(nc, em, ctx, E, inp, xn_bf, CTM, RNK, pg, pz, prow, pbf, eye_f,
                tmpf, lnrow):
    from concourse.expressions import RuntimeValue
    from concourse.bass_core_rust import IndirectOffsetOnAxis
    T = ctx.T
    NI = T // 128
    NB = T * 8 // 128 + 160
    BPC = (NB + 127) // 128
    U32 = mybir.dt.uint32
    I32 = mybir.dt.int32

    xn_tm_d = nc.dram_tensor("sp_xntm", [T, 1024], BF16, kind="Internal").ap()
    TAB = nc.dram_tensor("sp_tab", [NB * 128, 4], F32, kind="Internal").ap()
    EBD = nc.dram_tensor("sp_ebd", [BPC * 128], I32, kind="Internal").ap()
    Zacc = nc.dram_tensor("sp_zacc", [T + 128, 1024], F32, kind="Internal").ap()

    tabinit = np.zeros((NB * 128, 4), np.float32)
    tabinit[:, 1] = T + (np.arange(NB * 128) % 128)
    tabinit_d = nc.inline_tensor(tabinit, "sp_tabinit_d")
    tri_d = nc.inline_tensor(
        np.tril(np.ones((128, 128), np.float32), -1).T.astype(BF16NP), "sp_tri_d")
    iot_d = nc.inline_tensor(np.arange(128, dtype=np.float32)[:, None], "sp_iot_d")
    bcol_np = 128.0 * (np.arange(128)[:, None] + 128 * np.arange(BPC)[None, :])
    bcol_d = nc.inline_tensor(bcol_np.astype(np.float32), "sp_bcol_d")

    TRI = E(nc.sbuf_tensor("spTRI", [128, 128], BF16))
    em.dma("sync", out=TRI[:], in_=tri_d.ap(), writes=[TRI])
    IOT = E(nc.sbuf_tensor("spIOT", [128, 1], F32))
    em.dma("sync", out=IOT[:], in_=iot_d.ap(), writes=[IOT])
    BCOL = E(nc.sbuf_tensor("spBCOL", [128, BPC], F32))
    em.dma("sync", out=BCOL[:], in_=bcol_d.ap(), writes=[BCOL])
    em.dma("sync", out=TAB, in_=tabinit_d.ap(), writes=["TAB"])

    selb = E(nc.sbuf_tensor("spselb", [128, 160], BF16))
    cnts = E(nc.sbuf_tensor("spcnts", [1, 160 * NI], F32))
    offs = E(nc.sbuf_tensor("spoffs", [1, 160 * NI], F32))
    rowt = [E(nc.sbuf_tensor(f"sprow{i}", [1, 160], F32)) for i in range(4)]
    possb = E(nc.sbuf_tensor("sppos", [128, 160], F32))
    Dsb = E(nc.sbuf_tensor("spD", [128, 160], F32))
    mskb = E(nc.sbuf_tensor("spmsk", [128, 160], F32))
    t160 = E(nc.sbuf_tensor("spt160", [128, 160], F32))
    tokf = E(nc.sbuf_tensor("sptok", [128, 1], F32))
    fcol = E(nc.sbuf_tensor("spfc", [128, 1], F32))
    cwcol = E(nc.sbuf_tensor("spcw", [128, 1], F32))
    val4 = E(nc.sbuf_tensor("spval", [128, 4], F32))
    ufd = E(nc.sbuf_tensor("spufd", [128, 1], U32))
    ebf = E(nc.sbuf_tensor("spebf", [128, BPC], F32))
    ebi = E(nc.sbuf_tensor("spebi", [128, BPC], I32))
    ebrow = E(nc.sbuf_tensor("spebr", [1, BPC * 128], I32))
    xtm_st = [E(nc.sbuf_tensor(f"spxst{i}", [128, 1024], BF16)) for i in range(2)]
    wg_sb = [E(nc.sbuf_tensor(f"spwg{i}", [128, 4096], BF16)) for i in range(2)]
    wd_sb = [E(nc.sbuf_tensor(f"spwd{i}", [128, 4096], BF16)) for i in range(2)]
    xg_tm = [E(nc.sbuf_tensor(f"spxg{i}", [128, 1024], BF16)) for i in range(2)]
    xg_fm = [E(nc.sbuf_tensor(f"spxf{i}", [128, 1024], BF16)) for i in range(2)]
    hb = [E(nc.sbuf_tensor(f"sphb{i}", [128, 512], BF16)) for i in range(2)]
    z_bf = [E(nc.sbuf_tensor(f"spzb{i}", [128, 1024], BF16)) for i in range(2)]
    tab4 = [E(nc.sbuf_tensor(f"sptab{i}", [128, 4], F32)) for i in range(2)]
    utok = [E(nc.sbuf_tensor(f"sput{i}", [128, 1], U32)) for i in range(2)]
    uzd = [E(nc.sbuf_tensor(f"spuz{i}", [128, 1], U32)) for i in range(2)]

    wgflat = inp["wg_t"].rearrange("e c i -> (e c) i")
    wdflat = inp["wd_t"].rearrange("e i c -> (e i) c")

    for i in range(NI):
        st = xtm_st[i % 2]
        for j in range(8):
            transp(em, pbf[:, 128 * j:128 * (j + 1)],
                   xn_bf[j][:, 128 * i:128 * (i + 1)], ctx.eye[:])
        copy(em, "vector", st[:], pbf[0:128, 0:1024], wkey=st)
        em.dma("gpsimd", out=xn_tm_d[128 * i:128 * (i + 1), :], in_=st[:],
               reads=[st], writes_accum=["XNTM"])

    for i in range(NI):
        ts(em, "vector", selb[:], RNK[i][:], 0.5, None, ALU.is_ge, wkey=selb)
        mm(em, prow[0:1, 0:160], ctx.ones_col_bf[:], selb[:])
        copy(em, "vector", cnts[0:1, 160 * i:160 * (i + 1)], prow[0:1, 0:160],
             wkey=cnts)
    memset(em, "vector", offs[0:1, 0:160], 0.0, wkey=offs)
    for i in range(1, NI):
        tt(em, "vector", offs[0:1, 160 * i:160 * (i + 1)],
           offs[0:1, 160 * (i - 1):160 * i], cnts[0:1, 160 * (i - 1):160 * i],
           ALU.add, wkey=offs)
    tot, m1, padd, bstart = rowt
    tt(em, "vector", tot[:], offs[0:1, 160 * (NI - 1):160 * NI],
       cnts[0:1, 160 * (NI - 1):160 * NI], ALU.add, wkey=tot)
    ts(em, "vector", m1[:], tot[:], 128.0, None, ALU.mod, wkey=m1)
    ts(em, "vector", m1[:], m1[:], -1.0, 128.0, ALU.mult, ALU.add, wkey=m1)
    ts(em, "vector", m1[:], m1[:], 128.0, None, ALU.mod, wkey=m1)
    tt(em, "vector", padd[:], tot[:], m1[:], ALU.add, wkey=padd)
    scan(em, m1[:], padd[:], ctx.zeros[0:1, 0:160], 0.0, ALU.add, ALU.add)
    memset(em, "vector", bstart[0:1, 0:1], 0.0, wkey=bstart)
    copy(em, "vector", bstart[0:1, 1:160], m1[0:1, 0:159], wkey=bstart)

    mm(em, pg[0][:, 0:160], ctx.ones_f[:], bstart[:])
    for t in range(BPC):
        ts(em, "vector", t160[:], pg[0][:, 0:160], BCOL[:, t:t + 1], None,
           ALU.is_le, wkey=t160)
        reduce(em, ebf[:, t:t + 1], t160[:], ALU.add, wkey=ebf)
    ts(em, "vector", ebf[:], ebf[:], -1.0, None, ALU.add, wkey=ebf)
    copy(em, "vector", ebi[:], ebf[:], wkey=ebi)
    em.dma("gpsimd", out=EBD.rearrange("(t p) -> p t", p=128), in_=ebi[:],
           reads=[ebi], writes=["EBD"], allow_slow_non_contiguous=True)
    em.dma("sync", out=ebrow[0:1, :], in_=EBD[None, :], reads=["EBD"],
           writes=[ebrow])

    for i in range(NI):
        ts(em, "vector", selb[:], RNK[i][:], 0.5, None, ALU.is_ge, wkey=selb)
        mm(em, pg[1][:, 0:160], TRI[:], selb[:])
        copy(em, "vector", possb[:], pg[1][:, 0:160], wkey=possb)
        tt(em, "vector", rowt[1][:], bstart[:], offs[0:1, 160 * i:160 * (i + 1)],
           ALU.add, wkey=rowt[1])
        mm(em, pg[2][:, 0:160], ctx.ones_f[:], rowt[1][:])
        tt(em, "vector", Dsb[:], possb[:], pg[2][:, 0:160], ALU.add, wkey=Dsb)
        ts(em, "vector", tokf[:], IOT[:], float(128 * i), None, ALU.add, wkey=tokf)
        for r in range(8):
            ts(em, "vector", mskb[:], RNK[i][:], float(r + 1), None, ALU.is_equal,
               wkey=mskb)
            tt(em, "vector", t160[:], Dsb[:], mskb[:], ALU.mult, wkey=t160)
            reduce(em, fcol[:], t160[:], ALU.add, wkey=fcol)
            tt(em, "vector", t160[:], CTM[i][:], mskb[:], ALU.mult, wkey=t160)
            reduce(em, cwcol[:], t160[:], ALU.add, wkey=cwcol)
            copy(em, "vector", val4[:, 0:1], tokf[:], wkey=val4)
            copy(em, "vector", val4[:, 1:2], tokf[:], wkey=val4)
            copy(em, "vector", val4[:, 2:3], cwcol[:], wkey=val4)
            copy(em, "vector", ufd[:], fcol[:], wkey=ufd)
            em.op("gpsimd", lambda e: e.indirect_dma_start(
                out=TAB, out_offset=IndirectOffsetOnAxis(ufd[:], 0),
                in_=val4[:], in_offset=None,
                bounds_check=NB * 128 - 1, oob_is_err=False),
                reads=[val4, ufd, "TAB"], writes_accum=["TAB"], dma=True)

    for rr in range((T + 128) // 128):
        for hh in range(2):
            em.dma("gpsimd", out=Zacc[128 * rr:128 * (rr + 1), 512 * hh:512 * (hh + 1)],
                   in_=ctx.zeros[:, 0:512], writes_accum=["Zacc"])

    rgA = nc.gpsimd.alloc_register("sp_rga")
    rgB = nc.gpsimd.alloc_register("sp_rgb")
    for b in range(NB):
        p = b % 2
        ph = pg[p]
        pzh = (pz[0], pz[1]) if p == 0 else (pg[2], pg[3])
        t4 = tab4[p]
        em.dma("sync", out=t4[:], in_=TAB[128 * b:128 * (b + 1), :],
               reads=["TAB", t4], writes=[t4])
        copy(em, "vector", utok[p][:], t4[:, 0:1], wkey=utok[p])
        copy(em, "vector", uzd[p][:], t4[:, 1:2], wkey=uzd[p])
        em.op("gpsimd", lambda e, b=b: e.reg_load(rgA, ebrow[0:1, b:b + 1]),
              reads=[ebrow])
        em.op("gpsimd", lambda e: e.reg_mul(rgA, rgA, 1024))
        em.op("gpsimd", lambda e, w=wg_sb[p]: e.dma_start(
            out=w[:].rearrange("p (c i) -> p c i", c=8),
            in_=wgflat[bass.ds(RuntimeValue(rgA, min_val=0, max_val=159 * 1024), 1024),
                       :].rearrange("(c p) i -> p c i", p=128)),
            reads=[wg_sb[p]], writes=[wg_sb[p]], dma=True)
        em.op("gpsimd", lambda e, b=b: e.reg_load(rgB, ebrow[0:1, b:b + 1]),
              reads=[ebrow])
        em.op("gpsimd", lambda e: e.reg_mul(rgB, rgB, 512))
        em.op("gpsimd", lambda e, w=wd_sb[p]: e.dma_start(
            out=w[:].rearrange("p (c i) -> p c i", c=4),
            in_=wdflat[bass.ds(RuntimeValue(rgB, min_val=0, max_val=159 * 512), 512),
                       :].rearrange("(ci p) c -> p ci c", p=128)),
            reads=[wd_sb[p]], writes=[wd_sb[p]], dma=True)
        em.op("gpsimd", lambda e, p=p: e.indirect_dma_start(
            out=xg_tm[p][:], out_offset=None, in_=xn_tm_d,
            in_offset=IndirectOffsetOnAxis(utok[p][:], 0)),
            reads=["XNTM", utok[p], xg_tm[p]], writes=[xg_tm[p]], dma=True)
        for c in range(8):
            transp(em, pbf[:, 128 * c:128 * (c + 1)],
                   xg_tm[p][:, 128 * c:128 * (c + 1)], ctx.eye[:])
        copy(em, "vector", xg_fm[p][:], pbf[0:128, 0:1024], wkey=xg_fm[p])
        for c in range(8):
            for m in range(4):
                mm(em, ph[:, 128 * m:128 * (m + 1)],
                   wg_sb[p][:, 512 * c + 128 * m:512 * c + 128 * (m + 1)],
                   xg_fm[p][:, 128 * c:128 * (c + 1)],
                   start=(c == 0), stop=(c == 7))
        act(em, hb[p][:], ph[:, 0:512], AF.Silu, wkey=hb[p])
        for h in range(2):
            for ci in range(4):
                mm(em, pzh[h][:, 0:512], hb[p][:, 128 * ci:128 * (ci + 1)],
                   wd_sb[p][:, 1024 * ci + 512 * h:1024 * ci + 512 * (h + 1)],
                   start=(ci == 0), stop=(ci == 3))
            ts(em, "vector", z_bf[p][:, 512 * h:512 * (h + 1)], pzh[h][:, 0:512],
               t4[:, 2:3], None, ALU.mult, wkey=z_bf[p])
        _mode = os.environ.get("SPMODE", "add")
        if _mode == "add":
            em.op("gpsimd", lambda e, p=p: e.indirect_dma_start(
                out=Zacc, out_offset=IndirectOffsetOnAxis(uzd[p][:], 0),
                in_=z_bf[p][:], in_offset=None,
                bounds_check=T + 127, oob_is_err=True, compute_op=ALU.add),
                reads=[z_bf[p], uzd[p], "ZRMW"], writes=["ZRMW"],
                writes_accum=["Zacc"], dma=True)
        else:
            em.op("gpsimd", lambda e, p=p: e.indirect_dma_start(
                out=Zacc, out_offset=IndirectOffsetOnAxis(uzd[p][:], 0),
                in_=z_bf[p][:], in_offset=None,
                bounds_check=T + 127, oob_is_err=False),
                reads=[z_bf[p], uzd[p], "ZRMW"], writes=["ZRMW"],
                writes_accum=["Zacc"], dma=True)

    for i in range(NI):
        st = xtm_st[i % 2]
        em.dma("gpsimd", out=st[:], in_=Zacc[128 * i:128 * (i + 1), :],
               reads=["Zacc", "ZRMW", st], writes=[st])
        for j in range(8):
            transp(em, pbf[:, 128 * j:128 * (j + 1)], st[:, 128 * j:128 * (j + 1)],
                   ctx.eye[:])
        for j in range(8):
            tt(em, "vector", ctx.X[j][:, 128 * i:128 * (i + 1)],
               pbf[:, 128 * j:128 * (j + 1)], ctx.X[j][:, 128 * i:128 * (i + 1)],
               ALU.add, wkey=ctx.X[j])

def prep_phase2_inputs(inputs):
    d = {}
    d["router_wt"] = np.ascontiguousarray(inputs["L0_ffn_router_w"].T, np.float32)
    eb = np.zeros((128, 2), np.float32)
    eb[:, 0] = inputs["L0_ffn_expert_bias"][0:128]
    eb[0:32, 1] = inputs["L0_ffn_expert_bias"][128:160]
    d["ebias"] = eb
    d["wg_t"] = np.ascontiguousarray(np.transpose(inputs["L0_ffn_w_gate"], (0, 2, 1)),
                                     np.float32)
    d["wd_t"] = np.ascontiguousarray(np.transpose(inputs["L0_ffn_w_down"], (0, 2, 1)),
                                     np.float32)
    d["wgs_t"] = np.ascontiguousarray(inputs["L0_ffn_w_gate_shared"].T, np.float32)
    d["wds_t"] = np.ascontiguousarray(inputs["L0_ffn_w_down_shared"].T, np.float32)
    return d


PHASE2_TENSORS = [
    ("router_wt", [1024, 160]), ("ebias", [128, 2]),
    ("wg_t", [160, 1024, 512]), ("wd_t", [160, 512, 1024]),
    ("wgs_t", [1024, 512]), ("wds_t", [512, 1024]),
]


def build_nc_phase2(T=2048, debug=False, sparse=False):
    from contextlib import ExitStack
    nc = bass.Bass("TRN2", target_bir_lowering=False)
    x_in = nc.dram_tensor("x_fm", [1024, T], F32, kind="ExternalInput").ap()
    y_out = nc.dram_tensor("y_fm", [1024, T], F32, kind="ExternalOutput").ap()
    inp = {nm: nc.dram_tensor(nm, shp, F32, kind="ExternalInput").ap()
           for nm, shp in PHASE2_TENSORS}
    em = Em(nc)
    ctx = Ctx()
    ctx.fused = False
    ctx.debug = debug
    ctx.moe_sparse = sparse
    ctx.dbg_done = set()
    with ExitStack() as st:
        E = st.enter_context
        if debug:
            ctx.dbg_stage = E(nc.sbuf_tensor("dbgstage", [128, 1024], F32))
        ctx.X = alloc_stream(nc, E, T)
        setup_common(nc, em, ctx, E, T)
        load_stream(em, ctx, x_in)
        em.barrier()
        build_phase2(nc, em, ctx, E, inp)
        store_stream(em, ctx, y_out)
        em.build()
    print("phase2: ops=%d" % len(em.ops))
    return nc


def emit_ln_win(nc, em, ctx, t0, tw, src_of, dst_of, psums, eps_col,
                gamma=None, beta=None, lnrow=None, sq2=None, tmpf=None):
    pA, pB, pC, pD = psums
    for j in range(8):
        s = src_of(j)
        sq = sq2[j % 2]
        if s.tensor.dtype == F32:
            copy(em, "vector", sq[:, 0:tw], s, wkey=sq)
            mm(em, pA[0:1, 0:tw], ctx.ones_col_bf[:], sq[:, 0:tw],
               start=(j == 0), stop=(j == 7))
            act(em, sq[:, 0:tw], sq[:, 0:tw], AF.Square, wkey=sq)
        else:
            mm(em, pA[0:1, 0:tw], ctx.ones_col_bf[:], s, start=(j == 0), stop=(j == 7))
            act(em, sq[:, 0:tw], s, AF.Square, wkey=sq)
        mm(em, pB[0:1, 0:tw], ctx.ones_col_bf[:], sq[:, 0:tw],
           start=(j == 0), stop=(j == 7))
    mean, msq, var, rstd = lnrow
    ts(em, "vector", mean[:, 0:tw], pA[0:1, 0:tw], 1.0 / 1024, None, ALU.mult,
       wkey=mean)
    ts(em, "vector", msq[:, 0:tw], pB[0:1, 0:tw], 1.0 / 1024, None, ALU.mult,
       wkey=msq)
    tt(em, "vector", var[:, 0:tw], mean[:, 0:tw], mean[:, 0:tw], ALU.mult, wkey=var)
    tt(em, "vector", var[:, 0:tw], msq[:, 0:tw], var[:, 0:tw], ALU.subtract,
       wkey=var)
    act(em, rstd[:, 0:tw], var[:, 0:tw], AF.Sqrt, bias=eps_col, wkey=rstd)
    recip(em, rstd[:, 0:tw], rstd[:, 0:tw])
    mm(em, pC[:, 0:tw], ctx.ones_f[:], mean[:, 0:tw])
    mm(em, pD[:, 0:tw], ctx.ones_f[:], rstd[:, 0:tw])
    for j in range(8):
        tf = tmpf[j % 2]
        tt(em, "vector", tf[:, 0:tw], src_of(j), pC[:, 0:tw], ALU.subtract, wkey=tf)
        if gamma is None:
            dst_of(j, lambda out, tfv=tf: tt(em, "vector", out, tfv[:, 0:tw],
                                             pD[:, 0:tw], ALU.mult, wkey=None))
        else:
            tt(em, "vector", tf[:, 0:tw], tf[:, 0:tw], pD[:, 0:tw], ALU.mult,
               wkey=tf)
            dst_of(j, lambda out, tfv=tf, j=j: ts(em, "vector", out, tfv[:, 0:tw],
                                                  gamma(j), beta(j), ALU.mult,
                                                  ALU.add, wkey=None))


def stream_proj(nc, em, ctx, inp_ap, rhs_of, psums, consumer, wbuf, nK=8,
                TT_=512, npair=4):
    for pj in range(npair):
        for kk in range(nK):
            wb = wbuf[kk % len(wbuf)]
            em.dma("gpsimd", out=wb[:, 0:256],
                   in_=inp_ap[128 * kk:128 * (kk + 1), 256 * pj:256 * (pj + 1)],
                   reads=[wb], writes=[wb])
            mm(em, psums[0][:, 0:TT_], wb[:, 0:128], rhs_of(kk),
               start=(kk == 0), stop=(kk == nK - 1))
            mm(em, psums[1][:, 0:TT_], wb[:, 128:256], rhs_of(kk),
               start=(kk == 0), stop=(kk == nK - 1))
        consumer(2 * pj, psums[0])
        consumer(2 * pj + 1, psums[1])


def build_phase3(nc, em, ctx, E, inp, stop_at=None):
    T = ctx.T
    NTW = T // 512
    NI = T // 128

    pp = [E(nc.psum_tensor(f"p3p{i}", [128, 512], F32)) for i in range(7)]
    pbf = E(nc.psum_tensor("p3bf", [128, 1024], BF16))

    S1 = [E(nc.sbuf_tensor(f"p3S1_{j}", [128, T], BF16)) for j in range(8)]
    S2 = [E(nc.sbuf_tensor(f"p3S2_{j}", [128, T], BF16)) for j in range(8)]
    NI0 = T // 128
    S3 = [E(nc.sbuf_tensor(f"p3S3_{j}", [128, 1024], BF16)) for j in range(max(NI0, 8))]
    S4 = [E(nc.sbuf_tensor(f"p3S4_{j}", [128, T], BF16)) for j in range(8)]
    S5 = [E(nc.sbuf_tensor(f"p3S5_{j}", [128, 1024], BF16))
          for j in range(max(NI0, 16))]
    xt = [E(nc.sbuf_tensor(f"p3x{j}", [128, 512], F32)) for j in range(8)]
    s2b = lambda j: S3[j][:, 0:512]
    sq2 = [E(nc.sbuf_tensor(f"p3sq{i}", [128, 512], BF16)) for i in range(2)]
    tmpf = [E(nc.sbuf_tensor(f"p3tmp{i}", [128, 512], F32)) for i in range(2)]
    lnrow = [E(nc.sbuf_tensor(f"p3row{i}", [1, 512], F32)) for i in range(4)]
    wbuf = [E(nc.sbuf_tensor(f"p3wb{i}", [128, 256], BF16)) for i in range(4)]
    wbig3 = [E(nc.sbuf_tensor(f"p3wB{i}", [128, 512], BF16)) for i in range(3)]
    pv3 = E(nc.sbuf_tensor("p3pv", [128, 40], F32))
    em.dma("sync", out=pv3[:], in_=inp["pvec3"], writes=[pv3])
    IG1, IB1, IG2, IB2, IB2b = range(5)

    def pvc(idx, j):
        return pv3[:, idx * 8 + j:idx * 8 + j + 1]

    b1c = E(nc.sbuf_tensor("p3b1c", [128, 32], F32))
    em.dma("sync", out=b1c[:], in_=inp["b1cols"], writes=[b1c])
    projT2 = E(nc.sbuf_tensor("p3projT", [128, 64], BF16))
    pstage = E(nc.sbuf_tensor("p3pstage", [128, 64], F32))
    em.dma("sync", out=pstage[0:64, :], in_=inp["projT"], reads=[pstage],
           writes=[pstage])
    DN = 1.0 / np.sqrt(np.sqrt(64.0))
    ts(em, "vector", projT2[0:64, :], pstage[0:64, :], DN, None, ALU.mult,
       wkey=projT2)
    copy(em, "vector", projT2[64:128, :], projT2[0:64, :], wkey=projT2)
    bd = np.zeros((1024, 16), np.float32)
    for h in range(16):
        bd[64 * h:64 * (h + 1), h] = 1.0
    bd_d = nc.inline_tensor(bd.astype(BF16NP), "p3bd_d")
    BDH = [E(nc.sbuf_tensor(f"p3bd{c}", [128, 16], BF16)) for c in range(8)]
    for c in range(8):
        em.dma("sync", out=BDH[c][:], in_=bd_d.ap()[128 * c:128 * (c + 1), :],
               writes=[BDH[c]])
    diag_q = [E(nc.sbuf_tensor(f"p3dq{i}", [128, 16], F32)) for i in range(NI)]
    diag_k = [E(nc.sbuf_tensor(f"p3dk{i}", [128, 16], F32)) for i in range(NI)]
    ndmx = E(nc.sbuf_tensor("p3ndmx", [128, 16], F32))
    kv_sb = [E(nc.sbuf_tensor(f"p3kv{p}", [128, 65], BF16)) for p in range(8)]
    rc = E(nc.sbuf_tensor("p3rc", [128, 1], F32))
    mxcol = E(nc.sbuf_tensor("p3mx", [128, 1], F32))
    mxrow = E(nc.sbuf_tensor("p3mxr", [1, 16], F32))
    mxk_col = E(nc.sbuf_tensor("p3mxk", [128, 1], F32))
    cc_in = nc.dram_tensor("p3cc_in", [1, 16], F32)
    cc_out = nc.dram_tensor("p3cc_out", [1, 16], F32)

    em.barrier()

    for w in range(NTW):
        t0 = w * 512
        for j in range(8):
            em.dma("sync", out=xt[j][:], in_=ctx.x_in[128 * j:128 * (j + 1),
                                                      t0:t0 + 512],
                   reads=[xt[j]], writes=[xt[j]])
        emit_ln_win(nc, em, ctx, t0, 512, lambda j: xt[j][:],
                    lambda j, emitfn: emitfn(S1[j][:, t0:t0 + 512]),
                    (pp[0], pp[1], pp[2], pp[3]), ctx.eps5[0:1, 0:1],
                    lnrow=lnrow, sq2=sq2, tmpf=tmpf)

    if stop_at == 'ln':
        em.barrier()
        return
    def qk_path(wname, dst_tm, diag_dst, is_q):
        for w in range(NTW):
            t0 = w * 512
            stream_proj(nc, em, ctx, inp[wname],
                        lambda kk, t0=t0: S1[kk][:, t0:t0 + 512], (pp[0], pp[1]),
                        lambda j, ps, t0=t0: copy(em, "vector",
                                                  S2[j][:, t0:t0 + 512],
                                                  ps[:, 0:512], wkey=S2[j]),
                        wbuf)
        if stop_at == 'qk_mm':
            return
        for w in range(NTW):
            t0 = w * 512
            for j in range(8):
                act(em, sq2[j % 2][:], S2[j][:, t0:t0 + 512], AF.Square,
                    wkey=sq2[j % 2])
                mm(em, pp[4][0:16, 0:512], BDH[j][:], sq2[j % 2][:],
                   start=(j == 0), stop=(j == 7))
            for s in range(4):
                em.op("scalar", lambda e, s=s, w=w: e.activation(
                    tmpf[0][0:16, 128 * s:128 * (s + 1)],
                    pp[4][0:16, 128 * s:128 * (s + 1)], AF.Identity,
                    scale=float(DN * DN / 2.0)),
                    reads=[pp[4]], writes=[tmpf[0]])
            copy(em, "vector", sq2[0][0:16, 0:512], tmpf[0][0:16, 0:512],
                 wkey=sq2[0])
            for s in range(4):
                i = 4 * w + s
                transp(em, pbf[:, 0:16], sq2[0][0:16, 128 * s:128 * (s + 1)],
                       ctx.eye[0:16, 0:16])
                copy(em, "vector", diag_dst[i][:], pbf[0:128, 0:16],
                     wkey=diag_dst[i])
        if stop_at == 'qk_diag':
            return
        for i in range(NI):
            tc = i * 128
            for hh in range(2):
                for h4 in range(8):
                    h = 8 * hh + h4
                    par = h4 % 2
                    mm(em, pp[2 * hh + par][:, 64 * (h4 // 2):64 * (h4 // 2 + 1)],
                       S2[h // 2][64 * par:64 * par + 64, tc:tc + 128],
                       projT2[64 * par:64 * par + 64, :])
            for hh in range(2):
                for par in range(2):
                    dv = dst_tm(i).rearrange("p (g c) -> p g c", c=64)
                    dst = dv[:, 8 * hh + par:8 * hh + 8:2, :]
                    copy(em, "vector", dst, pp[2 * hh + par][:, 0:256], wkey=None)

    if stop_at == 'none2':
        em.barrier()
        return
    qdt = lambda i: S3[i][:, :]
    if stop_at == 'none3':
        em.barrier()
        return
    qk_path("Wq", qdt, diag_q, True)
    if stop_at == 'qproj':
        em.barrier()
        return
    for i in range(NI):
        dst = qdt(i)
        for h in range(16):
            reduce(em, mxcol[:], dst[:, 64 * h:64 * (h + 1)], ALU.max, wkey=mxcol)
            ts(em, "vector", ndmx[:, h:h + 1], diag_q[i][:, h:h + 1], mxcol[:],
               -1.0, ALU.add, ALU.mult, wkey=ndmx)
        for h in range(16):
            act(em, dst[:, 64 * h:64 * (h + 1)], dst[:, 64 * h:64 * (h + 1)],
                AF.Exp, bias=ndmx[:, h:h + 1], wkey=None)
        ts(em, "vector", dst[:], dst[:], 0.125, 0.125e-6, ALU.mult, ALU.add,
           wkey=None)
    if stop_at == 'qf':
        em.barrier()
        return
    for i in range(NI):
        src = qdt(i)
        for j in range(8):
            transp(em, pbf[:, 0:128], src[:, 128 * j:128 * (j + 1)], ctx.eye[:])
            copy(em, "vector", S4[j][:, 128 * i:128 * (i + 1)], pbf[0:128, 0:128],
                 wkey=S4[j])
    if stop_at == 'qftr':
        em.barrier()
        return
    qk_path("Wk", qdt, diag_k, False)
    if stop_at == 'kproj':
        em.barrier()
        return
    memset(em, "vector", mxk_col[:], -1e30, wkey=mxk_col)
    for i in range(NI):
        reduce(em, mxcol[:], qdt(i)[:], ALU.max, wkey=mxcol)
        tt(em, "vector", mxk_col[:], mxk_col[:], mxcol[:], ALU.max, wkey=mxk_col)
    copy(em, "vector", sq2[0][0:128, 0:1], mxk_col[:], wkey=sq2[0])
    transp(em, pbf[0:1, 0:128], sq2[0][0:128, 0:1], ctx.eye[:])
    copy(em, "vector", sq2[0][0:1, 0:128], pbf[0:1, 0:128], wkey=sq2[0])
    reduce(em, mxrow[0:1, 0:1], sq2[0][0:1, 0:128], ALU.max, wkey=mxrow)
    ts(em, "vector", mxrow[0:1, 0:16], ctx.zeros[0:1, 0:16], mxrow[0:1, 0:1],
       None, ALU.add, wkey=mxrow)
    em.dma("gpsimd", out=cc_in.ap(), in_=mxrow[0:1, 0:16], reads=[mxrow],
           writes=["p3ccin"])
    em.op("gpsimd", lambda e: e.collective_compute(
        "AllReduce", ALU.max, replica_groups=[list(range(ctx.ncores))],
        ins=[cc_in.ap().opt()], outs=[cc_out.ap().opt()]),
        reads=["p3ccin"], writes=["p3ccout"])
    em.dma("sync", out=mxrow[0:1, 0:16], in_=cc_out.ap(), reads=["p3ccout", mxrow],
           writes=[mxrow])
    mm(em, pp[6][:, 0:1], ctx.ones_f[:], mxrow[0:1, 0:1])
    copy(em, "vector", mxk_col[:], pp[6][:, 0:1], wkey=mxk_col)
    if stop_at == 'kmax':
        em.barrier()
        return
    for i in range(NI):
        dst = qdt(i)
        for h in range(16):
            ts(em, "vector", ndmx[:, h:h + 1], diag_k[i][:, h:h + 1], mxk_col[:],
               -1.0, ALU.add, ALU.mult, wkey=ndmx)
        for h in range(16):
            act(em, dst[:, 64 * h:64 * (h + 1)], dst[:, 64 * h:64 * (h + 1)],
                AF.Exp, bias=ndmx[:, h:h + 1], wkey=None)
        ts(em, "vector", dst[:], dst[:], 0.125, 0.125e-6, ALU.mult, ALU.add,
           wkey=None)
    if stop_at == 'kf':
        em.barrier()
        return
    for w in range(NTW):
        t0 = w * 512
        for nh in range(2):
            for kk in range(8):
                wb = wbig3[kk % 3]
                em.dma("gpsimd", out=wb[:], in_=inp["Wv"][
                    128 * kk:128 * (kk + 1), 512 * nh:512 * (nh + 1)],
                    reads=[wb], writes=[wb])
                for s in range(4):
                    mm(em, pp[s][:, 0:512],
                       S1[kk][:, t0 + 128 * s:t0 + 128 * (s + 1)], wb[:],
                       start=(kk == 0), stop=(kk == 7))
            for s in range(4):
                i = 4 * w + s
                copy(em, "vector", S5[i][:, 512 * nh:512 * (nh + 1)],
                     pp[s][:, 0:512], wkey=S5[i])
    if stop_at == 'vproj':
        em.barrier()
        return
    vt = lambda i: S5[i][:, :]
    for h in range(16):
        pr = h % 2
        pkv = pp[pr]
        pden = pp[2 + pr]
        for i in range(NI):
            mm(em, pkv[0:64, 0:64], qdt(i)[:, 64 * h:64 * (h + 1)],
               vt(i)[:, 64 * h:64 * (h + 1)], start=(i == 0), stop=(i == NI - 1))
            mm(em, pden[0:64, 0:1], qdt(i)[:, 64 * h:64 * (h + 1)],
               ctx.ones_col_bf[:], start=(i == 0), stop=(i == NI - 1))
        sl = slice(64 * pr, 64 * (pr + 1))
        copy(em, "vector", kv_sb[h // 2][sl, 0:64], pkv[0:64, 0:64],
             wkey=kv_sb[h // 2])
        copy(em, "vector", kv_sb[h // 2][sl, 64:65], pden[0:64, 0:1],
             wkey=kv_sb[h // 2])
    if stop_at == 'kv':
        em.barrier()
        return
    for i in range(NI):
        dst = qdt(i)
        for h in range(16):
            pr = h % 2
            sl = slice(64 * pr, 64 * (pr + 1))
            pn = pp[4 + pr]
            mm(em, pn[:, 0:65], S4[h // 2][sl, 128 * i:128 * (i + 1)],
               kv_sb[h // 2][sl, 0:65])
            recip(em, rc[:], pn[:, 64:65], wkey=rc)
            ts(em, "vector", dst[:, 64 * h:64 * (h + 1)], pn[:, 0:64], rc[:],
               None, ALU.mult, wkey=None)
    if stop_at == 'num':
        em.barrier()
        return
    for i in range(NI):
        src = qdt(i)
        for j in range(8):
            transp(em, pbf[:, 0:128], src[:, 128 * j:128 * (j + 1)], ctx.eye[:])
            copy(em, "vector", S4[j][:, 128 * i:128 * (i + 1)], pbf[0:128, 0:128],
                 wkey=S4[j])
    if stop_at == 'attntr':
        em.barrier()
        return
    for w in range(NTW):
        t0 = w * 512
        stream_proj(nc, em, ctx, inp["Wo"],
                    lambda kk, t0=t0: S4[kk][:, t0:t0 + 512], (pp[0], pp[1]),
                    lambda j, ps, t0=t0: tt(em, "vector", S1[j][:, t0:t0 + 512],
                                            ps[:, 0:512], S1[j][:, t0:t0 + 512],
                                            ALU.add, wkey=S1[j]),
                    wbuf)
    if stop_at == 'wo':
        em.barrier()
        return
    for w in range(NTW):
        t0 = w * 512
        emit_ln_win(nc, em, ctx, t0, 512, lambda j: S1[j][:, t0:t0 + 512],
                    lambda j, emitfn: emitfn(S2[j][:, t0:t0 + 512]),
                    (pp[0], pp[1], pp[2], pp[3]), ctx.eps6[0:1, 0:1],
                    gamma=lambda j: pvc(IG1, j), beta=lambda j: pvc(IB1, j),
                    lnrow=lnrow, sq2=sq2, tmpf=tmpf)
    if stop_at == 'ln1':
        em.barrier()
        return
    helu = lambda m: S5[m // 2][:, 512 * (m % 2):512 * (m % 2 + 1)]
    for w in range(NTW):
        t0 = w * 512
        for mp in range(16):
            for kk in range(8):
                wb = wbuf[kk % 4]
                em.dma("gpsimd", out=wb[:, 0:256],
                       in_=inp["W1"][128 * kk:128 * (kk + 1),
                                     256 * mp:256 * (mp + 1)],
                       reads=[wb], writes=[wb])
                mm(em, pp[0][:, 0:512], wb[:, 0:128], S2[kk][:, t0:t0 + 512],
                   start=(kk == 0), stop=(kk == 7))
                mm(em, pp[1][:, 0:512], wb[:, 128:256], S2[kk][:, t0:t0 + 512],
                   start=(kk == 0), stop=(kk == 7))
            for half in range(2):
                m = 2 * mp + half
                ps = pp[half]
                tf = tmpf[half]
                ts(em, "vector", tf[:], ps[:, 0:512], b1c[:, m:m + 1], 0.0,
                   ALU.add, ALU.min, wkey=tf)
                act(em, tf[:], tf[:], AF.Exp, wkey=tf)
                ts(em, "vector", xt[half][:], ps[:, 0:512], b1c[:, m:m + 1], 0.0,
                   ALU.add, ALU.max, wkey=xt[half])
                stt(em, "vector", helu(m)[:], tf[:], -1.0, xt[half][:], ALU.add,
                    ALU.add, wkey=None)
        for pj in range(4):
            for kk in range(32):
                wb = wbuf[kk % 4]
                em.dma("gpsimd", out=wb[:, 0:256],
                       in_=inp["W2"][128 * kk:128 * (kk + 1),
                                     256 * pj:256 * (pj + 1)],
                       reads=[wb], writes=[wb])
                mm(em, pp[2][:, 0:512], wb[:, 0:128], helu(kk)[:],
                   start=(kk == 0), stop=(kk == 31))
                mm(em, pp[3][:, 0:512], wb[:, 128:256], helu(kk)[:],
                   start=(kk == 0), stop=(kk == 31))
            for half in range(2):
                j = 2 * pj + half
                stt(em, "vector", s2b(j), pp[2 + half][:, 0:512],
                    pvc(IB2b, j), S2[j][:, t0:t0 + 512], ALU.add, ALU.add,
                    wkey=None)
        for j in range(8):
            em.dma("sync", out=xt[j][:], in_=ctx.x_in[128 * j:128 * (j + 1),
                                                      t0:t0 + 512],
                   reads=[xt[j]], writes=[xt[j]])
        def dst3(j, emitfn):
            tf2 = sq2[0][:, 0:512]
            emitfn(tf2)
            tt(em, "vector", xt[j][:], tf2, xt[j][:], ALU.add, wkey=xt[j])
            em.dma("gpsimd", out=ctx.y_out[128 * j:128 * (j + 1), t0:t0 + 512],
                   in_=xt[j][:], reads=[xt[j]], writes=[f"P3OUT{j}"])
        emit_ln_win(nc, em, ctx, t0, 512, lambda j: s2b(j),
                    dst3, (pp[0], pp[1], pp[2], pp[3]), ctx.eps6[0:1, 0:1],
                    gamma=lambda j: pvc(IG2, j), beta=lambda j: pvc(IB2, j),
                    lnrow=lnrow, sq2=sq2, tmpf=tmpf)
    em.barrier()


def prep_phase3_inputs(inputs):
    d = {}
    d["Wq"] = np.ascontiguousarray(inputs["L1_attn_Wq"].reshape(1024, 1024),
                                   np.float32)
    d["Wk"] = np.ascontiguousarray(inputs["L1_attn_Wk"].reshape(1024, 1024),
                                   np.float32)
    d["Wv"] = np.ascontiguousarray(inputs["L1_attn_Wv"].reshape(1024, 1024),
                                   np.float32)
    d["Wo"] = np.ascontiguousarray(inputs["L1_attn_Wo"].reshape(1024, 1024),
                                   np.float32)
    d["projT"] = np.ascontiguousarray(inputs["L1_attn_proj"].T, np.float32)
    d["W1"] = np.ascontiguousarray(inputs["L1_attn_W1"], np.float32)
    d["W2"] = np.ascontiguousarray(inputs["L1_attn_W2"], np.float32)
    d["b1cols"] = np.ascontiguousarray(inputs["L1_attn_b1"].reshape(32, 128).T,
                                       np.float32)
    d["pvec3"] = _pack_pvec([
        inputs["L1_attn_ln1_g"], inputs["L1_attn_ln1_b"],
        inputs["L1_attn_ln2_g"], inputs["L1_attn_ln2_b"], inputs["L1_attn_b2"]])
    return d


PHASE3_TENSORS = [
    ("Wq", [1024, 1024]), ("Wk", [1024, 1024]), ("Wv", [1024, 1024]),
    ("Wo", [1024, 1024]), ("projT", [64, 64]), ("W1", [1024, 4096]),
    ("W2", [4096, 1024]), ("b1cols", [128, 32]), ("pvec3", [128, 40]),
]


def build_nc_phase3(T=2048, ncores=8, debug=False, stop_at=None):
    from contextlib import ExitStack
    nc = bass.Bass("TRN2", target_bir_lowering=False)
    x_in = nc.dram_tensor("x_fm", [1024, T], F32, kind="ExternalInput").ap()
    y_out = nc.dram_tensor("y_fm", [1024, T], F32, kind="ExternalOutput").ap()
    inp = {nm: nc.dram_tensor(nm, shp, F32, kind="ExternalInput").ap()
           for nm, shp in PHASE3_TENSORS}
    em = Em(nc)
    ctx = Ctx()
    ctx.fused = False
    ctx.ncores = ncores
    ctx.debug = debug
    ctx.dbg_done = set()
    ctx.x_in = x_in
    ctx.y_out = y_out
    with ExitStack() as st:
        E = st.enter_context
        if debug:
            ctx.dbg_stage = E(nc.sbuf_tensor("dbgstage", [128, 1024], F32))
        setup_common(nc, em, ctx, E, T)
        build_phase3(nc, em, ctx, E, inp, stop_at=stop_at)
        em.build()
    print("phase3: ops=%d" % len(em.ops))
    return nc


def _p4_consts(T):
    DH, WIN = 64, 512
    half = np.arange(0, DH, 2, dtype=np.float64)
    invf = 1.0 / 10000.0 ** (half / DH)
    t = np.arange(T, dtype=np.float64)
    fr = t[:, None] * invf[None, :]
    fr = np.concatenate([fr, fr], axis=-1)
    base = (half + 0.4 * DH) / (1.4 * DH)
    power = (t - T // 2) / float(WIN // 2)
    sc = base[None, :] ** power[:, None]
    sc = np.concatenate([sc, sc], axis=-1)
    cos, sin = np.cos(fr), np.sin(fr)
    Aq = (cos * sc).T; Bq = (sin * sc).T
    Ak = (cos / sc).T; Bk = (sin / sc).T
    pack = lambda a: np.ascontiguousarray(np.tile(a, (2, 1))).astype(BF16NP)
    RT = np.zeros((64, 64), np.float32)
    for do in range(32):
        RT[do + 32, do] = -1.0
        RT[do, do + 32] = 1.0
    RT2 = np.zeros((128, 128), np.float32)
    RT2[0:64, 0:64] = RT; RT2[64:128, 64:128] = RT
    M = np.zeros((4, 128, 1024), np.float32)
    for qs in range(4):
        for p in range(128):
            i = qs * 128 + p
            M[qs, p, i:i + 513] = 1.0
    M0 = np.zeros((4, 128, 512), np.float32)
    for qs in range(4):
        for p in range(128):
            M0[qs, p, 0:qs * 128 + p + 1] = 1.0
    G8 = np.zeros((8, 512), np.float32)
    for h in range(8):
        G8[h, 64 * h:64 * (h + 1)] = 1.0
    BD8 = np.zeros((512, 8), np.float32)
    for h in range(8):
        BD8[64 * h:64 * (h + 1), h] = 1.0
    return (pack(Aq), pack(Bq), pack(Ak), pack(Bk), RT2.astype(BF16NP),
            M.astype(BF16NP), M0.astype(BF16NP), G8.astype(BF16NP),
            BD8.astype(BF16NP))


def build_phase4(nc, em, ctx, E, inp):
    T = ctx.T
    NTW = T // 512
    NI = T // 128
    NW = max(1, T // 512)

    pp = [E(nc.psum_tensor(f"p4p{i}", [128, 512], F32)) for i in range(7)]
    pbf = E(nc.psum_tensor("p4bf", [128, 1024], BF16))

    Aq_np, Bq_np, Ak_np, Bk_np, RT2_np, M_np, M0_np, G8_np, BD8_np = _p4_consts(T)
    def const(nm, arr):
        d = nc.inline_tensor(arr, nm + "_d")
        t = E(nc.sbuf_tensor(nm, list(arr.shape), BF16))
        em.dma("sync", out=t[:], in_=d.ap(), writes=[t])
        return t
    Aq = const("p4Aq", Aq_np); Bq = const("p4Bq", Bq_np)
    Ak = const("p4Ak", Ak_np); Bk = const("p4Bk", Bk_np)
    RT2 = const("p4RT", RT2_np)
    G8 = const("p4G8", G8_np)
    bd8_d = nc.inline_tensor(BD8_np, "p4BD_d")
    BD8 = [E(nc.sbuf_tensor(f"p4BD{c}", [128, 8], BF16)) for c in range(4)]
    for c in range(4):
        em.dma("sync", out=BD8[c][:], in_=bd8_d.ap()[128 * c:128 * (c + 1), :],
               writes=[BD8[c]])
    Mm = [const(f"p4M{qs}", M_np[qs]) for qs in range(4)]
    Mm0 = [const(f"p4M0{qs}", M0_np[qs]) for qs in range(4)]

    pv4 = E(nc.sbuf_tensor("p4pv", [128, 64], F32))
    em.dma("sync", out=pv4[:], in_=inp["pvec4"], writes=[pv4])

    def pvc(l, idx, j):
        base = l * 32
        if idx < 3:
            return pv4[:, base + idx * 8 + j:base + idx * 8 + j + 1]
        return pv4[:, base + 24 + (idx - 3) * 4 + j:base + 24 + (idx - 3) * 4 + j + 1]

    Z = [E(nc.sbuf_tensor(f"p4Z{j}", [128, T], BF16)) for j in range(8)]
    hs4 = [E(nc.sbuf_tensor(f"p4hs{j}", [128, T], BF16)) for j in range(4)]
    q_fm = [E(nc.sbuf_tensor(f"p4q{j}", [128, T], BF16)) for j in range(4)]
    k_fm = [E(nc.sbuf_tensor(f"p4k{j}", [128, T], BF16)) for j in range(4)]
    v_tm = [E(nc.sbuf_tensor(f"p4v{i}", [128, 512], BF16)) for i in range(NI)]
    o_fm = [E(nc.sbuf_tensor(f"p4o{j}", [128, max(T, 1024)], BF16)) for j in range(4)]
    helu = [E(nc.sbuf_tensor(f"p4he{m}", [128, 512], BF16)) for m in range(22)]
    attn_sb = [E(nc.sbuf_tensor(f"p4at{i}", [128, 1024], BF16)) for i in range(2)]
    attnT = [E(nc.sbuf_tensor(f"p4aT{i}", [128, 1024], BF16)) for i in range(2)]
    rnb = E(nc.sbuf_tensor("p4rn", [8, 512], BF16))
    rnf = E(nc.sbuf_tensor("p4rnf", [8, 512], F32))
    tb = [E(nc.sbuf_tensor(f"p4tb{i}", [128, 512], BF16)) for i in range(2)]
    mxc = E(nc.sbuf_tensor("p4mx", [128, 1], F32))
    mx2 = E(nc.sbuf_tensor("p4mx2", [128, 1], F32))
    rc = E(nc.sbuf_tensor("p4rc", [128, 1], F32))
    sq2 = [E(nc.sbuf_tensor(f"p4sq{i}", [128, 512], BF16)) for i in range(2)]
    tmpf = [E(nc.sbuf_tensor(f"p4tmp{i}", [128, 512], F32)) for i in range(2)]
    lnrow = [E(nc.sbuf_tensor(f"p4row{i}", [1, 512], F32)) for i in range(4)]
    xt = [E(nc.sbuf_tensor(f"p4x{j}", [128, 512], BF16)) for j in range(8)]
    wbig = [E(nc.sbuf_tensor(f"p4wB{i}", [128, 512], BF16)) for i in range(3)]
    wbuf = [E(nc.sbuf_tensor(f"p4wb{i}", [128, 256], BF16)) for i in range(4)]

    em.barrier()

    for w in range(NTW):
        t0 = w * 512
        for j in range(8):
            em.dma("gpsimd", out=xt[j][:], in_=ctx.x_in[128 * j:128 * (j + 1),
                                                        t0:t0 + 512],
                   reads=[xt[j]], writes=[xt[j]])
        emit_ln_win(nc, em, ctx, t0, 512, lambda j: xt[j][:],
                    lambda j, emitfn: emitfn(Z[j][:, t0:t0 + 512]),
                    (pp[0], pp[1], pp[2], pp[3]), ctx.eps5[0:1, 0:1],
                    lnrow=lnrow, sq2=sq2, tmpf=tmpf)

    for l in range(2):
        for j in range(4):
            copy(em, "vector", hs4[j][:, 1:T], Z[j + 4][:, 0:T - 1], wkey=hs4[j])
            memset(em, "vector", hs4[j][:, 0:1], 0.0, wkey=hs4[j])
        hsrc = lambda j: Z[j] if j < 4 else hs4[j - 4]
        for w in range(NTW):
            t0 = w * 512
            for pj in range(4):
                dst = q_fm[2 * pj] if pj < 2 else k_fm[2 * (pj - 2)]
                dst2 = q_fm[2 * pj + 1] if pj < 2 else k_fm[2 * (pj - 2) + 1]
                for kk in range(8):
                    wb = wbuf[kk % 4]
                    em.dma("gpsimd", out=wb[:, 0:256],
                           in_=inp["wqkv"][l, 128 * kk:128 * (kk + 1),
                                           256 * pj:256 * (pj + 1)],
                           reads=[wb], writes=[wb])
                    mm(em, pp[0][:, 0:512], wb[:, 0:128], hsrc(kk)[:, t0:t0 + 512],
                       start=(kk == 0), stop=(kk == 7))
                    mm(em, pp[1][:, 0:512], wb[:, 128:256], hsrc(kk)[:, t0:t0 + 512],
                       start=(kk == 0), stop=(kk == 7))
                copy(em, "vector", dst[:, t0:t0 + 512], pp[0][:, 0:512], wkey=dst)
                copy(em, "vector", dst2[:, t0:t0 + 512], pp[1][:, 0:512], wkey=dst2)
            for kk in range(8):
                wb = wbig[kk % 3]
                em.dma("gpsimd", out=wb[:],
                       in_=inp["wqkv"][l, 128 * kk:128 * (kk + 1), 1024:1536],
                       reads=[wb], writes=[wb])
                for s in range(4):
                    mm(em, pp[2 + s][:, 0:512],
                       hsrc(kk)[:, t0 + 128 * s:t0 + 128 * (s + 1)], wb[:],
                       start=(kk == 0), stop=(kk == 7))
            for s in range(4):
                copy(em, "vector", v_tm[4 * w + s][:], pp[2 + s][:, 0:512],
                     wkey=v_tm[4 * w + s])
        for (fm, Ac, Bc, isq) in ((q_fm, Aq, Bq, True), (k_fm, Ak, Bk, False)):
            for w in range(NTW):
                t0 = w * 512
                for j in range(4):
                    act(em, sq2[j % 2][:], fm[j][:, t0:t0 + 512], AF.Square,
                        wkey=sq2[j % 2])
                    mm(em, pp[0][0:8, 0:512], BD8[j][:],
                       sq2[j % 2][:], start=(j == 0), stop=(j == 3))
                act(em, rnf[:, 0:512], pp[0][0:8, 0:512], AF.Sqrt, wkey=rnf)
                ts(em, "vector", rnf[:, 0:512], rnf[:, 0:512], 1e-12, None,
                   ALU.max, wkey=rnf)
                recip(em, rnf[:, 0:512], rnf[:, 0:512])
                copy(em, "vector", rnb[:, 0:512], rnf[:, 0:512], wkey=rnb)
                for j in range(4):
                    mm(em, pp[1][:, 0:512], G8[:, 128 * j:128 * (j + 1)],
                       rnb[:, 0:512])
                    tt(em, "vector", fm[j][:, t0:t0 + 512], fm[j][:, t0:t0 + 512],
                       pp[1][:, 0:512], ALU.mult, wkey=fm[j])
                    qsc = pvc(l, 3 if isq else 4, j)
                    ts(em, "vector", fm[j][:, t0:t0 + 512], fm[j][:, t0:t0 + 512],
                       qsc, 8.0 if isq else 1.0, ALU.mult, ALU.mult, wkey=fm[j])
                    mm(em, pp[2][:, 0:512], RT2[:], fm[j][:, t0:t0 + 512])
                    tt(em, "vector", tb[0][:], fm[j][:, t0:t0 + 512],
                       Ac[:, t0:t0 + 512], ALU.mult, wkey=tb[0])
                    tt(em, "vector", tb[1][:], pp[2][:, 0:512],
                       Bc[:, t0:t0 + 512], ALU.mult, wkey=tb[1])
                    tt(em, "vector", fm[j][:, t0:t0 + 512], tb[0][:], tb[1][:],
                       ALU.add, wkey=fm[j])
        for h in range(8):
            hj, par = h // 2, h % 2
            sl = slice(64 * par, 64 * (par + 1))
            for w in range(NW):
                kb = (w - 1) * 4
                kcol0 = (w - 1) * 512
                nk = 512 if w == 0 else 1024
                for qs in range(4):
                    qc0 = w * 512 + qs * 128
                    ia = qs % 2
                    if w == 0:
                        mm(em, pp[0][:, 0:512], q_fm[hj][sl, qc0:qc0 + 128],
                           k_fm[hj][sl, 0:512])
                        reduce(em, mxc[:], pp[0][:, 0:512], ALU.max, wkey=mxc)
                        ts(em, "vector", mxc[:], mxc[:], -1.0, None, ALU.mult,
                           wkey=mxc)
                        act(em, attn_sb[ia][:, 0:512], pp[0][:, 0:512], AF.Exp,
                            bias=mxc[:], wkey=attn_sb[ia])
                        tt(em, "vector", attn_sb[ia][:, 0:512],
                           attn_sb[ia][:, 0:512], Mm0[qs][:], ALU.mult,
                           wkey=attn_sb[ia])
                        reduce(em, rc[:], attn_sb[ia][:, 0:512], ALU.add, wkey=rc)
                        recip(em, rc[:], rc[:])
                        ts(em, "vector", attn_sb[ia][:, 0:512],
                           attn_sb[ia][:, 0:512], rc[:], None, ALU.mult,
                           wkey=attn_sb[ia])
                        nblk = 4
                    else:
                        mm(em, pp[0][:, 0:512], q_fm[hj][sl, qc0:qc0 + 128],
                           k_fm[hj][sl, kcol0:kcol0 + 512])
                        mm(em, pp[1][:, 0:512], q_fm[hj][sl, qc0:qc0 + 128],
                           k_fm[hj][sl, kcol0 + 512:kcol0 + 1024])
                        reduce(em, mxc[:], pp[0][:, 0:512], ALU.max, wkey=mxc)
                        reduce(em, mx2[:], pp[1][:, 0:512], ALU.max, wkey=mx2)
                        tt(em, "vector", mxc[:], mxc[:], mx2[:], ALU.max, wkey=mxc)
                        ts(em, "vector", mxc[:], mxc[:], -1.0, None, ALU.mult,
                           wkey=mxc)
                        act(em, attn_sb[ia][:, 0:512], pp[0][:, 0:512], AF.Exp,
                            bias=mxc[:], wkey=attn_sb[ia])
                        act(em, attn_sb[ia][:, 512:1024], pp[1][:, 0:512], AF.Exp,
                            bias=mxc[:], wkey=attn_sb[ia])
                        tt(em, "vector", attn_sb[ia][:], attn_sb[ia][:], Mm[qs][:],
                           ALU.mult, wkey=attn_sb[ia])
                        reduce(em, rc[:], attn_sb[ia][:], ALU.add, wkey=rc)
                        recip(em, rc[:], rc[:])
                        ts(em, "vector", attn_sb[ia][:], attn_sb[ia][:], rc[:],
                           None, ALU.mult, wkey=attn_sb[ia])
                        nblk = 8
                    for b in range(nblk):
                        transp(em, pbf[:, 128 * b:128 * (b + 1)],
                               attn_sb[ia][:, 128 * b:128 * (b + 1)], ctx.eye[:])
                    copy(em, "vector", attnT[ia][:, 0:128 * nblk],
                         pbf[:, 0:128 * nblk], wkey=attnT[ia])
                    pav = pp[4 + (qs % 2)]
                    for b in range(nblk):
                        vi = (kb + b) if w > 0 else b
                        mm(em, pav[0:64, 0:128], v_tm[vi][:, 64 * h:64 * (h + 1)],
                           attnT[ia][:, 128 * b:128 * (b + 1)],
                           start=(b == 0), stop=(b == nblk - 1))
                    copy(em, "vector", o_fm[hj][sl, qc0:qc0 + 128],
                         pav[0:64, 0:128], wkey=o_fm[hj])
        for w in range(NTW):
            t0 = w * 512
            for pj in range(4):
                for kk in range(4):
                    wb = wbuf[kk % 4]
                    em.dma("gpsimd", out=wb[:, 0:256],
                           in_=inp["wo"][l, 128 * kk:128 * (kk + 1),
                                         256 * pj:256 * (pj + 1)],
                           reads=[wb], writes=[wb])
                    mm(em, pp[0][:, 0:512], wb[:, 0:128], o_fm[kk][:, t0:t0 + 512],
                       start=(kk == 0), stop=(kk == 3))
                    mm(em, pp[1][:, 0:512], wb[:, 128:256], o_fm[kk][:, t0:t0 + 512],
                       start=(kk == 0), stop=(kk == 3))
                for half in range(2):
                    j = 2 * pj + half
                    tt(em, "vector", Z[j][:, t0:t0 + 512],
                       pp[half][:, 0:512], Z[j][:, t0:t0 + 512], ALU.add,
                       wkey=Z[j])
        for j in range(4):
            copy(em, "vector", hs4[j][:, 1:T], Z[j + 4][:, 0:T - 1], wkey=hs4[j])
            memset(em, "vector", hs4[j][:, 0:1], 0.0, wkey=hs4[j])
        hffn = lambda j: o_fm[j // 2][:, 512 * (j % 2):512 * (j % 2) + 512]
        for w in range(NTW):
            t0 = w * 512
            emit_ln_win(nc, em, ctx, t0, 512, lambda j: hsrc(j)[:, t0:t0 + 512],
                        lambda j, emitfn: emitfn(hffn(j)),
                        (pp[0], pp[1], pp[2], pp[3]), ctx.eps5[0:1, 0:1],
                        gamma=lambda j: pvc(l, 0, j), beta=lambda j: pvc(l, 1, j),
                        lnrow=lnrow, sq2=sq2, tmpf=tmpf)
            for mp in range(22):
                for kk in range(8):
                    wb = wbuf[kk % 4]
                    em.dma("gpsimd", out=wb[:, 0:256],
                           in_=inp["w1p"][l, 128 * kk:128 * (kk + 1),
                                          256 * mp:256 * (mp + 1)],
                           reads=[wb], writes=[wb])
                    mm(em, pp[0][:, 0:512], wb[:, 0:128], hffn(kk),
                       start=(kk == 0), stop=(kk == 7))
                    mm(em, pp[1][:, 0:512], wb[:, 128:256], hffn(kk),
                       start=(kk == 0), stop=(kk == 7))
                for half in range(2):
                    m = 2 * mp + half
                    if m < 22:
                        ts(em, "vector", helu[m][:], pp[half][:, 0:512],
                           inp_b1col(em, ctx, l, m), None, ALU.add, wkey=helu[m])
                    else:
                        act(em, sq2[0][:], pp[half][:, 0:512], AF.Gelu,
                            bias=inp_b1col(em, ctx, l, m), wkey=sq2[0])
                        tt(em, "vector", helu[m - 22][:], helu[m - 22][:],
                           sq2[0][:], ALU.mult, wkey=helu[m - 22])
            for pj in range(4):
                for kk in range(22):
                    wb = wbuf[kk % 4]
                    em.dma("gpsimd", out=wb[:, 0:256],
                           in_=inp["w2p"][l, 128 * kk:128 * (kk + 1),
                                          256 * pj:256 * (pj + 1)],
                           reads=[wb], writes=[wb])
                    mm(em, pp[2][:, 0:512], wb[:, 0:128], helu[kk][:],
                       start=(kk == 0), stop=(kk == 21))
                    mm(em, pp[3][:, 0:512], wb[:, 128:256], helu[kk][:],
                       start=(kk == 0), stop=(kk == 21))
                for half in range(2):
                    j = 2 * pj + half
                    stt(em, "vector", Z[j][:, t0:t0 + 512],
                        pp[2 + half][:, 0:512], pvc(l, 2, j),
                        Z[j][:, t0:t0 + 512], ALU.add, ALU.add, wkey=Z[j])
    if ctx.fused:
        for j in range(8):
            tt(em, "vector", ctx.X[j][:], Z[j][:], ctx.X[j][:], ALU.add,
               wkey=ctx.X[j])
    else:
        for w in range(NTW):
            t0 = w * 512
            for j in range(8):
                em.dma("gpsimd", out=xt[j][:], in_=ctx.x_in[128 * j:128 * (j + 1),
                                                            t0:t0 + 512],
                       reads=[xt[j]], writes=[xt[j]])
                tt(em, "vector", xt[j][:], Z[j][:, t0:t0 + 512], xt[j][:],
                   ALU.add, wkey=xt[j])
                em.dma("gpsimd", out=ctx.y_out[128 * j:128 * (j + 1), t0:t0 + 512],
                       in_=xt[j][:], reads=[xt[j]], writes=[f"P4OUT{j}"])
    em.barrier()


def inp_b1col(em, ctx, l, m):
    return ctx.p4b1c[l][:, m:m + 1]


def prep_phase4_inputs(inputs):
    d = {}
    d["wqkv"] = np.ascontiguousarray(inputs["L1_ffn_wqkv"], np.float32)
    d["wo"] = np.ascontiguousarray(inputs["L1_ffn_wo"], np.float32)
    w1 = inputs["L1_ffn_w1"]
    w1p = np.zeros((2, 1024, 5632), np.float32)
    w1p[:, :, 0:2730] = w1[:, :, 0:2730]
    w1p[:, :, 2816:2816 + 2730] = w1[:, :, 2730:5460]
    d["w1p"] = w1p
    w2 = inputs["L1_ffn_w2"]
    w2p = np.zeros((2, 2816, 1024), np.float32)
    w2p[:, 0:2730, :] = w2
    d["w2p"] = w2p
    b1 = inputs["L1_ffn_b1"]
    b1p = np.zeros((2, 5632), np.float32)
    b1p[:, 0:2730] = b1[:, 0:2730]
    b1p[:, 2816:2816 + 2730] = b1[:, 2730:5460]
    d["b1cols4"] = np.ascontiguousarray(
        np.stack([b1p[l].reshape(44, 128).T for l in range(2)]), np.float32)
    cols = []
    for l in range(2):
        sub = _pack_pvec([inputs["L1_ffn_ln_g"][l], inputs["L1_ffn_ln_b"][l],
                          inputs["L1_ffn_b2"][l]])
        qs = np.tile(inputs["L1_ffn_q_scale"][l], 8).reshape(4, 128).T
        ks = np.tile(inputs["L1_ffn_k_scale"][l], 8).reshape(4, 128).T
        cols.append(np.concatenate([sub, qs, ks], axis=1))
    d["pvec4"] = np.ascontiguousarray(np.concatenate(cols, axis=1), np.float32)
    return d


PHASE4_TENSORS = [
    ("wqkv", [2, 1024, 1536]), ("wo", [2, 512, 1024]), ("w1p", [2, 1024, 5632]),
    ("w2p", [2, 2816, 1024]), ("b1cols4", [2, 128, 44]), ("pvec4", [128, 64]),
]


def build_nc_phase4(T=2048, debug=False):
    from contextlib import ExitStack
    nc = bass.Bass("TRN2", target_bir_lowering=False)
    x_in = nc.dram_tensor("x_fm", [1024, T], F32, kind="ExternalInput").ap()
    y_out = nc.dram_tensor("y_fm", [1024, T], F32, kind="ExternalOutput").ap()
    inp = {nm: nc.dram_tensor(nm, shp, F32, kind="ExternalInput").ap()
           for nm, shp in PHASE4_TENSORS}
    em = Em(nc)
    ctx = Ctx()
    ctx.fused = False
    ctx.debug = debug
    ctx.dbg_done = set()
    ctx.x_in = x_in
    ctx.y_out = y_out
    with ExitStack() as st:
        E = st.enter_context
        if debug:
            ctx.dbg_stage = E(nc.sbuf_tensor("dbgstage", [128, 1024], F32))
        setup_common(nc, em, ctx, E, T)
        ctx.p4b1c = [E(nc.sbuf_tensor(f"p4b1c{l}", [128, 44], F32)) for l in range(2)]
        for l in range(2):
            em.dma("sync", out=ctx.p4b1c[l][:], in_=inp["b1cols4"][l],
                   writes=[ctx.p4b1c[l]])
        build_phase4(nc, em, ctx, E, inp)
        em.build()
    print("phase4: ops=%d" % len(em.ops))
    return nc


def _pack_pvec(vectors):
    cols = [np.ascontiguousarray(v.reshape(8, 128).T) for v in vectors]
    return np.ascontiguousarray(np.concatenate(cols, axis=1).astype(np.float32))


def prep_phase1_inputs(inputs):
    d = {}
    d["Wr"] = np.ascontiguousarray(inputs["L0_attn_Wr"], np.float32)
    d["Wk"] = np.ascontiguousarray(inputs["L0_attn_Wk"], np.float32)
    d["Wv"] = np.ascontiguousarray(inputs["L0_attn_Wv"], np.float32)
    d["Wg"] = np.ascontiguousarray(inputs["L0_attn_Wg"], np.float32)
    d["Wo"] = np.ascontiguousarray(inputs["L0_attn_Wo"], np.float32)
    d["tm_w1"] = np.ascontiguousarray(inputs["L0_attn_tm_w1"], np.float32)
    d["tm_w2"] = np.ascontiguousarray(inputs["L0_attn_tm_w2"], np.float32)
    d["td_w1"] = np.ascontiguousarray(inputs["L0_attn_td_w1"], np.float32)
    d["td_w2"] = np.ascontiguousarray(inputs["L0_attn_td_w2"], np.float32)
    d["pvec"] = _pack_pvec([
        inputs["L0_attn_x_maa"], inputs["L0_attn_w_maa"], inputs["L0_attn_k_maa"],
        inputs["L0_attn_v_maa"], inputs["L0_attn_r_maa"], inputs["L0_attn_g_maa"],
        inputs["L0_attn_time_decay"].reshape(-1), inputs["L0_attn_time_first"].reshape(-1),
        inputs["L0_attn_ln_w"], inputs["L0_attn_ln_b"]])
    d["ln_w"] = np.ascontiguousarray(inputs["L0_attn_ln_w"], np.float32)
    d["ln_b"] = np.ascontiguousarray(inputs["L0_attn_ln_b"], np.float32)
    return d


PHASE1_TENSORS = [
    ("Wr", [1024, 1024]), ("Wk", [1024, 1024]), ("Wv", [1024, 1024]),
    ("Wg", [1024, 1024]), ("Wo", [1024, 1024]), ("tm_w1", [1024, 160]),
    ("tm_w2", [5, 32, 1024]), ("td_w1", [1024, 64]), ("td_w2", [64, 1024]),
    ("pvec", [128, 80]), ("ln_w", [1024]), ("ln_b", [1024]),
]


def build_nc_phase1(T=2048, debug=False):
    from contextlib import ExitStack
    nc = bass.Bass("TRN2", target_bir_lowering=False)
    x_in = nc.dram_tensor("x_fm", [1024, T], F32, kind="ExternalInput").ap()
    y_out = nc.dram_tensor("y_fm", [1024, T], F32, kind="ExternalOutput").ap()
    inp = {nm: nc.dram_tensor(nm, shp, F32, kind="ExternalInput").ap()
           for nm, shp in PHASE1_TENSORS}
    em = Em(nc)
    ctx = Ctx()
    ctx.fused = False
    ctx.debug = debug
    ctx.dbg_done = set()
    ctx.x_in = x_in
    ctx.y_out = y_out
    with ExitStack() as st:
        E = st.enter_context
        if debug:
            ctx.dbg_stage = E(nc.sbuf_tensor("dbgstage", [128, 1024], F32))
        setup_common(nc, em, ctx, E, T)
        build_phase1(nc, em, ctx, E, inp)
        em.build()
    print("phase1: ops=%d sbuf_left=%d psum_left=%d" %
          (len(em.ops), nc.sbuf_bytes_remaining, nc.psum_banks_remaining))
    return nc


def build_nc_fused(T=2048, ncores=8):
    from contextlib import ExitStack
    nc = bass.Bass("TRN2", target_bir_lowering=False)
    x_in = nc.dram_tensor("x_fm", [1024, T], F32, kind="ExternalInput").ap()
    y_out = nc.dram_tensor("y_fm", [1024, T], F32, kind="ExternalOutput").ap()
    xs = [nc.dram_tensor(f"xs{i}", [1024, T], F32, kind="Internal").ap()
          for i in range(3)]
    tens = [PHASE1_TENSORS, PHASE2_TENSORS, PHASE3_TENSORS, PHASE4_TENSORS]
    inps = []
    for ph in range(4):
        inps.append({nm: nc.dram_tensor(f"p{ph + 1}_{nm}", shp, F32,
                                        kind="ExternalInput").ap()
                     for nm, shp in tens[ph]})
    em = Em(nc)
    ctx = Ctx()
    ctx.ncores = ncores
    ctx.moe_sparse = False
    ctx.fused = False
    ctx.debug = False
    ctx.dbg_done = set()
    with ExitStack() as outer:
        E0 = outer.enter_context
        setup_common(nc, em, ctx, E0, T)
        em.barrier()
        ctx.x_in, ctx.y_out = x_in, xs[0]
        with ExitStack() as s:
            build_phase1(nc, em, ctx, s.enter_context, inps[0])
        with ExitStack() as s:
            E = s.enter_context
            ctx.X = alloc_stream(nc, E, T)
            load_stream(em, ctx, xs[0])
            em.barrier()
            build_phase2(nc, em, ctx, E, inps[1])
            for j in range(8):
                em.dma("gpsimd", out=xs[1][128 * j:128 * (j + 1), :],
                       in_=ctx.X[j][:], reads=[ctx.X[j]], writes=[f"XS1_{j}"])
            em.barrier()
        ctx.x_in, ctx.y_out = xs[1], xs[2]
        with ExitStack() as s:
            build_phase3(nc, em, ctx, s.enter_context, inps[2])
        ctx.x_in, ctx.y_out = xs[2], y_out
        with ExitStack() as s:
            E = s.enter_context
            ctx.p4b1c = [E(nc.sbuf_tensor(f"p4b1c{l}", [128, 44], F32))
                         for l in range(2)]
            for l in range(2):
                em.dma("sync", out=ctx.p4b1c[l][:], in_=inps[3]["b1cols4"][l],
                       writes=[ctx.p4b1c[l]])
            build_phase4(nc, em, ctx, E, inps[3])
        em.build()
    print("fused: ops=%d" % len(em.ops))
    return nc


def kernel(**inputs):
    T = 2048
    x = np.asarray(inputs["L0_attn_x"], np.float32)
    preps = [prep_phase1_inputs(inputs), prep_phase2_inputs(inputs),
             prep_phase3_inputs(inputs), prep_phase4_inputs(inputs)]
    nc = build_nc_fused(T, NCORE)
    base = {}
    for ph in range(4):
        for k, v in preps[ph].items():
            base[f"p{ph + 1}_{k}"] = v
    in_maps = []
    for c in range(NCORE):
        m = dict(base)
        m["x_fm"] = np.ascontiguousarray(x[c].T)
        in_maps.append(m)
    res = run_bass_kernel_spmd(nc, in_maps, core_ids=list(range(NCORE)))
    out = np.stack([np.ascontiguousarray(np.asarray(r["y_fm"]).T)
                    for r in res.results])
    return out.astype(np.float32)


# version 16
# speedup vs baseline: 1.0469x; 1.0469x over previous
import os
import numpy as np
import ml_dtypes
import concourse.bass as bass
import concourse.mybir as mybir
from concourse.bass_utils import run_bass_kernel_spmd

F32 = mybir.dt.float32
BF16 = mybir.dt.bfloat16
AF = mybir.ActivationFunctionType
ALU = mybir.AluOpType
AX = mybir.AxisListType
BF16NP = ml_dtypes.bfloat16

D = 1024
NCORE = 8
CENG = ("sync", "tensor", "vector", "scalar", "gpsimd")


class Em:
    def __init__(self, nc):
        self.nc = nc
        self.ops = []
        self.last_w = {}
        self.readers = {}

    @staticmethod
    def _k(h):
        if isinstance(h, str):
            return h
        name = getattr(h, "name", None)
        if name is None:
            name = h.tensor.name
        return name

    def op(self, eng, fn, reads=(), writes=(), dma=False, writes_accum=()):
        if not hasattr(self, "accum"):
            self.accum = {}
        i = len(self.ops)
        deps = set()
        for h in reads:
            k = self._k(h)
            j = self.last_w.get(k)
            if j is not None:
                deps.add(j)
            deps.update(self.accum.get(k, ()))
        for h in writes:
            k = self._k(h)
            j = self.last_w.get(k)
            if j is not None:
                deps.add(j)
            deps.update(self.readers.get(k, ()))
            deps.update(self.accum.get(k, ()))
        for h in writes_accum:
            k = self._k(h)
            j = self.last_w.get(k)
            if j is not None:
                deps.add(j)
            deps.update(self.readers.get(k, ()))
        o = {"i": i, "eng": eng, "fn": fn, "dma": bool(dma), "deps": deps,
             "inc": bool(dma), "bar": False}
        self.ops.append(o)
        for h in reads:
            self.readers.setdefault(self._k(h), set()).add(i)
        for h in writes:
            k = self._k(h)
            self.last_w[k] = i
            self.readers[k] = set()
            self.accum[k] = set()
        for h in writes_accum:
            self.accum.setdefault(self._k(h), set()).add(i)
        return o

    def barrier(self):
        self.ops.append({"i": len(self.ops), "eng": None, "fn": None,
                         "dma": False, "deps": set(), "inc": False, "bar": True})
        self.last_w = {}
        self.readers = {}
        self.accum = {}

    def dma(self, eng, out, in_, reads=(), writes=(), writes_accum=(), **kw):
        return self.op(eng,
                       lambda e, out=out, in_=in_, kw=kw: e.dma_start(out=out, in_=in_, **kw),
                       reads=reads, writes=writes, writes_accum=writes_accum, dma=True)

    def build(self):
        NQ = 8
        ops = self.ops
        if not (ops and ops[-1]["bar"]):
            self.barrier()
            ops = self.ops
        for o in ops:
            if o["bar"]:
                continue
            for d in o["deps"]:
                p = ops[d]
                if (p["eng"] != o["eng"]) or p["dma"]:
                    p["inc"] = True
        lastc = {e: None for e in CENG}
        for o in ops:
            if o["bar"]:
                for e in CENG:
                    if lastc[e] is not None:
                        ops[lastc[e]]["inc"] = True
                continue
            if not o["dma"]:
                lastc[o["eng"]] = o["i"]
        ccnt = {e: 0 for e in CENG}
        dcnt = {}
        dseq = {e: 0 for e in CENG}
        for o in ops:
            if o["bar"]:
                o["floor"] = (dict(ccnt), dict(dcnt))
                continue
            if o["dma"]:
                slot = dseq[o["eng"]] % NQ
                dseq[o["eng"]] += 1
                key = (o["eng"], slot)
                o["qkey"] = key
                o["prev"] = dcnt.get(key, 0) * 16
                dcnt[key] = dcnt.get(key, 0) + 1
                o["cval"] = dcnt[key] * 16
            elif o["inc"]:
                ccnt[o["eng"]] += 1
                o["cval"] = ccnt[o["eng"]]

        nc = self.nc
        csem = {e: nc.alloc_semaphore(f"c_{e}") for e in CENG}
        qsem = {(e, s): nc.alloc_semaphore(f"q_{e}{s}") for e in CENG for s in range(NQ)}

        def emit_engine(e_obj, ename):
            wm_c = {x: 0 for x in CENG}
            wm_d = {}
            drained_upto = -1
            last_own = -1
            pend_floor = None

            def wait_d(key, v):
                if v > wm_d.get(key, 0):
                    e_obj.wait_ge(qsem[key], v)
                    wm_d[key] = v

            def do_floor():
                nonlocal drained_upto, pend_floor
                fc, fd = pend_floor
                for x in CENG:
                    if fc[x] > wm_c[x]:
                        e_obj.wait_ge(csem[x], fc[x])
                        wm_c[x] = fc[x]
                for key, n in fd.items():
                    wait_d(key, n * 16)
                drained_upto = last_own
                pend_floor = None

            for o in ops:
                if o["bar"]:
                    pend_floor = o["floor"]
                    continue
                if o["eng"] != ename:
                    continue
                if pend_floor is not None:
                    do_floor()
                need_drain = False
                for d in sorted(o["deps"]):
                    p = ops[d]
                    if p["dma"]:
                        wait_d(p["qkey"], p["cval"])
                    elif p["eng"] == ename:
                        if ename != "tensor" and p["i"] > drained_upto:
                            need_drain = True
                    else:
                        v = p["cval"]
                        if v > wm_c[p["eng"]]:
                            e_obj.wait_ge(csem[p["eng"]], v)
                            wm_c[p["eng"]] = v
                if o["dma"] and o["prev"] > 0:
                    wait_d(o["qkey"], o["prev"])
                if need_drain:
                    e_obj.drain()
                    drained_upto = last_own
                try:
                    inst = o["fn"](e_obj)
                except Exception:
                    import traceback as _tb
                    print("FAILING OP idx", o["i"], "eng", o["eng"])
                    _tb.print_exc()
                    raise
                if o["dma"]:
                    inst.then_inc(qsem[o["qkey"]], 16)
                elif o["inc"]:
                    inst.then_inc(csem[o["eng"]], 1)
                last_own = o["i"]
            if pend_floor is not None:
                do_floor()

        with nc.Block() as block:
            for ename in CENG:
                getattr(block, ename)(lambda e, ename=ename: emit_engine(e, ename))


def tt(em, eng, out, in0, in1, op, wkey=None):
    em.op(eng, lambda e, out=out, in0=in0, in1=in1, op=op: e.tensor_tensor(out, in0, in1, op),
          reads=[in0, in1], writes=[wkey if wkey is not None else out])


def ts(em, eng, out, in0, s1, s2, op0, op1=None, wkey=None):
    def fn(e, out=out, in0=in0, s1=s1, s2=s2, op0=op0, op1=op1):
        if op1 is None:
            return e.tensor_single_scalar(out, in0, s1, op0)
        return e.tensor_scalar(out, in0, s1, s2, op0, op1)
    rd = [in0]
    for s in (s1, s2):
        if hasattr(s, "tensor"):
            rd.append(s)
    em.op(eng, fn, reads=rd, writes=[wkey if wkey is not None else out])


def stt(em, eng, out, in0, scalar, in1, op0, op1, wkey=None):
    rd = [in0, in1]
    if hasattr(scalar, "tensor"):
        rd.append(scalar)
    em.op(eng, lambda e, out=out, in0=in0, scalar=scalar, in1=in1, op0=op0, op1=op1:
          e.scalar_tensor_tensor(out, in0, scalar, in1, op0, op1),
          reads=rd, writes=[wkey if wkey is not None else out])


def act(em, out, in_, func, bias=0.0, scale=1.0, wkey=None):
    rd = [in_]
    if hasattr(bias, "tensor"):
        rd.append(bias)
    em.op("scalar", lambda e, out=out, in_=in_, func=func, bias=bias, scale=scale:
          e.activation(out, in_, func, bias=bias, scale=scale),
          reads=rd, writes=[wkey if wkey is not None else out])


def mm(em, out, lhsT, rhs, start=True, stop=True):
    em.op("tensor", lambda e, out=out, lhsT=lhsT, rhs=rhs, start=start, stop=stop:
          e.matmul(out, lhsT, rhs, start=start, stop=stop),
          reads=[lhsT, rhs], writes=[out])


def transp(em, out, in_, ident):
    em.op("tensor", lambda e, out=out, in_=in_, ident=ident: e.transpose(out, in_, ident),
          reads=[in_, ident], writes=[out])


def copy(em, eng, out, in_, wkey=None):
    em.op(eng, lambda e, out=out, in_=in_: e.tensor_copy(out, in_),
          reads=[in_], writes=[wkey if wkey is not None else out])


def scopy(em, out, in_, wkey=None):
    em.op("scalar", lambda e, out=out, in_=in_: e.copy(out, in_),
          reads=[in_], writes=[wkey if wkey is not None else out])


def memset(em, eng, ap_, val, wkey):
    em.op(eng, lambda e, ap_=ap_, val=val: e.memset(ap_, val), writes=[wkey])


def reduce(em, out, in_, op, eng="vector", wkey=None):
    em.op(eng, lambda e, out=out, in_=in_, op=op: e.tensor_reduce(out, in_, AX.X, op),
          reads=[in_], writes=[wkey if wkey is not None else out])


def recip(em, out, in_, wkey=None):
    em.op("vector", lambda e, out=out, in_=in_: e.reciprocal(out, in_),
          reads=[in_], writes=[wkey if wkey is not None else out])


def scan(em, out, d0, d1, init, op0, op1):
    em.op("vector", lambda e, out=out, d0=d0, d1=d1, init=init, op0=op0, op1=op1:
          e.tensor_tensor_scan(out, d0, d1, init, op0, op1),
          reads=[d0, d1], writes=[out])


class Ctx:
    pass


def setup_common(nc, em, ctx, E, T):
    ctx.T = T
    eye_d = nc.inline_tensor(np.eye(128).astype(BF16NP), "eye_bf_d")
    ctx.eye = E(nc.sbuf_tensor("eye", [128, 128], BF16))
    em.dma("sync", out=ctx.eye[:], in_=eye_d.ap(), writes=[ctx.eye])
    ones_d = nc.inline_tensor(np.ones((1, 128), np.float32), "ones_f32_d")
    ctx.ones_f = E(nc.sbuf_tensor("ones_f", [1, 128], F32))
    em.dma("sync", out=ctx.ones_f[:], in_=ones_d.ap(), writes=[ctx.ones_f])
    onesb_d = nc.inline_tensor(np.ones((128, 1), BF16NP), "ones_col_bf_d")
    ctx.ones_col_bf = E(nc.sbuf_tensor("ones_col_bf", [128, 1], BF16))
    em.dma("sync", out=ctx.ones_col_bf[:], in_=onesb_d.ap(), writes=[ctx.ones_col_bf])
    ones1b_d = nc.inline_tensor(np.ones((1, 128), BF16NP), "ones_row_bf_d")
    ctx.ones_row_bf = E(nc.sbuf_tensor("ones_row_bf", [1, 128], BF16))
    em.dma("sync", out=ctx.ones_row_bf[:], in_=ones1b_d.ap(), writes=[ctx.ones_row_bf])
    ctx.zeros = E(nc.sbuf_tensor("zeros", [128, 512], F32))
    memset(em, "vector", ctx.zeros[:], 0.0, wkey=ctx.zeros)
    ctx.eps5 = E(nc.sbuf_tensor("eps5", [128, 1], F32))
    memset(em, "vector", ctx.eps5[:], 1e-5, wkey=ctx.eps5)
    ctx.eps6 = E(nc.sbuf_tensor("eps6", [128, 1], F32))
    memset(em, "vector", ctx.eps6[:], 1e-6, wkey=ctx.eps6)


def alloc_stream(nc, E, T):
    return [E(nc.sbuf_tensor(f"X{j}", [128, T], F32)) for j in range(8)]


def load_stream(em, ctx, x_fm_ap):
    for j in range(8):
        em.dma("sync", out=ctx.X[j][:], in_=x_fm_ap[128 * j:128 * (j + 1), :],
               writes=[ctx.X[j]])


def store_stream(em, ctx, y_fm_ap):
    for j in range(8):
        em.dma("gpsimd", out=y_fm_ap[128 * j:128 * (j + 1), :], in_=ctx.X[j][:],
               reads=[ctx.X[j]], writes=["OUT_y%d" % j])


def cast_weight(nc, em, ctx, E, name, dram_ap, nrow, ncol, stage=None):
    chunks = []
    for c in range(nrow // 128):
        tile = E(nc.sbuf_tensor(f"{name}_{c}", [128, ncol], BF16))
        em.dma("gpsimd", out=tile[:], in_=dram_ap[128 * c:128 * (c + 1), :],
               reads=[tile], writes=[tile])
        chunks.append(tile)
    return chunks


def dbg(em, ctx, name, tile, rows, cols):
    if not getattr(ctx, "debug", False) or name in ctx.dbg_done:
        return
    ctx.dbg_done.add(name)
    d = em.nc.dram_tensor("dbg_" + name, [rows, cols], F32, kind="ExternalOutput").ap()
    t = ctx.dbg_stage
    copy(em, "vector", t[0:rows, 0:cols], tile, wkey=t)
    em.dma("gpsimd", out=d, in_=t[0:rows, 0:cols], reads=[t], writes=["DBG" + name])


def build_phase1(nc, em, ctx, E, inp):
    T = ctx.T
    TT_ = 256
    L = 64
    NT = T // TT_
    NCH = TT_ // L
    NSUB = TT_ // 128

    P = [E(nc.psum_tensor(f"p1P{i}", [128, 512], F32)) for i in range(6)]
    P7 = E(nc.psum_tensor("p1P7", [128, 1024], BF16))

    tm_w1 = cast_weight(nc, em, ctx, E, "tmw1", inp["tm_w1"], 1024, 160)
    td_w1 = cast_weight(nc, em, ctx, E, "tdw1", inp["td_w1"], 1024, 64)
    tm_w2 = []
    for f in range(5):
        tile = E(nc.sbuf_tensor(f"tmw2_{f}", [32, 1024], BF16))
        em.dma("gpsimd", out=tile[:], in_=inp["tm_w2"][f], reads=[tile], writes=[tile])
        tm_w2.append(tile)
    td_w2 = E(nc.sbuf_tensor("tdw2", [64, 1024], BF16))
    em.dma("gpsimd", out=td_w2[:], in_=inp["td_w2"], reads=[td_w2], writes=[td_w2])

    pv = E(nc.sbuf_tensor("p1pv", [128, 80], F32))
    em.dma("sync", out=pv[:], in_=inp["pvec"], writes=[pv])

    def pvcol(idx, j, sl=None):
        if sl is not None:
            return pv[sl, idx * 8 + j:idx * 8 + j + 1]
        return pv[:, idx * 8 + j:idx * 8 + j + 1]
    IX, IW, IK, IV, IR, IG, ITD, ITF, ILW, ILB = range(10)

    lnrow_st = E(nc.sbuf_tensor("p1lnr", [1, 1024], BF16))
    ones164 = E(nc.sbuf_tensor("ones164", [1, 64], BF16))
    memset(em, "vector", ones164[:], 1.0, wkey=ones164)
    lnw_b = E(nc.sbuf_tensor("lnw_b", [64, 1024], BF16))
    lnb_b = E(nc.sbuf_tensor("lnb_b", [64, 1024], BF16))
    for dst, nm in ((lnw_b, "ln_w"), (lnb_b, "ln_b")):
        em.dma("gpsimd", out=lnrow_st[0:1, :], in_=inp[nm][None, :],
               reads=[lnrow_st], writes=[lnrow_st])
        mm(em, P[0][0:64, 0:512], ones164[:], lnrow_st[0:1, 0:512])
        mm(em, P[1][0:64, 0:512], ones164[:], lnrow_st[0:1, 512:1024])
        copy(em, "vector", dst[:, 0:512], P[0][0:64, 0:512], wkey=dst)
        copy(em, "vector", dst[:, 512:1024], P[1][0:64, 0:512], wkey=dst)

    mu_d = nc.inline_tensor(np.triu(np.ones((L, L), np.float32), 1).astype(BF16NP),
                            "maskU64")
    maskU = E(nc.sbuf_tensor("maskU", [L, L], BF16))
    em.dma("sync", out=maskU[:], in_=mu_d.ap(), writes=[maskU])

    em.barrier()

    xt = [E(nc.sbuf_tensor(f"p1x{j}", [128, TT_], F32)) for j in range(8)]
    xn_bf = [E(nc.sbuf_tensor(f"xn{j}", [128, TT_], BF16)) for j in range(8)]
    sx_bf = [E(nc.sbuf_tensor(f"sx{j}", [128, TT_], BF16)) for j in range(8)]
    u_p = [E(nc.sbuf_tensor(f"up{j}", [128, TT_], BF16)) for j in range(8)]
    u_g = [E(nc.sbuf_tensor(f"ug{j}", [128, TT_], BF16)) for j in range(8)]
    xn_last = E(nc.sbuf_tensor("xnlast", [128, 8], BF16))
    memset(em, "vector", xn_last[:], 0.0, wkey=xn_last)
    sq2 = [E(nc.sbuf_tensor(f"p1sq{i}", [128, TT_], BF16)) for i in range(2)]
    tmpf = [E(nc.sbuf_tensor(f"p1tmp{i}", [128, TT_], F32)) for i in range(2)]
    lnrow = [E(nc.sbuf_tensor(f"p1row{i}", [1, TT_], F32)) for i in range(4)]
    tm_f = [E(nc.sbuf_tensor(f"tmf{f}", [32, TT_], BF16)) for f in range(5)]
    tdh = E(nc.sbuf_tensor("tdh", [64, TT_], BF16))
    exp_w = E(nc.sbuf_tensor("expw", [128, TT_], F32))
    negew = E(nc.sbuf_tensor("negew", [128, TT_], F32))
    c_rel = [E(nc.sbuf_tensor(f"crel{j}", [128, TT_], F32)) for j in range(8)]
    E1 = [E(nc.sbuf_tensor(f"E1_{j}", [128, TT_], BF16)) for j in range(8)]
    E2 = [E(nc.sbuf_tensor(f"E2_{j}", [128, TT_], BF16)) for j in range(8)]
    E4h = [E(nc.sbuf_tensor(f"E4h{h}", [64, NCH], F32)) for h in range(16)]
    r_t = [E(nc.sbuf_tensor(f"rt{h}", [64, TT_], BF16)) for h in range(16)]
    r_raw = [E(nc.sbuf_tensor(f"rr{h}", [64, TT_], BF16)) for h in range(16)]
    k_t = [E(nc.sbuf_tensor(f"kt{h}", [64, TT_], BF16)) for h in range(16)]
    ku = [E(nc.sbuf_tensor(f"kuv{h}", [64, TT_], BF16)) for h in range(16)]
    kh_tmp = [E(nc.sbuf_tensor(f"khtmp{i}", [64, 64], BF16)) for i in range(2)]
    khT = [E(nc.sbuf_tensor(f"khT{c}", [64, 1024], BF16)) for c in range(NCH)]
    v_tm = [E(nc.sbuf_tensor(f"vtm{c}", [64, 1024], BF16)) for c in range(NCH)]
    g_tm = [E(nc.sbuf_tensor(f"gtm{c}", [64, 1024], BF16)) for c in range(NCH)]
    S_f = [E(nc.sbuf_tensor(f"Sf{h}", [64, 64], F32)) for h in range(16)]
    S_b = [E(nc.sbuf_tensor(f"Sb{h}", [64, 64], BF16)) for h in range(16)]
    for h in range(16):
        memset(em, "vector", S_f[h][:], 0.0, wkey=S_f[h])
        memset(em, "vector", S_b[h][:], 0.0, wkey=S_b[h])
    At = [E(nc.sbuf_tensor(f"At{i}", [64, 64], BF16)) for i in range(2)]
    At2 = [E(nc.sbuf_tensor(f"At2{i}", [64, 64], F32)) for i in range(2)]
    y_sb = E(nc.sbuf_tensor("ysb", [64, 1024], F32))
    gn_sq = E(nc.sbuf_tensor("gnsq", [64, 1024], F32))
    gn_mu = E(nc.sbuf_tensor("gnmu", [64, 16], F32))
    gn_s2 = E(nc.sbuf_tensor("gns2", [64, 16], F32))
    gn_rs = E(nc.sbuf_tensor("gnrs", [64, 16], F32))
    y2 = [E(nc.sbuf_tensor(f"y2_{c}", [64, 1024], BF16)) for c in range(NCH)]
    y_fm = [E(nc.sbuf_tensor(f"yfm{j}", [128, TT_], BF16)) for j in range(8)]
    wbuf = [E(nc.sbuf_tensor(f"p1wb{i}", [128, 256], BF16)) for i in range(4)]
    wbig = [E(nc.sbuf_tensor(f"p1wB{i}", [128, 512], BF16)) for i in range(4)]

    def proj_std(wname, consumer):
        for pj in range(4):
            for kk in range(8):
                wb = wbuf[kk % 4]
                em.dma("gpsimd", out=wb[:],
                       in_=inp[wname][128 * kk:128 * (kk + 1), 256 * pj:256 * (pj + 1)],
                       reads=[wb], writes=[wb])
                mm(em, P[2][:, 0:TT_], wb[:, 0:128], u_p[kk][:],
                   start=(kk == 0), stop=(kk == 7))
                mm(em, P[3][:, 0:TT_], wb[:, 128:256], u_p[kk][:],
                   start=(kk == 0), stop=(kk == 7))
            consumer(2 * pj, P[2])
            consumer(2 * pj + 1, P[3])

    def proj_flip(wname, src, consumer):
        for nh in range(2):
            for kk in range(8):
                wb = wbig[kk % 4]
                em.dma("gpsimd", out=wb[:],
                       in_=inp[wname][128 * kk:128 * (kk + 1), 512 * nh:512 * (nh + 1)],
                       reads=[wb], writes=[wb])
                for s in range(NSUB):
                    mm(em, P[2 + s][:, 0:512], src[kk][:, 128 * s:128 * (s + 1)], wb[:],
                       start=(kk == 0), stop=(kk == 7))
            for s in range(NSUB):
                consumer(s, nh, P[2 + s])

    for it in range(NT):
        t0 = it * TT_
        if ctx.fused:
            for j in range(8):
                copy(em, "vector", xt[j][:], ctx.X[j][:, t0:t0 + TT_], wkey=xt[j])
        else:
            for j in range(8):
                em.dma("sync", out=xt[j][:], in_=ctx.x_in[128 * j:128 * (j + 1), t0:t0 + TT_],
                       reads=[xt[j]], writes=[xt[j]])
        for j in range(8):
            copy(em, "vector", xn_bf[j][:], xt[j][:], wkey=xn_bf[j])
            act(em, sq2[j % 2][:], xn_bf[j][:], AF.Square, wkey=sq2[j % 2])
            mm(em, P[4][0:1, 0:TT_], ctx.ones_col_bf[:], xn_bf[j][:],
               start=(j == 0), stop=(j == 7))
            mm(em, P[5][0:1, 0:TT_], ctx.ones_col_bf[:], sq2[j % 2][:],
               start=(j == 0), stop=(j == 7))
        mean, msq, var, rstd = lnrow
        ts(em, "vector", mean[:], P[4][0:1, 0:TT_], 1.0 / 1024, None, ALU.mult)
        ts(em, "vector", msq[:], P[5][0:1, 0:TT_], 1.0 / 1024, None, ALU.mult)
        tt(em, "vector", var[:], mean[:], mean[:], ALU.mult)
        tt(em, "vector", var[:], msq[:], var[:], ALU.subtract)
        act(em, rstd[:], var[:], AF.Sqrt, bias=ctx.eps5[0:1, 0:1], wkey=rstd)
        recip(em, rstd[:], rstd[:])
        mm(em, P[0][:, 0:TT_], ctx.ones_f[:], mean[:])
        mm(em, P[1][:, 0:TT_], ctx.ones_f[:], rstd[:])
        for j in range(8):
            tf = tmpf[j % 2]
            tt(em, "vector", tf[:], xt[j][:], P[0][:, 0:TT_], ALU.subtract, wkey=tf)
            tt(em, "vector", xn_bf[j][:], tf[:], P[1][:, 0:TT_], ALU.mult)
        dbg(em, ctx, "xn0", xn_bf[0][:], 128, TT_)
        for j in range(8):
            eng = "gpsimd" if (j % 2 == 1) else "vector"
            tt(em, eng, sx_bf[j][:, 1:TT_], xn_bf[j][:, 0:TT_ - 1],
               xn_bf[j][:, 1:TT_], ALU.subtract, wkey=sx_bf[j])
            tt(em, eng, sx_bf[j][:, 0:1], xn_last[:, j:j + 1],
               xn_bf[j][:, 0:1], ALU.subtract, wkey=sx_bf[j])
            copy(em, eng, xn_last[:, j:j + 1], xn_bf[j][:, TT_ - 1:TT_],
                 wkey=xn_last)
        for j in range(8):
            stt(em, "vector", u_p[j][:],
                sx_bf[j][:], pvcol(IX, j), xn_bf[j][:], ALU.mult, ALU.add)
        for j in range(8):
            mm(em, P[2][0:96, 0:TT_], tm_w1[j][:, 0:96], u_p[j][:],
               start=(j == 0), stop=(j == 7))
            mm(em, P[3][0:64, 0:TT_], tm_w1[j][:, 96:160], u_p[j][:],
               start=(j == 0), stop=(j == 7))
        for f in range(3):
            act(em, tm_f[f][:], P[2][32 * f:32 * (f + 1), 0:TT_], AF.Tanh,
                wkey=tm_f[f])
        for f in range(2):
            act(em, tm_f[3 + f][:], P[3][32 * f:32 * (f + 1), 0:TT_], AF.Tanh,
                wkey=tm_f[3 + f])

        def emit_path(f_idx, maa_idx, dest):
            for j in range(8):
                pm = P[j % 2]
                mm(em, pm[:, 0:TT_], tm_w2[f_idx][:, 128 * j:128 * (j + 1)],
                   tm_f[f_idx][:])
                tf = tmpf[j % 2]
                eng = "gpsimd" if (j % 2 == 1) else "vector"
                ts(em, "vector", tf[:], pm[:, 0:TT_], pvcol(maa_idx, j), None,
                   ALU.add, wkey=tf)
                tt(em, eng, tf[:], tf[:], sx_bf[j][:], ALU.mult, wkey=tf)
                tt(em, eng, dest[j][:], tf[:], xn_bf[j][:], ALU.add)

        dbg(em, ctx, "sx0", sx_bf[0][:], 128, TT_)
        dbg(em, ctx, "tmf0", tm_f[0][:], 32, TT_)
        dbg(em, ctx, "tmf4", tm_f[4][:], 32, TT_)
        emit_path(0, IW, u_p)
        dbg(em, ctx, "uw0", u_p[0][:], 128, TT_)
        for j in range(8):
            mm(em, P[4][0:64, 0:TT_], td_w1[j][:, 0:64], u_p[j][:],
               start=(j == 0), stop=(j == 7))
        act(em, tdh[:], P[4][0:64, 0:TT_], AF.Tanh, wkey=tdh)
        for j in range(8):
            pm = P[j % 2]
            mm(em, pm[:, 0:TT_], td_w2[:, 128 * j:128 * (j + 1)], tdh[:])
            act(em, exp_w[:], pm[:, 0:TT_], AF.Exp, bias=pvcol(ITD, j), wkey=exp_w)
            ts(em, "vector", negew[:], exp_w[:], -1.0, None, ALU.mult, wkey=negew)
            scan(em, c_rel[j][:], negew[:], ctx.zeros[:, 0:TT_], 0.0,
                 ALU.add, ALU.add)
            for c in range(NCH - 1, 0, -1):
                cs = c * L
                ts(em, "vector", c_rel[j][:, cs:cs + L], c_rel[j][:, cs:cs + L],
                   c_rel[j][:, cs - 1:cs], None, ALU.subtract, wkey=c_rel[j])
            tf = tmpf[j % 2]
            tt(em, "vector", tf[:], c_rel[j][:], exp_w[:], ALU.add, wkey=tf)
            act(em, E1[j][:], tf[:], AF.Exp, wkey=E1[j])
            ts(em, "vector", tf[:], c_rel[j][:], -1.0, 70.0, ALU.mult, ALU.min,
               wkey=tf)
            act(em, E2[j][:], tf[:], AF.Exp, wkey=E2[j])
            em.op("scalar", lambda e, j=j: e.activation(
                E4h[2 * j][:, 0:NCH], c_rel[j][0:64, L - 1:TT_:L], AF.Exp),
                reads=[c_rel[j]], writes=[E4h[2 * j]])
            em.op("scalar", lambda e, j=j: e.activation(
                E4h[2 * j + 1][:, 0:NCH], c_rel[j][64:128, L - 1:TT_:L], AF.Exp),
                reads=[c_rel[j]], writes=[E4h[2 * j + 1]])

        dbg(em, ctx, "expw7", exp_w[:], 128, TT_)
        dbg(em, ctx, "crel0", c_rel[0][:], 128, TT_)
        dbg(em, ctx, "E1_0", E1[0][:], 128, TT_)
        dbg(em, ctx, "E2_0", E2[0][:], 128, TT_)
        dbg(em, ctx, "E4_0", E4h[0][:], 64, NCH)
        emit_path(1, IK, u_p)

        def k_consumer(j, pp):
            for half in range(2):
                h = 2 * j + half
                sl = slice(64 * half, 64 * (half + 1))
                tt(em, "vector", k_t[h][:], pp[sl, 0:TT_], E1[j][sl, :] if False
                   else E2[j][sl, :], ALU.mult)
                ts(em, "vector", ku[h][:], pp[sl, 0:TT_], pvcol(ITF, j, sl), None,
                   ALU.mult)
        proj_std("Wk", k_consumer)
        for c in range(NCH):
            for h in range(16):
                j, half = h // 2, h % 2
                kt_ = kh_tmp[h % 2]
                ts(em, "vector", kt_[:], k_t[h][:, c * L:(c + 1) * L],
                   E4h[h][:, c:c + 1], None, ALU.mult, wkey=kt_)
                transp(em, P7[0:64, 64 * h:64 * (h + 1)], kt_[:], ctx.eye[0:64, 0:64])
            copy(em, "vector", khT[c][:], P7[0:64, 0:1024], wkey=khT[c])

        dbg(em, ctx, "kt0", k_t[0][:], 64, TT_)
        dbg(em, ctx, "ku0", ku[0][:], 64, TT_)
        dbg(em, ctx, "khT0", khT[0][:], 64, 1024)
        emit_path(2, IV, u_p)

        def v_consumer(s, nh, pp):
            for half in range(2):
                c = 2 * s + half
                copy(em, "vector", v_tm[c][:, 512 * nh:512 * (nh + 1)],
                     pp[64 * half:64 * (half + 1), 0:512], wkey=v_tm[c])
        proj_flip("Wv", u_p, v_consumer)

        dbg(em, ctx, "vtm0", v_tm[0][:], 64, 1024)
        emit_path(3, IR, u_p)

        def r_consumer(j, pp):
            for half in range(2):
                h = 2 * j + half
                sl = slice(64 * half, 64 * (half + 1))
                tt(em, "vector", r_t[h][:], pp[sl, 0:TT_], E1[j][sl, :], ALU.mult)
                copy(em, "vector", r_raw[h][:], pp[sl, 0:TT_])
        proj_std("Wr", r_consumer)

        dbg(em, ctx, "rt0", r_t[0][:], 64, TT_)
        dbg(em, ctx, "rr0", r_raw[0][:], 64, TT_)
        emit_path(4, IG, u_g)

        def g_consumer(s, nh, pp):
            for half in range(2):
                c = 2 * s + half
                act(em, g_tm[c][:, 512 * nh:512 * (nh + 1)],
                    pp[64 * half:64 * (half + 1), 0:512], AF.Silu, wkey=g_tm[c])
        proj_flip("Wg", u_g, g_consumer)

        pA, pA2, pY, pS = P[0], P[1], P[2], P[3]
        for c in range(NCH):
            cs = c * L
            for h in range(16):
                j, half = h // 2, h % 2
                ia = h % 2
                mm(em, pA[0:64, 0:64], k_t[h][:, cs:cs + L], r_t[h][:, cs:cs + L])
                mm(em, pA2[0:64, 0:64], ku[h][:, cs:cs + L], r_raw[h][:, cs:cs + L])
                tt(em, "vector", At[ia][:], pA[0:64, 0:64], maskU[:], ALU.mult)
                tt(em, "vector", At2[ia][:], pA2[0:64, 0:64], ctx.eye[0:64, 0:64],
                   ALU.mult)
                tt(em, "vector", At[ia][:], At[ia][:], At2[ia][:], ALU.add)
                mm(em, pY[0:64, 0:64], At[ia][:], v_tm[c][:, 64 * h:64 * (h + 1)],
                   start=True, stop=False)
                mm(em, pY[0:64, 0:64], r_t[h][:, cs:cs + L], S_b[h][:],
                   start=False, stop=True)
                act(em, y_sb[:, 64 * h:64 * (h + 1)], pY[0:64, 0:64], AF.Identity,
                    scale=0.125, wkey=y_sb)
                mm(em, pS[0:64, 0:64], khT[c][:, 64 * h:64 * (h + 1)],
                   v_tm[c][:, 64 * h:64 * (h + 1)])
                stt(em, "vector", S_f[h][:], S_f[h][:],
                    E4h[h][:, c:c + 1], pS[0:64, 0:64],
                    ALU.mult, ALU.add)
                copy(em, "gpsimd" if h % 2 else "vector", S_b[h][:], S_f[h][:])
            act(em, gn_sq[:], y_sb[:], AF.Square, wkey=gn_sq)
            for h in range(16):
                reduce(em, gn_mu[:, h:h + 1], y_sb[:, 64 * h:64 * (h + 1)], ALU.add,
                       wkey=gn_mu)
                reduce(em, gn_s2[:, h:h + 1], gn_sq[:, 64 * h:64 * (h + 1)], ALU.add,
                       wkey=gn_s2)
            ts(em, "vector", gn_mu[:], gn_mu[:], 1.0 / 64, None, ALU.mult)
            ts(em, "vector", gn_s2[:], gn_s2[:], 1.0 / 64, None, ALU.mult)
            tt(em, "vector", gn_rs[:], gn_mu[:], gn_mu[:], ALU.mult)
            tt(em, "vector", gn_rs[:], gn_s2[:], gn_rs[:], ALU.subtract)
            act(em, gn_rs[:], gn_rs[:], AF.Sqrt, bias=ctx.eps5[0:64, 0:1], wkey=gn_rs)
            recip(em, gn_rs[:], gn_rs[:])
            for h in range(16):
                ts(em, "vector", y_sb[:, 64 * h:64 * (h + 1)],
                   y_sb[:, 64 * h:64 * (h + 1)], gn_mu[:, h:h + 1],
                   gn_rs[:, h:h + 1], ALU.subtract, ALU.mult, wkey=y_sb)
            if c == 0:
                dbg(em, ctx, "ysb0", y_sb[:], 64, 1024)
            tt(em, "gpsimd", y2[c][:], y_sb[:], lnw_b[:], ALU.mult)
            tt(em, "gpsimd", y2[c][:], y2[c][:], lnb_b[:], ALU.add)
            tt(em, "gpsimd", y2[c][:], y2[c][:], g_tm[c][:], ALU.mult)

        dbg(em, ctx, "gtm0", g_tm[0][:], 64, 1024)
        dbg(em, ctx, "y2_0", y2[0][:], 64, 1024)
        for j in range(8):
            for c in range(NCH):
                transp(em, P7[0:128, 64 * c:64 * (c + 1)],
                       y2[c][:, 128 * j:128 * (j + 1)], ctx.eye[0:64, 0:64])
            copy(em, "vector", y_fm[j][:], P7[0:128, 0:TT_], wkey=y_fm[j])
        for pj in range(4):
            for kk in range(8):
                wb = wbuf[kk % 4]
                em.dma("gpsimd", out=wb[:],
                       in_=inp["Wo"][128 * kk:128 * (kk + 1), 256 * pj:256 * (pj + 1)],
                       reads=[wb], writes=[wb])
                mm(em, P[4][:, 0:TT_], wb[:, 0:128], y_fm[kk][:],
                   start=(kk == 0), stop=(kk == 7))
                mm(em, P[5][:, 0:TT_], wb[:, 128:256], y_fm[kk][:],
                   start=(kk == 0), stop=(kk == 7))
            for half in range(2):
                j = 2 * pj + half
                tt(em, "vector", xt[j][:], P[4 + half][:, 0:TT_], xt[j][:], ALU.add,
                   wkey=xt[j])
        if ctx.fused:
            for j in range(8):
                copy(em, "vector", ctx.X[j][:, t0:t0 + TT_], xt[j][:], wkey=ctx.X[j])
        else:
            for j in range(8):
                em.dma("gpsimd", out=ctx.y_out[128 * j:128 * (j + 1), t0:t0 + TT_],
                       in_=xt[j][:], reads=[xt[j]], writes=[f"OUTY{j}"])

    em.barrier()


def build_phase2(nc, em, ctx, E, inp):
    T = ctx.T
    NTW = T // 512
    NE = int(os.environ.get('MOENE', '160'))

    pg = [E(nc.psum_tensor(f"p2g{i}", [128, 512], F32)) for i in range(4)]
    pz = [E(nc.psum_tensor(f"p2z{i}", [128, 512], F32)) for i in range(2)]
    pxtra = E(nc.psum_tensor("p2xtra", [128, 512], F32))
    prow = pxtra
    pbc = pxtra


    eye32_d = nc.inline_tensor(np.eye(128).astype(np.float32), "eye_f32_d")
    eye_f = E(nc.sbuf_tensor("eye_f", [128, 128], F32))
    em.dma("sync", out=eye_f[:], in_=eye32_d.ap(), writes=[eye_f])

    rw = []
    for c in range(8):
        t = E(nc.sbuf_tensor(f"p2rw{c}", [128, 160], F32))
        em.dma("sync", out=t[:], in_=inp["router_wt"][128 * c:128 * (c + 1), :],
               writes=[t])
        rw.append(t)
    ebias = E(nc.sbuf_tensor("p2eb", [128, 2], F32))
    em.dma("sync", out=ebias[:], in_=inp["ebias"], writes=[ebias])

    xn_bf = [E(nc.sbuf_tensor(f"p2xn{j}", [128, T], BF16)) for j in range(8)]
    _sp = getattr(ctx, "moe_sparse", False)
    RNK = [E(nc.sbuf_tensor(f"p2rnk{i}", [128, 160], BF16)) for i in range(T // 128)] if _sp else None
    CTM = [E(nc.sbuf_tensor(f"p2ctm{i}", [128, 160], BF16)) for i in range(T // 128)] if _sp else None
    comb_lo = E(nc.sbuf_tensor("p2clo", [128, T], BF16))
    comb_hi = E(nc.sbuf_tensor("p2chi", [32, T], BF16))
    mcol = E(nc.sbuf_tensor("p2mcol", [128, 1], F32))
    tsum = E(nc.sbuf_tensor("p2tsum", [128, 1], F32))
    ctm = E(nc.sbuf_tensor("p2ctm", [128, 160], BF16))
    sq2 = [E(nc.sbuf_tensor(f"p2sq{i}", [128, 512], BF16)) for i in range(2)]
    tmpf = [E(nc.sbuf_tensor(f"p2tmp{i}", [128, 512], F32)) for i in range(2)]
    lnrow = [E(nc.sbuf_tensor(f"p2row{i}", [1, 512], F32)) for i in range(4)]
    h_bf = [E(nc.sbuf_tensor(f"p2h{i}", [128, 512], BF16)) for i in range(4)]
    xnf = [E(nc.sbuf_tensor(f"p2xnf{i}", [128, 512], F32)) for i in range(2)]
    rowb = E(nc.sbuf_tensor("p2rowb", [1, 512], BF16))
    NSET = 1 if getattr(ctx, "moe_sparse", False) else 2
    wg_bf = [[E(nc.sbuf_tensor(f"p2wg{b}_{c}", [128, 512], BF16)) for c in range(8)]
             for b in range(NSET)]
    wd_bf = [[E(nc.sbuf_tensor(f"p2wd{b}_{c}", [128, 1024], BF16)) for c in range(4)]
             for b in range(NSET)]
    from contextlib import ExitStack as _ES
    _pre = _ES()
    E2 = _pre.enter_context
    pbf = E2(nc.psum_tensor("p2bf", [128, 1024], BF16))
    probs_lo = E2(nc.sbuf_tensor("p2plo", [128, T], F32))
    probs_hi = E2(nc.sbuf_tensor("p2phi", [32, T], F32))
    ptm = [E2(nc.sbuf_tensor(f"p2ptm{i}", [128, 160], F32)) for i in range(T // 128)]
    work = E2(nc.sbuf_tensor("p2work", [128, 160], F32))
    sel = E2(nc.sbuf_tensor("p2sel", [128, 160], F32))
    msk = E2(nc.sbuf_tensor("p2msk", [128, 160], F32))

    for w in range(NTW):
        t0 = w * 512
        for j in range(8):
            copy(em, "vector", xn_bf[j][:, t0:t0 + 512], ctx.X[j][:, t0:t0 + 512],
                 wkey=xn_bf[j])
            act(em, sq2[j % 2][:], xn_bf[j][:, t0:t0 + 512], AF.Square,
                wkey=sq2[j % 2])
            mm(em, pg[0][0:1, 0:512], ctx.ones_col_bf[:], xn_bf[j][:, t0:t0 + 512],
               start=(j == 0), stop=(j == 7))
            mm(em, pg[1][0:1, 0:512], ctx.ones_col_bf[:], sq2[j % 2][:],
               start=(j == 0), stop=(j == 7))
        mean, msq, var, rstd = lnrow
        ts(em, "vector", mean[:], pg[0][0:1, 0:512], 1.0 / 1024, None, ALU.mult)
        ts(em, "vector", msq[:], pg[1][0:1, 0:512], 1.0 / 1024, None, ALU.mult)
        tt(em, "vector", var[:], mean[:], mean[:], ALU.mult)
        tt(em, "vector", var[:], msq[:], var[:], ALU.subtract)
        act(em, rstd[:], var[:], AF.Sqrt, bias=ctx.eps5[0:1, 0:1], wkey=rstd)
        recip(em, rstd[:], rstd[:])
        mm(em, pg[2][:, 0:512], ctx.ones_f[:], mean[:])
        mm(em, pg[3][:, 0:512], ctx.ones_f[:], rstd[:])
        for j in range(8):
            tf = tmpf[j % 2]
            xf = xnf[j % 2]
            tt(em, "vector", tf[:], ctx.X[j][:, t0:t0 + 512], pg[2][:, 0:512],
               ALU.subtract, wkey=tf)
            tt(em, "vector", xf[:], tf[:], pg[3][:, 0:512], ALU.mult, wkey=xf)
            copy(em, "vector", xn_bf[j][:, t0:t0 + 512], xf[:], wkey=xn_bf[j])
            mm(em, pg[0][:, 0:512], rw[j][:, 0:128], xf[:],
               start=(j == 0), stop=(j == 7))
            mm(em, pg[1][0:32, 0:512], rw[j][:, 128:160], xf[:],
               start=(j == 0), stop=(j == 7))
        act(em, probs_lo[:, t0:t0 + 512], pg[0][:, 0:512], AF.Sigmoid,
            bias=ebias[:, 0:1], wkey=probs_lo)
        act(em, probs_hi[:, t0:t0 + 512], pg[1][0:32, 0:512], AF.Sigmoid,
            bias=ebias[0:32, 1:2], wkey=probs_hi)

    for i in range(T // 128):
        c0 = i * 128
        transp(em, pg[0][:, 0:128], probs_lo[:, c0:c0 + 128], eye_f[:])
        transp(em, pg[0][:, 128:160], probs_hi[:, c0:c0 + 128], eye_f[0:32, 0:32])
        copy(em, "vector", ptm[i][:, 0:128], pg[0][0:128, 0:128], wkey=ptm[i])
        copy(em, "vector", ptm[i][:, 128:160], pg[0][0:128, 128:160], wkey=ptm[i])
    for i in range(T // 128):
        copy(em, "vector", work[:], ptm[i][:], wkey=work)
        memset(em, "vector", sel[:], 0.0, wkey=sel)
        if _sp:
            memset(em, "vector", RNK[i][:], 0.0, wkey=RNK[i])
        for it8 in range(8):
            reduce(em, mcol[:], work[:], ALU.max, wkey=mcol)
            ts(em, "vector", msk[:], work[:], mcol[:], None, ALU.is_ge, wkey=msk)
            tt(em, "vector", sel[:], sel[:], msk[:], ALU.add)
            if _sp:
                stt(em, "vector", RNK[i][:], msk[:], float(it8 + 1), RNK[i][:],
                    ALU.mult, ALU.add, wkey=RNK[i])
            tt(em, "vector", msk[:], msk[:], work[:], ALU.mult)
            tt(em, "vector", work[:], work[:], msk[:], ALU.subtract)
        tt(em, "vector", sel[:], sel[:], ptm[i][:], ALU.mult)
        reduce(em, tsum[:], sel[:], ALU.add, wkey=tsum)
        ts(em, "vector", tsum[:], tsum[:], 1e-9, None, ALU.add, wkey=tsum)
        recip(em, tsum[:], tsum[:])
        if _sp:
            ts(em, "vector", CTM[i][:], sel[:], tsum[:], 2.5, ALU.mult, ALU.mult,
               wkey=CTM[i])
            copy(em, "vector", ctm[:], CTM[i][:], wkey=ctm)
        else:
            ts(em, "vector", ctm[:], sel[:], tsum[:], 2.5, ALU.mult, ALU.mult,
               wkey=ctm)
        transp(em, pbf[0:128, 0:128], ctm[:, 0:128], ctx.eye[:])
        transp(em, pbf[0:32, 128:256], ctm[:, 128:160], ctx.eye[:])
        copy(em, "vector", comb_lo[:, i * 128:(i + 1) * 128], pbf[0:128, 0:128],
             wkey=comb_lo)
        copy(em, "vector", comb_hi[:, i * 128:(i + 1) * 128], pbf[0:32, 128:256],
             wkey=comb_hi)

    _pre.close()
    em.barrier()
    h16 = [E(nc.sbuf_tensor(f"p2h16_{i}", [128, 512], BF16)) for i in range(4)]
    for c in range(8):
        em.dma("gpsimd", out=wg_bf[0][c][:],
               in_=inp["wgs_t"][128 * c:128 * (c + 1), :],
               reads=[wg_bf[0][c]], writes=[wg_bf[0][c]])
    for c in range(4):
        em.dma("gpsimd", out=wd_bf[0][c][:],
               in_=inp["wds_t"][128 * c:128 * (c + 1), :],
               reads=[wd_bf[0][c]], writes=[wd_bf[0][c]])
    for w in range(NTW):
        t0 = w * 512
        for mi in range(4):
            for kk in range(8):
                mm(em, pg[mi][:, 0:512], wg_bf[0][kk][:, 128 * mi:128 * (mi + 1)],
                   xn_bf[kk][:, t0:t0 + 512], start=(kk == 0), stop=(kk == 7))
            act(em, h_bf[mi][:], pg[mi][:, 0:512], AF.Silu, wkey=h_bf[mi])
        for j in range(8):
            for kk in range(4):
                mm(em, pz[j % 2][:, 0:512], wd_bf[0][kk][:, 128 * j:128 * (j + 1)],
                   h_bf[kk][:], start=(kk == 0), stop=(kk == 3))
            tt(em, "vector", ctx.X[j][:, t0:t0 + 512], pz[j % 2][:, 0:512],
               ctx.X[j][:, t0:t0 + 512], ALU.add, wkey=ctx.X[j])

    if getattr(ctx, "moe_sparse", False):
        _moe_sparse(nc, em, ctx, E, inp, xn_bf, CTM, RNK, pg, pz, prow, pbf, eye_f,
                    tmpf, lnrow)
        em.barrier()
        return

    def load_expert(e, b):
        for c in range(8):
            em.dma("gpsimd", out=wg_bf[b][c][:],
                   in_=inp["wg_t"][e, 128 * c:128 * (c + 1), :],
                   reads=[wg_bf[b][c]], writes=[wg_bf[b][c]])
        for c in range(4):
            em.dma("gpsimd", out=wd_bf[b][c][:],
                   in_=inp["wd_t"][e, 128 * c:128 * (c + 1), :],
                   reads=[wd_bf[b][c]], writes=[wd_bf[b][c]])

    def emit_unit(e, b, w):
        t0 = w * 512
        for mi in range(4):
            for kk in range(8):
                mm(em, pg[mi][:, 0:512], wg_bf[b][kk][:, 128 * mi:128 * (mi + 1)],
                   xn_bf[kk][:, t0:t0 + 512], start=(kk == 0), stop=(kk == 7))
        if e < 128:
            mm(em, prow[0:1, 0:512], ctx.eye[:, e:e + 1], comb_lo[:, t0:t0 + 512])
        else:
            mm(em, prow[0:1, 0:512], ctx.eye[0:32, e - 128:e - 127],
               comb_hi[:, t0:t0 + 512])
        copy(em, "vector", rowb[:], prow[0:1, 0:512], wkey=rowb)
        mm(em, pbc[:, 0:512], ctx.ones_row_bf[:], rowb[:])
        for mi in range(4):
            act(em, h16[mi][:], pg[mi][:, 0:512], AF.Silu, wkey=h16[mi])
            tt(em, "vector", h16[mi][:], h16[mi][:], pbc[:, 0:512], ALU.mult)
        for j in range(8):
            for kk in range(4):
                mm(em, pz[j % 2][:, 0:512], wd_bf[b][kk][:, 128 * j:128 * (j + 1)],
                   h16[kk][:], start=(kk == 0), stop=(kk == 3))
            tt(em, "vector", ctx.X[j][:, t0:t0 + 512], pz[j % 2][:, 0:512],
               ctx.X[j][:, t0:t0 + 512], ALU.add, wkey=ctx.X[j])

    load_expert(0, 0)
    for e in range(NE):
        b = e % 2
        for w in range(NTW):
            emit_unit(e, b, w)
            if w == 0 and e + 1 < NE:
                load_expert(e + 1, 1 - b)
def _moe_sparse(nc, em, ctx, E, inp, xn_bf, CTM, RNK, pg, pz, prow, pbf, eye_f,
                tmpf, lnrow):
    from concourse.expressions import RuntimeValue
    from concourse.bass_core_rust import IndirectOffsetOnAxis
    T = ctx.T
    NI = T // 128
    NB = T * 8 // 128 + 160
    BPC = (NB + 127) // 128
    U32 = mybir.dt.uint32
    I32 = mybir.dt.int32

    xn_tm_d = nc.dram_tensor("sp_xntm", [T, 1024], BF16, kind="Internal").ap()
    TAB = nc.dram_tensor("sp_tab", [NB * 128, 4], F32, kind="Internal").ap()
    EBD = nc.dram_tensor("sp_ebd", [BPC * 128], I32, kind="Internal").ap()
    Zacc = nc.dram_tensor("sp_zacc", [T + 128, 1024], F32, kind="Internal").ap()

    tabinit = np.zeros((NB * 128, 4), np.float32)
    tabinit[:, 1] = T + (np.arange(NB * 128) % 128)
    tabinit_d = nc.inline_tensor(tabinit, "sp_tabinit_d")
    tri_d = nc.inline_tensor(
        np.tril(np.ones((128, 128), np.float32), -1).T.astype(BF16NP), "sp_tri_d")
    iot_d = nc.inline_tensor(np.arange(128, dtype=np.float32)[:, None], "sp_iot_d")
    bcol_np = 128.0 * (np.arange(128)[:, None] + 128 * np.arange(BPC)[None, :])
    bcol_d = nc.inline_tensor(bcol_np.astype(np.float32), "sp_bcol_d")

    TRI = E(nc.sbuf_tensor("spTRI", [128, 128], BF16))
    em.dma("sync", out=TRI[:], in_=tri_d.ap(), writes=[TRI])
    IOT = E(nc.sbuf_tensor("spIOT", [128, 1], F32))
    em.dma("sync", out=IOT[:], in_=iot_d.ap(), writes=[IOT])
    BCOL = E(nc.sbuf_tensor("spBCOL", [128, BPC], F32))
    em.dma("sync", out=BCOL[:], in_=bcol_d.ap(), writes=[BCOL])
    em.dma("sync", out=TAB, in_=tabinit_d.ap(), writes=["TAB"])

    selb = E(nc.sbuf_tensor("spselb", [128, 160], BF16))
    cnts = E(nc.sbuf_tensor("spcnts", [1, 160 * NI], F32))
    offs = E(nc.sbuf_tensor("spoffs", [1, 160 * NI], F32))
    rowt = [E(nc.sbuf_tensor(f"sprow{i}", [1, 160], F32)) for i in range(4)]
    possb = E(nc.sbuf_tensor("sppos", [128, 160], F32))
    Dsb = E(nc.sbuf_tensor("spD", [128, 160], F32))
    mskb = E(nc.sbuf_tensor("spmsk", [128, 160], F32))
    t160 = E(nc.sbuf_tensor("spt160", [128, 160], F32))
    tokf = E(nc.sbuf_tensor("sptok", [128, 1], F32))
    fcol = E(nc.sbuf_tensor("spfc", [128, 1], F32))
    cwcol = E(nc.sbuf_tensor("spcw", [128, 1], F32))
    val4 = E(nc.sbuf_tensor("spval", [128, 4], F32))
    ufd = E(nc.sbuf_tensor("spufd", [128, 1], U32))
    ebf = E(nc.sbuf_tensor("spebf", [128, BPC], F32))
    ebi = E(nc.sbuf_tensor("spebi", [128, BPC], I32))
    ebrow = E(nc.sbuf_tensor("spebr", [1, BPC * 128], I32))
    xtm_st = [E(nc.sbuf_tensor(f"spxst{i}", [128, 1024], BF16)) for i in range(2)]
    wg_sb = [E(nc.sbuf_tensor(f"spwg{i}", [128, 4096], BF16)) for i in range(2)]
    wd_sb = [E(nc.sbuf_tensor(f"spwd{i}", [128, 4096], BF16)) for i in range(2)]
    xg_tm = [E(nc.sbuf_tensor(f"spxg{i}", [128, 1024], BF16)) for i in range(2)]
    xg_fm = [E(nc.sbuf_tensor(f"spxf{i}", [128, 1024], BF16)) for i in range(2)]
    hb = [E(nc.sbuf_tensor(f"sphb{i}", [128, 512], BF16)) for i in range(2)]
    z_bf = [E(nc.sbuf_tensor(f"spzb{i}", [128, 1024], BF16)) for i in range(2)]
    tab4 = [E(nc.sbuf_tensor(f"sptab{i}", [128, 4], F32)) for i in range(2)]
    utok = [E(nc.sbuf_tensor(f"sput{i}", [128, 1], U32)) for i in range(2)]
    uzd = [E(nc.sbuf_tensor(f"spuz{i}", [128, 1], U32)) for i in range(2)]

    wgflat = inp["wg_t"].rearrange("e c i -> (e c) i")
    wdflat = inp["wd_t"].rearrange("e i c -> (e i) c")

    for i in range(NI):
        st = xtm_st[i % 2]
        for j in range(8):
            transp(em, pbf[:, 128 * j:128 * (j + 1)],
                   xn_bf[j][:, 128 * i:128 * (i + 1)], ctx.eye[:])
        copy(em, "vector", st[:], pbf[0:128, 0:1024], wkey=st)
        em.dma("gpsimd", out=xn_tm_d[128 * i:128 * (i + 1), :], in_=st[:],
               reads=[st], writes_accum=["XNTM"])

    for i in range(NI):
        ts(em, "vector", selb[:], RNK[i][:], 0.5, None, ALU.is_ge, wkey=selb)
        mm(em, prow[0:1, 0:160], ctx.ones_col_bf[:], selb[:])
        copy(em, "vector", cnts[0:1, 160 * i:160 * (i + 1)], prow[0:1, 0:160],
             wkey=cnts)
    memset(em, "vector", offs[0:1, 0:160], 0.0, wkey=offs)
    for i in range(1, NI):
        tt(em, "vector", offs[0:1, 160 * i:160 * (i + 1)],
           offs[0:1, 160 * (i - 1):160 * i], cnts[0:1, 160 * (i - 1):160 * i],
           ALU.add, wkey=offs)
    tot, m1, padd, bstart = rowt
    tt(em, "vector", tot[:], offs[0:1, 160 * (NI - 1):160 * NI],
       cnts[0:1, 160 * (NI - 1):160 * NI], ALU.add, wkey=tot)
    ts(em, "vector", m1[:], tot[:], 128.0, None, ALU.mod, wkey=m1)
    ts(em, "vector", m1[:], m1[:], -1.0, 128.0, ALU.mult, ALU.add, wkey=m1)
    ts(em, "vector", m1[:], m1[:], 128.0, None, ALU.mod, wkey=m1)
    tt(em, "vector", padd[:], tot[:], m1[:], ALU.add, wkey=padd)
    scan(em, m1[:], padd[:], ctx.zeros[0:1, 0:160], 0.0, ALU.add, ALU.add)
    memset(em, "vector", bstart[0:1, 0:1], 0.0, wkey=bstart)
    copy(em, "vector", bstart[0:1, 1:160], m1[0:1, 0:159], wkey=bstart)

    mm(em, pg[0][:, 0:160], ctx.ones_f[:], bstart[:])
    for t in range(BPC):
        ts(em, "vector", t160[:], pg[0][:, 0:160], BCOL[:, t:t + 1], None,
           ALU.is_le, wkey=t160)
        reduce(em, ebf[:, t:t + 1], t160[:], ALU.add, wkey=ebf)
    ts(em, "vector", ebf[:], ebf[:], -1.0, None, ALU.add, wkey=ebf)
    copy(em, "vector", ebi[:], ebf[:], wkey=ebi)
    em.dma("gpsimd", out=EBD.rearrange("(t p) -> p t", p=128), in_=ebi[:],
           reads=[ebi], writes=["EBD"], allow_slow_non_contiguous=True)
    em.dma("sync", out=ebrow[0:1, :], in_=EBD[None, :], reads=["EBD"],
           writes=[ebrow])

    for i in range(NI):
        ts(em, "vector", selb[:], RNK[i][:], 0.5, None, ALU.is_ge, wkey=selb)
        mm(em, pg[1][:, 0:160], TRI[:], selb[:])
        copy(em, "vector", possb[:], pg[1][:, 0:160], wkey=possb)
        tt(em, "vector", rowt[1][:], bstart[:], offs[0:1, 160 * i:160 * (i + 1)],
           ALU.add, wkey=rowt[1])
        mm(em, pg[2][:, 0:160], ctx.ones_f[:], rowt[1][:])
        tt(em, "vector", Dsb[:], possb[:], pg[2][:, 0:160], ALU.add, wkey=Dsb)
        ts(em, "vector", tokf[:], IOT[:], float(128 * i), None, ALU.add, wkey=tokf)
        for r in range(8):
            ts(em, "vector", mskb[:], RNK[i][:], float(r + 1), None, ALU.is_equal,
               wkey=mskb)
            tt(em, "vector", t160[:], Dsb[:], mskb[:], ALU.mult, wkey=t160)
            reduce(em, fcol[:], t160[:], ALU.add, wkey=fcol)
            tt(em, "vector", t160[:], CTM[i][:], mskb[:], ALU.mult, wkey=t160)
            reduce(em, cwcol[:], t160[:], ALU.add, wkey=cwcol)
            copy(em, "vector", val4[:, 0:1], tokf[:], wkey=val4)
            copy(em, "vector", val4[:, 1:2], tokf[:], wkey=val4)
            copy(em, "vector", val4[:, 2:3], cwcol[:], wkey=val4)
            copy(em, "vector", ufd[:], fcol[:], wkey=ufd)
            em.op("gpsimd", lambda e: e.indirect_dma_start(
                out=TAB, out_offset=IndirectOffsetOnAxis(ufd[:], 0),
                in_=val4[:], in_offset=None,
                bounds_check=NB * 128 - 1, oob_is_err=False),
                reads=[val4, ufd, "TAB"], writes_accum=["TAB"], dma=True)

    for rr in range((T + 128) // 128):
        for hh in range(2):
            em.dma("gpsimd", out=Zacc[128 * rr:128 * (rr + 1), 512 * hh:512 * (hh + 1)],
                   in_=ctx.zeros[:, 0:512], writes_accum=["Zacc"])

    rgA = nc.gpsimd.alloc_register("sp_rga")
    rgB = nc.gpsimd.alloc_register("sp_rgb")
    for b in range(NB):
        p = b % 2
        ph = pg[p]
        pzh = (pz[0], pz[1]) if p == 0 else (pg[2], pg[3])
        t4 = tab4[p]
        em.dma("sync", out=t4[:], in_=TAB[128 * b:128 * (b + 1), :],
               reads=["TAB", t4], writes=[t4])
        copy(em, "vector", utok[p][:], t4[:, 0:1], wkey=utok[p])
        copy(em, "vector", uzd[p][:], t4[:, 1:2], wkey=uzd[p])
        em.op("gpsimd", lambda e, b=b: e.reg_load(rgA, ebrow[0:1, b:b + 1]),
              reads=[ebrow])
        em.op("gpsimd", lambda e: e.reg_mul(rgA, rgA, 1024))
        em.op("gpsimd", lambda e, w=wg_sb[p]: e.dma_start(
            out=w[:].rearrange("p (c i) -> p c i", c=8),
            in_=wgflat[bass.ds(RuntimeValue(rgA, min_val=0, max_val=159 * 1024), 1024),
                       :].rearrange("(c p) i -> p c i", p=128)),
            reads=[wg_sb[p]], writes=[wg_sb[p]], dma=True)
        em.op("gpsimd", lambda e, b=b: e.reg_load(rgB, ebrow[0:1, b:b + 1]),
              reads=[ebrow])
        em.op("gpsimd", lambda e: e.reg_mul(rgB, rgB, 512))
        em.op("gpsimd", lambda e, w=wd_sb[p]: e.dma_start(
            out=w[:].rearrange("p (c i) -> p c i", c=4),
            in_=wdflat[bass.ds(RuntimeValue(rgB, min_val=0, max_val=159 * 512), 512),
                       :].rearrange("(ci p) c -> p ci c", p=128)),
            reads=[wd_sb[p]], writes=[wd_sb[p]], dma=True)
        em.op("gpsimd", lambda e, p=p: e.indirect_dma_start(
            out=xg_tm[p][:], out_offset=None, in_=xn_tm_d,
            in_offset=IndirectOffsetOnAxis(utok[p][:], 0)),
            reads=["XNTM", utok[p], xg_tm[p]], writes=[xg_tm[p]], dma=True)
        for c in range(8):
            transp(em, pbf[:, 128 * c:128 * (c + 1)],
                   xg_tm[p][:, 128 * c:128 * (c + 1)], ctx.eye[:])
        copy(em, "vector", xg_fm[p][:], pbf[0:128, 0:1024], wkey=xg_fm[p])
        for c in range(8):
            for m in range(4):
                mm(em, ph[:, 128 * m:128 * (m + 1)],
                   wg_sb[p][:, 512 * c + 128 * m:512 * c + 128 * (m + 1)],
                   xg_fm[p][:, 128 * c:128 * (c + 1)],
                   start=(c == 0), stop=(c == 7))
        act(em, hb[p][:], ph[:, 0:512], AF.Silu, wkey=hb[p])
        for h in range(2):
            for ci in range(4):
                mm(em, pzh[h][:, 0:512], hb[p][:, 128 * ci:128 * (ci + 1)],
                   wd_sb[p][:, 1024 * ci + 512 * h:1024 * ci + 512 * (h + 1)],
                   start=(ci == 0), stop=(ci == 3))
            ts(em, "vector", z_bf[p][:, 512 * h:512 * (h + 1)], pzh[h][:, 0:512],
               t4[:, 2:3], None, ALU.mult, wkey=z_bf[p])
        _mode = os.environ.get("SPMODE", "add")
        if _mode == "add":
            em.op("gpsimd", lambda e, p=p: e.indirect_dma_start(
                out=Zacc, out_offset=IndirectOffsetOnAxis(uzd[p][:], 0),
                in_=z_bf[p][:], in_offset=None,
                bounds_check=T + 127, oob_is_err=True, compute_op=ALU.add),
                reads=[z_bf[p], uzd[p], "ZRMW"], writes=["ZRMW"],
                writes_accum=["Zacc"], dma=True)
        else:
            em.op("gpsimd", lambda e, p=p: e.indirect_dma_start(
                out=Zacc, out_offset=IndirectOffsetOnAxis(uzd[p][:], 0),
                in_=z_bf[p][:], in_offset=None,
                bounds_check=T + 127, oob_is_err=False),
                reads=[z_bf[p], uzd[p], "ZRMW"], writes=["ZRMW"],
                writes_accum=["Zacc"], dma=True)

    for i in range(NI):
        st = xtm_st[i % 2]
        em.dma("gpsimd", out=st[:], in_=Zacc[128 * i:128 * (i + 1), :],
               reads=["Zacc", "ZRMW", st], writes=[st])
        for j in range(8):
            transp(em, pbf[:, 128 * j:128 * (j + 1)], st[:, 128 * j:128 * (j + 1)],
                   ctx.eye[:])
        for j in range(8):
            tt(em, "vector", ctx.X[j][:, 128 * i:128 * (i + 1)],
               pbf[:, 128 * j:128 * (j + 1)], ctx.X[j][:, 128 * i:128 * (i + 1)],
               ALU.add, wkey=ctx.X[j])

def prep_phase2_inputs(inputs):
    d = {}
    d["router_wt"] = np.ascontiguousarray(inputs["L0_ffn_router_w"].T, np.float32)
    eb = np.zeros((128, 2), np.float32)
    eb[:, 0] = inputs["L0_ffn_expert_bias"][0:128]
    eb[0:32, 1] = inputs["L0_ffn_expert_bias"][128:160]
    d["ebias"] = eb
    d["wg_t"] = np.ascontiguousarray(np.transpose(inputs["L0_ffn_w_gate"], (0, 2, 1)),
                                     np.float32)
    d["wd_t"] = np.ascontiguousarray(np.transpose(inputs["L0_ffn_w_down"], (0, 2, 1)),
                                     np.float32)
    d["wgs_t"] = np.ascontiguousarray(inputs["L0_ffn_w_gate_shared"].T, np.float32)
    d["wds_t"] = np.ascontiguousarray(inputs["L0_ffn_w_down_shared"].T, np.float32)
    return d


PHASE2_TENSORS = [
    ("router_wt", [1024, 160]), ("ebias", [128, 2]),
    ("wg_t", [160, 1024, 512]), ("wd_t", [160, 512, 1024]),
    ("wgs_t", [1024, 512]), ("wds_t", [512, 1024]),
]


def build_nc_phase2(T=2048, debug=False, sparse=False):
    from contextlib import ExitStack
    nc = bass.Bass("TRN2", target_bir_lowering=False)
    x_in = nc.dram_tensor("x_fm", [1024, T], F32, kind="ExternalInput").ap()
    y_out = nc.dram_tensor("y_fm", [1024, T], F32, kind="ExternalOutput").ap()
    inp = {nm: nc.dram_tensor(nm, shp, F32, kind="ExternalInput").ap()
           for nm, shp in PHASE2_TENSORS}
    em = Em(nc)
    ctx = Ctx()
    ctx.fused = False
    ctx.debug = debug
    ctx.moe_sparse = sparse
    ctx.dbg_done = set()
    with ExitStack() as st:
        E = st.enter_context
        if debug:
            ctx.dbg_stage = E(nc.sbuf_tensor("dbgstage", [128, 1024], F32))
        ctx.X = alloc_stream(nc, E, T)
        setup_common(nc, em, ctx, E, T)
        load_stream(em, ctx, x_in)
        em.barrier()
        build_phase2(nc, em, ctx, E, inp)
        store_stream(em, ctx, y_out)
        em.build()
    print("phase2: ops=%d" % len(em.ops))
    return nc


def emit_ln_win(nc, em, ctx, t0, tw, src_of, dst_of, psums, eps_col,
                gamma=None, beta=None, lnrow=None, sq2=None, tmpf=None):
    pA, pB, pC, pD = psums
    for j in range(8):
        s = src_of(j)
        sq = sq2[j % 2]
        if s.tensor.dtype == F32:
            copy(em, "vector", sq[:, 0:tw], s, wkey=sq)
            mm(em, pA[0:1, 0:tw], ctx.ones_col_bf[:], sq[:, 0:tw],
               start=(j == 0), stop=(j == 7))
            act(em, sq[:, 0:tw], sq[:, 0:tw], AF.Square, wkey=sq)
        else:
            mm(em, pA[0:1, 0:tw], ctx.ones_col_bf[:], s, start=(j == 0), stop=(j == 7))
            act(em, sq[:, 0:tw], s, AF.Square, wkey=sq)
        mm(em, pB[0:1, 0:tw], ctx.ones_col_bf[:], sq[:, 0:tw],
           start=(j == 0), stop=(j == 7))
    mean, msq, var, rstd = lnrow
    ts(em, "vector", mean[:, 0:tw], pA[0:1, 0:tw], 1.0 / 1024, None, ALU.mult,
       wkey=mean)
    ts(em, "vector", msq[:, 0:tw], pB[0:1, 0:tw], 1.0 / 1024, None, ALU.mult,
       wkey=msq)
    tt(em, "vector", var[:, 0:tw], mean[:, 0:tw], mean[:, 0:tw], ALU.mult, wkey=var)
    tt(em, "vector", var[:, 0:tw], msq[:, 0:tw], var[:, 0:tw], ALU.subtract,
       wkey=var)
    act(em, rstd[:, 0:tw], var[:, 0:tw], AF.Sqrt, bias=eps_col, wkey=rstd)
    recip(em, rstd[:, 0:tw], rstd[:, 0:tw])
    mm(em, pC[:, 0:tw], ctx.ones_f[:], mean[:, 0:tw])
    mm(em, pD[:, 0:tw], ctx.ones_f[:], rstd[:, 0:tw])
    for j in range(8):
        tf = tmpf[j % 2]
        tt(em, "vector", tf[:, 0:tw], src_of(j), pC[:, 0:tw], ALU.subtract, wkey=tf)
        if gamma is None:
            dst_of(j, lambda out, tfv=tf: tt(em, "vector", out, tfv[:, 0:tw],
                                             pD[:, 0:tw], ALU.mult, wkey=None))
        else:
            tt(em, "vector", tf[:, 0:tw], tf[:, 0:tw], pD[:, 0:tw], ALU.mult,
               wkey=tf)
            dst_of(j, lambda out, tfv=tf, j=j: ts(em, "vector", out, tfv[:, 0:tw],
                                                  gamma(j), beta(j), ALU.mult,
                                                  ALU.add, wkey=None))


def stream_proj(nc, em, ctx, inp_ap, rhs_of, psums, consumer, wbuf, nK=8,
                TT_=512, npair=4):
    for pj in range(npair):
        for kk in range(nK):
            wb = wbuf[kk % len(wbuf)]
            em.dma("gpsimd", out=wb[:, 0:256],
                   in_=inp_ap[128 * kk:128 * (kk + 1), 256 * pj:256 * (pj + 1)],
                   reads=[wb], writes=[wb])
            mm(em, psums[0][:, 0:TT_], wb[:, 0:128], rhs_of(kk),
               start=(kk == 0), stop=(kk == nK - 1))
            mm(em, psums[1][:, 0:TT_], wb[:, 128:256], rhs_of(kk),
               start=(kk == 0), stop=(kk == nK - 1))
        consumer(2 * pj, psums[0])
        consumer(2 * pj + 1, psums[1])


def build_phase3(nc, em, ctx, E, inp, stop_at=None):
    T = ctx.T
    NTW = T // 512
    NI = T // 128

    pp = [E(nc.psum_tensor(f"p3p{i}", [128, 512], F32)) for i in range(7)]
    pbf = E(nc.psum_tensor("p3bf", [128, 1024], BF16))

    S1 = [E(nc.sbuf_tensor(f"p3S1_{j}", [128, T], BF16)) for j in range(8)]
    S2 = [E(nc.sbuf_tensor(f"p3S2_{j}", [128, T], BF16)) for j in range(8)]
    NI0 = T // 128
    S3 = [E(nc.sbuf_tensor(f"p3S3_{j}", [128, 1024], BF16)) for j in range(max(NI0, 8))]
    S4 = [E(nc.sbuf_tensor(f"p3S4_{j}", [128, T], BF16)) for j in range(8)]
    S5 = [E(nc.sbuf_tensor(f"p3S5_{j}", [128, 1024], BF16))
          for j in range(max(NI0, 16))]
    xt = [E(nc.sbuf_tensor(f"p3x{j}", [128, 512], F32)) for j in range(8)]
    s2b = lambda j: S3[j][:, 0:512]
    sq2 = [E(nc.sbuf_tensor(f"p3sq{i}", [128, 512], BF16)) for i in range(2)]
    tmpf = [E(nc.sbuf_tensor(f"p3tmp{i}", [128, 512], F32)) for i in range(2)]
    lnrow = [E(nc.sbuf_tensor(f"p3row{i}", [1, 512], F32)) for i in range(4)]
    wbuf = [E(nc.sbuf_tensor(f"p3wb{i}", [128, 256], BF16)) for i in range(4)]
    wbig3 = [E(nc.sbuf_tensor(f"p3wB{i}", [128, 512], BF16)) for i in range(3)]
    pv3 = E(nc.sbuf_tensor("p3pv", [128, 40], F32))
    em.dma("sync", out=pv3[:], in_=inp["pvec3"], writes=[pv3])
    IG1, IB1, IG2, IB2, IB2b = range(5)

    def pvc(idx, j):
        return pv3[:, idx * 8 + j:idx * 8 + j + 1]

    b1c = E(nc.sbuf_tensor("p3b1c", [128, 32], F32))
    em.dma("sync", out=b1c[:], in_=inp["b1cols"], writes=[b1c])
    projT2 = E(nc.sbuf_tensor("p3projT", [128, 64], BF16))
    pstage = E(nc.sbuf_tensor("p3pstage", [128, 64], F32))
    em.dma("sync", out=pstage[0:64, :], in_=inp["projT"], reads=[pstage],
           writes=[pstage])
    DN = 1.0 / np.sqrt(np.sqrt(64.0))
    ts(em, "vector", projT2[0:64, :], pstage[0:64, :], DN, None, ALU.mult,
       wkey=projT2)
    copy(em, "vector", projT2[64:128, :], projT2[0:64, :], wkey=projT2)
    bd = np.zeros((1024, 16), np.float32)
    for h in range(16):
        bd[64 * h:64 * (h + 1), h] = 1.0
    bd_d = nc.inline_tensor(bd.astype(BF16NP), "p3bd_d")
    BDH = [E(nc.sbuf_tensor(f"p3bd{c}", [128, 16], BF16)) for c in range(8)]
    for c in range(8):
        em.dma("sync", out=BDH[c][:], in_=bd_d.ap()[128 * c:128 * (c + 1), :],
               writes=[BDH[c]])
    diag_q = [E(nc.sbuf_tensor(f"p3dq{i}", [128, 16], F32)) for i in range(NI)]
    diag_k = [E(nc.sbuf_tensor(f"p3dk{i}", [128, 16], F32)) for i in range(NI)]
    ndmx = E(nc.sbuf_tensor("p3ndmx", [128, 16], F32))
    kv_sb = [E(nc.sbuf_tensor(f"p3kv{p}", [128, 65], BF16)) for p in range(8)]
    rc = E(nc.sbuf_tensor("p3rc", [128, 1], F32))
    mxcol = E(nc.sbuf_tensor("p3mx", [128, 1], F32))
    mxrow = E(nc.sbuf_tensor("p3mxr", [1, 16], F32))
    mxk_col = E(nc.sbuf_tensor("p3mxk", [128, 1], F32))
    cc_in = nc.dram_tensor("p3cc_in", [1, 16], F32)
    cc_out = nc.dram_tensor("p3cc_out", [1, 16], F32)

    em.barrier()

    for w in range(NTW):
        t0 = w * 512
        for j in range(8):
            em.dma("sync", out=xt[j][:], in_=ctx.x_in[128 * j:128 * (j + 1),
                                                      t0:t0 + 512],
                   reads=[xt[j]], writes=[xt[j]])
        emit_ln_win(nc, em, ctx, t0, 512, lambda j: xt[j][:],
                    lambda j, emitfn: emitfn(S1[j][:, t0:t0 + 512]),
                    (pp[0], pp[1], pp[2], pp[3]), ctx.eps5[0:1, 0:1],
                    lnrow=lnrow, sq2=sq2, tmpf=tmpf)

    if stop_at == 'ln':
        em.barrier()
        return
    def qk_path(wname, dst_tm, diag_dst, is_q):
        for w in range(NTW):
            t0 = w * 512
            stream_proj(nc, em, ctx, inp[wname],
                        lambda kk, t0=t0: S1[kk][:, t0:t0 + 512], (pp[0], pp[1]),
                        lambda j, ps, t0=t0: copy(em, "vector",
                                                  S2[j][:, t0:t0 + 512],
                                                  ps[:, 0:512], wkey=S2[j]),
                        wbuf)
        if stop_at == 'qk_mm':
            return
        for w in range(NTW):
            t0 = w * 512
            for j in range(8):
                act(em, sq2[j % 2][:], S2[j][:, t0:t0 + 512], AF.Square,
                    wkey=sq2[j % 2])
                mm(em, pp[4][0:16, 0:512], BDH[j][:], sq2[j % 2][:],
                   start=(j == 0), stop=(j == 7))
            for s in range(4):
                em.op("scalar", lambda e, s=s, w=w: e.activation(
                    tmpf[0][0:16, 128 * s:128 * (s + 1)],
                    pp[4][0:16, 128 * s:128 * (s + 1)], AF.Identity,
                    scale=float(DN * DN / 2.0)),
                    reads=[pp[4]], writes=[tmpf[0]])
            copy(em, "vector", sq2[0][0:16, 0:512], tmpf[0][0:16, 0:512],
                 wkey=sq2[0])
            for s in range(4):
                i = 4 * w + s
                transp(em, pbf[:, 0:16], sq2[0][0:16, 128 * s:128 * (s + 1)],
                       ctx.eye[0:16, 0:16])
                copy(em, "vector", diag_dst[i][:], pbf[0:128, 0:16],
                     wkey=diag_dst[i])
        if stop_at == 'qk_diag':
            return
        for i in range(NI):
            tc = i * 128
            for hh in range(2):
                for h4 in range(8):
                    h = 8 * hh + h4
                    par = h4 % 2
                    mm(em, pp[2 * hh + par][:, 64 * (h4 // 2):64 * (h4 // 2 + 1)],
                       S2[h // 2][64 * par:64 * par + 64, tc:tc + 128],
                       projT2[64 * par:64 * par + 64, :])
            for hh in range(2):
                for par in range(2):
                    dv = dst_tm(i).rearrange("p (g c) -> p g c", c=64)
                    dst = dv[:, 8 * hh + par:8 * hh + 8:2, :]
                    copy(em, "vector", dst, pp[2 * hh + par][:, 0:256], wkey=None)

    if stop_at == 'none2':
        em.barrier()
        return
    qdt = lambda i: S3[i][:, :]
    if stop_at == 'none3':
        em.barrier()
        return
    qk_path("Wq", qdt, diag_q, True)
    if stop_at == 'qproj':
        em.barrier()
        return
    for i in range(NI):
        dst = qdt(i)
        for h in range(16):
            reduce(em, mxcol[:], dst[:, 64 * h:64 * (h + 1)], ALU.max, wkey=mxcol)
            ts(em, "vector", ndmx[:, h:h + 1], diag_q[i][:, h:h + 1], mxcol[:],
               -1.0, ALU.add, ALU.mult, wkey=ndmx)
        for h in range(16):
            act(em, dst[:, 64 * h:64 * (h + 1)], dst[:, 64 * h:64 * (h + 1)],
                AF.Exp, bias=ndmx[:, h:h + 1], wkey=None)
        ts(em, "vector", dst[:], dst[:], 0.125, 0.125e-6, ALU.mult, ALU.add,
           wkey=None)
    if stop_at == 'qf':
        em.barrier()
        return
    for i in range(NI):
        src = qdt(i)
        for j in range(8):
            transp(em, pbf[:, 0:128], src[:, 128 * j:128 * (j + 1)], ctx.eye[:])
            copy(em, "vector", S4[j][:, 128 * i:128 * (i + 1)], pbf[0:128, 0:128],
                 wkey=S4[j])
    if stop_at == 'qftr':
        em.barrier()
        return
    qk_path("Wk", qdt, diag_k, False)
    if stop_at == 'kproj':
        em.barrier()
        return
    memset(em, "vector", mxk_col[:], -1e30, wkey=mxk_col)
    for i in range(NI):
        reduce(em, mxcol[:], qdt(i)[:], ALU.max, wkey=mxcol)
        tt(em, "vector", mxk_col[:], mxk_col[:], mxcol[:], ALU.max, wkey=mxk_col)
    copy(em, "vector", sq2[0][0:128, 0:1], mxk_col[:], wkey=sq2[0])
    transp(em, pbf[0:1, 0:128], sq2[0][0:128, 0:1], ctx.eye[:])
    copy(em, "vector", sq2[0][0:1, 0:128], pbf[0:1, 0:128], wkey=sq2[0])
    reduce(em, mxrow[0:1, 0:1], sq2[0][0:1, 0:128], ALU.max, wkey=mxrow)
    ts(em, "vector", mxrow[0:1, 0:16], ctx.zeros[0:1, 0:16], mxrow[0:1, 0:1],
       None, ALU.add, wkey=mxrow)
    em.dma("gpsimd", out=cc_in.ap(), in_=mxrow[0:1, 0:16], reads=[mxrow],
           writes=["p3ccin"])
    em.op("gpsimd", lambda e: e.collective_compute(
        "AllReduce", ALU.max, replica_groups=[list(range(ctx.ncores))],
        ins=[cc_in.ap().opt()], outs=[cc_out.ap().opt()]),
        reads=["p3ccin"], writes=["p3ccout"])
    em.dma("sync", out=mxrow[0:1, 0:16], in_=cc_out.ap(), reads=["p3ccout", mxrow],
           writes=[mxrow])
    mm(em, pp[6][:, 0:1], ctx.ones_f[:], mxrow[0:1, 0:1])
    copy(em, "vector", mxk_col[:], pp[6][:, 0:1], wkey=mxk_col)
    if stop_at == 'kmax':
        em.barrier()
        return
    for i in range(NI):
        dst = qdt(i)
        for h in range(16):
            ts(em, "vector", ndmx[:, h:h + 1], diag_k[i][:, h:h + 1], mxk_col[:],
               -1.0, ALU.add, ALU.mult, wkey=ndmx)
        for h in range(16):
            act(em, dst[:, 64 * h:64 * (h + 1)], dst[:, 64 * h:64 * (h + 1)],
                AF.Exp, bias=ndmx[:, h:h + 1], wkey=None)
        ts(em, "vector", dst[:], dst[:], 0.125, 0.125e-6, ALU.mult, ALU.add,
           wkey=None)
    if stop_at == 'kf':
        em.barrier()
        return
    for w in range(NTW):
        t0 = w * 512
        for nh in range(2):
            for kk in range(8):
                wb = wbig3[kk % 3]
                em.dma("gpsimd", out=wb[:], in_=inp["Wv"][
                    128 * kk:128 * (kk + 1), 512 * nh:512 * (nh + 1)],
                    reads=[wb], writes=[wb])
                for s in range(4):
                    mm(em, pp[s][:, 0:512],
                       S1[kk][:, t0 + 128 * s:t0 + 128 * (s + 1)], wb[:],
                       start=(kk == 0), stop=(kk == 7))
            for s in range(4):
                i = 4 * w + s
                copy(em, "vector", S5[i][:, 512 * nh:512 * (nh + 1)],
                     pp[s][:, 0:512], wkey=S5[i])
    if stop_at == 'vproj':
        em.barrier()
        return
    vt = lambda i: S5[i][:, :]
    for h in range(16):
        pr = h % 2
        pkv = pp[pr]
        pden = pp[2 + pr]
        for i in range(NI):
            mm(em, pkv[0:64, 0:64], qdt(i)[:, 64 * h:64 * (h + 1)],
               vt(i)[:, 64 * h:64 * (h + 1)], start=(i == 0), stop=(i == NI - 1))
            mm(em, pden[0:64, 0:1], qdt(i)[:, 64 * h:64 * (h + 1)],
               ctx.ones_col_bf[:], start=(i == 0), stop=(i == NI - 1))
        sl = slice(64 * pr, 64 * (pr + 1))
        copy(em, "vector", kv_sb[h // 2][sl, 0:64], pkv[0:64, 0:64],
             wkey=kv_sb[h // 2])
        copy(em, "vector", kv_sb[h // 2][sl, 64:65], pden[0:64, 0:1],
             wkey=kv_sb[h // 2])
    if stop_at == 'kv':
        em.barrier()
        return
    for i in range(NI):
        dst = qdt(i)
        for h in range(16):
            pr = h % 2
            sl = slice(64 * pr, 64 * (pr + 1))
            pn = pp[4 + pr]
            mm(em, pn[:, 0:65], S4[h // 2][sl, 128 * i:128 * (i + 1)],
               kv_sb[h // 2][sl, 0:65])
            recip(em, rc[:], pn[:, 64:65], wkey=rc)
            ts(em, "vector", dst[:, 64 * h:64 * (h + 1)], pn[:, 0:64], rc[:],
               None, ALU.mult, wkey=None)
    if stop_at == 'num':
        em.barrier()
        return
    for i in range(NI):
        src = qdt(i)
        for j in range(8):
            transp(em, pbf[:, 0:128], src[:, 128 * j:128 * (j + 1)], ctx.eye[:])
            copy(em, "vector", S4[j][:, 128 * i:128 * (i + 1)], pbf[0:128, 0:128],
                 wkey=S4[j])
    if stop_at == 'attntr':
        em.barrier()
        return
    for w in range(NTW):
        t0 = w * 512
        stream_proj(nc, em, ctx, inp["Wo"],
                    lambda kk, t0=t0: S4[kk][:, t0:t0 + 512], (pp[0], pp[1]),
                    lambda j, ps, t0=t0: tt(em, "vector", S1[j][:, t0:t0 + 512],
                                            ps[:, 0:512], S1[j][:, t0:t0 + 512],
                                            ALU.add, wkey=S1[j]),
                    wbuf)
    if stop_at == 'wo':
        em.barrier()
        return
    for w in range(NTW):
        t0 = w * 512
        emit_ln_win(nc, em, ctx, t0, 512, lambda j: S1[j][:, t0:t0 + 512],
                    lambda j, emitfn: emitfn(S2[j][:, t0:t0 + 512]),
                    (pp[0], pp[1], pp[2], pp[3]), ctx.eps6[0:1, 0:1],
                    gamma=lambda j: pvc(IG1, j), beta=lambda j: pvc(IB1, j),
                    lnrow=lnrow, sq2=sq2, tmpf=tmpf)
    if stop_at == 'ln1':
        em.barrier()
        return
    helu = lambda m: S5[m // 2][:, 512 * (m % 2):512 * (m % 2 + 1)]
    for w in range(NTW):
        t0 = w * 512
        for mp in range(16):
            for kk in range(8):
                wb = wbuf[kk % 4]
                em.dma("gpsimd", out=wb[:, 0:256],
                       in_=inp["W1"][128 * kk:128 * (kk + 1),
                                     256 * mp:256 * (mp + 1)],
                       reads=[wb], writes=[wb])
                mm(em, pp[0][:, 0:512], wb[:, 0:128], S2[kk][:, t0:t0 + 512],
                   start=(kk == 0), stop=(kk == 7))
                mm(em, pp[1][:, 0:512], wb[:, 128:256], S2[kk][:, t0:t0 + 512],
                   start=(kk == 0), stop=(kk == 7))
            for half in range(2):
                m = 2 * mp + half
                ps = pp[half]
                tf = tmpf[half]
                ts(em, "vector", tf[:], ps[:, 0:512], b1c[:, m:m + 1], 0.0,
                   ALU.add, ALU.min, wkey=tf)
                act(em, tf[:], tf[:], AF.Exp, wkey=tf)
                ts(em, "vector", xt[half][:], ps[:, 0:512], b1c[:, m:m + 1], 0.0,
                   ALU.add, ALU.max, wkey=xt[half])
                stt(em, "vector", helu(m)[:], tf[:], -1.0, xt[half][:], ALU.add,
                    ALU.add, wkey=None)
        for pj in range(4):
            for kk in range(32):
                wb = wbuf[kk % 4]
                em.dma("gpsimd", out=wb[:, 0:256],
                       in_=inp["W2"][128 * kk:128 * (kk + 1),
                                     256 * pj:256 * (pj + 1)],
                       reads=[wb], writes=[wb])
                mm(em, pp[2][:, 0:512], wb[:, 0:128], helu(kk)[:],
                   start=(kk == 0), stop=(kk == 31))
                mm(em, pp[3][:, 0:512], wb[:, 128:256], helu(kk)[:],
                   start=(kk == 0), stop=(kk == 31))
            for half in range(2):
                j = 2 * pj + half
                stt(em, "vector", s2b(j), pp[2 + half][:, 0:512],
                    pvc(IB2b, j), S2[j][:, t0:t0 + 512], ALU.add, ALU.add,
                    wkey=None)
        for j in range(8):
            em.dma("sync", out=xt[j][:], in_=ctx.x_in[128 * j:128 * (j + 1),
                                                      t0:t0 + 512],
                   reads=[xt[j]], writes=[xt[j]])
        def dst3(j, emitfn):
            tf2 = sq2[0][:, 0:512]
            emitfn(tf2)
            tt(em, "vector", xt[j][:], tf2, xt[j][:], ALU.add, wkey=xt[j])
            em.dma("gpsimd", out=ctx.y_out[128 * j:128 * (j + 1), t0:t0 + 512],
                   in_=xt[j][:], reads=[xt[j]], writes=[f"P3OUT{j}"])
        emit_ln_win(nc, em, ctx, t0, 512, lambda j: s2b(j),
                    dst3, (pp[0], pp[1], pp[2], pp[3]), ctx.eps6[0:1, 0:1],
                    gamma=lambda j: pvc(IG2, j), beta=lambda j: pvc(IB2, j),
                    lnrow=lnrow, sq2=sq2, tmpf=tmpf)
    em.barrier()


def prep_phase3_inputs(inputs):
    d = {}
    d["Wq"] = np.ascontiguousarray(inputs["L1_attn_Wq"].reshape(1024, 1024),
                                   np.float32)
    d["Wk"] = np.ascontiguousarray(inputs["L1_attn_Wk"].reshape(1024, 1024),
                                   np.float32)
    d["Wv"] = np.ascontiguousarray(inputs["L1_attn_Wv"].reshape(1024, 1024),
                                   np.float32)
    d["Wo"] = np.ascontiguousarray(inputs["L1_attn_Wo"].reshape(1024, 1024),
                                   np.float32)
    d["projT"] = np.ascontiguousarray(inputs["L1_attn_proj"].T, np.float32)
    d["W1"] = np.ascontiguousarray(inputs["L1_attn_W1"], np.float32)
    d["W2"] = np.ascontiguousarray(inputs["L1_attn_W2"], np.float32)
    d["b1cols"] = np.ascontiguousarray(inputs["L1_attn_b1"].reshape(32, 128).T,
                                       np.float32)
    d["pvec3"] = _pack_pvec([
        inputs["L1_attn_ln1_g"], inputs["L1_attn_ln1_b"],
        inputs["L1_attn_ln2_g"], inputs["L1_attn_ln2_b"], inputs["L1_attn_b2"]])
    return d


PHASE3_TENSORS = [
    ("Wq", [1024, 1024]), ("Wk", [1024, 1024]), ("Wv", [1024, 1024]),
    ("Wo", [1024, 1024]), ("projT", [64, 64]), ("W1", [1024, 4096]),
    ("W2", [4096, 1024]), ("b1cols", [128, 32]), ("pvec3", [128, 40]),
]


def build_nc_phase3(T=2048, ncores=8, debug=False, stop_at=None):
    from contextlib import ExitStack
    nc = bass.Bass("TRN2", target_bir_lowering=False)
    x_in = nc.dram_tensor("x_fm", [1024, T], F32, kind="ExternalInput").ap()
    y_out = nc.dram_tensor("y_fm", [1024, T], F32, kind="ExternalOutput").ap()
    inp = {nm: nc.dram_tensor(nm, shp, F32, kind="ExternalInput").ap()
           for nm, shp in PHASE3_TENSORS}
    em = Em(nc)
    ctx = Ctx()
    ctx.fused = False
    ctx.ncores = ncores
    ctx.debug = debug
    ctx.dbg_done = set()
    ctx.x_in = x_in
    ctx.y_out = y_out
    with ExitStack() as st:
        E = st.enter_context
        if debug:
            ctx.dbg_stage = E(nc.sbuf_tensor("dbgstage", [128, 1024], F32))
        setup_common(nc, em, ctx, E, T)
        build_phase3(nc, em, ctx, E, inp, stop_at=stop_at)
        em.build()
    print("phase3: ops=%d" % len(em.ops))
    return nc


def _p4_consts(T):
    DH, WIN = 64, 512
    half = np.arange(0, DH, 2, dtype=np.float64)
    invf = 1.0 / 10000.0 ** (half / DH)
    t = np.arange(T, dtype=np.float64)
    fr = t[:, None] * invf[None, :]
    fr = np.concatenate([fr, fr], axis=-1)
    base = (half + 0.4 * DH) / (1.4 * DH)
    power = (t - T // 2) / float(WIN // 2)
    sc = base[None, :] ** power[:, None]
    sc = np.concatenate([sc, sc], axis=-1)
    cos, sin = np.cos(fr), np.sin(fr)
    Aq = (cos * sc).T; Bq = (sin * sc).T
    Ak = (cos / sc).T; Bk = (sin / sc).T
    pack = lambda a: np.ascontiguousarray(np.tile(a, (2, 1))).astype(BF16NP)
    RT = np.zeros((64, 64), np.float32)
    for do in range(32):
        RT[do + 32, do] = -1.0
        RT[do, do + 32] = 1.0
    RT2 = np.zeros((128, 128), np.float32)
    RT2[0:64, 0:64] = RT; RT2[64:128, 64:128] = RT
    M = np.zeros((4, 128, 1024), np.float32)
    for qs in range(4):
        for p in range(128):
            i = qs * 128 + p
            M[qs, p, i:i + 513] = 1.0
    M0 = np.zeros((4, 128, 512), np.float32)
    for qs in range(4):
        for p in range(128):
            M0[qs, p, 0:qs * 128 + p + 1] = 1.0
    G8 = np.zeros((8, 512), np.float32)
    for h in range(8):
        G8[h, 64 * h:64 * (h + 1)] = 1.0
    BD8 = np.zeros((512, 8), np.float32)
    for h in range(8):
        BD8[64 * h:64 * (h + 1), h] = 1.0
    return (pack(Aq), pack(Bq), pack(Ak), pack(Bk), RT2.astype(BF16NP),
            M.astype(BF16NP), M0.astype(BF16NP), G8.astype(BF16NP),
            BD8.astype(BF16NP))


def build_phase4(nc, em, ctx, E, inp):
    T = ctx.T
    NTW = T // 512
    NI = T // 128
    NW = max(1, T // 512)

    pp = [E(nc.psum_tensor(f"p4p{i}", [128, 512], F32)) for i in range(7)]
    pbf = E(nc.psum_tensor("p4bf", [128, 1024], BF16))

    Aq_np, Bq_np, Ak_np, Bk_np, RT2_np, M_np, M0_np, G8_np, BD8_np = _p4_consts(T)
    def const(nm, arr):
        d = nc.inline_tensor(arr, nm + "_d")
        t = E(nc.sbuf_tensor(nm, list(arr.shape), BF16))
        em.dma("sync", out=t[:], in_=d.ap(), writes=[t])
        return t
    Aq = const("p4Aq", Aq_np); Bq = const("p4Bq", Bq_np)
    Ak = const("p4Ak", Ak_np); Bk = const("p4Bk", Bk_np)
    RT2 = const("p4RT", RT2_np)
    G8 = const("p4G8", G8_np)
    bd8_d = nc.inline_tensor(BD8_np, "p4BD_d")
    BD8 = [E(nc.sbuf_tensor(f"p4BD{c}", [128, 8], BF16)) for c in range(4)]
    for c in range(4):
        em.dma("sync", out=BD8[c][:], in_=bd8_d.ap()[128 * c:128 * (c + 1), :],
               writes=[BD8[c]])
    Mm = [const(f"p4M{qs}", M_np[qs]) for qs in range(4)]
    Mm0 = [const(f"p4M0{qs}", M0_np[qs]) for qs in range(4)]

    pv4 = E(nc.sbuf_tensor("p4pv", [128, 64], F32))
    em.dma("sync", out=pv4[:], in_=inp["pvec4"], writes=[pv4])

    def pvc(l, idx, j):
        base = l * 32
        if idx < 3:
            return pv4[:, base + idx * 8 + j:base + idx * 8 + j + 1]
        return pv4[:, base + 24 + (idx - 3) * 4 + j:base + 24 + (idx - 3) * 4 + j + 1]

    Z = [E(nc.sbuf_tensor(f"p4Z{j}", [128, T], BF16)) for j in range(8)]
    hs4 = [E(nc.sbuf_tensor(f"p4hs{j}", [128, T], BF16)) for j in range(4)]
    q_fm = [E(nc.sbuf_tensor(f"p4q{j}", [128, T], BF16)) for j in range(4)]
    k_fm = [E(nc.sbuf_tensor(f"p4k{j}", [128, T], BF16)) for j in range(4)]
    v_tm = [E(nc.sbuf_tensor(f"p4v{i}", [128, 512], BF16)) for i in range(NI)]
    o_fm = [E(nc.sbuf_tensor(f"p4o{j}", [128, max(T, 1024)], BF16)) for j in range(4)]
    helu = [E(nc.sbuf_tensor(f"p4he{m}", [128, 512], BF16)) for m in range(22)]
    attn_sb = [E(nc.sbuf_tensor(f"p4at{i}", [128, 1024], BF16)) for i in range(2)]
    attnT = [E(nc.sbuf_tensor(f"p4aT{i}", [128, 1024], BF16)) for i in range(2)]
    rnb = E(nc.sbuf_tensor("p4rn", [8, 512], BF16))
    rnf = E(nc.sbuf_tensor("p4rnf", [8, 512], F32))
    tb = [E(nc.sbuf_tensor(f"p4tb{i}", [128, 512], BF16)) for i in range(2)]
    mxc = E(nc.sbuf_tensor("p4mx", [128, 1], F32))
    mx2 = E(nc.sbuf_tensor("p4mx2", [128, 1], F32))
    rc = E(nc.sbuf_tensor("p4rc", [128, 1], F32))
    sq2 = [E(nc.sbuf_tensor(f"p4sq{i}", [128, 512], BF16)) for i in range(2)]
    tmpf = [E(nc.sbuf_tensor(f"p4tmp{i}", [128, 512], F32)) for i in range(2)]
    lnrow = [E(nc.sbuf_tensor(f"p4row{i}", [1, 512], F32)) for i in range(4)]
    xt = [E(nc.sbuf_tensor(f"p4x{j}", [128, 512], BF16)) for j in range(8)]
    wbig = [E(nc.sbuf_tensor(f"p4wB{i}", [128, 512], BF16)) for i in range(3)]
    wbuf = [E(nc.sbuf_tensor(f"p4wb{i}", [128, 256], BF16)) for i in range(4)]

    em.barrier()

    for w in range(NTW):
        t0 = w * 512
        for j in range(8):
            em.dma("gpsimd", out=xt[j][:], in_=ctx.x_in[128 * j:128 * (j + 1),
                                                        t0:t0 + 512],
                   reads=[xt[j]], writes=[xt[j]])
        emit_ln_win(nc, em, ctx, t0, 512, lambda j: xt[j][:],
                    lambda j, emitfn: emitfn(Z[j][:, t0:t0 + 512]),
                    (pp[0], pp[1], pp[2], pp[3]), ctx.eps5[0:1, 0:1],
                    lnrow=lnrow, sq2=sq2, tmpf=tmpf)

    for l in range(2):
        for j in range(4):
            eng = "gpsimd" if (j % 2 == 1) else "vector"
            copy(em, eng, hs4[j][:, 1:T], Z[j + 4][:, 0:T - 1], wkey=hs4[j])
            memset(em, eng, hs4[j][:, 0:1], 0.0, wkey=hs4[j])
        hsrc = lambda j: Z[j] if j < 4 else hs4[j - 4]
        for w in range(NTW):
            t0 = w * 512
            for pj in range(4):
                dst = q_fm[2 * pj] if pj < 2 else k_fm[2 * (pj - 2)]
                dst2 = q_fm[2 * pj + 1] if pj < 2 else k_fm[2 * (pj - 2) + 1]
                for kk in range(8):
                    wb = wbuf[kk % 4]
                    em.dma("gpsimd", out=wb[:, 0:256],
                           in_=inp["wqkv"][l, 128 * kk:128 * (kk + 1),
                                           256 * pj:256 * (pj + 1)],
                           reads=[wb], writes=[wb])
                    mm(em, pp[0][:, 0:512], wb[:, 0:128], hsrc(kk)[:, t0:t0 + 512],
                       start=(kk == 0), stop=(kk == 7))
                    mm(em, pp[1][:, 0:512], wb[:, 128:256], hsrc(kk)[:, t0:t0 + 512],
                       start=(kk == 0), stop=(kk == 7))
                copy(em, "vector", dst[:, t0:t0 + 512], pp[0][:, 0:512], wkey=dst)
                copy(em, "vector", dst2[:, t0:t0 + 512], pp[1][:, 0:512], wkey=dst2)
            for kk in range(8):
                wb = wbig[kk % 3]
                em.dma("gpsimd", out=wb[:],
                       in_=inp["wqkv"][l, 128 * kk:128 * (kk + 1), 1024:1536],
                       reads=[wb], writes=[wb])
                for s in range(4):
                    mm(em, pp[2 + s][:, 0:512],
                       hsrc(kk)[:, t0 + 128 * s:t0 + 128 * (s + 1)], wb[:],
                       start=(kk == 0), stop=(kk == 7))
            for s in range(4):
                copy(em, "vector", v_tm[4 * w + s][:], pp[2 + s][:, 0:512],
                     wkey=v_tm[4 * w + s])
        for (fm, Ac, Bc, isq) in ((q_fm, Aq, Bq, True), (k_fm, Ak, Bk, False)):
            for w in range(NTW):
                t0 = w * 512
                for j in range(4):
                    act(em, sq2[j % 2][:], fm[j][:, t0:t0 + 512], AF.Square,
                        wkey=sq2[j % 2])
                    mm(em, pp[0][0:8, 0:512], BD8[j][:],
                       sq2[j % 2][:], start=(j == 0), stop=(j == 3))
                act(em, rnf[:, 0:512], pp[0][0:8, 0:512], AF.Sqrt, wkey=rnf)
                ts(em, "vector", rnf[:, 0:512], rnf[:, 0:512], 1e-12, None,
                   ALU.max, wkey=rnf)
                recip(em, rnf[:, 0:512], rnf[:, 0:512])
                copy(em, "vector", rnb[:, 0:512], rnf[:, 0:512], wkey=rnb)
                for j in range(4):
                    mm(em, pp[1][:, 0:512], G8[:, 128 * j:128 * (j + 1)],
                       rnb[:, 0:512])
                    tt(em, "vector", fm[j][:, t0:t0 + 512], fm[j][:, t0:t0 + 512],
                       pp[1][:, 0:512], ALU.mult, wkey=fm[j])
                    qsc = pvc(l, 3 if isq else 4, j)
                    ts(em, "vector", fm[j][:, t0:t0 + 512], fm[j][:, t0:t0 + 512],
                       qsc, 8.0 if isq else 1.0, ALU.mult, ALU.mult, wkey=fm[j])
                    mm(em, pp[2][:, 0:512], RT2[:], fm[j][:, t0:t0 + 512])
                    tt(em, "gpsimd", tb[0][:], fm[j][:, t0:t0 + 512],
                       Ac[:, t0:t0 + 512], ALU.mult, wkey=tb[0])
                    tt(em, "vector", tb[1][:], pp[2][:, 0:512],
                       Bc[:, t0:t0 + 512], ALU.mult, wkey=tb[1])
                    tt(em, "gpsimd", fm[j][:, t0:t0 + 512], tb[0][:], tb[1][:],
                       ALU.add, wkey=fm[j])
        for h in range(8):
            hj, par = h // 2, h % 2
            sl = slice(64 * par, 64 * (par + 1))
            for w in range(NW):
                kb = (w - 1) * 4
                kcol0 = (w - 1) * 512
                nk = 512 if w == 0 else 1024
                for qs in range(4):
                    qc0 = w * 512 + qs * 128
                    ia = qs % 2
                    if w == 0:
                        mm(em, pp[0][:, 0:512], q_fm[hj][sl, qc0:qc0 + 128],
                           k_fm[hj][sl, 0:512])
                        reduce(em, mxc[:], pp[0][:, 0:512], ALU.max, wkey=mxc)
                        ts(em, "vector", mxc[:], mxc[:], -1.0, None, ALU.mult,
                           wkey=mxc)
                        act(em, attn_sb[ia][:, 0:512], pp[0][:, 0:512], AF.Exp,
                            bias=mxc[:], wkey=attn_sb[ia])
                        tt(em, "gpsimd" if qs % 2 else "vector",
                           attn_sb[ia][:, 0:512],
                           attn_sb[ia][:, 0:512], Mm0[qs][:], ALU.mult,
                           wkey=attn_sb[ia])
                        reduce(em, rc[:], attn_sb[ia][:, 0:512], ALU.add, wkey=rc)
                        recip(em, rc[:], rc[:])
                        ts(em, "vector", attn_sb[ia][:, 0:512],
                           attn_sb[ia][:, 0:512], rc[:], None, ALU.mult,
                           wkey=attn_sb[ia])
                        nblk = 4
                    else:
                        mm(em, pp[0][:, 0:512], q_fm[hj][sl, qc0:qc0 + 128],
                           k_fm[hj][sl, kcol0:kcol0 + 512])
                        mm(em, pp[1][:, 0:512], q_fm[hj][sl, qc0:qc0 + 128],
                           k_fm[hj][sl, kcol0 + 512:kcol0 + 1024])
                        reduce(em, mxc[:], pp[0][:, 0:512], ALU.max, wkey=mxc)
                        reduce(em, mx2[:], pp[1][:, 0:512], ALU.max, wkey=mx2)
                        tt(em, "vector", mxc[:], mxc[:], mx2[:], ALU.max, wkey=mxc)
                        ts(em, "vector", mxc[:], mxc[:], -1.0, None, ALU.mult,
                           wkey=mxc)
                        act(em, attn_sb[ia][:, 0:512], pp[0][:, 0:512], AF.Exp,
                            bias=mxc[:], wkey=attn_sb[ia])
                        act(em, attn_sb[ia][:, 512:1024], pp[1][:, 0:512], AF.Exp,
                            bias=mxc[:], wkey=attn_sb[ia])
                        tt(em, "gpsimd" if qs % 2 else "vector",
                           attn_sb[ia][:], attn_sb[ia][:], Mm[qs][:],
                           ALU.mult, wkey=attn_sb[ia])
                        reduce(em, rc[:], attn_sb[ia][:], ALU.add, wkey=rc)
                        recip(em, rc[:], rc[:])
                        ts(em, "vector", attn_sb[ia][:], attn_sb[ia][:], rc[:],
                           None, ALU.mult, wkey=attn_sb[ia])
                        nblk = 8
                    for b in range(nblk):
                        transp(em, pbf[:, 128 * b:128 * (b + 1)],
                               attn_sb[ia][:, 128 * b:128 * (b + 1)], ctx.eye[:])
                    copy(em, "vector", attnT[ia][:, 0:128 * nblk],
                         pbf[:, 0:128 * nblk], wkey=attnT[ia])
                    pav = pp[4 + (qs % 2)]
                    for b in range(nblk):
                        vi = (kb + b) if w > 0 else b
                        mm(em, pav[0:64, 0:128], v_tm[vi][:, 64 * h:64 * (h + 1)],
                           attnT[ia][:, 128 * b:128 * (b + 1)],
                           start=(b == 0), stop=(b == nblk - 1))
                    copy(em, "vector", o_fm[hj][sl, qc0:qc0 + 128],
                         pav[0:64, 0:128], wkey=o_fm[hj])
        for w in range(NTW):
            t0 = w * 512
            for pj in range(4):
                for kk in range(4):
                    wb = wbuf[kk % 4]
                    em.dma("gpsimd", out=wb[:, 0:256],
                           in_=inp["wo"][l, 128 * kk:128 * (kk + 1),
                                         256 * pj:256 * (pj + 1)],
                           reads=[wb], writes=[wb])
                    mm(em, pp[0][:, 0:512], wb[:, 0:128], o_fm[kk][:, t0:t0 + 512],
                       start=(kk == 0), stop=(kk == 3))
                    mm(em, pp[1][:, 0:512], wb[:, 128:256], o_fm[kk][:, t0:t0 + 512],
                       start=(kk == 0), stop=(kk == 3))
                for half in range(2):
                    j = 2 * pj + half
                    tt(em, "vector", Z[j][:, t0:t0 + 512],
                       pp[half][:, 0:512], Z[j][:, t0:t0 + 512], ALU.add,
                       wkey=Z[j])
        for j in range(4):
            copy(em, "vector", hs4[j][:, 1:T], Z[j + 4][:, 0:T - 1], wkey=hs4[j])
            memset(em, "vector", hs4[j][:, 0:1], 0.0, wkey=hs4[j])
        hffn = lambda j: o_fm[j // 2][:, 512 * (j % 2):512 * (j % 2) + 512]
        for w in range(NTW):
            t0 = w * 512
            emit_ln_win(nc, em, ctx, t0, 512, lambda j: hsrc(j)[:, t0:t0 + 512],
                        lambda j, emitfn: emitfn(hffn(j)),
                        (pp[0], pp[1], pp[2], pp[3]), ctx.eps5[0:1, 0:1],
                        gamma=lambda j: pvc(l, 0, j), beta=lambda j: pvc(l, 1, j),
                        lnrow=lnrow, sq2=sq2, tmpf=tmpf)
            for mp in range(22):
                for kk in range(8):
                    wb = wbuf[kk % 4]
                    em.dma("gpsimd", out=wb[:, 0:256],
                           in_=inp["w1p"][l, 128 * kk:128 * (kk + 1),
                                          256 * mp:256 * (mp + 1)],
                           reads=[wb], writes=[wb])
                    mm(em, pp[0][:, 0:512], wb[:, 0:128], hffn(kk),
                       start=(kk == 0), stop=(kk == 7))
                    mm(em, pp[1][:, 0:512], wb[:, 128:256], hffn(kk),
                       start=(kk == 0), stop=(kk == 7))
                for half in range(2):
                    m = 2 * mp + half
                    if m < 22:
                        ts(em, "vector", helu[m][:], pp[half][:, 0:512],
                           inp_b1col(em, ctx, l, m), None, ALU.add, wkey=helu[m])
                    else:
                        act(em, sq2[0][:], pp[half][:, 0:512], AF.Gelu,
                            bias=inp_b1col(em, ctx, l, m), wkey=sq2[0])
                        tt(em, "gpsimd", helu[m - 22][:], helu[m - 22][:],
                           sq2[0][:], ALU.mult, wkey=helu[m - 22])
            for pj in range(4):
                for kk in range(22):
                    wb = wbuf[kk % 4]
                    em.dma("gpsimd", out=wb[:, 0:256],
                           in_=inp["w2p"][l, 128 * kk:128 * (kk + 1),
                                          256 * pj:256 * (pj + 1)],
                           reads=[wb], writes=[wb])
                    mm(em, pp[2][:, 0:512], wb[:, 0:128], helu[kk][:],
                       start=(kk == 0), stop=(kk == 21))
                    mm(em, pp[3][:, 0:512], wb[:, 128:256], helu[kk][:],
                       start=(kk == 0), stop=(kk == 21))
                for half in range(2):
                    j = 2 * pj + half
                    stt(em, "vector", Z[j][:, t0:t0 + 512],
                        pp[2 + half][:, 0:512], pvc(l, 2, j),
                        Z[j][:, t0:t0 + 512], ALU.add, ALU.add, wkey=Z[j])
    if ctx.fused:
        for j in range(8):
            tt(em, "vector", ctx.X[j][:], Z[j][:], ctx.X[j][:], ALU.add,
               wkey=ctx.X[j])
    else:
        for w in range(NTW):
            t0 = w * 512
            for j in range(8):
                em.dma("gpsimd", out=xt[j][:], in_=ctx.x_in[128 * j:128 * (j + 1),
                                                            t0:t0 + 512],
                       reads=[xt[j]], writes=[xt[j]])
                tt(em, "vector", xt[j][:], Z[j][:, t0:t0 + 512], xt[j][:],
                   ALU.add, wkey=xt[j])
                em.dma("gpsimd", out=ctx.y_out[128 * j:128 * (j + 1), t0:t0 + 512],
                       in_=xt[j][:], reads=[xt[j]], writes=[f"P4OUT{j}"])
    em.barrier()


def inp_b1col(em, ctx, l, m):
    return ctx.p4b1c[l][:, m:m + 1]


def prep_phase4_inputs(inputs):
    d = {}
    d["wqkv"] = np.ascontiguousarray(inputs["L1_ffn_wqkv"], np.float32)
    d["wo"] = np.ascontiguousarray(inputs["L1_ffn_wo"], np.float32)
    w1 = inputs["L1_ffn_w1"]
    w1p = np.zeros((2, 1024, 5632), np.float32)
    w1p[:, :, 0:2730] = w1[:, :, 0:2730]
    w1p[:, :, 2816:2816 + 2730] = w1[:, :, 2730:5460]
    d["w1p"] = w1p
    w2 = inputs["L1_ffn_w2"]
    w2p = np.zeros((2, 2816, 1024), np.float32)
    w2p[:, 0:2730, :] = w2
    d["w2p"] = w2p
    b1 = inputs["L1_ffn_b1"]
    b1p = np.zeros((2, 5632), np.float32)
    b1p[:, 0:2730] = b1[:, 0:2730]
    b1p[:, 2816:2816 + 2730] = b1[:, 2730:5460]
    d["b1cols4"] = np.ascontiguousarray(
        np.stack([b1p[l].reshape(44, 128).T for l in range(2)]), np.float32)
    cols = []
    for l in range(2):
        sub = _pack_pvec([inputs["L1_ffn_ln_g"][l], inputs["L1_ffn_ln_b"][l],
                          inputs["L1_ffn_b2"][l]])
        qs = np.tile(inputs["L1_ffn_q_scale"][l], 8).reshape(4, 128).T
        ks = np.tile(inputs["L1_ffn_k_scale"][l], 8).reshape(4, 128).T
        cols.append(np.concatenate([sub, qs, ks], axis=1))
    d["pvec4"] = np.ascontiguousarray(np.concatenate(cols, axis=1), np.float32)
    return d


PHASE4_TENSORS = [
    ("wqkv", [2, 1024, 1536]), ("wo", [2, 512, 1024]), ("w1p", [2, 1024, 5632]),
    ("w2p", [2, 2816, 1024]), ("b1cols4", [2, 128, 44]), ("pvec4", [128, 64]),
]


def build_nc_phase4(T=2048, debug=False):
    from contextlib import ExitStack
    nc = bass.Bass("TRN2", target_bir_lowering=False)
    x_in = nc.dram_tensor("x_fm", [1024, T], F32, kind="ExternalInput").ap()
    y_out = nc.dram_tensor("y_fm", [1024, T], F32, kind="ExternalOutput").ap()
    inp = {nm: nc.dram_tensor(nm, shp, F32, kind="ExternalInput").ap()
           for nm, shp in PHASE4_TENSORS}
    em = Em(nc)
    ctx = Ctx()
    ctx.fused = False
    ctx.debug = debug
    ctx.dbg_done = set()
    ctx.x_in = x_in
    ctx.y_out = y_out
    with ExitStack() as st:
        E = st.enter_context
        if debug:
            ctx.dbg_stage = E(nc.sbuf_tensor("dbgstage", [128, 1024], F32))
        setup_common(nc, em, ctx, E, T)
        ctx.p4b1c = [E(nc.sbuf_tensor(f"p4b1c{l}", [128, 44], F32)) for l in range(2)]
        for l in range(2):
            em.dma("sync", out=ctx.p4b1c[l][:], in_=inp["b1cols4"][l],
                   writes=[ctx.p4b1c[l]])
        build_phase4(nc, em, ctx, E, inp)
        em.build()
    print("phase4: ops=%d" % len(em.ops))
    return nc


def _pack_pvec(vectors):
    cols = [np.ascontiguousarray(v.reshape(8, 128).T) for v in vectors]
    return np.ascontiguousarray(np.concatenate(cols, axis=1).astype(np.float32))


def prep_phase1_inputs(inputs):
    d = {}
    d["Wr"] = np.ascontiguousarray(inputs["L0_attn_Wr"], np.float32)
    d["Wk"] = np.ascontiguousarray(inputs["L0_attn_Wk"], np.float32)
    d["Wv"] = np.ascontiguousarray(inputs["L0_attn_Wv"], np.float32)
    d["Wg"] = np.ascontiguousarray(inputs["L0_attn_Wg"], np.float32)
    d["Wo"] = np.ascontiguousarray(inputs["L0_attn_Wo"], np.float32)
    d["tm_w1"] = np.ascontiguousarray(inputs["L0_attn_tm_w1"], np.float32)
    d["tm_w2"] = np.ascontiguousarray(inputs["L0_attn_tm_w2"], np.float32)
    d["td_w1"] = np.ascontiguousarray(inputs["L0_attn_td_w1"], np.float32)
    d["td_w2"] = np.ascontiguousarray(inputs["L0_attn_td_w2"], np.float32)
    d["pvec"] = _pack_pvec([
        inputs["L0_attn_x_maa"], inputs["L0_attn_w_maa"], inputs["L0_attn_k_maa"],
        inputs["L0_attn_v_maa"], inputs["L0_attn_r_maa"], inputs["L0_attn_g_maa"],
        inputs["L0_attn_time_decay"].reshape(-1), inputs["L0_attn_time_first"].reshape(-1),
        inputs["L0_attn_ln_w"], inputs["L0_attn_ln_b"]])
    d["ln_w"] = np.ascontiguousarray(inputs["L0_attn_ln_w"], np.float32)
    d["ln_b"] = np.ascontiguousarray(inputs["L0_attn_ln_b"], np.float32)
    return d


PHASE1_TENSORS = [
    ("Wr", [1024, 1024]), ("Wk", [1024, 1024]), ("Wv", [1024, 1024]),
    ("Wg", [1024, 1024]), ("Wo", [1024, 1024]), ("tm_w1", [1024, 160]),
    ("tm_w2", [5, 32, 1024]), ("td_w1", [1024, 64]), ("td_w2", [64, 1024]),
    ("pvec", [128, 80]), ("ln_w", [1024]), ("ln_b", [1024]),
]


def build_nc_phase1(T=2048, debug=False):
    from contextlib import ExitStack
    nc = bass.Bass("TRN2", target_bir_lowering=False)
    x_in = nc.dram_tensor("x_fm", [1024, T], F32, kind="ExternalInput").ap()
    y_out = nc.dram_tensor("y_fm", [1024, T], F32, kind="ExternalOutput").ap()
    inp = {nm: nc.dram_tensor(nm, shp, F32, kind="ExternalInput").ap()
           for nm, shp in PHASE1_TENSORS}
    em = Em(nc)
    ctx = Ctx()
    ctx.fused = False
    ctx.debug = debug
    ctx.dbg_done = set()
    ctx.x_in = x_in
    ctx.y_out = y_out
    with ExitStack() as st:
        E = st.enter_context
        if debug:
            ctx.dbg_stage = E(nc.sbuf_tensor("dbgstage", [128, 1024], F32))
        setup_common(nc, em, ctx, E, T)
        build_phase1(nc, em, ctx, E, inp)
        em.build()
    print("phase1: ops=%d sbuf_left=%d psum_left=%d" %
          (len(em.ops), nc.sbuf_bytes_remaining, nc.psum_banks_remaining))
    return nc


def build_nc_fused(T=2048, ncores=8):
    from contextlib import ExitStack
    nc = bass.Bass("TRN2", target_bir_lowering=False)
    x_in = nc.dram_tensor("x_fm", [1024, T], F32, kind="ExternalInput").ap()
    y_out = nc.dram_tensor("y_fm", [1024, T], F32, kind="ExternalOutput").ap()
    xs = [nc.dram_tensor(f"xs{i}", [1024, T], F32, kind="Internal").ap()
          for i in range(3)]
    tens = [PHASE1_TENSORS, PHASE2_TENSORS, PHASE3_TENSORS, PHASE4_TENSORS]
    inps = []
    for ph in range(4):
        inps.append({nm: nc.dram_tensor(f"p{ph + 1}_{nm}", shp, F32,
                                        kind="ExternalInput").ap()
                     for nm, shp in tens[ph]})
    em = Em(nc)
    ctx = Ctx()
    ctx.ncores = ncores
    ctx.moe_sparse = False
    ctx.fused = False
    ctx.debug = False
    ctx.dbg_done = set()
    with ExitStack() as outer:
        E0 = outer.enter_context
        setup_common(nc, em, ctx, E0, T)
        em.barrier()
        ctx.x_in, ctx.y_out = x_in, xs[0]
        with ExitStack() as s:
            build_phase1(nc, em, ctx, s.enter_context, inps[0])
        with ExitStack() as s:
            E = s.enter_context
            ctx.X = alloc_stream(nc, E, T)
            load_stream(em, ctx, xs[0])
            em.barrier()
            build_phase2(nc, em, ctx, E, inps[1])
            for j in range(8):
                em.dma("gpsimd", out=xs[1][128 * j:128 * (j + 1), :],
                       in_=ctx.X[j][:], reads=[ctx.X[j]], writes=[f"XS1_{j}"])
            em.barrier()
        ctx.x_in, ctx.y_out = xs[1], xs[2]
        with ExitStack() as s:
            build_phase3(nc, em, ctx, s.enter_context, inps[2])
        ctx.x_in, ctx.y_out = xs[2], y_out
        with ExitStack() as s:
            E = s.enter_context
            ctx.p4b1c = [E(nc.sbuf_tensor(f"p4b1c{l}", [128, 44], F32))
                         for l in range(2)]
            for l in range(2):
                em.dma("sync", out=ctx.p4b1c[l][:], in_=inps[3]["b1cols4"][l],
                       writes=[ctx.p4b1c[l]])
            build_phase4(nc, em, ctx, E, inps[3])
        em.build()
    print("fused: ops=%d" % len(em.ops))
    return nc


def kernel(**inputs):
    T = 2048
    x = np.asarray(inputs["L0_attn_x"], np.float32)
    preps = [prep_phase1_inputs(inputs), prep_phase2_inputs(inputs),
             prep_phase3_inputs(inputs), prep_phase4_inputs(inputs)]
    nc = build_nc_fused(T, NCORE)
    base = {}
    for ph in range(4):
        for k, v in preps[ph].items():
            base[f"p{ph + 1}_{k}"] = v
    in_maps = []
    for c in range(NCORE):
        m = dict(base)
        m["x_fm"] = np.ascontiguousarray(x[c].T)
        in_maps.append(m)
    res = run_bass_kernel_spmd(nc, in_maps, core_ids=list(range(NCORE)))
    out = np.stack([np.ascontiguousarray(np.asarray(r["y_fm"]).T)
                    for r in res.results])
    return out.astype(np.float32)


# version 17
# speedup vs baseline: 1.1419x; 1.0907x over previous
import os
import numpy as np
import ml_dtypes
import concourse.bass as bass
import concourse.mybir as mybir
from concourse.bass_utils import run_bass_kernel_spmd

F32 = mybir.dt.float32
BF16 = mybir.dt.bfloat16
AF = mybir.ActivationFunctionType
ALU = mybir.AluOpType
AX = mybir.AxisListType
BF16NP = ml_dtypes.bfloat16

D = 1024
NCORE = 8
CENG = ("sync", "tensor", "vector", "scalar", "gpsimd")


class Em:
    def __init__(self, nc):
        self.nc = nc
        self.ops = []
        self.last_w = {}
        self.readers = {}

    @staticmethod
    def _k(h):
        if isinstance(h, str):
            return h
        name = getattr(h, "name", None)
        if name is None:
            name = h.tensor.name
        return name

    def op(self, eng, fn, reads=(), writes=(), dma=False, writes_accum=()):
        if not hasattr(self, "accum"):
            self.accum = {}
        i = len(self.ops)
        deps = set()
        for h in reads:
            k = self._k(h)
            j = self.last_w.get(k)
            if j is not None:
                deps.add(j)
            deps.update(self.accum.get(k, ()))
        for h in writes:
            k = self._k(h)
            j = self.last_w.get(k)
            if j is not None:
                deps.add(j)
            deps.update(self.readers.get(k, ()))
            deps.update(self.accum.get(k, ()))
        for h in writes_accum:
            k = self._k(h)
            j = self.last_w.get(k)
            if j is not None:
                deps.add(j)
            deps.update(self.readers.get(k, ()))
        o = {"i": i, "eng": eng, "fn": fn, "dma": bool(dma), "deps": deps,
             "inc": bool(dma), "bar": False}
        self.ops.append(o)
        for h in reads:
            self.readers.setdefault(self._k(h), set()).add(i)
        for h in writes:
            k = self._k(h)
            self.last_w[k] = i
            self.readers[k] = set()
            self.accum[k] = set()
        for h in writes_accum:
            self.accum.setdefault(self._k(h), set()).add(i)
        return o

    def barrier(self):
        self.ops.append({"i": len(self.ops), "eng": None, "fn": None,
                         "dma": False, "deps": set(), "inc": False, "bar": True})
        self.last_w = {}
        self.readers = {}
        self.accum = {}

    def dma(self, eng, out, in_, reads=(), writes=(), writes_accum=(), **kw):
        return self.op(eng,
                       lambda e, out=out, in_=in_, kw=kw: e.dma_start(out=out, in_=in_, **kw),
                       reads=reads, writes=writes, writes_accum=writes_accum, dma=True)

    def build(self):
        NQ = 8
        ops = self.ops
        if not (ops and ops[-1]["bar"]):
            self.barrier()
            ops = self.ops
        for o in ops:
            if o["bar"]:
                continue
            for d in o["deps"]:
                p = ops[d]
                if (p["eng"] != o["eng"]) or p["dma"]:
                    p["inc"] = True
        lastc = {e: None for e in CENG}
        for o in ops:
            if o["bar"]:
                for e in CENG:
                    if lastc[e] is not None:
                        ops[lastc[e]]["inc"] = True
                continue
            if not o["dma"]:
                lastc[o["eng"]] = o["i"]
        ccnt = {e: 0 for e in CENG}
        dcnt = {}
        dseq = {e: 0 for e in CENG}
        for o in ops:
            if o["bar"]:
                o["floor"] = (dict(ccnt), dict(dcnt))
                continue
            if o["dma"]:
                slot = dseq[o["eng"]] % NQ
                dseq[o["eng"]] += 1
                key = (o["eng"], slot)
                o["qkey"] = key
                o["prev"] = dcnt.get(key, 0) * 16
                dcnt[key] = dcnt.get(key, 0) + 1
                o["cval"] = dcnt[key] * 16
            elif o["inc"]:
                ccnt[o["eng"]] += 1
                o["cval"] = ccnt[o["eng"]]

        nc = self.nc
        csem = {e: nc.alloc_semaphore(f"c_{e}") for e in CENG}
        qsem = {(e, s): nc.alloc_semaphore(f"q_{e}{s}") for e in CENG for s in range(NQ)}

        def emit_engine(e_obj, ename):
            wm_c = {x: 0 for x in CENG}
            wm_d = {}
            drained_upto = -1
            last_own = -1
            pend_floor = None

            def wait_d(key, v):
                if v > wm_d.get(key, 0):
                    e_obj.wait_ge(qsem[key], v)
                    wm_d[key] = v

            def do_floor():
                nonlocal drained_upto, pend_floor
                fc, fd = pend_floor
                for x in CENG:
                    if fc[x] > wm_c[x]:
                        e_obj.wait_ge(csem[x], fc[x])
                        wm_c[x] = fc[x]
                for key, n in fd.items():
                    wait_d(key, n * 16)
                drained_upto = last_own
                pend_floor = None

            for o in ops:
                if o["bar"]:
                    pend_floor = o["floor"]
                    continue
                if o["eng"] != ename:
                    continue
                if pend_floor is not None:
                    do_floor()
                need_drain = False
                for d in sorted(o["deps"]):
                    p = ops[d]
                    if p["dma"]:
                        wait_d(p["qkey"], p["cval"])
                    elif p["eng"] == ename:
                        if ename != "tensor" and p["i"] > drained_upto:
                            need_drain = True
                    else:
                        v = p["cval"]
                        if v > wm_c[p["eng"]]:
                            e_obj.wait_ge(csem[p["eng"]], v)
                            wm_c[p["eng"]] = v
                if o["dma"] and o["prev"] > 0:
                    wait_d(o["qkey"], o["prev"])
                if need_drain:
                    e_obj.drain()
                    drained_upto = last_own
                try:
                    inst = o["fn"](e_obj)
                except Exception:
                    import traceback as _tb
                    print("FAILING OP idx", o["i"], "eng", o["eng"])
                    _tb.print_exc()
                    raise
                if o["dma"]:
                    inst.then_inc(qsem[o["qkey"]], 16)
                elif o["inc"]:
                    inst.then_inc(csem[o["eng"]], 1)
                last_own = o["i"]
            if pend_floor is not None:
                do_floor()

        with nc.Block() as block:
            for ename in CENG:
                getattr(block, ename)(lambda e, ename=ename: emit_engine(e, ename))


def tt(em, eng, out, in0, in1, op, wkey=None):
    em.op(eng, lambda e, out=out, in0=in0, in1=in1, op=op: e.tensor_tensor(out, in0, in1, op),
          reads=[in0, in1], writes=[wkey if wkey is not None else out])


def ts(em, eng, out, in0, s1, s2, op0, op1=None, wkey=None):
    def fn(e, out=out, in0=in0, s1=s1, s2=s2, op0=op0, op1=op1):
        if op1 is None:
            return e.tensor_single_scalar(out, in0, s1, op0)
        return e.tensor_scalar(out, in0, s1, s2, op0, op1)
    rd = [in0]
    for s in (s1, s2):
        if hasattr(s, "tensor"):
            rd.append(s)
    em.op(eng, fn, reads=rd, writes=[wkey if wkey is not None else out])


def stt(em, eng, out, in0, scalar, in1, op0, op1, wkey=None):
    rd = [in0, in1]
    if hasattr(scalar, "tensor"):
        rd.append(scalar)
    em.op(eng, lambda e, out=out, in0=in0, scalar=scalar, in1=in1, op0=op0, op1=op1:
          e.scalar_tensor_tensor(out, in0, scalar, in1, op0, op1),
          reads=rd, writes=[wkey if wkey is not None else out])


def act(em, out, in_, func, bias=0.0, scale=1.0, wkey=None):
    rd = [in_]
    if hasattr(bias, "tensor"):
        rd.append(bias)
    em.op("scalar", lambda e, out=out, in_=in_, func=func, bias=bias, scale=scale:
          e.activation(out, in_, func, bias=bias, scale=scale),
          reads=rd, writes=[wkey if wkey is not None else out])


def mm(em, out, lhsT, rhs, start=True, stop=True):
    em.op("tensor", lambda e, out=out, lhsT=lhsT, rhs=rhs, start=start, stop=stop:
          e.matmul(out, lhsT, rhs, start=start, stop=stop),
          reads=[lhsT, rhs], writes=[out])


def transp(em, out, in_, ident):
    em.op("tensor", lambda e, out=out, in_=in_, ident=ident: e.transpose(out, in_, ident),
          reads=[in_, ident], writes=[out])


def copy(em, eng, out, in_, wkey=None):
    em.op(eng, lambda e, out=out, in_=in_: e.tensor_copy(out, in_),
          reads=[in_], writes=[wkey if wkey is not None else out])


def scopy(em, out, in_, wkey=None):
    em.op("scalar", lambda e, out=out, in_=in_: e.copy(out, in_),
          reads=[in_], writes=[wkey if wkey is not None else out])


def memset(em, eng, ap_, val, wkey):
    em.op(eng, lambda e, ap_=ap_, val=val: e.memset(ap_, val), writes=[wkey])


def reduce(em, out, in_, op, eng="vector", wkey=None):
    em.op(eng, lambda e, out=out, in_=in_, op=op: e.tensor_reduce(out, in_, AX.X, op),
          reads=[in_], writes=[wkey if wkey is not None else out])


def recip(em, out, in_, wkey=None):
    em.op("vector", lambda e, out=out, in_=in_: e.reciprocal(out, in_),
          reads=[in_], writes=[wkey if wkey is not None else out])


def scan(em, out, d0, d1, init, op0, op1):
    em.op("vector", lambda e, out=out, d0=d0, d1=d1, init=init, op0=op0, op1=op1:
          e.tensor_tensor_scan(out, d0, d1, init, op0, op1),
          reads=[d0, d1], writes=[out])


class Ctx:
    pass


def setup_common(nc, em, ctx, E, T):
    ctx.T = T
    eye_d = nc.inline_tensor(np.eye(128).astype(BF16NP), "eye_bf_d")
    ctx.eye = E(nc.sbuf_tensor("eye", [128, 128], BF16))
    em.dma("sync", out=ctx.eye[:], in_=eye_d.ap(), writes=[ctx.eye])
    ones_d = nc.inline_tensor(np.ones((1, 128), np.float32), "ones_f32_d")
    ctx.ones_f = E(nc.sbuf_tensor("ones_f", [1, 128], F32))
    em.dma("sync", out=ctx.ones_f[:], in_=ones_d.ap(), writes=[ctx.ones_f])
    onesb_d = nc.inline_tensor(np.ones((128, 1), BF16NP), "ones_col_bf_d")
    ctx.ones_col_bf = E(nc.sbuf_tensor("ones_col_bf", [128, 1], BF16))
    em.dma("sync", out=ctx.ones_col_bf[:], in_=onesb_d.ap(), writes=[ctx.ones_col_bf])
    ones1b_d = nc.inline_tensor(np.ones((1, 128), BF16NP), "ones_row_bf_d")
    ctx.ones_row_bf = E(nc.sbuf_tensor("ones_row_bf", [1, 128], BF16))
    em.dma("sync", out=ctx.ones_row_bf[:], in_=ones1b_d.ap(), writes=[ctx.ones_row_bf])
    ctx.zeros = E(nc.sbuf_tensor("zeros", [128, 512], F32))
    memset(em, "vector", ctx.zeros[:], 0.0, wkey=ctx.zeros)
    ctx.eps5 = E(nc.sbuf_tensor("eps5", [128, 1], F32))
    memset(em, "vector", ctx.eps5[:], 1e-5, wkey=ctx.eps5)
    ctx.eps6 = E(nc.sbuf_tensor("eps6", [128, 1], F32))
    memset(em, "vector", ctx.eps6[:], 1e-6, wkey=ctx.eps6)


def alloc_stream(nc, E, T):
    return [E(nc.sbuf_tensor(f"X{j}", [128, T], F32)) for j in range(8)]


def load_stream(em, ctx, x_fm_ap):
    for j in range(8):
        em.dma("sync", out=ctx.X[j][:], in_=x_fm_ap[128 * j:128 * (j + 1), :],
               writes=[ctx.X[j]])


def store_stream(em, ctx, y_fm_ap):
    for j in range(8):
        em.dma("gpsimd", out=y_fm_ap[128 * j:128 * (j + 1), :], in_=ctx.X[j][:],
               reads=[ctx.X[j]], writes=["OUT_y%d" % j])


def cast_weight(nc, em, ctx, E, name, dram_ap, nrow, ncol, stage=None):
    chunks = []
    for c in range(nrow // 128):
        tile = E(nc.sbuf_tensor(f"{name}_{c}", [128, ncol], BF16))
        em.dma("gpsimd", out=tile[:], in_=dram_ap[128 * c:128 * (c + 1), :],
               reads=[tile], writes=[tile])
        chunks.append(tile)
    return chunks


def dbg(em, ctx, name, tile, rows, cols):
    if not getattr(ctx, "debug", False) or name in ctx.dbg_done:
        return
    ctx.dbg_done.add(name)
    d = em.nc.dram_tensor("dbg_" + name, [rows, cols], F32, kind="ExternalOutput").ap()
    t = ctx.dbg_stage
    copy(em, "vector", t[0:rows, 0:cols], tile, wkey=t)
    em.dma("gpsimd", out=d, in_=t[0:rows, 0:cols], reads=[t], writes=["DBG" + name])


def build_phase1(nc, em, ctx, E, inp):
    T = ctx.T
    TT_ = 256
    L = 64
    NT = T // TT_
    NCH = TT_ // L
    NSUB = TT_ // 128

    P = [E(nc.psum_tensor(f"p1P{i}", [128, 512], F32)) for i in range(6)]
    P7 = E(nc.psum_tensor("p1P7", [128, 1024], BF16))

    tm_w1 = cast_weight(nc, em, ctx, E, "tmw1", inp["tm_w1"], 1024, 160)
    td_w1 = cast_weight(nc, em, ctx, E, "tdw1", inp["td_w1"], 1024, 64)
    tm_w2 = []
    for f in range(5):
        tile = E(nc.sbuf_tensor(f"tmw2_{f}", [32, 1024], BF16))
        em.dma("gpsimd", out=tile[:], in_=inp["tm_w2"][f], reads=[tile], writes=[tile])
        tm_w2.append(tile)
    td_w2 = E(nc.sbuf_tensor("tdw2", [64, 1024], BF16))
    em.dma("gpsimd", out=td_w2[:], in_=inp["td_w2"], reads=[td_w2], writes=[td_w2])

    pv = E(nc.sbuf_tensor("p1pv", [128, 80], F32))
    em.dma("sync", out=pv[:], in_=inp["pvec"], writes=[pv])

    def pvcol(idx, j, sl=None):
        if sl is not None:
            return pv[sl, idx * 8 + j:idx * 8 + j + 1]
        return pv[:, idx * 8 + j:idx * 8 + j + 1]
    IX, IW, IK, IV, IR, IG, ITD, ITF, ILW, ILB = range(10)

    lnrow_st = E(nc.sbuf_tensor("p1lnr", [1, 1024], BF16))
    ones164 = E(nc.sbuf_tensor("ones164", [1, 64], BF16))
    memset(em, "vector", ones164[:], 1.0, wkey=ones164)
    lnw_b = E(nc.sbuf_tensor("lnw_b", [64, 1024], BF16))
    lnb_b = E(nc.sbuf_tensor("lnb_b", [64, 1024], BF16))
    for dst, nm in ((lnw_b, "ln_w"), (lnb_b, "ln_b")):
        em.dma("gpsimd", out=lnrow_st[0:1, :], in_=inp[nm][None, :],
               reads=[lnrow_st], writes=[lnrow_st])
        mm(em, P[0][0:64, 0:512], ones164[:], lnrow_st[0:1, 0:512])
        mm(em, P[1][0:64, 0:512], ones164[:], lnrow_st[0:1, 512:1024])
        copy(em, "vector", dst[:, 0:512], P[0][0:64, 0:512], wkey=dst)
        copy(em, "vector", dst[:, 512:1024], P[1][0:64, 0:512], wkey=dst)

    mu_d = nc.inline_tensor(np.triu(np.ones((L, L), np.float32), 1).astype(BF16NP),
                            "maskU64")
    maskU = E(nc.sbuf_tensor("maskU", [L, L], BF16))
    em.dma("sync", out=maskU[:], in_=mu_d.ap(), writes=[maskU])

    em.barrier()

    xt = [E(nc.sbuf_tensor(f"p1x{j}", [128, TT_], F32)) for j in range(8)]
    xn_bf = [E(nc.sbuf_tensor(f"xn{j}", [128, TT_], BF16)) for j in range(8)]
    sx_bf = [E(nc.sbuf_tensor(f"sx{j}", [128, TT_], BF16)) for j in range(8)]
    u_p = [E(nc.sbuf_tensor(f"up{j}", [128, TT_], BF16)) for j in range(8)]
    u_g = [E(nc.sbuf_tensor(f"ug{j}", [128, TT_], BF16)) for j in range(8)]
    xn_last = E(nc.sbuf_tensor("xnlast", [128, 8], BF16))
    memset(em, "vector", xn_last[:], 0.0, wkey=xn_last)
    sq2 = [E(nc.sbuf_tensor(f"p1sq{i}", [128, TT_], BF16)) for i in range(2)]
    tmpf = [E(nc.sbuf_tensor(f"p1tmp{i}", [128, TT_], F32)) for i in range(2)]
    lnrow = [E(nc.sbuf_tensor(f"p1row{i}", [1, TT_], F32)) for i in range(4)]
    tm_f = [E(nc.sbuf_tensor(f"tmf{f}", [32, TT_], BF16)) for f in range(5)]
    tdh = E(nc.sbuf_tensor("tdh", [64, TT_], BF16))
    exp_w = E(nc.sbuf_tensor("expw", [128, TT_], F32))
    negew = E(nc.sbuf_tensor("negew", [128, TT_], F32))
    c_rel = [E(nc.sbuf_tensor(f"crel{j}", [128, TT_], F32)) for j in range(8)]
    E1 = [E(nc.sbuf_tensor(f"E1_{j}", [128, TT_], BF16)) for j in range(8)]
    E2 = [E(nc.sbuf_tensor(f"E2_{j}", [128, TT_], BF16)) for j in range(8)]
    E4h = [E(nc.sbuf_tensor(f"E4h{h}", [64, NCH], F32)) for h in range(16)]
    r_t = [E(nc.sbuf_tensor(f"rt{h}", [64, TT_], BF16)) for h in range(16)]
    r_raw = [E(nc.sbuf_tensor(f"rr{h}", [64, TT_], BF16)) for h in range(16)]
    k_t = [E(nc.sbuf_tensor(f"kt{h}", [64, TT_], BF16)) for h in range(16)]
    ku = [E(nc.sbuf_tensor(f"kuv{h}", [64, TT_], BF16)) for h in range(16)]
    kh_tmp = [E(nc.sbuf_tensor(f"khtmp{i}", [64, 64], BF16)) for i in range(2)]
    khT = [E(nc.sbuf_tensor(f"khT{c}", [64, 1024], BF16)) for c in range(NCH)]
    v_tm = [E(nc.sbuf_tensor(f"vtm{c}", [64, 1024], BF16)) for c in range(NCH)]
    g_tm = [E(nc.sbuf_tensor(f"gtm{c}", [64, 1024], BF16)) for c in range(NCH)]
    S_f = [E(nc.sbuf_tensor(f"Sf{h}", [64, 64], F32)) for h in range(16)]
    S_b = [E(nc.sbuf_tensor(f"Sb{h}", [64, 64], BF16)) for h in range(16)]
    for h in range(16):
        memset(em, "vector", S_f[h][:], 0.0, wkey=S_f[h])
        memset(em, "vector", S_b[h][:], 0.0, wkey=S_b[h])
    At = [E(nc.sbuf_tensor(f"At{i}", [64, 64], BF16)) for i in range(2)]
    At2 = [E(nc.sbuf_tensor(f"At2{i}", [64, 64], F32)) for i in range(2)]
    y_sb = E(nc.sbuf_tensor("ysb", [64, 1024], F32))
    gn_sq = E(nc.sbuf_tensor("gnsq", [64, 1024], F32))
    gn_mu = E(nc.sbuf_tensor("gnmu", [64, 16], F32))
    gn_s2 = E(nc.sbuf_tensor("gns2", [64, 16], F32))
    gn_rs = E(nc.sbuf_tensor("gnrs", [64, 16], F32))
    y2 = [E(nc.sbuf_tensor(f"y2_{c}", [64, 1024], BF16)) for c in range(NCH)]
    y_fm = [E(nc.sbuf_tensor(f"yfm{j}", [128, TT_], BF16)) for j in range(8)]
    wbuf = [E(nc.sbuf_tensor(f"p1wb{i}", [128, 256], BF16)) for i in range(4)]
    wbig = [E(nc.sbuf_tensor(f"p1wB{i}", [128, 512], BF16)) for i in range(4)]

    def proj_std(wname, consumer):
        for pj in range(4):
            for kk in range(8):
                wb = wbuf[kk % 4]
                em.dma("gpsimd", out=wb[:],
                       in_=inp[wname][128 * kk:128 * (kk + 1), 256 * pj:256 * (pj + 1)],
                       reads=[wb], writes=[wb])
                mm(em, P[2][:, 0:TT_], wb[:, 0:128], u_p[kk][:],
                   start=(kk == 0), stop=(kk == 7))
                mm(em, P[3][:, 0:TT_], wb[:, 128:256], u_p[kk][:],
                   start=(kk == 0), stop=(kk == 7))
            consumer(2 * pj, P[2])
            consumer(2 * pj + 1, P[3])

    def proj_flip(wname, src, consumer):
        for nh in range(2):
            for kk in range(8):
                wb = wbig[kk % 4]
                em.dma("gpsimd", out=wb[:],
                       in_=inp[wname][128 * kk:128 * (kk + 1), 512 * nh:512 * (nh + 1)],
                       reads=[wb], writes=[wb])
                for s in range(NSUB):
                    mm(em, P[2 + s][:, 0:512], src[kk][:, 128 * s:128 * (s + 1)], wb[:],
                       start=(kk == 0), stop=(kk == 7))
            for s in range(NSUB):
                consumer(s, nh, P[2 + s])

    for it in range(NT):
        t0 = it * TT_
        if ctx.fused:
            for j in range(8):
                copy(em, "vector", xt[j][:], ctx.X[j][:, t0:t0 + TT_], wkey=xt[j])
        else:
            for j in range(8):
                em.dma("sync", out=xt[j][:], in_=ctx.x_in[128 * j:128 * (j + 1), t0:t0 + TT_],
                       reads=[xt[j]], writes=[xt[j]])
        for j in range(8):
            copy(em, "vector", xn_bf[j][:], xt[j][:], wkey=xn_bf[j])
            act(em, sq2[j % 2][:], xn_bf[j][:], AF.Square, wkey=sq2[j % 2])
            mm(em, P[4][0:1, 0:TT_], ctx.ones_col_bf[:], xn_bf[j][:],
               start=(j == 0), stop=(j == 7))
            mm(em, P[5][0:1, 0:TT_], ctx.ones_col_bf[:], sq2[j % 2][:],
               start=(j == 0), stop=(j == 7))
        mean, msq, var, rstd = lnrow
        ts(em, "vector", mean[:], P[4][0:1, 0:TT_], 1.0 / 1024, None, ALU.mult)
        ts(em, "vector", msq[:], P[5][0:1, 0:TT_], 1.0 / 1024, None, ALU.mult)
        tt(em, "vector", var[:], mean[:], mean[:], ALU.mult)
        tt(em, "vector", var[:], msq[:], var[:], ALU.subtract)
        act(em, rstd[:], var[:], AF.Sqrt, bias=ctx.eps5[0:1, 0:1], wkey=rstd)
        recip(em, rstd[:], rstd[:])
        mm(em, P[0][:, 0:TT_], ctx.ones_f[:], mean[:])
        mm(em, P[1][:, 0:TT_], ctx.ones_f[:], rstd[:])
        for j in range(8):
            tf = tmpf[j % 2]
            tt(em, "vector", tf[:], xt[j][:], P[0][:, 0:TT_], ALU.subtract, wkey=tf)
            tt(em, "vector", xn_bf[j][:], tf[:], P[1][:, 0:TT_], ALU.mult)
        dbg(em, ctx, "xn0", xn_bf[0][:], 128, TT_)
        for j in range(8):
            tt(em, "vector", sx_bf[j][:, 1:TT_], xn_bf[j][:, 0:TT_ - 1],
               xn_bf[j][:, 1:TT_], ALU.subtract, wkey=sx_bf[j])
            tt(em, "vector", sx_bf[j][:, 0:1], xn_last[:, j:j + 1],
               xn_bf[j][:, 0:1], ALU.subtract, wkey=sx_bf[j])
            copy(em, "vector", xn_last[:, j:j + 1], xn_bf[j][:, TT_ - 1:TT_],
                 wkey=xn_last)
        for j in range(8):
            stt(em, "vector", u_p[j][:],
                sx_bf[j][:], pvcol(IX, j), xn_bf[j][:], ALU.mult, ALU.add)
        for j in range(8):
            mm(em, P[2][0:96, 0:TT_], tm_w1[j][:, 0:96], u_p[j][:],
               start=(j == 0), stop=(j == 7))
            mm(em, P[3][0:64, 0:TT_], tm_w1[j][:, 96:160], u_p[j][:],
               start=(j == 0), stop=(j == 7))
        for f in range(3):
            act(em, tm_f[f][:], P[2][32 * f:32 * (f + 1), 0:TT_], AF.Tanh,
                wkey=tm_f[f])
        for f in range(2):
            act(em, tm_f[3 + f][:], P[3][32 * f:32 * (f + 1), 0:TT_], AF.Tanh,
                wkey=tm_f[3 + f])

        def emit_path(f_idx, maa_idx, dest):
            for j in range(8):
                pm = P[j % 2]
                mm(em, pm[:, 0:TT_], tm_w2[f_idx][:, 128 * j:128 * (j + 1)],
                   tm_f[f_idx][:])
                tf = tmpf[j % 2]
                ts(em, "vector", tf[:], pm[:, 0:TT_], pvcol(maa_idx, j), None,
                   ALU.add, wkey=tf)
                tt(em, "vector", tf[:], tf[:], sx_bf[j][:], ALU.mult, wkey=tf)
                tt(em, "vector", dest[j][:], tf[:], xn_bf[j][:], ALU.add)

        dbg(em, ctx, "sx0", sx_bf[0][:], 128, TT_)
        dbg(em, ctx, "tmf0", tm_f[0][:], 32, TT_)
        dbg(em, ctx, "tmf4", tm_f[4][:], 32, TT_)
        emit_path(0, IW, u_p)
        dbg(em, ctx, "uw0", u_p[0][:], 128, TT_)
        for j in range(8):
            mm(em, P[4][0:64, 0:TT_], td_w1[j][:, 0:64], u_p[j][:],
               start=(j == 0), stop=(j == 7))
        act(em, tdh[:], P[4][0:64, 0:TT_], AF.Tanh, wkey=tdh)
        for j in range(8):
            pm = P[j % 2]
            mm(em, pm[:, 0:TT_], td_w2[:, 128 * j:128 * (j + 1)], tdh[:])
            act(em, exp_w[:], pm[:, 0:TT_], AF.Exp, bias=pvcol(ITD, j), wkey=exp_w)
            ts(em, "vector", negew[:], exp_w[:], -1.0, None, ALU.mult, wkey=negew)
            scan(em, c_rel[j][:], negew[:], ctx.zeros[:, 0:TT_], 0.0,
                 ALU.add, ALU.add)
            for c in range(NCH - 1, 0, -1):
                cs = c * L
                ts(em, "vector", c_rel[j][:, cs:cs + L], c_rel[j][:, cs:cs + L],
                   c_rel[j][:, cs - 1:cs], None, ALU.subtract, wkey=c_rel[j])
            tf = tmpf[j % 2]
            tt(em, "vector", tf[:], c_rel[j][:], exp_w[:], ALU.add, wkey=tf)
            act(em, E1[j][:], tf[:], AF.Exp, wkey=E1[j])
            ts(em, "vector", tf[:], c_rel[j][:], -1.0, 70.0, ALU.mult, ALU.min,
               wkey=tf)
            act(em, E2[j][:], tf[:], AF.Exp, wkey=E2[j])
            em.op("scalar", lambda e, j=j: e.activation(
                E4h[2 * j][:, 0:NCH], c_rel[j][0:64, L - 1:TT_:L], AF.Exp),
                reads=[c_rel[j]], writes=[E4h[2 * j]])
            em.op("scalar", lambda e, j=j: e.activation(
                E4h[2 * j + 1][:, 0:NCH], c_rel[j][64:128, L - 1:TT_:L], AF.Exp),
                reads=[c_rel[j]], writes=[E4h[2 * j + 1]])

        dbg(em, ctx, "expw7", exp_w[:], 128, TT_)
        dbg(em, ctx, "crel0", c_rel[0][:], 128, TT_)
        dbg(em, ctx, "E1_0", E1[0][:], 128, TT_)
        dbg(em, ctx, "E2_0", E2[0][:], 128, TT_)
        dbg(em, ctx, "E4_0", E4h[0][:], 64, NCH)
        emit_path(1, IK, u_p)

        def k_consumer(j, pp):
            for half in range(2):
                h = 2 * j + half
                sl = slice(64 * half, 64 * (half + 1))
                tt(em, "vector", k_t[h][:], pp[sl, 0:TT_], E1[j][sl, :] if False
                   else E2[j][sl, :], ALU.mult)
                ts(em, "vector", ku[h][:], pp[sl, 0:TT_], pvcol(ITF, j, sl), None,
                   ALU.mult)
        proj_std("Wk", k_consumer)
        for c in range(NCH):
            for h in range(16):
                j, half = h // 2, h % 2
                kt_ = kh_tmp[h % 2]
                ts(em, "vector", kt_[:], k_t[h][:, c * L:(c + 1) * L],
                   E4h[h][:, c:c + 1], None, ALU.mult, wkey=kt_)
                transp(em, P7[0:64, 64 * h:64 * (h + 1)], kt_[:], ctx.eye[0:64, 0:64])
            copy(em, "vector", khT[c][:], P7[0:64, 0:1024], wkey=khT[c])

        dbg(em, ctx, "kt0", k_t[0][:], 64, TT_)
        dbg(em, ctx, "ku0", ku[0][:], 64, TT_)
        dbg(em, ctx, "khT0", khT[0][:], 64, 1024)
        emit_path(2, IV, u_p)

        def v_consumer(s, nh, pp):
            for half in range(2):
                c = 2 * s + half
                copy(em, "vector", v_tm[c][:, 512 * nh:512 * (nh + 1)],
                     pp[64 * half:64 * (half + 1), 0:512], wkey=v_tm[c])
        proj_flip("Wv", u_p, v_consumer)

        dbg(em, ctx, "vtm0", v_tm[0][:], 64, 1024)
        emit_path(3, IR, u_p)

        def r_consumer(j, pp):
            for half in range(2):
                h = 2 * j + half
                sl = slice(64 * half, 64 * (half + 1))
                tt(em, "vector", r_t[h][:], pp[sl, 0:TT_], E1[j][sl, :], ALU.mult)
                copy(em, "vector", r_raw[h][:], pp[sl, 0:TT_])
        proj_std("Wr", r_consumer)

        dbg(em, ctx, "rt0", r_t[0][:], 64, TT_)
        dbg(em, ctx, "rr0", r_raw[0][:], 64, TT_)
        emit_path(4, IG, u_g)

        def g_consumer(s, nh, pp):
            for half in range(2):
                c = 2 * s + half
                act(em, g_tm[c][:, 512 * nh:512 * (nh + 1)],
                    pp[64 * half:64 * (half + 1), 0:512], AF.Silu, wkey=g_tm[c])
        proj_flip("Wg", u_g, g_consumer)

        pA, pA2, pY, pS = P[0], P[1], P[2], P[3]
        for c in range(NCH):
            cs = c * L
            for h in range(16):
                j, half = h // 2, h % 2
                ia = h % 2
                mm(em, pA[0:64, 0:64], k_t[h][:, cs:cs + L], r_t[h][:, cs:cs + L])
                mm(em, pA2[0:64, 0:64], ku[h][:, cs:cs + L], r_raw[h][:, cs:cs + L])
                tt(em, "vector", At[ia][:], pA[0:64, 0:64], maskU[:], ALU.mult)
                tt(em, "vector", At2[ia][:], pA2[0:64, 0:64], ctx.eye[0:64, 0:64],
                   ALU.mult)
                tt(em, "vector", At[ia][:], At[ia][:], At2[ia][:], ALU.add)
                mm(em, pY[0:64, 0:64], At[ia][:], v_tm[c][:, 64 * h:64 * (h + 1)],
                   start=True, stop=False)
                mm(em, pY[0:64, 0:64], r_t[h][:, cs:cs + L], S_b[h][:],
                   start=False, stop=True)
                act(em, y_sb[:, 64 * h:64 * (h + 1)], pY[0:64, 0:64], AF.Identity,
                    scale=0.125, wkey=y_sb)
                mm(em, pS[0:64, 0:64], khT[c][:, 64 * h:64 * (h + 1)],
                   v_tm[c][:, 64 * h:64 * (h + 1)])
                stt(em, "vector", S_f[h][:], S_f[h][:],
                    E4h[h][:, c:c + 1], pS[0:64, 0:64],
                    ALU.mult, ALU.add)
                copy(em, "vector", S_b[h][:], S_f[h][:])
            act(em, gn_sq[:], y_sb[:], AF.Square, wkey=gn_sq)
            for h in range(16):
                reduce(em, gn_mu[:, h:h + 1], y_sb[:, 64 * h:64 * (h + 1)], ALU.add,
                       wkey=gn_mu)
                reduce(em, gn_s2[:, h:h + 1], gn_sq[:, 64 * h:64 * (h + 1)], ALU.add,
                       wkey=gn_s2)
            ts(em, "vector", gn_mu[:], gn_mu[:], 1.0 / 64, None, ALU.mult)
            ts(em, "vector", gn_s2[:], gn_s2[:], 1.0 / 64, None, ALU.mult)
            tt(em, "vector", gn_rs[:], gn_mu[:], gn_mu[:], ALU.mult)
            tt(em, "vector", gn_rs[:], gn_s2[:], gn_rs[:], ALU.subtract)
            act(em, gn_rs[:], gn_rs[:], AF.Sqrt, bias=ctx.eps5[0:64, 0:1], wkey=gn_rs)
            recip(em, gn_rs[:], gn_rs[:])
            for h in range(16):
                ts(em, "vector", y_sb[:, 64 * h:64 * (h + 1)],
                   y_sb[:, 64 * h:64 * (h + 1)], gn_mu[:, h:h + 1],
                   gn_rs[:, h:h + 1], ALU.subtract, ALU.mult, wkey=y_sb)
            if c == 0:
                dbg(em, ctx, "ysb0", y_sb[:], 64, 1024)
            tt(em, "vector", y2[c][:], y_sb[:], lnw_b[:], ALU.mult)
            tt(em, "vector", y2[c][:], y2[c][:], lnb_b[:], ALU.add)
            tt(em, "vector", y2[c][:], y2[c][:], g_tm[c][:], ALU.mult)

        dbg(em, ctx, "gtm0", g_tm[0][:], 64, 1024)
        dbg(em, ctx, "y2_0", y2[0][:], 64, 1024)
        for j in range(8):
            for c in range(NCH):
                transp(em, P7[0:128, 64 * c:64 * (c + 1)],
                       y2[c][:, 128 * j:128 * (j + 1)], ctx.eye[0:64, 0:64])
            copy(em, "vector", y_fm[j][:], P7[0:128, 0:TT_], wkey=y_fm[j])
        for pj in range(4):
            for kk in range(8):
                wb = wbuf[kk % 4]
                em.dma("gpsimd", out=wb[:],
                       in_=inp["Wo"][128 * kk:128 * (kk + 1), 256 * pj:256 * (pj + 1)],
                       reads=[wb], writes=[wb])
                mm(em, P[4][:, 0:TT_], wb[:, 0:128], y_fm[kk][:],
                   start=(kk == 0), stop=(kk == 7))
                mm(em, P[5][:, 0:TT_], wb[:, 128:256], y_fm[kk][:],
                   start=(kk == 0), stop=(kk == 7))
            for half in range(2):
                j = 2 * pj + half
                tt(em, "vector", xt[j][:], P[4 + half][:, 0:TT_], xt[j][:], ALU.add,
                   wkey=xt[j])
        if ctx.fused:
            for j in range(8):
                copy(em, "vector", ctx.X[j][:, t0:t0 + TT_], xt[j][:], wkey=ctx.X[j])
        else:
            for j in range(8):
                em.dma("gpsimd", out=ctx.y_out[128 * j:128 * (j + 1), t0:t0 + TT_],
                       in_=xt[j][:], reads=[xt[j]], writes=[f"OUTY{j}"])

    em.barrier()


def build_phase2(nc, em, ctx, E, inp):
    T = ctx.T
    NTW = T // 512
    NE = int(os.environ.get('MOENE', '160'))

    pg = [E(nc.psum_tensor(f"p2g{i}", [128, 512], F32)) for i in range(4)]
    pz = [E(nc.psum_tensor(f"p2z{i}", [128, 512], F32)) for i in range(2)]
    pxtra = E(nc.psum_tensor("p2xtra", [128, 512], F32))
    prow = pxtra
    pbc = pxtra


    eye32_d = nc.inline_tensor(np.eye(128).astype(np.float32), "eye_f32_d")
    eye_f = E(nc.sbuf_tensor("eye_f", [128, 128], F32))
    em.dma("sync", out=eye_f[:], in_=eye32_d.ap(), writes=[eye_f])

    rw = []
    for c in range(8):
        t = E(nc.sbuf_tensor(f"p2rw{c}", [128, 160], F32))
        em.dma("sync", out=t[:], in_=inp["router_wt"][128 * c:128 * (c + 1), :],
               writes=[t])
        rw.append(t)
    ebias = E(nc.sbuf_tensor("p2eb", [128, 2], F32))
    em.dma("sync", out=ebias[:], in_=inp["ebias"], writes=[ebias])

    xn_bf = [E(nc.sbuf_tensor(f"p2xn{j}", [128, T], BF16)) for j in range(8)]
    _sp = getattr(ctx, "moe_sparse", False)
    RNK = [E(nc.sbuf_tensor(f"p2rnk{i}", [128, 160], BF16)) for i in range(T // 128)] if _sp else None
    CTM = [E(nc.sbuf_tensor(f"p2ctm{i}", [128, 160], BF16)) for i in range(T // 128)] if _sp else None
    comb_lo = E(nc.sbuf_tensor("p2clo", [128, T], BF16))
    comb_hi = E(nc.sbuf_tensor("p2chi", [32, T], BF16))
    mcol = E(nc.sbuf_tensor("p2mcol", [128, 1], F32))
    tsum = E(nc.sbuf_tensor("p2tsum", [128, 1], F32))
    ctm = E(nc.sbuf_tensor("p2ctm", [128, 160], BF16))
    sq2 = [E(nc.sbuf_tensor(f"p2sq{i}", [128, 512], BF16)) for i in range(2)]
    tmpf = [E(nc.sbuf_tensor(f"p2tmp{i}", [128, 512], F32)) for i in range(2)]
    lnrow = [E(nc.sbuf_tensor(f"p2row{i}", [1, 512], F32)) for i in range(4)]
    h_bf = [E(nc.sbuf_tensor(f"p2h{i}", [128, 512], BF16)) for i in range(4)]
    xnf = [E(nc.sbuf_tensor(f"p2xnf{i}", [128, 512], F32)) for i in range(2)]
    rowb = E(nc.sbuf_tensor("p2rowb", [1, 512], BF16))
    NSET = 1 if getattr(ctx, "moe_sparse", False) else 2
    wg_bf = [[E(nc.sbuf_tensor(f"p2wg{b}_{c}", [128, 512], BF16)) for c in range(8)]
             for b in range(NSET)]
    wd_bf = [[E(nc.sbuf_tensor(f"p2wd{b}_{c}", [128, 1024], BF16)) for c in range(4)]
             for b in range(NSET)]
    from contextlib import ExitStack as _ES
    _pre = _ES()
    E2 = _pre.enter_context
    pbf = E2(nc.psum_tensor("p2bf", [128, 1024], BF16))
    probs_lo = E2(nc.sbuf_tensor("p2plo", [128, T], F32))
    probs_hi = E2(nc.sbuf_tensor("p2phi", [32, T], F32))
    ptm = [E2(nc.sbuf_tensor(f"p2ptm{i}", [128, 160], F32)) for i in range(T // 128)]
    work = E2(nc.sbuf_tensor("p2work", [128, 160], F32))
    sel = E2(nc.sbuf_tensor("p2sel", [128, 160], F32))
    msk = E2(nc.sbuf_tensor("p2msk", [128, 160], F32))

    for w in range(NTW):
        t0 = w * 512
        for j in range(8):
            copy(em, "vector", xn_bf[j][:, t0:t0 + 512], ctx.X[j][:, t0:t0 + 512],
                 wkey=xn_bf[j])
            act(em, sq2[j % 2][:], xn_bf[j][:, t0:t0 + 512], AF.Square,
                wkey=sq2[j % 2])
            mm(em, pg[0][0:1, 0:512], ctx.ones_col_bf[:], xn_bf[j][:, t0:t0 + 512],
               start=(j == 0), stop=(j == 7))
            mm(em, pg[1][0:1, 0:512], ctx.ones_col_bf[:], sq2[j % 2][:],
               start=(j == 0), stop=(j == 7))
        mean, msq, var, rstd = lnrow
        ts(em, "vector", mean[:], pg[0][0:1, 0:512], 1.0 / 1024, None, ALU.mult)
        ts(em, "vector", msq[:], pg[1][0:1, 0:512], 1.0 / 1024, None, ALU.mult)
        tt(em, "vector", var[:], mean[:], mean[:], ALU.mult)
        tt(em, "vector", var[:], msq[:], var[:], ALU.subtract)
        act(em, rstd[:], var[:], AF.Sqrt, bias=ctx.eps5[0:1, 0:1], wkey=rstd)
        recip(em, rstd[:], rstd[:])
        mm(em, pg[2][:, 0:512], ctx.ones_f[:], mean[:])
        mm(em, pg[3][:, 0:512], ctx.ones_f[:], rstd[:])
        for j in range(8):
            tf = tmpf[j % 2]
            xf = xnf[j % 2]
            tt(em, "vector", tf[:], ctx.X[j][:, t0:t0 + 512], pg[2][:, 0:512],
               ALU.subtract, wkey=tf)
            tt(em, "vector", xf[:], tf[:], pg[3][:, 0:512], ALU.mult, wkey=xf)
            copy(em, "vector", xn_bf[j][:, t0:t0 + 512], xf[:], wkey=xn_bf[j])
            mm(em, pg[0][:, 0:512], rw[j][:, 0:128], xf[:],
               start=(j == 0), stop=(j == 7))
            mm(em, pg[1][0:32, 0:512], rw[j][:, 128:160], xf[:],
               start=(j == 0), stop=(j == 7))
        act(em, probs_lo[:, t0:t0 + 512], pg[0][:, 0:512], AF.Sigmoid,
            bias=ebias[:, 0:1], wkey=probs_lo)
        act(em, probs_hi[:, t0:t0 + 512], pg[1][0:32, 0:512], AF.Sigmoid,
            bias=ebias[0:32, 1:2], wkey=probs_hi)

    for i in range(T // 128):
        c0 = i * 128
        transp(em, pg[0][:, 0:128], probs_lo[:, c0:c0 + 128], eye_f[:])
        transp(em, pg[0][:, 128:160], probs_hi[:, c0:c0 + 128], eye_f[0:32, 0:32])
        copy(em, "vector", ptm[i][:, 0:128], pg[0][0:128, 0:128], wkey=ptm[i])
        copy(em, "vector", ptm[i][:, 128:160], pg[0][0:128, 128:160], wkey=ptm[i])
    for i in range(T // 128):
        copy(em, "vector", work[:], ptm[i][:], wkey=work)
        memset(em, "vector", sel[:], 0.0, wkey=sel)
        if _sp:
            memset(em, "vector", RNK[i][:], 0.0, wkey=RNK[i])
        for it8 in range(8):
            reduce(em, mcol[:], work[:], ALU.max, wkey=mcol)
            ts(em, "vector", msk[:], work[:], mcol[:], None, ALU.is_ge, wkey=msk)
            tt(em, "vector", sel[:], sel[:], msk[:], ALU.add)
            if _sp:
                stt(em, "vector", RNK[i][:], msk[:], float(it8 + 1), RNK[i][:],
                    ALU.mult, ALU.add, wkey=RNK[i])
            tt(em, "vector", msk[:], msk[:], work[:], ALU.mult)
            tt(em, "vector", work[:], work[:], msk[:], ALU.subtract)
        tt(em, "vector", sel[:], sel[:], ptm[i][:], ALU.mult)
        reduce(em, tsum[:], sel[:], ALU.add, wkey=tsum)
        ts(em, "vector", tsum[:], tsum[:], 1e-9, None, ALU.add, wkey=tsum)
        recip(em, tsum[:], tsum[:])
        if _sp:
            ts(em, "vector", CTM[i][:], sel[:], tsum[:], 2.5, ALU.mult, ALU.mult,
               wkey=CTM[i])
            copy(em, "vector", ctm[:], CTM[i][:], wkey=ctm)
        else:
            ts(em, "vector", ctm[:], sel[:], tsum[:], 2.5, ALU.mult, ALU.mult,
               wkey=ctm)
        transp(em, pbf[0:128, 0:128], ctm[:, 0:128], ctx.eye[:])
        transp(em, pbf[0:32, 128:256], ctm[:, 128:160], ctx.eye[:])
        copy(em, "vector", comb_lo[:, i * 128:(i + 1) * 128], pbf[0:128, 0:128],
             wkey=comb_lo)
        copy(em, "vector", comb_hi[:, i * 128:(i + 1) * 128], pbf[0:32, 128:256],
             wkey=comb_hi)

    _pre.close()
    em.barrier()
    h16 = [E(nc.sbuf_tensor(f"p2h16_{i}", [128, 512], BF16)) for i in range(8)]
    for c in range(8):
        em.dma("gpsimd", out=wg_bf[0][c][:],
               in_=inp["wgs_t"][128 * c:128 * (c + 1), :],
               reads=[wg_bf[0][c]], writes=[wg_bf[0][c]])
    for c in range(4):
        em.dma("gpsimd", out=wd_bf[0][c][:],
               in_=inp["wds_t"][128 * c:128 * (c + 1), :],
               reads=[wd_bf[0][c]], writes=[wd_bf[0][c]])
    for w in range(NTW):
        t0 = w * 512
        for mi in range(4):
            for kk in range(8):
                mm(em, pg[mi][:, 0:512], wg_bf[0][kk][:, 128 * mi:128 * (mi + 1)],
                   xn_bf[kk][:, t0:t0 + 512], start=(kk == 0), stop=(kk == 7))
            act(em, h_bf[mi][:], pg[mi][:, 0:512], AF.Silu, wkey=h_bf[mi])
        for j in range(8):
            for kk in range(4):
                mm(em, pz[j % 2][:, 0:512], wd_bf[0][kk][:, 128 * j:128 * (j + 1)],
                   h_bf[kk][:], start=(kk == 0), stop=(kk == 3))
            tt(em, "vector", ctx.X[j][:, t0:t0 + 512], pz[j % 2][:, 0:512],
               ctx.X[j][:, t0:t0 + 512], ALU.add, wkey=ctx.X[j])

    if getattr(ctx, "moe_sparse", False):
        _moe_sparse(nc, em, ctx, E, inp, xn_bf, CTM, RNK, pg, pz, prow, pbf, eye_f,
                    tmpf, lnrow)
        em.barrier()
        return

    def load_expert(e, b):
        for c in range(8):
            em.dma("gpsimd", out=wg_bf[b][c][:],
                   in_=inp["wg_t"][e, 128 * c:128 * (c + 1), :],
                   reads=[wg_bf[b][c]], writes=[wg_bf[b][c]])
        for c in range(4):
            em.dma("gpsimd", out=wd_bf[b][c][:],
                   in_=inp["wd_t"][e, 128 * c:128 * (c + 1), :],
                   reads=[wd_bf[b][c]], writes=[wd_bf[b][c]])

    cwr = [E(nc.sbuf_tensor(f"p2cwr{i}", [128, 512], BF16)) for i in range(4)]
    rowr = [E(nc.sbuf_tensor(f"p2rowr{i}", [1, 512], BF16)) for i in range(2)]
    UN = [(e, w) for e in range(NE) for w in range(NTW)]

    def cw_prep(k):
        e, w = UN[k]
        t0 = w * 512
        if e < 128:
            mm(em, prow[0:1, 0:512], ctx.eye[:, e:e + 1], comb_lo[:, t0:t0 + 512])
        else:
            mm(em, prow[0:1, 0:512], ctx.eye[0:32, e - 128:e - 127],
               comb_hi[:, t0:t0 + 512])
        scopy(em, rowr[k % 2][:], prow[0:1, 0:512], wkey=rowr[k % 2])
        mm(em, pbc[:, 0:512], ctx.ones_row_bf[:], rowr[k % 2][:])
        scopy(em, cwr[k % 4][:], pbc[:, 0:512], wkey=cwr[k % 4])

    def emit_gate(k):
        e, w = UN[k]
        b = e % 2
        t0 = w * 512
        for mi in range(4):
            for kk in range(8):
                mm(em, pg[mi][:, 0:512], wg_bf[b][kk][:, 128 * mi:128 * (mi + 1)],
                   xn_bf[kk][:, t0:t0 + 512], start=(kk == 0), stop=(kk == 7))
        hp = (k % 2) * 4
        for mi in range(4):
            act(em, h16[hp + mi][:], pg[mi][:, 0:512], AF.Silu, wkey=h16[hp + mi])
            tt(em, "vector", h16[hp + mi][:], h16[hp + mi][:], cwr[k % 4][:],
               ALU.mult)

    def emit_down(k):
        e, w = UN[k]
        b = e % 2
        t0 = w * 512
        hp = (k % 2) * 4
        for j in range(8):
            for kk in range(4):
                mm(em, pz[j % 2][:, 0:512], wd_bf[b][kk][:, 128 * j:128 * (j + 1)],
                   h16[hp + kk][:], start=(kk == 0), stop=(kk == 3))
            tt(em, "vector", ctx.X[j][:, t0:t0 + 512], pz[j % 2][:, 0:512],
               ctx.X[j][:, t0:t0 + 512], ALU.add, wkey=ctx.X[j])

    load_expert(0, 0)
    cw_prep(0)
    if len(UN) > 1:
        cw_prep(1)
    for k in range(len(UN)):
        e, w = UN[k]
        emit_gate(k)
        if k + 2 < len(UN):
            cw_prep(k + 2)
        if k > 0:
            emit_down(k - 1)
        if w == 0 and e + 1 < NE:
            load_expert(e + 1, 1 - (e % 2))
    emit_down(len(UN) - 1)
def _moe_sparse(nc, em, ctx, E, inp, xn_bf, CTM, RNK, pg, pz, prow, pbf, eye_f,
                tmpf, lnrow):
    from concourse.expressions import RuntimeValue
    from concourse.bass_core_rust import IndirectOffsetOnAxis
    T = ctx.T
    NI = T // 128
    NB = T * 8 // 128 + 160
    BPC = (NB + 127) // 128
    U32 = mybir.dt.uint32
    I32 = mybir.dt.int32

    xn_tm_d = nc.dram_tensor("sp_xntm", [T, 1024], BF16, kind="Internal").ap()
    TAB = nc.dram_tensor("sp_tab", [NB * 128, 4], F32, kind="Internal").ap()
    EBD = nc.dram_tensor("sp_ebd", [BPC * 128], I32, kind="Internal").ap()
    Zacc = nc.dram_tensor("sp_zacc", [T + 128, 1024], F32, kind="Internal").ap()

    tabinit = np.zeros((NB * 128, 4), np.float32)
    tabinit[:, 1] = T + (np.arange(NB * 128) % 128)
    tabinit_d = nc.inline_tensor(tabinit, "sp_tabinit_d")
    tri_d = nc.inline_tensor(
        np.tril(np.ones((128, 128), np.float32), -1).T.astype(BF16NP), "sp_tri_d")
    iot_d = nc.inline_tensor(np.arange(128, dtype=np.float32)[:, None], "sp_iot_d")
    bcol_np = 128.0 * (np.arange(128)[:, None] + 128 * np.arange(BPC)[None, :])
    bcol_d = nc.inline_tensor(bcol_np.astype(np.float32), "sp_bcol_d")

    TRI = E(nc.sbuf_tensor("spTRI", [128, 128], BF16))
    em.dma("sync", out=TRI[:], in_=tri_d.ap(), writes=[TRI])
    IOT = E(nc.sbuf_tensor("spIOT", [128, 1], F32))
    em.dma("sync", out=IOT[:], in_=iot_d.ap(), writes=[IOT])
    BCOL = E(nc.sbuf_tensor("spBCOL", [128, BPC], F32))
    em.dma("sync", out=BCOL[:], in_=bcol_d.ap(), writes=[BCOL])
    em.dma("sync", out=TAB, in_=tabinit_d.ap(), writes=["TAB"])

    selb = E(nc.sbuf_tensor("spselb", [128, 160], BF16))
    cnts = E(nc.sbuf_tensor("spcnts", [1, 160 * NI], F32))
    offs = E(nc.sbuf_tensor("spoffs", [1, 160 * NI], F32))
    rowt = [E(nc.sbuf_tensor(f"sprow{i}", [1, 160], F32)) for i in range(4)]
    possb = E(nc.sbuf_tensor("sppos", [128, 160], F32))
    Dsb = E(nc.sbuf_tensor("spD", [128, 160], F32))
    mskb = E(nc.sbuf_tensor("spmsk", [128, 160], F32))
    t160 = E(nc.sbuf_tensor("spt160", [128, 160], F32))
    tokf = E(nc.sbuf_tensor("sptok", [128, 1], F32))
    fcol = E(nc.sbuf_tensor("spfc", [128, 1], F32))
    cwcol = E(nc.sbuf_tensor("spcw", [128, 1], F32))
    val4 = E(nc.sbuf_tensor("spval", [128, 4], F32))
    ufd = E(nc.sbuf_tensor("spufd", [128, 1], U32))
    ebf = E(nc.sbuf_tensor("spebf", [128, BPC], F32))
    ebi = E(nc.sbuf_tensor("spebi", [128, BPC], I32))
    ebrow = E(nc.sbuf_tensor("spebr", [1, BPC * 128], I32))
    xtm_st = [E(nc.sbuf_tensor(f"spxst{i}", [128, 1024], BF16)) for i in range(2)]
    wg_sb = [E(nc.sbuf_tensor(f"spwg{i}", [128, 4096], BF16)) for i in range(2)]
    wd_sb = [E(nc.sbuf_tensor(f"spwd{i}", [128, 4096], BF16)) for i in range(2)]
    xg_tm = [E(nc.sbuf_tensor(f"spxg{i}", [128, 1024], BF16)) for i in range(2)]
    xg_fm = [E(nc.sbuf_tensor(f"spxf{i}", [128, 1024], BF16)) for i in range(2)]
    hb = [E(nc.sbuf_tensor(f"sphb{i}", [128, 512], BF16)) for i in range(2)]
    z_bf = [E(nc.sbuf_tensor(f"spzb{i}", [128, 1024], BF16)) for i in range(2)]
    tab4 = [E(nc.sbuf_tensor(f"sptab{i}", [128, 4], F32)) for i in range(2)]
    utok = [E(nc.sbuf_tensor(f"sput{i}", [128, 1], U32)) for i in range(2)]
    uzd = [E(nc.sbuf_tensor(f"spuz{i}", [128, 1], U32)) for i in range(2)]

    wgflat = inp["wg_t"].rearrange("e c i -> (e c) i")
    wdflat = inp["wd_t"].rearrange("e i c -> (e i) c")

    for i in range(NI):
        st = xtm_st[i % 2]
        for j in range(8):
            transp(em, pbf[:, 128 * j:128 * (j + 1)],
                   xn_bf[j][:, 128 * i:128 * (i + 1)], ctx.eye[:])
        copy(em, "vector", st[:], pbf[0:128, 0:1024], wkey=st)
        em.dma("gpsimd", out=xn_tm_d[128 * i:128 * (i + 1), :], in_=st[:],
               reads=[st], writes_accum=["XNTM"])

    for i in range(NI):
        ts(em, "vector", selb[:], RNK[i][:], 0.5, None, ALU.is_ge, wkey=selb)
        mm(em, prow[0:1, 0:160], ctx.ones_col_bf[:], selb[:])
        copy(em, "vector", cnts[0:1, 160 * i:160 * (i + 1)], prow[0:1, 0:160],
             wkey=cnts)
    memset(em, "vector", offs[0:1, 0:160], 0.0, wkey=offs)
    for i in range(1, NI):
        tt(em, "vector", offs[0:1, 160 * i:160 * (i + 1)],
           offs[0:1, 160 * (i - 1):160 * i], cnts[0:1, 160 * (i - 1):160 * i],
           ALU.add, wkey=offs)
    tot, m1, padd, bstart = rowt
    tt(em, "vector", tot[:], offs[0:1, 160 * (NI - 1):160 * NI],
       cnts[0:1, 160 * (NI - 1):160 * NI], ALU.add, wkey=tot)
    ts(em, "vector", m1[:], tot[:], 128.0, None, ALU.mod, wkey=m1)
    ts(em, "vector", m1[:], m1[:], -1.0, 128.0, ALU.mult, ALU.add, wkey=m1)
    ts(em, "vector", m1[:], m1[:], 128.0, None, ALU.mod, wkey=m1)
    tt(em, "vector", padd[:], tot[:], m1[:], ALU.add, wkey=padd)
    scan(em, m1[:], padd[:], ctx.zeros[0:1, 0:160], 0.0, ALU.add, ALU.add)
    memset(em, "vector", bstart[0:1, 0:1], 0.0, wkey=bstart)
    copy(em, "vector", bstart[0:1, 1:160], m1[0:1, 0:159], wkey=bstart)

    mm(em, pg[0][:, 0:160], ctx.ones_f[:], bstart[:])
    for t in range(BPC):
        ts(em, "vector", t160[:], pg[0][:, 0:160], BCOL[:, t:t + 1], None,
           ALU.is_le, wkey=t160)
        reduce(em, ebf[:, t:t + 1], t160[:], ALU.add, wkey=ebf)
    ts(em, "vector", ebf[:], ebf[:], -1.0, None, ALU.add, wkey=ebf)
    copy(em, "vector", ebi[:], ebf[:], wkey=ebi)
    em.dma("gpsimd", out=EBD.rearrange("(t p) -> p t", p=128), in_=ebi[:],
           reads=[ebi], writes=["EBD"], allow_slow_non_contiguous=True)
    em.dma("sync", out=ebrow[0:1, :], in_=EBD[None, :], reads=["EBD"],
           writes=[ebrow])

    for i in range(NI):
        ts(em, "vector", selb[:], RNK[i][:], 0.5, None, ALU.is_ge, wkey=selb)
        mm(em, pg[1][:, 0:160], TRI[:], selb[:])
        copy(em, "vector", possb[:], pg[1][:, 0:160], wkey=possb)
        tt(em, "vector", rowt[1][:], bstart[:], offs[0:1, 160 * i:160 * (i + 1)],
           ALU.add, wkey=rowt[1])
        mm(em, pg[2][:, 0:160], ctx.ones_f[:], rowt[1][:])
        tt(em, "vector", Dsb[:], possb[:], pg[2][:, 0:160], ALU.add, wkey=Dsb)
        ts(em, "vector", tokf[:], IOT[:], float(128 * i), None, ALU.add, wkey=tokf)
        for r in range(8):
            ts(em, "vector", mskb[:], RNK[i][:], float(r + 1), None, ALU.is_equal,
               wkey=mskb)
            tt(em, "vector", t160[:], Dsb[:], mskb[:], ALU.mult, wkey=t160)
            reduce(em, fcol[:], t160[:], ALU.add, wkey=fcol)
            tt(em, "vector", t160[:], CTM[i][:], mskb[:], ALU.mult, wkey=t160)
            reduce(em, cwcol[:], t160[:], ALU.add, wkey=cwcol)
            copy(em, "vector", val4[:, 0:1], tokf[:], wkey=val4)
            copy(em, "vector", val4[:, 1:2], tokf[:], wkey=val4)
            copy(em, "vector", val4[:, 2:3], cwcol[:], wkey=val4)
            copy(em, "vector", ufd[:], fcol[:], wkey=ufd)
            em.op("gpsimd", lambda e: e.indirect_dma_start(
                out=TAB, out_offset=IndirectOffsetOnAxis(ufd[:], 0),
                in_=val4[:], in_offset=None,
                bounds_check=NB * 128 - 1, oob_is_err=False),
                reads=[val4, ufd, "TAB"], writes_accum=["TAB"], dma=True)

    for rr in range((T + 128) // 128):
        for hh in range(2):
            em.dma("gpsimd", out=Zacc[128 * rr:128 * (rr + 1), 512 * hh:512 * (hh + 1)],
                   in_=ctx.zeros[:, 0:512], writes_accum=["Zacc"])

    rgA = nc.gpsimd.alloc_register("sp_rga")
    rgB = nc.gpsimd.alloc_register("sp_rgb")
    for b in range(NB):
        p = b % 2
        ph = pg[p]
        pzh = (pz[0], pz[1]) if p == 0 else (pg[2], pg[3])
        t4 = tab4[p]
        em.dma("sync", out=t4[:], in_=TAB[128 * b:128 * (b + 1), :],
               reads=["TAB", t4], writes=[t4])
        copy(em, "vector", utok[p][:], t4[:, 0:1], wkey=utok[p])
        copy(em, "vector", uzd[p][:], t4[:, 1:2], wkey=uzd[p])
        em.op("gpsimd", lambda e, b=b: e.reg_load(rgA, ebrow[0:1, b:b + 1]),
              reads=[ebrow])
        em.op("gpsimd", lambda e: e.reg_mul(rgA, rgA, 1024))
        em.op("gpsimd", lambda e, w=wg_sb[p]: e.dma_start(
            out=w[:].rearrange("p (c i) -> p c i", c=8),
            in_=wgflat[bass.ds(RuntimeValue(rgA, min_val=0, max_val=159 * 1024), 1024),
                       :].rearrange("(c p) i -> p c i", p=128)),
            reads=[wg_sb[p]], writes=[wg_sb[p]], dma=True)
        em.op("gpsimd", lambda e, b=b: e.reg_load(rgB, ebrow[0:1, b:b + 1]),
              reads=[ebrow])
        em.op("gpsimd", lambda e: e.reg_mul(rgB, rgB, 512))
        em.op("gpsimd", lambda e, w=wd_sb[p]: e.dma_start(
            out=w[:].rearrange("p (c i) -> p c i", c=4),
            in_=wdflat[bass.ds(RuntimeValue(rgB, min_val=0, max_val=159 * 512), 512),
                       :].rearrange("(ci p) c -> p ci c", p=128)),
            reads=[wd_sb[p]], writes=[wd_sb[p]], dma=True)
        em.op("gpsimd", lambda e, p=p: e.indirect_dma_start(
            out=xg_tm[p][:], out_offset=None, in_=xn_tm_d,
            in_offset=IndirectOffsetOnAxis(utok[p][:], 0)),
            reads=["XNTM", utok[p], xg_tm[p]], writes=[xg_tm[p]], dma=True)
        for c in range(8):
            transp(em, pbf[:, 128 * c:128 * (c + 1)],
                   xg_tm[p][:, 128 * c:128 * (c + 1)], ctx.eye[:])
        copy(em, "vector", xg_fm[p][:], pbf[0:128, 0:1024], wkey=xg_fm[p])
        for c in range(8):
            for m in range(4):
                mm(em, ph[:, 128 * m:128 * (m + 1)],
                   wg_sb[p][:, 512 * c + 128 * m:512 * c + 128 * (m + 1)],
                   xg_fm[p][:, 128 * c:128 * (c + 1)],
                   start=(c == 0), stop=(c == 7))
        act(em, hb[p][:], ph[:, 0:512], AF.Silu, wkey=hb[p])
        for h in range(2):
            for ci in range(4):
                mm(em, pzh[h][:, 0:512], hb[p][:, 128 * ci:128 * (ci + 1)],
                   wd_sb[p][:, 1024 * ci + 512 * h:1024 * ci + 512 * (h + 1)],
                   start=(ci == 0), stop=(ci == 3))
            ts(em, "vector", z_bf[p][:, 512 * h:512 * (h + 1)], pzh[h][:, 0:512],
               t4[:, 2:3], None, ALU.mult, wkey=z_bf[p])
        _mode = os.environ.get("SPMODE", "add")
        if _mode == "add":
            em.op("gpsimd", lambda e, p=p: e.indirect_dma_start(
                out=Zacc, out_offset=IndirectOffsetOnAxis(uzd[p][:], 0),
                in_=z_bf[p][:], in_offset=None,
                bounds_check=T + 127, oob_is_err=True, compute_op=ALU.add),
                reads=[z_bf[p], uzd[p], "ZRMW"], writes=["ZRMW"],
                writes_accum=["Zacc"], dma=True)
        else:
            em.op("gpsimd", lambda e, p=p: e.indirect_dma_start(
                out=Zacc, out_offset=IndirectOffsetOnAxis(uzd[p][:], 0),
                in_=z_bf[p][:], in_offset=None,
                bounds_check=T + 127, oob_is_err=False),
                reads=[z_bf[p], uzd[p], "ZRMW"], writes=["ZRMW"],
                writes_accum=["Zacc"], dma=True)

    for i in range(NI):
        st = xtm_st[i % 2]
        em.dma("gpsimd", out=st[:], in_=Zacc[128 * i:128 * (i + 1), :],
               reads=["Zacc", "ZRMW", st], writes=[st])
        for j in range(8):
            transp(em, pbf[:, 128 * j:128 * (j + 1)], st[:, 128 * j:128 * (j + 1)],
                   ctx.eye[:])
        for j in range(8):
            tt(em, "vector", ctx.X[j][:, 128 * i:128 * (i + 1)],
               pbf[:, 128 * j:128 * (j + 1)], ctx.X[j][:, 128 * i:128 * (i + 1)],
               ALU.add, wkey=ctx.X[j])

def prep_phase2_inputs(inputs):
    d = {}
    d["router_wt"] = np.ascontiguousarray(inputs["L0_ffn_router_w"].T, np.float32)
    eb = np.zeros((128, 2), np.float32)
    eb[:, 0] = inputs["L0_ffn_expert_bias"][0:128]
    eb[0:32, 1] = inputs["L0_ffn_expert_bias"][128:160]
    d["ebias"] = eb
    d["wg_t"] = np.ascontiguousarray(np.transpose(inputs["L0_ffn_w_gate"], (0, 2, 1)),
                                     np.float32)
    d["wd_t"] = np.ascontiguousarray(np.transpose(inputs["L0_ffn_w_down"], (0, 2, 1)),
                                     np.float32)
    d["wgs_t"] = np.ascontiguousarray(inputs["L0_ffn_w_gate_shared"].T, np.float32)
    d["wds_t"] = np.ascontiguousarray(inputs["L0_ffn_w_down_shared"].T, np.float32)
    return d


PHASE2_TENSORS = [
    ("router_wt", [1024, 160]), ("ebias", [128, 2]),
    ("wg_t", [160, 1024, 512]), ("wd_t", [160, 512, 1024]),
    ("wgs_t", [1024, 512]), ("wds_t", [512, 1024]),
]


def build_nc_phase2(T=2048, debug=False, sparse=False):
    from contextlib import ExitStack
    nc = bass.Bass("TRN2", target_bir_lowering=False)
    x_in = nc.dram_tensor("x_fm", [1024, T], F32, kind="ExternalInput").ap()
    y_out = nc.dram_tensor("y_fm", [1024, T], F32, kind="ExternalOutput").ap()
    inp = {nm: nc.dram_tensor(nm, shp, F32, kind="ExternalInput").ap()
           for nm, shp in PHASE2_TENSORS}
    em = Em(nc)
    ctx = Ctx()
    ctx.fused = False
    ctx.debug = debug
    ctx.moe_sparse = sparse
    ctx.dbg_done = set()
    with ExitStack() as st:
        E = st.enter_context
        if debug:
            ctx.dbg_stage = E(nc.sbuf_tensor("dbgstage", [128, 1024], F32))
        ctx.X = alloc_stream(nc, E, T)
        setup_common(nc, em, ctx, E, T)
        load_stream(em, ctx, x_in)
        em.barrier()
        build_phase2(nc, em, ctx, E, inp)
        store_stream(em, ctx, y_out)
        em.build()
    print("phase2: ops=%d" % len(em.ops))
    return nc


def emit_ln_win(nc, em, ctx, t0, tw, src_of, dst_of, psums, eps_col,
                gamma=None, beta=None, lnrow=None, sq2=None, tmpf=None):
    pA, pB, pC, pD = psums
    for j in range(8):
        s = src_of(j)
        sq = sq2[j % 2]
        if s.tensor.dtype == F32:
            copy(em, "vector", sq[:, 0:tw], s, wkey=sq)
            mm(em, pA[0:1, 0:tw], ctx.ones_col_bf[:], sq[:, 0:tw],
               start=(j == 0), stop=(j == 7))
            act(em, sq[:, 0:tw], sq[:, 0:tw], AF.Square, wkey=sq)
        else:
            mm(em, pA[0:1, 0:tw], ctx.ones_col_bf[:], s, start=(j == 0), stop=(j == 7))
            act(em, sq[:, 0:tw], s, AF.Square, wkey=sq)
        mm(em, pB[0:1, 0:tw], ctx.ones_col_bf[:], sq[:, 0:tw],
           start=(j == 0), stop=(j == 7))
    mean, msq, var, rstd = lnrow
    ts(em, "vector", mean[:, 0:tw], pA[0:1, 0:tw], 1.0 / 1024, None, ALU.mult,
       wkey=mean)
    ts(em, "vector", msq[:, 0:tw], pB[0:1, 0:tw], 1.0 / 1024, None, ALU.mult,
       wkey=msq)
    tt(em, "vector", var[:, 0:tw], mean[:, 0:tw], mean[:, 0:tw], ALU.mult, wkey=var)
    tt(em, "vector", var[:, 0:tw], msq[:, 0:tw], var[:, 0:tw], ALU.subtract,
       wkey=var)
    act(em, rstd[:, 0:tw], var[:, 0:tw], AF.Sqrt, bias=eps_col, wkey=rstd)
    recip(em, rstd[:, 0:tw], rstd[:, 0:tw])
    mm(em, pC[:, 0:tw], ctx.ones_f[:], mean[:, 0:tw])
    mm(em, pD[:, 0:tw], ctx.ones_f[:], rstd[:, 0:tw])
    for j in range(8):
        tf = tmpf[j % 2]
        tt(em, "vector", tf[:, 0:tw], src_of(j), pC[:, 0:tw], ALU.subtract, wkey=tf)
        if gamma is None:
            dst_of(j, lambda out, tfv=tf: tt(em, "vector", out, tfv[:, 0:tw],
                                             pD[:, 0:tw], ALU.mult, wkey=None))
        else:
            tt(em, "vector", tf[:, 0:tw], tf[:, 0:tw], pD[:, 0:tw], ALU.mult,
               wkey=tf)
            dst_of(j, lambda out, tfv=tf, j=j: ts(em, "vector", out, tfv[:, 0:tw],
                                                  gamma(j), beta(j), ALU.mult,
                                                  ALU.add, wkey=None))


def stream_proj(nc, em, ctx, inp_ap, rhs_of, psums, consumer, wbuf, nK=8,
                TT_=512, npair=4):
    for pj in range(npair):
        for kk in range(nK):
            wb = wbuf[kk % len(wbuf)]
            em.dma("gpsimd", out=wb[:, 0:256],
                   in_=inp_ap[128 * kk:128 * (kk + 1), 256 * pj:256 * (pj + 1)],
                   reads=[wb], writes=[wb])
            mm(em, psums[0][:, 0:TT_], wb[:, 0:128], rhs_of(kk),
               start=(kk == 0), stop=(kk == nK - 1))
            mm(em, psums[1][:, 0:TT_], wb[:, 128:256], rhs_of(kk),
               start=(kk == 0), stop=(kk == nK - 1))
        consumer(2 * pj, psums[0])
        consumer(2 * pj + 1, psums[1])


def build_phase3(nc, em, ctx, E, inp, stop_at=None):
    T = ctx.T
    NTW = T // 512
    NI = T // 128

    pp = [E(nc.psum_tensor(f"p3p{i}", [128, 512], F32)) for i in range(7)]
    pbf = E(nc.psum_tensor("p3bf", [128, 1024], BF16))

    S1 = [E(nc.sbuf_tensor(f"p3S1_{j}", [128, T], BF16)) for j in range(8)]
    S2 = [E(nc.sbuf_tensor(f"p3S2_{j}", [128, T], BF16)) for j in range(8)]
    NI0 = T // 128
    S3 = [E(nc.sbuf_tensor(f"p3S3_{j}", [128, 1024], BF16)) for j in range(max(NI0, 8))]
    S4 = [E(nc.sbuf_tensor(f"p3S4_{j}", [128, T], BF16)) for j in range(8)]
    S5 = [E(nc.sbuf_tensor(f"p3S5_{j}", [128, 1024], BF16))
          for j in range(max(NI0, 16))]
    xt = [E(nc.sbuf_tensor(f"p3x{j}", [128, 512], F32)) for j in range(8)]
    s2b = lambda j: S3[j][:, 0:512]
    sq2 = [E(nc.sbuf_tensor(f"p3sq{i}", [128, 512], BF16)) for i in range(2)]
    tmpf = [E(nc.sbuf_tensor(f"p3tmp{i}", [128, 512], F32)) for i in range(2)]
    lnrow = [E(nc.sbuf_tensor(f"p3row{i}", [1, 512], F32)) for i in range(4)]
    wbuf = [E(nc.sbuf_tensor(f"p3wb{i}", [128, 256], BF16)) for i in range(4)]
    wbig3 = [E(nc.sbuf_tensor(f"p3wB{i}", [128, 512], BF16)) for i in range(3)]
    pv3 = E(nc.sbuf_tensor("p3pv", [128, 40], F32))
    em.dma("sync", out=pv3[:], in_=inp["pvec3"], writes=[pv3])
    IG1, IB1, IG2, IB2, IB2b = range(5)

    def pvc(idx, j):
        return pv3[:, idx * 8 + j:idx * 8 + j + 1]

    b1c = E(nc.sbuf_tensor("p3b1c", [128, 32], F32))
    em.dma("sync", out=b1c[:], in_=inp["b1cols"], writes=[b1c])
    projT2 = E(nc.sbuf_tensor("p3projT", [128, 64], BF16))
    pstage = E(nc.sbuf_tensor("p3pstage", [128, 64], F32))
    em.dma("sync", out=pstage[0:64, :], in_=inp["projT"], reads=[pstage],
           writes=[pstage])
    DN = 1.0 / np.sqrt(np.sqrt(64.0))
    ts(em, "vector", projT2[0:64, :], pstage[0:64, :], DN, None, ALU.mult,
       wkey=projT2)
    copy(em, "vector", projT2[64:128, :], projT2[0:64, :], wkey=projT2)
    bd = np.zeros((1024, 16), np.float32)
    for h in range(16):
        bd[64 * h:64 * (h + 1), h] = 1.0
    bd_d = nc.inline_tensor(bd.astype(BF16NP), "p3bd_d")
    BDH = [E(nc.sbuf_tensor(f"p3bd{c}", [128, 16], BF16)) for c in range(8)]
    for c in range(8):
        em.dma("sync", out=BDH[c][:], in_=bd_d.ap()[128 * c:128 * (c + 1), :],
               writes=[BDH[c]])
    diag_q = [E(nc.sbuf_tensor(f"p3dq{i}", [128, 16], F32)) for i in range(NI)]
    diag_k = [E(nc.sbuf_tensor(f"p3dk{i}", [128, 16], F32)) for i in range(NI)]
    ndmx = E(nc.sbuf_tensor("p3ndmx", [128, 16], F32))
    kv_sb = [E(nc.sbuf_tensor(f"p3kv{p}", [128, 65], BF16)) for p in range(8)]
    rc = E(nc.sbuf_tensor("p3rc", [128, 1], F32))
    mxcol = E(nc.sbuf_tensor("p3mx", [128, 1], F32))
    mxrow = E(nc.sbuf_tensor("p3mxr", [1, 16], F32))
    mxk_col = E(nc.sbuf_tensor("p3mxk", [128, 1], F32))
    cc_in = nc.dram_tensor("p3cc_in", [1, 16], F32)
    cc_out = nc.dram_tensor("p3cc_out", [1, 16], F32)

    em.barrier()

    for w in range(NTW):
        t0 = w * 512
        for j in range(8):
            em.dma("sync", out=xt[j][:], in_=ctx.x_in[128 * j:128 * (j + 1),
                                                      t0:t0 + 512],
                   reads=[xt[j]], writes=[xt[j]])
        emit_ln_win(nc, em, ctx, t0, 512, lambda j: xt[j][:],
                    lambda j, emitfn: emitfn(S1[j][:, t0:t0 + 512]),
                    (pp[0], pp[1], pp[2], pp[3]), ctx.eps5[0:1, 0:1],
                    lnrow=lnrow, sq2=sq2, tmpf=tmpf)

    if stop_at == 'ln':
        em.barrier()
        return
    def qk_path(wname, dst_tm, diag_dst, is_q):
        for w in range(NTW):
            t0 = w * 512
            stream_proj(nc, em, ctx, inp[wname],
                        lambda kk, t0=t0: S1[kk][:, t0:t0 + 512], (pp[0], pp[1]),
                        lambda j, ps, t0=t0: copy(em, "vector",
                                                  S2[j][:, t0:t0 + 512],
                                                  ps[:, 0:512], wkey=S2[j]),
                        wbuf)
        if stop_at == 'qk_mm':
            return
        for w in range(NTW):
            t0 = w * 512
            for j in range(8):
                act(em, sq2[j % 2][:], S2[j][:, t0:t0 + 512], AF.Square,
                    wkey=sq2[j % 2])
                mm(em, pp[4][0:16, 0:512], BDH[j][:], sq2[j % 2][:],
                   start=(j == 0), stop=(j == 7))
            for s in range(4):
                em.op("scalar", lambda e, s=s, w=w: e.activation(
                    tmpf[0][0:16, 128 * s:128 * (s + 1)],
                    pp[4][0:16, 128 * s:128 * (s + 1)], AF.Identity,
                    scale=float(DN * DN / 2.0)),
                    reads=[pp[4]], writes=[tmpf[0]])
            copy(em, "vector", sq2[0][0:16, 0:512], tmpf[0][0:16, 0:512],
                 wkey=sq2[0])
            for s in range(4):
                i = 4 * w + s
                transp(em, pbf[:, 0:16], sq2[0][0:16, 128 * s:128 * (s + 1)],
                       ctx.eye[0:16, 0:16])
                copy(em, "vector", diag_dst[i][:], pbf[0:128, 0:16],
                     wkey=diag_dst[i])
        if stop_at == 'qk_diag':
            return
        for i in range(NI):
            tc = i * 128
            for hh in range(2):
                for h4 in range(8):
                    h = 8 * hh + h4
                    par = h4 % 2
                    mm(em, pp[2 * hh + par][:, 64 * (h4 // 2):64 * (h4 // 2 + 1)],
                       S2[h // 2][64 * par:64 * par + 64, tc:tc + 128],
                       projT2[64 * par:64 * par + 64, :])
            for hh in range(2):
                for par in range(2):
                    dv = dst_tm(i).rearrange("p (g c) -> p g c", c=64)
                    dst = dv[:, 8 * hh + par:8 * hh + 8:2, :]
                    copy(em, "vector", dst, pp[2 * hh + par][:, 0:256], wkey=None)

    if stop_at == 'none2':
        em.barrier()
        return
    qdt = lambda i: S3[i][:, :]
    if stop_at == 'none3':
        em.barrier()
        return
    qk_path("Wq", qdt, diag_q, True)
    if stop_at == 'qproj':
        em.barrier()
        return
    for i in range(NI):
        dst = qdt(i)
        for h in range(16):
            reduce(em, mxcol[:], dst[:, 64 * h:64 * (h + 1)], ALU.max, wkey=mxcol)
            ts(em, "vector", ndmx[:, h:h + 1], diag_q[i][:, h:h + 1], mxcol[:],
               -1.0, ALU.add, ALU.mult, wkey=ndmx)
        for h in range(16):
            act(em, dst[:, 64 * h:64 * (h + 1)], dst[:, 64 * h:64 * (h + 1)],
                AF.Exp, bias=ndmx[:, h:h + 1], wkey=None)
        ts(em, "vector", dst[:], dst[:], 0.125, 0.125e-6, ALU.mult, ALU.add,
           wkey=None)
    if stop_at == 'qf':
        em.barrier()
        return
    for i in range(NI):
        src = qdt(i)
        for j in range(8):
            transp(em, pbf[:, 0:128], src[:, 128 * j:128 * (j + 1)], ctx.eye[:])
            copy(em, "vector", S4[j][:, 128 * i:128 * (i + 1)], pbf[0:128, 0:128],
                 wkey=S4[j])
    if stop_at == 'qftr':
        em.barrier()
        return
    qk_path("Wk", qdt, diag_k, False)
    if stop_at == 'kproj':
        em.barrier()
        return
    memset(em, "vector", mxk_col[:], -1e30, wkey=mxk_col)
    for i in range(NI):
        reduce(em, mxcol[:], qdt(i)[:], ALU.max, wkey=mxcol)
        tt(em, "vector", mxk_col[:], mxk_col[:], mxcol[:], ALU.max, wkey=mxk_col)
    copy(em, "vector", sq2[0][0:128, 0:1], mxk_col[:], wkey=sq2[0])
    transp(em, pbf[0:1, 0:128], sq2[0][0:128, 0:1], ctx.eye[:])
    copy(em, "vector", sq2[0][0:1, 0:128], pbf[0:1, 0:128], wkey=sq2[0])
    reduce(em, mxrow[0:1, 0:1], sq2[0][0:1, 0:128], ALU.max, wkey=mxrow)
    ts(em, "vector", mxrow[0:1, 0:16], ctx.zeros[0:1, 0:16], mxrow[0:1, 0:1],
       None, ALU.add, wkey=mxrow)
    em.dma("gpsimd", out=cc_in.ap(), in_=mxrow[0:1, 0:16], reads=[mxrow],
           writes=["p3ccin"])
    em.op("gpsimd", lambda e: e.collective_compute(
        "AllReduce", ALU.max, replica_groups=[list(range(ctx.ncores))],
        ins=[cc_in.ap().opt()], outs=[cc_out.ap().opt()]),
        reads=["p3ccin"], writes=["p3ccout"])
    em.dma("sync", out=mxrow[0:1, 0:16], in_=cc_out.ap(), reads=["p3ccout", mxrow],
           writes=[mxrow])
    mm(em, pp[6][:, 0:1], ctx.ones_f[:], mxrow[0:1, 0:1])
    copy(em, "vector", mxk_col[:], pp[6][:, 0:1], wkey=mxk_col)
    if stop_at == 'kmax':
        em.barrier()
        return
    for i in range(NI):
        dst = qdt(i)
        for h in range(16):
            ts(em, "vector", ndmx[:, h:h + 1], diag_k[i][:, h:h + 1], mxk_col[:],
               -1.0, ALU.add, ALU.mult, wkey=ndmx)
        for h in range(16):
            act(em, dst[:, 64 * h:64 * (h + 1)], dst[:, 64 * h:64 * (h + 1)],
                AF.Exp, bias=ndmx[:, h:h + 1], wkey=None)
        ts(em, "vector", dst[:], dst[:], 0.125, 0.125e-6, ALU.mult, ALU.add,
           wkey=None)
    if stop_at == 'kf':
        em.barrier()
        return
    for w in range(NTW):
        t0 = w * 512
        for nh in range(2):
            for kk in range(8):
                wb = wbig3[kk % 3]
                em.dma("gpsimd", out=wb[:], in_=inp["Wv"][
                    128 * kk:128 * (kk + 1), 512 * nh:512 * (nh + 1)],
                    reads=[wb], writes=[wb])
                for s in range(4):
                    mm(em, pp[s][:, 0:512],
                       S1[kk][:, t0 + 128 * s:t0 + 128 * (s + 1)], wb[:],
                       start=(kk == 0), stop=(kk == 7))
            for s in range(4):
                i = 4 * w + s
                copy(em, "vector", S5[i][:, 512 * nh:512 * (nh + 1)],
                     pp[s][:, 0:512], wkey=S5[i])
    if stop_at == 'vproj':
        em.barrier()
        return
    vt = lambda i: S5[i][:, :]
    for h in range(16):
        pr = h % 2
        pkv = pp[pr]
        pden = pp[2 + pr]
        for i in range(NI):
            mm(em, pkv[0:64, 0:64], qdt(i)[:, 64 * h:64 * (h + 1)],
               vt(i)[:, 64 * h:64 * (h + 1)], start=(i == 0), stop=(i == NI - 1))
            mm(em, pden[0:64, 0:1], qdt(i)[:, 64 * h:64 * (h + 1)],
               ctx.ones_col_bf[:], start=(i == 0), stop=(i == NI - 1))
        sl = slice(64 * pr, 64 * (pr + 1))
        copy(em, "vector", kv_sb[h // 2][sl, 0:64], pkv[0:64, 0:64],
             wkey=kv_sb[h // 2])
        copy(em, "vector", kv_sb[h // 2][sl, 64:65], pden[0:64, 0:1],
             wkey=kv_sb[h // 2])
    if stop_at == 'kv':
        em.barrier()
        return
    for i in range(NI):
        dst = qdt(i)
        for h in range(16):
            pr = h % 2
            sl = slice(64 * pr, 64 * (pr + 1))
            pn = pp[4 + pr]
            mm(em, pn[:, 0:65], S4[h // 2][sl, 128 * i:128 * (i + 1)],
               kv_sb[h // 2][sl, 0:65])
            recip(em, rc[:], pn[:, 64:65], wkey=rc)
            ts(em, "vector", dst[:, 64 * h:64 * (h + 1)], pn[:, 0:64], rc[:],
               None, ALU.mult, wkey=None)
    if stop_at == 'num':
        em.barrier()
        return
    for i in range(NI):
        src = qdt(i)
        for j in range(8):
            transp(em, pbf[:, 0:128], src[:, 128 * j:128 * (j + 1)], ctx.eye[:])
            copy(em, "vector", S4[j][:, 128 * i:128 * (i + 1)], pbf[0:128, 0:128],
                 wkey=S4[j])
    if stop_at == 'attntr':
        em.barrier()
        return
    for w in range(NTW):
        t0 = w * 512
        stream_proj(nc, em, ctx, inp["Wo"],
                    lambda kk, t0=t0: S4[kk][:, t0:t0 + 512], (pp[0], pp[1]),
                    lambda j, ps, t0=t0: tt(em, "vector", S1[j][:, t0:t0 + 512],
                                            ps[:, 0:512], S1[j][:, t0:t0 + 512],
                                            ALU.add, wkey=S1[j]),
                    wbuf)
    if stop_at == 'wo':
        em.barrier()
        return
    for w in range(NTW):
        t0 = w * 512
        emit_ln_win(nc, em, ctx, t0, 512, lambda j: S1[j][:, t0:t0 + 512],
                    lambda j, emitfn: emitfn(S2[j][:, t0:t0 + 512]),
                    (pp[0], pp[1], pp[2], pp[3]), ctx.eps6[0:1, 0:1],
                    gamma=lambda j: pvc(IG1, j), beta=lambda j: pvc(IB1, j),
                    lnrow=lnrow, sq2=sq2, tmpf=tmpf)
    if stop_at == 'ln1':
        em.barrier()
        return
    helu = lambda m: S5[m // 2][:, 512 * (m % 2):512 * (m % 2 + 1)]
    for w in range(NTW):
        t0 = w * 512
        for mp in range(16):
            for kk in range(8):
                wb = wbuf[kk % 4]
                em.dma("gpsimd", out=wb[:, 0:256],
                       in_=inp["W1"][128 * kk:128 * (kk + 1),
                                     256 * mp:256 * (mp + 1)],
                       reads=[wb], writes=[wb])
                mm(em, pp[0][:, 0:512], wb[:, 0:128], S2[kk][:, t0:t0 + 512],
                   start=(kk == 0), stop=(kk == 7))
                mm(em, pp[1][:, 0:512], wb[:, 128:256], S2[kk][:, t0:t0 + 512],
                   start=(kk == 0), stop=(kk == 7))
            for half in range(2):
                m = 2 * mp + half
                ps = pp[half]
                tf = tmpf[half]
                ts(em, "vector", tf[:], ps[:, 0:512], b1c[:, m:m + 1], 0.0,
                   ALU.add, ALU.min, wkey=tf)
                act(em, tf[:], tf[:], AF.Exp, wkey=tf)
                ts(em, "vector", xt[half][:], ps[:, 0:512], b1c[:, m:m + 1], 0.0,
                   ALU.add, ALU.max, wkey=xt[half])
                stt(em, "vector", helu(m)[:], tf[:], -1.0, xt[half][:], ALU.add,
                    ALU.add, wkey=None)
        for pj in range(4):
            for kk in range(32):
                wb = wbuf[kk % 4]
                em.dma("gpsimd", out=wb[:, 0:256],
                       in_=inp["W2"][128 * kk:128 * (kk + 1),
                                     256 * pj:256 * (pj + 1)],
                       reads=[wb], writes=[wb])
                mm(em, pp[2][:, 0:512], wb[:, 0:128], helu(kk)[:],
                   start=(kk == 0), stop=(kk == 31))
                mm(em, pp[3][:, 0:512], wb[:, 128:256], helu(kk)[:],
                   start=(kk == 0), stop=(kk == 31))
            for half in range(2):
                j = 2 * pj + half
                stt(em, "vector", s2b(j), pp[2 + half][:, 0:512],
                    pvc(IB2b, j), S2[j][:, t0:t0 + 512], ALU.add, ALU.add,
                    wkey=None)
        for j in range(8):
            em.dma("sync", out=xt[j][:], in_=ctx.x_in[128 * j:128 * (j + 1),
                                                      t0:t0 + 512],
                   reads=[xt[j]], writes=[xt[j]])
        def dst3(j, emitfn):
            tf2 = sq2[0][:, 0:512]
            emitfn(tf2)
            tt(em, "vector", xt[j][:], tf2, xt[j][:], ALU.add, wkey=xt[j])
            em.dma("gpsimd", out=ctx.y_out[128 * j:128 * (j + 1), t0:t0 + 512],
                   in_=xt[j][:], reads=[xt[j]], writes=[f"P3OUT{j}"])
        emit_ln_win(nc, em, ctx, t0, 512, lambda j: s2b(j),
                    dst3, (pp[0], pp[1], pp[2], pp[3]), ctx.eps6[0:1, 0:1],
                    gamma=lambda j: pvc(IG2, j), beta=lambda j: pvc(IB2, j),
                    lnrow=lnrow, sq2=sq2, tmpf=tmpf)
    em.barrier()


def prep_phase3_inputs(inputs):
    d = {}
    d["Wq"] = np.ascontiguousarray(inputs["L1_attn_Wq"].reshape(1024, 1024),
                                   np.float32)
    d["Wk"] = np.ascontiguousarray(inputs["L1_attn_Wk"].reshape(1024, 1024),
                                   np.float32)
    d["Wv"] = np.ascontiguousarray(inputs["L1_attn_Wv"].reshape(1024, 1024),
                                   np.float32)
    d["Wo"] = np.ascontiguousarray(inputs["L1_attn_Wo"].reshape(1024, 1024),
                                   np.float32)
    d["projT"] = np.ascontiguousarray(inputs["L1_attn_proj"].T, np.float32)
    d["W1"] = np.ascontiguousarray(inputs["L1_attn_W1"], np.float32)
    d["W2"] = np.ascontiguousarray(inputs["L1_attn_W2"], np.float32)
    d["b1cols"] = np.ascontiguousarray(inputs["L1_attn_b1"].reshape(32, 128).T,
                                       np.float32)
    d["pvec3"] = _pack_pvec([
        inputs["L1_attn_ln1_g"], inputs["L1_attn_ln1_b"],
        inputs["L1_attn_ln2_g"], inputs["L1_attn_ln2_b"], inputs["L1_attn_b2"]])
    return d


PHASE3_TENSORS = [
    ("Wq", [1024, 1024]), ("Wk", [1024, 1024]), ("Wv", [1024, 1024]),
    ("Wo", [1024, 1024]), ("projT", [64, 64]), ("W1", [1024, 4096]),
    ("W2", [4096, 1024]), ("b1cols", [128, 32]), ("pvec3", [128, 40]),
]


def build_nc_phase3(T=2048, ncores=8, debug=False, stop_at=None):
    from contextlib import ExitStack
    nc = bass.Bass("TRN2", target_bir_lowering=False)
    x_in = nc.dram_tensor("x_fm", [1024, T], F32, kind="ExternalInput").ap()
    y_out = nc.dram_tensor("y_fm", [1024, T], F32, kind="ExternalOutput").ap()
    inp = {nm: nc.dram_tensor(nm, shp, F32, kind="ExternalInput").ap()
           for nm, shp in PHASE3_TENSORS}
    em = Em(nc)
    ctx = Ctx()
    ctx.fused = False
    ctx.ncores = ncores
    ctx.debug = debug
    ctx.dbg_done = set()
    ctx.x_in = x_in
    ctx.y_out = y_out
    with ExitStack() as st:
        E = st.enter_context
        if debug:
            ctx.dbg_stage = E(nc.sbuf_tensor("dbgstage", [128, 1024], F32))
        setup_common(nc, em, ctx, E, T)
        build_phase3(nc, em, ctx, E, inp, stop_at=stop_at)
        em.build()
    print("phase3: ops=%d" % len(em.ops))
    return nc


def _p4_consts(T):
    DH, WIN = 64, 512
    half = np.arange(0, DH, 2, dtype=np.float64)
    invf = 1.0 / 10000.0 ** (half / DH)
    t = np.arange(T, dtype=np.float64)
    fr = t[:, None] * invf[None, :]
    fr = np.concatenate([fr, fr], axis=-1)
    base = (half + 0.4 * DH) / (1.4 * DH)
    power = (t - T // 2) / float(WIN // 2)
    sc = base[None, :] ** power[:, None]
    sc = np.concatenate([sc, sc], axis=-1)
    cos, sin = np.cos(fr), np.sin(fr)
    Aq = (cos * sc).T; Bq = (sin * sc).T
    Ak = (cos / sc).T; Bk = (sin / sc).T
    pack = lambda a: np.ascontiguousarray(np.tile(a, (2, 1))).astype(BF16NP)
    RT = np.zeros((64, 64), np.float32)
    for do in range(32):
        RT[do + 32, do] = -1.0
        RT[do, do + 32] = 1.0
    RT2 = np.zeros((128, 128), np.float32)
    RT2[0:64, 0:64] = RT; RT2[64:128, 64:128] = RT
    M = np.zeros((4, 128, 1024), np.float32)
    for qs in range(4):
        for p in range(128):
            i = qs * 128 + p
            M[qs, p, i:i + 513] = 1.0
    M0 = np.zeros((4, 128, 512), np.float32)
    for qs in range(4):
        for p in range(128):
            M0[qs, p, 0:qs * 128 + p + 1] = 1.0
    G8 = np.zeros((8, 512), np.float32)
    for h in range(8):
        G8[h, 64 * h:64 * (h + 1)] = 1.0
    BD8 = np.zeros((512, 8), np.float32)
    for h in range(8):
        BD8[64 * h:64 * (h + 1), h] = 1.0
    return (pack(Aq), pack(Bq), pack(Ak), pack(Bk), RT2.astype(BF16NP),
            M.astype(BF16NP), M0.astype(BF16NP), G8.astype(BF16NP),
            BD8.astype(BF16NP))


def build_phase4(nc, em, ctx, E, inp):
    T = ctx.T
    NTW = T // 512
    NI = T // 128
    NW = max(1, T // 512)

    pp = [E(nc.psum_tensor(f"p4p{i}", [128, 512], F32)) for i in range(7)]
    pbf = E(nc.psum_tensor("p4bf", [128, 1024], BF16))

    Aq_np, Bq_np, Ak_np, Bk_np, RT2_np, M_np, M0_np, G8_np, BD8_np = _p4_consts(T)
    def const(nm, arr):
        d = nc.inline_tensor(arr, nm + "_d")
        t = E(nc.sbuf_tensor(nm, list(arr.shape), BF16))
        em.dma("sync", out=t[:], in_=d.ap(), writes=[t])
        return t
    Aq = const("p4Aq", Aq_np); Bq = const("p4Bq", Bq_np)
    Ak = const("p4Ak", Ak_np); Bk = const("p4Bk", Bk_np)
    RT2 = const("p4RT", RT2_np)
    G8 = const("p4G8", G8_np)
    bd8_d = nc.inline_tensor(BD8_np, "p4BD_d")
    BD8 = [E(nc.sbuf_tensor(f"p4BD{c}", [128, 8], BF16)) for c in range(4)]
    for c in range(4):
        em.dma("sync", out=BD8[c][:], in_=bd8_d.ap()[128 * c:128 * (c + 1), :],
               writes=[BD8[c]])
    Mm = [const(f"p4M{qs}", M_np[qs]) for qs in range(4)]
    Mm0 = [const(f"p4M0{qs}", M0_np[qs]) for qs in range(4)]

    pv4 = E(nc.sbuf_tensor("p4pv", [128, 64], F32))
    em.dma("sync", out=pv4[:], in_=inp["pvec4"], writes=[pv4])

    def pvc(l, idx, j):
        base = l * 32
        if idx < 3:
            return pv4[:, base + idx * 8 + j:base + idx * 8 + j + 1]
        return pv4[:, base + 24 + (idx - 3) * 4 + j:base + 24 + (idx - 3) * 4 + j + 1]

    Z = [E(nc.sbuf_tensor(f"p4Z{j}", [128, T], BF16)) for j in range(8)]
    hs4 = [E(nc.sbuf_tensor(f"p4hs{j}", [128, T], BF16)) for j in range(4)]
    q_fm = [E(nc.sbuf_tensor(f"p4q{j}", [128, T], BF16)) for j in range(4)]
    k_fm = [E(nc.sbuf_tensor(f"p4k{j}", [128, T], BF16)) for j in range(4)]
    v_tm = [E(nc.sbuf_tensor(f"p4v{i}", [128, 512], BF16)) for i in range(NI)]
    o_fm = [E(nc.sbuf_tensor(f"p4o{j}", [128, max(T, 1024)], BF16)) for j in range(4)]
    helu = [E(nc.sbuf_tensor(f"p4he{m}", [128, 512], BF16)) for m in range(22)]
    attn_sb = [E(nc.sbuf_tensor(f"p4at{i}", [128, 1024], BF16)) for i in range(2)]
    attnT = [E(nc.sbuf_tensor(f"p4aT{i}", [128, 1024], BF16)) for i in range(2)]
    rnb = E(nc.sbuf_tensor("p4rn", [8, 512], BF16))
    rnf = E(nc.sbuf_tensor("p4rnf", [8, 512], F32))
    tb = [E(nc.sbuf_tensor(f"p4tb{i}", [128, 512], BF16)) for i in range(2)]
    mxc = E(nc.sbuf_tensor("p4mx", [128, 1], F32))
    mx2 = E(nc.sbuf_tensor("p4mx2", [128, 1], F32))
    rc = E(nc.sbuf_tensor("p4rc", [128, 1], F32))
    sq2 = [E(nc.sbuf_tensor(f"p4sq{i}", [128, 512], BF16)) for i in range(2)]
    tmpf = [E(nc.sbuf_tensor(f"p4tmp{i}", [128, 512], F32)) for i in range(2)]
    lnrow = [E(nc.sbuf_tensor(f"p4row{i}", [1, 512], F32)) for i in range(4)]
    xt = [E(nc.sbuf_tensor(f"p4x{j}", [128, 512], BF16)) for j in range(8)]
    wbig = [E(nc.sbuf_tensor(f"p4wB{i}", [128, 512], BF16)) for i in range(3)]
    wbuf = [E(nc.sbuf_tensor(f"p4wb{i}", [128, 256], BF16)) for i in range(4)]

    em.barrier()

    for w in range(NTW):
        t0 = w * 512
        for j in range(8):
            em.dma("gpsimd", out=xt[j][:], in_=ctx.x_in[128 * j:128 * (j + 1),
                                                        t0:t0 + 512],
                   reads=[xt[j]], writes=[xt[j]])
        emit_ln_win(nc, em, ctx, t0, 512, lambda j: xt[j][:],
                    lambda j, emitfn: emitfn(Z[j][:, t0:t0 + 512]),
                    (pp[0], pp[1], pp[2], pp[3]), ctx.eps5[0:1, 0:1],
                    lnrow=lnrow, sq2=sq2, tmpf=tmpf)

    for l in range(2):
        for j in range(4):
            copy(em, "vector", hs4[j][:, 1:T], Z[j + 4][:, 0:T - 1], wkey=hs4[j])
            memset(em, "vector", hs4[j][:, 0:1], 0.0, wkey=hs4[j])
        hsrc = lambda j: Z[j] if j < 4 else hs4[j - 4]
        for w in range(NTW):
            t0 = w * 512
            for pj in range(4):
                dst = q_fm[2 * pj] if pj < 2 else k_fm[2 * (pj - 2)]
                dst2 = q_fm[2 * pj + 1] if pj < 2 else k_fm[2 * (pj - 2) + 1]
                for kk in range(8):
                    wb = wbuf[kk % 4]
                    em.dma("gpsimd", out=wb[:, 0:256],
                           in_=inp["wqkv"][l, 128 * kk:128 * (kk + 1),
                                           256 * pj:256 * (pj + 1)],
                           reads=[wb], writes=[wb])
                    mm(em, pp[0][:, 0:512], wb[:, 0:128], hsrc(kk)[:, t0:t0 + 512],
                       start=(kk == 0), stop=(kk == 7))
                    mm(em, pp[1][:, 0:512], wb[:, 128:256], hsrc(kk)[:, t0:t0 + 512],
                       start=(kk == 0), stop=(kk == 7))
                copy(em, "vector", dst[:, t0:t0 + 512], pp[0][:, 0:512], wkey=dst)
                copy(em, "vector", dst2[:, t0:t0 + 512], pp[1][:, 0:512], wkey=dst2)
            for kk in range(8):
                wb = wbig[kk % 3]
                em.dma("gpsimd", out=wb[:],
                       in_=inp["wqkv"][l, 128 * kk:128 * (kk + 1), 1024:1536],
                       reads=[wb], writes=[wb])
                for s in range(4):
                    mm(em, pp[2 + s][:, 0:512],
                       hsrc(kk)[:, t0 + 128 * s:t0 + 128 * (s + 1)], wb[:],
                       start=(kk == 0), stop=(kk == 7))
            for s in range(4):
                copy(em, "vector", v_tm[4 * w + s][:], pp[2 + s][:, 0:512],
                     wkey=v_tm[4 * w + s])
        for (fm, Ac, Bc, isq) in ((q_fm, Aq, Bq, True), (k_fm, Ak, Bk, False)):
            for w in range(NTW):
                t0 = w * 512
                for j in range(4):
                    act(em, sq2[j % 2][:], fm[j][:, t0:t0 + 512], AF.Square,
                        wkey=sq2[j % 2])
                    mm(em, pp[0][0:8, 0:512], BD8[j][:],
                       sq2[j % 2][:], start=(j == 0), stop=(j == 3))
                act(em, rnf[:, 0:512], pp[0][0:8, 0:512], AF.Sqrt, wkey=rnf)
                ts(em, "vector", rnf[:, 0:512], rnf[:, 0:512], 1e-12, None,
                   ALU.max, wkey=rnf)
                recip(em, rnf[:, 0:512], rnf[:, 0:512])
                copy(em, "vector", rnb[:, 0:512], rnf[:, 0:512], wkey=rnb)
                for j in range(4):
                    mm(em, pp[1][:, 0:512], G8[:, 128 * j:128 * (j + 1)],
                       rnb[:, 0:512])
                    tt(em, "vector", fm[j][:, t0:t0 + 512], fm[j][:, t0:t0 + 512],
                       pp[1][:, 0:512], ALU.mult, wkey=fm[j])
                    qsc = pvc(l, 3 if isq else 4, j)
                    ts(em, "vector", fm[j][:, t0:t0 + 512], fm[j][:, t0:t0 + 512],
                       qsc, 8.0 if isq else 1.0, ALU.mult, ALU.mult, wkey=fm[j])
                    mm(em, pp[2][:, 0:512], RT2[:], fm[j][:, t0:t0 + 512])
                    tt(em, "vector", tb[0][:], fm[j][:, t0:t0 + 512],
                       Ac[:, t0:t0 + 512], ALU.mult, wkey=tb[0])
                    tt(em, "vector", tb[1][:], pp[2][:, 0:512],
                       Bc[:, t0:t0 + 512], ALU.mult, wkey=tb[1])
                    tt(em, "vector", fm[j][:, t0:t0 + 512], tb[0][:], tb[1][:],
                       ALU.add, wkey=fm[j])
        for h in range(8):
            hj, par = h // 2, h % 2
            sl = slice(64 * par, 64 * (par + 1))
            for w in range(NW):
                kb = (w - 1) * 4
                kcol0 = (w - 1) * 512
                nk = 512 if w == 0 else 1024
                for qs in range(4):
                    qc0 = w * 512 + qs * 128
                    ia = qs % 2
                    if w == 0:
                        mm(em, pp[0][:, 0:512], q_fm[hj][sl, qc0:qc0 + 128],
                           k_fm[hj][sl, 0:512])
                        reduce(em, mxc[:], pp[0][:, 0:512], ALU.max, wkey=mxc)
                        ts(em, "vector", mxc[:], mxc[:], -1.0, None, ALU.mult,
                           wkey=mxc)
                        act(em, attn_sb[ia][:, 0:512], pp[0][:, 0:512], AF.Exp,
                            bias=mxc[:], wkey=attn_sb[ia])
                        tt(em, "vector", attn_sb[ia][:, 0:512],
                           attn_sb[ia][:, 0:512], Mm0[qs][:], ALU.mult,
                           wkey=attn_sb[ia])
                        reduce(em, rc[:], attn_sb[ia][:, 0:512], ALU.add, wkey=rc)
                        recip(em, rc[:], rc[:])
                        ts(em, "vector", attn_sb[ia][:, 0:512],
                           attn_sb[ia][:, 0:512], rc[:], None, ALU.mult,
                           wkey=attn_sb[ia])
                        nblk = 4
                    else:
                        mm(em, pp[0][:, 0:512], q_fm[hj][sl, qc0:qc0 + 128],
                           k_fm[hj][sl, kcol0:kcol0 + 512])
                        mm(em, pp[1][:, 0:512], q_fm[hj][sl, qc0:qc0 + 128],
                           k_fm[hj][sl, kcol0 + 512:kcol0 + 1024])
                        reduce(em, mxc[:], pp[0][:, 0:512], ALU.max, wkey=mxc)
                        reduce(em, mx2[:], pp[1][:, 0:512], ALU.max, wkey=mx2)
                        tt(em, "vector", mxc[:], mxc[:], mx2[:], ALU.max, wkey=mxc)
                        ts(em, "vector", mxc[:], mxc[:], -1.0, None, ALU.mult,
                           wkey=mxc)
                        act(em, attn_sb[ia][:, 0:512], pp[0][:, 0:512], AF.Exp,
                            bias=mxc[:], wkey=attn_sb[ia])
                        act(em, attn_sb[ia][:, 512:1024], pp[1][:, 0:512], AF.Exp,
                            bias=mxc[:], wkey=attn_sb[ia])
                        tt(em, "vector", attn_sb[ia][:], attn_sb[ia][:], Mm[qs][:],
                           ALU.mult, wkey=attn_sb[ia])
                        reduce(em, rc[:], attn_sb[ia][:], ALU.add, wkey=rc)
                        recip(em, rc[:], rc[:])
                        ts(em, "vector", attn_sb[ia][:], attn_sb[ia][:], rc[:],
                           None, ALU.mult, wkey=attn_sb[ia])
                        nblk = 8
                    for b in range(nblk):
                        transp(em, pbf[:, 128 * b:128 * (b + 1)],
                               attn_sb[ia][:, 128 * b:128 * (b + 1)], ctx.eye[:])
                    copy(em, "vector", attnT[ia][:, 0:128 * nblk],
                         pbf[:, 0:128 * nblk], wkey=attnT[ia])
                    pav = pp[4 + (qs % 2)]
                    for b in range(nblk):
                        vi = (kb + b) if w > 0 else b
                        mm(em, pav[0:64, 0:128], v_tm[vi][:, 64 * h:64 * (h + 1)],
                           attnT[ia][:, 128 * b:128 * (b + 1)],
                           start=(b == 0), stop=(b == nblk - 1))
                    copy(em, "vector", o_fm[hj][sl, qc0:qc0 + 128],
                         pav[0:64, 0:128], wkey=o_fm[hj])
        for w in range(NTW):
            t0 = w * 512
            for pj in range(4):
                for kk in range(4):
                    wb = wbuf[kk % 4]
                    em.dma("gpsimd", out=wb[:, 0:256],
                           in_=inp["wo"][l, 128 * kk:128 * (kk + 1),
                                         256 * pj:256 * (pj + 1)],
                           reads=[wb], writes=[wb])
                    mm(em, pp[0][:, 0:512], wb[:, 0:128], o_fm[kk][:, t0:t0 + 512],
                       start=(kk == 0), stop=(kk == 3))
                    mm(em, pp[1][:, 0:512], wb[:, 128:256], o_fm[kk][:, t0:t0 + 512],
                       start=(kk == 0), stop=(kk == 3))
                for half in range(2):
                    j = 2 * pj + half
                    tt(em, "vector", Z[j][:, t0:t0 + 512],
                       pp[half][:, 0:512], Z[j][:, t0:t0 + 512], ALU.add,
                       wkey=Z[j])
        for j in range(4):
            copy(em, "vector", hs4[j][:, 1:T], Z[j + 4][:, 0:T - 1], wkey=hs4[j])
            memset(em, "vector", hs4[j][:, 0:1], 0.0, wkey=hs4[j])
        hffn = lambda j: o_fm[j // 2][:, 512 * (j % 2):512 * (j % 2) + 512]
        for w in range(NTW):
            t0 = w * 512
            emit_ln_win(nc, em, ctx, t0, 512, lambda j: hsrc(j)[:, t0:t0 + 512],
                        lambda j, emitfn: emitfn(hffn(j)),
                        (pp[0], pp[1], pp[2], pp[3]), ctx.eps5[0:1, 0:1],
                        gamma=lambda j: pvc(l, 0, j), beta=lambda j: pvc(l, 1, j),
                        lnrow=lnrow, sq2=sq2, tmpf=tmpf)
            for mp in range(22):
                for kk in range(8):
                    wb = wbuf[kk % 4]
                    em.dma("gpsimd", out=wb[:, 0:256],
                           in_=inp["w1p"][l, 128 * kk:128 * (kk + 1),
                                          256 * mp:256 * (mp + 1)],
                           reads=[wb], writes=[wb])
                    mm(em, pp[0][:, 0:512], wb[:, 0:128], hffn(kk),
                       start=(kk == 0), stop=(kk == 7))
                    mm(em, pp[1][:, 0:512], wb[:, 128:256], hffn(kk),
                       start=(kk == 0), stop=(kk == 7))
                for half in range(2):
                    m = 2 * mp + half
                    if m < 22:
                        ts(em, "vector", helu[m][:], pp[half][:, 0:512],
                           inp_b1col(em, ctx, l, m), None, ALU.add, wkey=helu[m])
                    else:
                        act(em, sq2[0][:], pp[half][:, 0:512], AF.Gelu,
                            bias=inp_b1col(em, ctx, l, m), wkey=sq2[0])
                        tt(em, "vector", helu[m - 22][:], helu[m - 22][:],
                           sq2[0][:], ALU.mult, wkey=helu[m - 22])
            for pj in range(4):
                for kk in range(22):
                    wb = wbuf[kk % 4]
                    em.dma("gpsimd", out=wb[:, 0:256],
                           in_=inp["w2p"][l, 128 * kk:128 * (kk + 1),
                                          256 * pj:256 * (pj + 1)],
                           reads=[wb], writes=[wb])
                    mm(em, pp[2][:, 0:512], wb[:, 0:128], helu[kk][:],
                       start=(kk == 0), stop=(kk == 21))
                    mm(em, pp[3][:, 0:512], wb[:, 128:256], helu[kk][:],
                       start=(kk == 0), stop=(kk == 21))
                for half in range(2):
                    j = 2 * pj + half
                    stt(em, "vector", Z[j][:, t0:t0 + 512],
                        pp[2 + half][:, 0:512], pvc(l, 2, j),
                        Z[j][:, t0:t0 + 512], ALU.add, ALU.add, wkey=Z[j])
    if ctx.fused:
        for j in range(8):
            tt(em, "vector", ctx.X[j][:], Z[j][:], ctx.X[j][:], ALU.add,
               wkey=ctx.X[j])
    else:
        for w in range(NTW):
            t0 = w * 512
            for j in range(8):
                em.dma("gpsimd", out=xt[j][:], in_=ctx.x_in[128 * j:128 * (j + 1),
                                                            t0:t0 + 512],
                       reads=[xt[j]], writes=[xt[j]])
                tt(em, "vector", xt[j][:], Z[j][:, t0:t0 + 512], xt[j][:],
                   ALU.add, wkey=xt[j])
                em.dma("gpsimd", out=ctx.y_out[128 * j:128 * (j + 1), t0:t0 + 512],
                       in_=xt[j][:], reads=[xt[j]], writes=[f"P4OUT{j}"])
    em.barrier()


def inp_b1col(em, ctx, l, m):
    return ctx.p4b1c[l][:, m:m + 1]


def prep_phase4_inputs(inputs):
    d = {}
    d["wqkv"] = np.ascontiguousarray(inputs["L1_ffn_wqkv"], np.float32)
    d["wo"] = np.ascontiguousarray(inputs["L1_ffn_wo"], np.float32)
    w1 = inputs["L1_ffn_w1"]
    w1p = np.zeros((2, 1024, 5632), np.float32)
    w1p[:, :, 0:2730] = w1[:, :, 0:2730]
    w1p[:, :, 2816:2816 + 2730] = w1[:, :, 2730:5460]
    d["w1p"] = w1p
    w2 = inputs["L1_ffn_w2"]
    w2p = np.zeros((2, 2816, 1024), np.float32)
    w2p[:, 0:2730, :] = w2
    d["w2p"] = w2p
    b1 = inputs["L1_ffn_b1"]
    b1p = np.zeros((2, 5632), np.float32)
    b1p[:, 0:2730] = b1[:, 0:2730]
    b1p[:, 2816:2816 + 2730] = b1[:, 2730:5460]
    d["b1cols4"] = np.ascontiguousarray(
        np.stack([b1p[l].reshape(44, 128).T for l in range(2)]), np.float32)
    cols = []
    for l in range(2):
        sub = _pack_pvec([inputs["L1_ffn_ln_g"][l], inputs["L1_ffn_ln_b"][l],
                          inputs["L1_ffn_b2"][l]])
        qs = np.tile(inputs["L1_ffn_q_scale"][l], 8).reshape(4, 128).T
        ks = np.tile(inputs["L1_ffn_k_scale"][l], 8).reshape(4, 128).T
        cols.append(np.concatenate([sub, qs, ks], axis=1))
    d["pvec4"] = np.ascontiguousarray(np.concatenate(cols, axis=1), np.float32)
    return d


PHASE4_TENSORS = [
    ("wqkv", [2, 1024, 1536]), ("wo", [2, 512, 1024]), ("w1p", [2, 1024, 5632]),
    ("w2p", [2, 2816, 1024]), ("b1cols4", [2, 128, 44]), ("pvec4", [128, 64]),
]


def build_nc_phase4(T=2048, debug=False):
    from contextlib import ExitStack
    nc = bass.Bass("TRN2", target_bir_lowering=False)
    x_in = nc.dram_tensor("x_fm", [1024, T], F32, kind="ExternalInput").ap()
    y_out = nc.dram_tensor("y_fm", [1024, T], F32, kind="ExternalOutput").ap()
    inp = {nm: nc.dram_tensor(nm, shp, F32, kind="ExternalInput").ap()
           for nm, shp in PHASE4_TENSORS}
    em = Em(nc)
    ctx = Ctx()
    ctx.fused = False
    ctx.debug = debug
    ctx.dbg_done = set()
    ctx.x_in = x_in
    ctx.y_out = y_out
    with ExitStack() as st:
        E = st.enter_context
        if debug:
            ctx.dbg_stage = E(nc.sbuf_tensor("dbgstage", [128, 1024], F32))
        setup_common(nc, em, ctx, E, T)
        ctx.p4b1c = [E(nc.sbuf_tensor(f"p4b1c{l}", [128, 44], F32)) for l in range(2)]
        for l in range(2):
            em.dma("sync", out=ctx.p4b1c[l][:], in_=inp["b1cols4"][l],
                   writes=[ctx.p4b1c[l]])
        build_phase4(nc, em, ctx, E, inp)
        em.build()
    print("phase4: ops=%d" % len(em.ops))
    return nc


def _pack_pvec(vectors):
    cols = [np.ascontiguousarray(v.reshape(8, 128).T) for v in vectors]
    return np.ascontiguousarray(np.concatenate(cols, axis=1).astype(np.float32))


def prep_phase1_inputs(inputs):
    d = {}
    d["Wr"] = np.ascontiguousarray(inputs["L0_attn_Wr"], np.float32)
    d["Wk"] = np.ascontiguousarray(inputs["L0_attn_Wk"], np.float32)
    d["Wv"] = np.ascontiguousarray(inputs["L0_attn_Wv"], np.float32)
    d["Wg"] = np.ascontiguousarray(inputs["L0_attn_Wg"], np.float32)
    d["Wo"] = np.ascontiguousarray(inputs["L0_attn_Wo"], np.float32)
    d["tm_w1"] = np.ascontiguousarray(inputs["L0_attn_tm_w1"], np.float32)
    d["tm_w2"] = np.ascontiguousarray(inputs["L0_attn_tm_w2"], np.float32)
    d["td_w1"] = np.ascontiguousarray(inputs["L0_attn_td_w1"], np.float32)
    d["td_w2"] = np.ascontiguousarray(inputs["L0_attn_td_w2"], np.float32)
    d["pvec"] = _pack_pvec([
        inputs["L0_attn_x_maa"], inputs["L0_attn_w_maa"], inputs["L0_attn_k_maa"],
        inputs["L0_attn_v_maa"], inputs["L0_attn_r_maa"], inputs["L0_attn_g_maa"],
        inputs["L0_attn_time_decay"].reshape(-1), inputs["L0_attn_time_first"].reshape(-1),
        inputs["L0_attn_ln_w"], inputs["L0_attn_ln_b"]])
    d["ln_w"] = np.ascontiguousarray(inputs["L0_attn_ln_w"], np.float32)
    d["ln_b"] = np.ascontiguousarray(inputs["L0_attn_ln_b"], np.float32)
    return d


PHASE1_TENSORS = [
    ("Wr", [1024, 1024]), ("Wk", [1024, 1024]), ("Wv", [1024, 1024]),
    ("Wg", [1024, 1024]), ("Wo", [1024, 1024]), ("tm_w1", [1024, 160]),
    ("tm_w2", [5, 32, 1024]), ("td_w1", [1024, 64]), ("td_w2", [64, 1024]),
    ("pvec", [128, 80]), ("ln_w", [1024]), ("ln_b", [1024]),
]


def build_nc_phase1(T=2048, debug=False):
    from contextlib import ExitStack
    nc = bass.Bass("TRN2", target_bir_lowering=False)
    x_in = nc.dram_tensor("x_fm", [1024, T], F32, kind="ExternalInput").ap()
    y_out = nc.dram_tensor("y_fm", [1024, T], F32, kind="ExternalOutput").ap()
    inp = {nm: nc.dram_tensor(nm, shp, F32, kind="ExternalInput").ap()
           for nm, shp in PHASE1_TENSORS}
    em = Em(nc)
    ctx = Ctx()
    ctx.fused = False
    ctx.debug = debug
    ctx.dbg_done = set()
    ctx.x_in = x_in
    ctx.y_out = y_out
    with ExitStack() as st:
        E = st.enter_context
        if debug:
            ctx.dbg_stage = E(nc.sbuf_tensor("dbgstage", [128, 1024], F32))
        setup_common(nc, em, ctx, E, T)
        build_phase1(nc, em, ctx, E, inp)
        em.build()
    print("phase1: ops=%d sbuf_left=%d psum_left=%d" %
          (len(em.ops), nc.sbuf_bytes_remaining, nc.psum_banks_remaining))
    return nc


def build_nc_fused(T=2048, ncores=8):
    from contextlib import ExitStack
    nc = bass.Bass("TRN2", target_bir_lowering=False)
    x_in = nc.dram_tensor("x_fm", [1024, T], F32, kind="ExternalInput").ap()
    y_out = nc.dram_tensor("y_fm", [1024, T], F32, kind="ExternalOutput").ap()
    xs = [nc.dram_tensor(f"xs{i}", [1024, T], F32, kind="Internal").ap()
          for i in range(3)]
    tens = [PHASE1_TENSORS, PHASE2_TENSORS, PHASE3_TENSORS, PHASE4_TENSORS]
    inps = []
    for ph in range(4):
        inps.append({nm: nc.dram_tensor(f"p{ph + 1}_{nm}", shp, F32,
                                        kind="ExternalInput").ap()
                     for nm, shp in tens[ph]})
    em = Em(nc)
    ctx = Ctx()
    ctx.ncores = ncores
    ctx.moe_sparse = False
    ctx.fused = False
    ctx.debug = False
    ctx.dbg_done = set()
    with ExitStack() as outer:
        E0 = outer.enter_context
        setup_common(nc, em, ctx, E0, T)
        em.barrier()
        ctx.x_in, ctx.y_out = x_in, xs[0]
        with ExitStack() as s:
            build_phase1(nc, em, ctx, s.enter_context, inps[0])
        with ExitStack() as s:
            E = s.enter_context
            ctx.X = alloc_stream(nc, E, T)
            load_stream(em, ctx, xs[0])
            em.barrier()
            build_phase2(nc, em, ctx, E, inps[1])
            for j in range(8):
                em.dma("gpsimd", out=xs[1][128 * j:128 * (j + 1), :],
                       in_=ctx.X[j][:], reads=[ctx.X[j]], writes=[f"XS1_{j}"])
            em.barrier()
        ctx.x_in, ctx.y_out = xs[1], xs[2]
        with ExitStack() as s:
            build_phase3(nc, em, ctx, s.enter_context, inps[2])
        ctx.x_in, ctx.y_out = xs[2], y_out
        with ExitStack() as s:
            E = s.enter_context
            ctx.p4b1c = [E(nc.sbuf_tensor(f"p4b1c{l}", [128, 44], F32))
                         for l in range(2)]
            for l in range(2):
                em.dma("sync", out=ctx.p4b1c[l][:], in_=inps[3]["b1cols4"][l],
                       writes=[ctx.p4b1c[l]])
            build_phase4(nc, em, ctx, E, inps[3])
        em.build()
    print("fused: ops=%d" % len(em.ops))
    return nc


def kernel(**inputs):
    T = 2048
    x = np.asarray(inputs["L0_attn_x"], np.float32)
    preps = [prep_phase1_inputs(inputs), prep_phase2_inputs(inputs),
             prep_phase3_inputs(inputs), prep_phase4_inputs(inputs)]
    nc = build_nc_fused(T, NCORE)
    base = {}
    for ph in range(4):
        for k, v in preps[ph].items():
            base[f"p{ph + 1}_{k}"] = v
    in_maps = []
    for c in range(NCORE):
        m = dict(base)
        m["x_fm"] = np.ascontiguousarray(x[c].T)
        in_maps.append(m)
    res = run_bass_kernel_spmd(nc, in_maps, core_ids=list(range(NCORE)))
    out = np.stack([np.ascontiguousarray(np.asarray(r["y_fm"]).T)
                    for r in res.results])
    return out.astype(np.float32)


# version 18
# speedup vs baseline: 1.2011x; 1.0519x over previous
import os
import numpy as np
import ml_dtypes
import concourse.bass as bass
import concourse.mybir as mybir
from concourse.bass_utils import run_bass_kernel_spmd

F32 = mybir.dt.float32
BF16 = mybir.dt.bfloat16
AF = mybir.ActivationFunctionType
ALU = mybir.AluOpType
AX = mybir.AxisListType
BF16NP = ml_dtypes.bfloat16

D = 1024
NCORE = 8
CENG = ("sync", "tensor", "vector", "scalar", "gpsimd")


class Em:
    def __init__(self, nc):
        self.nc = nc
        self.ops = []
        self.last_w = {}
        self.readers = {}

    @staticmethod
    def _k(h):
        if isinstance(h, str):
            return h
        name = getattr(h, "name", None)
        if name is None:
            name = h.tensor.name
        return name

    def op(self, eng, fn, reads=(), writes=(), dma=False, writes_accum=()):
        if not hasattr(self, "accum"):
            self.accum = {}
        i = len(self.ops)
        deps = set()
        for h in reads:
            k = self._k(h)
            j = self.last_w.get(k)
            if j is not None:
                deps.add(j)
            deps.update(self.accum.get(k, ()))
        for h in writes:
            k = self._k(h)
            j = self.last_w.get(k)
            if j is not None:
                deps.add(j)
            deps.update(self.readers.get(k, ()))
            deps.update(self.accum.get(k, ()))
        for h in writes_accum:
            k = self._k(h)
            j = self.last_w.get(k)
            if j is not None:
                deps.add(j)
            deps.update(self.readers.get(k, ()))
        o = {"i": i, "eng": eng, "fn": fn, "dma": bool(dma), "deps": deps,
             "inc": bool(dma), "bar": False}
        self.ops.append(o)
        for h in reads:
            self.readers.setdefault(self._k(h), set()).add(i)
        for h in writes:
            k = self._k(h)
            self.last_w[k] = i
            self.readers[k] = set()
            self.accum[k] = set()
        for h in writes_accum:
            self.accum.setdefault(self._k(h), set()).add(i)
        return o

    def barrier(self):
        self.ops.append({"i": len(self.ops), "eng": None, "fn": None,
                         "dma": False, "deps": set(), "inc": False, "bar": True})
        self.last_w = {}
        self.readers = {}
        self.accum = {}

    def dma(self, eng, out, in_, reads=(), writes=(), writes_accum=(), **kw):
        return self.op(eng,
                       lambda e, out=out, in_=in_, kw=kw: e.dma_start(out=out, in_=in_, **kw),
                       reads=reads, writes=writes, writes_accum=writes_accum, dma=True)

    def build(self):
        NQ = 8
        ops = self.ops
        if not (ops and ops[-1]["bar"]):
            self.barrier()
            ops = self.ops
        for o in ops:
            if o["bar"]:
                continue
            for d in o["deps"]:
                p = ops[d]
                if (p["eng"] != o["eng"]) or p["dma"]:
                    p["inc"] = True
        lastc = {e: None for e in CENG}
        for o in ops:
            if o["bar"]:
                for e in CENG:
                    if lastc[e] is not None:
                        ops[lastc[e]]["inc"] = True
                continue
            if not o["dma"]:
                lastc[o["eng"]] = o["i"]
        ccnt = {e: 0 for e in CENG}
        dcnt = {}
        dseq = {e: 0 for e in CENG}
        for o in ops:
            if o["bar"]:
                o["floor"] = (dict(ccnt), dict(dcnt))
                continue
            if o["dma"]:
                slot = dseq[o["eng"]] % NQ
                dseq[o["eng"]] += 1
                key = (o["eng"], slot)
                o["qkey"] = key
                o["prev"] = dcnt.get(key, 0) * 16
                dcnt[key] = dcnt.get(key, 0) + 1
                o["cval"] = dcnt[key] * 16
            elif o["inc"]:
                ccnt[o["eng"]] += 1
                o["cval"] = ccnt[o["eng"]]

        nc = self.nc
        csem = {e: nc.alloc_semaphore(f"c_{e}") for e in CENG}
        qsem = {(e, s): nc.alloc_semaphore(f"q_{e}{s}") for e in CENG for s in range(NQ)}

        def emit_engine(e_obj, ename):
            wm_c = {x: 0 for x in CENG}
            wm_d = {}
            drained_upto = -1
            last_own = -1
            pend_floor = None

            def wait_d(key, v):
                if v > wm_d.get(key, 0):
                    e_obj.wait_ge(qsem[key], v)
                    wm_d[key] = v

            def do_floor():
                nonlocal drained_upto, pend_floor
                fc, fd = pend_floor
                for x in CENG:
                    if fc[x] > wm_c[x]:
                        e_obj.wait_ge(csem[x], fc[x])
                        wm_c[x] = fc[x]
                for key, n in fd.items():
                    wait_d(key, n * 16)
                drained_upto = last_own
                pend_floor = None

            for o in ops:
                if o["bar"]:
                    pend_floor = o["floor"]
                    continue
                if o["eng"] != ename:
                    continue
                if pend_floor is not None:
                    do_floor()
                need_drain = False
                for d in sorted(o["deps"]):
                    p = ops[d]
                    if p["dma"]:
                        wait_d(p["qkey"], p["cval"])
                    elif p["eng"] == ename:
                        if ename != "tensor" and p["i"] > drained_upto:
                            need_drain = True
                    else:
                        v = p["cval"]
                        if v > wm_c[p["eng"]]:
                            e_obj.wait_ge(csem[p["eng"]], v)
                            wm_c[p["eng"]] = v
                if o["dma"] and o["prev"] > 0:
                    wait_d(o["qkey"], o["prev"])
                if need_drain:
                    e_obj.drain()
                    drained_upto = last_own
                try:
                    inst = o["fn"](e_obj)
                except Exception:
                    import traceback as _tb
                    print("FAILING OP idx", o["i"], "eng", o["eng"])
                    _tb.print_exc()
                    raise
                if o["dma"]:
                    inst.then_inc(qsem[o["qkey"]], 16)
                elif o["inc"]:
                    inst.then_inc(csem[o["eng"]], 1)
                last_own = o["i"]
            if pend_floor is not None:
                do_floor()

        with nc.Block() as block:
            for ename in CENG:
                getattr(block, ename)(lambda e, ename=ename: emit_engine(e, ename))


def tt(em, eng, out, in0, in1, op, wkey=None):
    em.op(eng, lambda e, out=out, in0=in0, in1=in1, op=op: e.tensor_tensor(out, in0, in1, op),
          reads=[in0, in1], writes=[wkey if wkey is not None else out])


def ts(em, eng, out, in0, s1, s2, op0, op1=None, wkey=None):
    def fn(e, out=out, in0=in0, s1=s1, s2=s2, op0=op0, op1=op1):
        if op1 is None:
            return e.tensor_single_scalar(out, in0, s1, op0)
        return e.tensor_scalar(out, in0, s1, s2, op0, op1)
    rd = [in0]
    for s in (s1, s2):
        if hasattr(s, "tensor"):
            rd.append(s)
    em.op(eng, fn, reads=rd, writes=[wkey if wkey is not None else out])


def stt(em, eng, out, in0, scalar, in1, op0, op1, wkey=None):
    rd = [in0, in1]
    if hasattr(scalar, "tensor"):
        rd.append(scalar)
    em.op(eng, lambda e, out=out, in0=in0, scalar=scalar, in1=in1, op0=op0, op1=op1:
          e.scalar_tensor_tensor(out, in0, scalar, in1, op0, op1),
          reads=rd, writes=[wkey if wkey is not None else out])


def act(em, out, in_, func, bias=0.0, scale=1.0, wkey=None):
    rd = [in_]
    if hasattr(bias, "tensor"):
        rd.append(bias)
    em.op("scalar", lambda e, out=out, in_=in_, func=func, bias=bias, scale=scale:
          e.activation(out, in_, func, bias=bias, scale=scale),
          reads=rd, writes=[wkey if wkey is not None else out])


def mm(em, out, lhsT, rhs, start=True, stop=True):
    em.op("tensor", lambda e, out=out, lhsT=lhsT, rhs=rhs, start=start, stop=stop:
          e.matmul(out, lhsT, rhs, start=start, stop=stop),
          reads=[lhsT, rhs], writes=[out])


def transp(em, out, in_, ident):
    em.op("tensor", lambda e, out=out, in_=in_, ident=ident: e.transpose(out, in_, ident),
          reads=[in_, ident], writes=[out])


def copy(em, eng, out, in_, wkey=None):
    em.op(eng, lambda e, out=out, in_=in_: e.tensor_copy(out, in_),
          reads=[in_], writes=[wkey if wkey is not None else out])


def scopy(em, out, in_, wkey=None):
    em.op("scalar", lambda e, out=out, in_=in_: e.copy(out, in_),
          reads=[in_], writes=[wkey if wkey is not None else out])


def memset(em, eng, ap_, val, wkey):
    em.op(eng, lambda e, ap_=ap_, val=val: e.memset(ap_, val), writes=[wkey])


def reduce(em, out, in_, op, eng="vector", wkey=None):
    em.op(eng, lambda e, out=out, in_=in_, op=op: e.tensor_reduce(out, in_, AX.X, op),
          reads=[in_], writes=[wkey if wkey is not None else out])


def recip(em, out, in_, wkey=None):
    em.op("vector", lambda e, out=out, in_=in_: e.reciprocal(out, in_),
          reads=[in_], writes=[wkey if wkey is not None else out])


def scan(em, out, d0, d1, init, op0, op1):
    em.op("vector", lambda e, out=out, d0=d0, d1=d1, init=init, op0=op0, op1=op1:
          e.tensor_tensor_scan(out, d0, d1, init, op0, op1),
          reads=[d0, d1], writes=[out])


class Ctx:
    pass


def setup_common(nc, em, ctx, E, T):
    ctx.T = T
    eye_d = nc.inline_tensor(np.eye(128).astype(BF16NP), "eye_bf_d")
    ctx.eye = E(nc.sbuf_tensor("eye", [128, 128], BF16))
    em.dma("sync", out=ctx.eye[:], in_=eye_d.ap(), writes=[ctx.eye])
    ones_d = nc.inline_tensor(np.ones((1, 128), np.float32), "ones_f32_d")
    ctx.ones_f = E(nc.sbuf_tensor("ones_f", [1, 128], F32))
    em.dma("sync", out=ctx.ones_f[:], in_=ones_d.ap(), writes=[ctx.ones_f])
    onesb_d = nc.inline_tensor(np.ones((128, 1), BF16NP), "ones_col_bf_d")
    ctx.ones_col_bf = E(nc.sbuf_tensor("ones_col_bf", [128, 1], BF16))
    em.dma("sync", out=ctx.ones_col_bf[:], in_=onesb_d.ap(), writes=[ctx.ones_col_bf])
    ones1b_d = nc.inline_tensor(np.ones((1, 128), BF16NP), "ones_row_bf_d")
    ctx.ones_row_bf = E(nc.sbuf_tensor("ones_row_bf", [1, 128], BF16))
    em.dma("sync", out=ctx.ones_row_bf[:], in_=ones1b_d.ap(), writes=[ctx.ones_row_bf])
    ctx.zeros = E(nc.sbuf_tensor("zeros", [128, 512], F32))
    memset(em, "vector", ctx.zeros[:], 0.0, wkey=ctx.zeros)
    ctx.eps5 = E(nc.sbuf_tensor("eps5", [128, 1], F32))
    memset(em, "vector", ctx.eps5[:], 1e-5, wkey=ctx.eps5)
    ctx.eps6 = E(nc.sbuf_tensor("eps6", [128, 1], F32))
    memset(em, "vector", ctx.eps6[:], 1e-6, wkey=ctx.eps6)


def alloc_stream(nc, E, T):
    return [E(nc.sbuf_tensor(f"X{j}", [128, T], F32)) for j in range(8)]


def load_stream(em, ctx, x_fm_ap):
    for j in range(8):
        em.dma("sync", out=ctx.X[j][:], in_=x_fm_ap[128 * j:128 * (j + 1), :],
               writes=[ctx.X[j]])


def store_stream(em, ctx, y_fm_ap):
    for j in range(8):
        em.dma("gpsimd", out=y_fm_ap[128 * j:128 * (j + 1), :], in_=ctx.X[j][:],
               reads=[ctx.X[j]], writes=["OUT_y%d" % j])


def cast_weight(nc, em, ctx, E, name, dram_ap, nrow, ncol, stage=None):
    chunks = []
    for c in range(nrow // 128):
        tile = E(nc.sbuf_tensor(f"{name}_{c}", [128, ncol], BF16))
        em.dma("gpsimd", out=tile[:], in_=dram_ap[128 * c:128 * (c + 1), :],
               reads=[tile], writes=[tile])
        chunks.append(tile)
    return chunks


def dbg(em, ctx, name, tile, rows, cols):
    if not getattr(ctx, "debug", False) or name in ctx.dbg_done:
        return
    ctx.dbg_done.add(name)
    d = em.nc.dram_tensor("dbg_" + name, [rows, cols], F32, kind="ExternalOutput").ap()
    t = ctx.dbg_stage
    copy(em, "vector", t[0:rows, 0:cols], tile, wkey=t)
    em.dma("gpsimd", out=d, in_=t[0:rows, 0:cols], reads=[t], writes=["DBG" + name])


def build_phase1(nc, em, ctx, E, inp):
    T = ctx.T
    TT_ = 256
    L = 64
    NT = T // TT_
    NCH = TT_ // L
    NSUB = TT_ // 128

    P = [E(nc.psum_tensor(f"p1P{i}", [128, 512], F32)) for i in range(6)]
    P7 = E(nc.psum_tensor("p1P7", [128, 1024], BF16))

    tm_w1 = cast_weight(nc, em, ctx, E, "tmw1", inp["tm_w1"], 1024, 160)
    td_w1 = cast_weight(nc, em, ctx, E, "tdw1", inp["td_w1"], 1024, 64)
    tm_w2 = []
    for f in range(5):
        tile = E(nc.sbuf_tensor(f"tmw2_{f}", [32, 1024], BF16))
        em.dma("gpsimd", out=tile[:], in_=inp["tm_w2"][f], reads=[tile], writes=[tile])
        tm_w2.append(tile)
    td_w2 = E(nc.sbuf_tensor("tdw2", [64, 1024], BF16))
    em.dma("gpsimd", out=td_w2[:], in_=inp["td_w2"], reads=[td_w2], writes=[td_w2])

    pv = E(nc.sbuf_tensor("p1pv", [128, 80], F32))
    em.dma("sync", out=pv[:], in_=inp["pvec"], writes=[pv])

    def pvcol(idx, j, sl=None):
        if sl is not None:
            return pv[sl, idx * 8 + j:idx * 8 + j + 1]
        return pv[:, idx * 8 + j:idx * 8 + j + 1]
    IX, IW, IK, IV, IR, IG, ITD, ITF, ILW, ILB = range(10)

    lnrow_st = E(nc.sbuf_tensor("p1lnr", [1, 1024], BF16))
    ones164 = E(nc.sbuf_tensor("ones164", [1, 64], BF16))
    memset(em, "vector", ones164[:], 1.0, wkey=ones164)
    lnw_b = E(nc.sbuf_tensor("lnw_b", [64, 1024], BF16))
    lnb_b = E(nc.sbuf_tensor("lnb_b", [64, 1024], BF16))
    for dst, nm in ((lnw_b, "ln_w"), (lnb_b, "ln_b")):
        em.dma("gpsimd", out=lnrow_st[0:1, :], in_=inp[nm][None, :],
               reads=[lnrow_st], writes=[lnrow_st])
        mm(em, P[0][0:64, 0:512], ones164[:], lnrow_st[0:1, 0:512])
        mm(em, P[1][0:64, 0:512], ones164[:], lnrow_st[0:1, 512:1024])
        copy(em, "vector", dst[:, 0:512], P[0][0:64, 0:512], wkey=dst)
        copy(em, "vector", dst[:, 512:1024], P[1][0:64, 0:512], wkey=dst)

    mu_d = nc.inline_tensor(np.triu(np.ones((L, L), np.float32), 1).astype(BF16NP),
                            "maskU64")
    maskU = E(nc.sbuf_tensor("maskU", [L, L], BF16))
    em.dma("sync", out=maskU[:], in_=mu_d.ap(), writes=[maskU])

    em.barrier()

    xt = [E(nc.sbuf_tensor(f"p1x{j}", [128, TT_], F32)) for j in range(8)]
    xn_bf = [E(nc.sbuf_tensor(f"xn{j}", [128, TT_], BF16)) for j in range(8)]
    sx_bf = [E(nc.sbuf_tensor(f"sx{j}", [128, TT_], BF16)) for j in range(8)]
    u_p = [E(nc.sbuf_tensor(f"up{j}", [128, TT_], BF16)) for j in range(8)]
    u_g = [E(nc.sbuf_tensor(f"ug{j}", [128, TT_], BF16)) for j in range(8)]
    xn_last = E(nc.sbuf_tensor("xnlast", [128, 8], BF16))
    memset(em, "vector", xn_last[:], 0.0, wkey=xn_last)
    sq2 = [E(nc.sbuf_tensor(f"p1sq{i}", [128, TT_], BF16)) for i in range(2)]
    tmpf = [E(nc.sbuf_tensor(f"p1tmp{i}", [128, TT_], F32)) for i in range(2)]
    lnrow = [E(nc.sbuf_tensor(f"p1row{i}", [1, TT_], F32)) for i in range(4)]
    tm_f = [E(nc.sbuf_tensor(f"tmf{f}", [32, TT_], BF16)) for f in range(5)]
    tdh = E(nc.sbuf_tensor("tdh", [64, TT_], BF16))
    exp_w = E(nc.sbuf_tensor("expw", [128, TT_], F32))
    negew = E(nc.sbuf_tensor("negew", [128, TT_], F32))
    c_rel = [E(nc.sbuf_tensor(f"crel{j}", [128, TT_], F32)) for j in range(8)]
    E1 = [E(nc.sbuf_tensor(f"E1_{j}", [128, TT_], BF16)) for j in range(8)]
    E2 = [E(nc.sbuf_tensor(f"E2_{j}", [128, TT_], BF16)) for j in range(8)]
    E4h = [E(nc.sbuf_tensor(f"E4h{h}", [64, NCH], F32)) for h in range(16)]
    r_t = [E(nc.sbuf_tensor(f"rt{h}", [64, TT_], BF16)) for h in range(16)]
    r_raw = [E(nc.sbuf_tensor(f"rr{h}", [64, TT_], BF16)) for h in range(16)]
    k_t = [E(nc.sbuf_tensor(f"kt{h}", [64, TT_], BF16)) for h in range(16)]
    ku = [E(nc.sbuf_tensor(f"kuv{h}", [64, TT_], BF16)) for h in range(16)]
    kh_tmp = [E(nc.sbuf_tensor(f"khtmp{i}", [64, 64], BF16)) for i in range(2)]
    khT = [E(nc.sbuf_tensor(f"khT{c}", [64, 1024], BF16)) for c in range(NCH)]
    v_tm = [E(nc.sbuf_tensor(f"vtm{c}", [64, 1024], BF16)) for c in range(NCH)]
    g_tm = [E(nc.sbuf_tensor(f"gtm{c}", [64, 1024], BF16)) for c in range(NCH)]
    S_f = [E(nc.sbuf_tensor(f"Sf{h}", [64, 64], F32)) for h in range(16)]
    S_b = [E(nc.sbuf_tensor(f"Sb{h}", [64, 64], BF16)) for h in range(16)]
    for h in range(16):
        memset(em, "vector", S_f[h][:], 0.0, wkey=S_f[h])
        memset(em, "vector", S_b[h][:], 0.0, wkey=S_b[h])
    At = [E(nc.sbuf_tensor(f"At{i}", [64, 64], BF16)) for i in range(2)]
    At2 = [E(nc.sbuf_tensor(f"At2{i}", [64, 64], F32)) for i in range(2)]
    y_sb = E(nc.sbuf_tensor("ysb", [64, 1024], F32))
    gn_sq = E(nc.sbuf_tensor("gnsq", [64, 1024], F32))
    gn_mu = E(nc.sbuf_tensor("gnmu", [64, 16], F32))
    gn_s2 = E(nc.sbuf_tensor("gns2", [64, 16], F32))
    gn_rs = E(nc.sbuf_tensor("gnrs", [64, 16], F32))
    y2 = [E(nc.sbuf_tensor(f"y2_{c}", [64, 1024], BF16)) for c in range(NCH)]
    y_fm = [E(nc.sbuf_tensor(f"yfm{j}", [128, TT_], BF16)) for j in range(8)]
    wbuf = [E(nc.sbuf_tensor(f"p1wb{i}", [128, 256], BF16)) for i in range(4)]
    wbig = [E(nc.sbuf_tensor(f"p1wB{i}", [128, 512], BF16)) for i in range(4)]

    def proj_std(wname, consumer):
        for pj in range(4):
            for kk in range(8):
                wb = wbuf[kk % 4]
                em.dma("gpsimd", out=wb[:],
                       in_=inp[wname][128 * kk:128 * (kk + 1), 256 * pj:256 * (pj + 1)],
                       reads=[wb], writes=[wb])
                mm(em, P[2][:, 0:TT_], wb[:, 0:128], u_p[kk][:],
                   start=(kk == 0), stop=(kk == 7))
                mm(em, P[3][:, 0:TT_], wb[:, 128:256], u_p[kk][:],
                   start=(kk == 0), stop=(kk == 7))
            consumer(2 * pj, P[2])
            consumer(2 * pj + 1, P[3])

    def proj_flip(wname, src, consumer):
        for nh in range(2):
            for kk in range(8):
                wb = wbig[kk % 4]
                em.dma("gpsimd", out=wb[:],
                       in_=inp[wname][128 * kk:128 * (kk + 1), 512 * nh:512 * (nh + 1)],
                       reads=[wb], writes=[wb])
                for s in range(NSUB):
                    mm(em, P[2 + s][:, 0:512], src[kk][:, 128 * s:128 * (s + 1)], wb[:],
                       start=(kk == 0), stop=(kk == 7))
            for s in range(NSUB):
                consumer(s, nh, P[2 + s])

    for it in range(NT):
        t0 = it * TT_
        if ctx.fused:
            for j in range(8):
                copy(em, "vector", xt[j][:], ctx.X[j][:, t0:t0 + TT_], wkey=xt[j])
        else:
            for j in range(8):
                em.dma("sync", out=xt[j][:], in_=ctx.x_in[128 * j:128 * (j + 1), t0:t0 + TT_],
                       reads=[xt[j]], writes=[xt[j]])
        for j in range(8):
            copy(em, "vector", xn_bf[j][:], xt[j][:], wkey=xn_bf[j])
            act(em, sq2[j % 2][:], xn_bf[j][:], AF.Square, wkey=sq2[j % 2])
            mm(em, P[4][0:1, 0:TT_], ctx.ones_col_bf[:], xn_bf[j][:],
               start=(j == 0), stop=(j == 7))
            mm(em, P[5][0:1, 0:TT_], ctx.ones_col_bf[:], sq2[j % 2][:],
               start=(j == 0), stop=(j == 7))
        mean, msq, var, rstd = lnrow
        ts(em, "vector", mean[:], P[4][0:1, 0:TT_], 1.0 / 1024, None, ALU.mult)
        ts(em, "vector", msq[:], P[5][0:1, 0:TT_], 1.0 / 1024, None, ALU.mult)
        tt(em, "vector", var[:], mean[:], mean[:], ALU.mult)
        tt(em, "vector", var[:], msq[:], var[:], ALU.subtract)
        act(em, rstd[:], var[:], AF.Sqrt, bias=ctx.eps5[0:1, 0:1], wkey=rstd)
        recip(em, rstd[:], rstd[:])
        mm(em, P[0][:, 0:TT_], ctx.ones_f[:], mean[:])
        mm(em, P[1][:, 0:TT_], ctx.ones_f[:], rstd[:])
        for j in range(8):
            tf = tmpf[j % 2]
            tt(em, "vector", tf[:], xt[j][:], P[0][:, 0:TT_], ALU.subtract, wkey=tf)
            tt(em, "vector", xn_bf[j][:], tf[:], P[1][:, 0:TT_], ALU.mult)
        dbg(em, ctx, "xn0", xn_bf[0][:], 128, TT_)
        for j in range(8):
            tt(em, "vector", sx_bf[j][:, 1:TT_], xn_bf[j][:, 0:TT_ - 1],
               xn_bf[j][:, 1:TT_], ALU.subtract, wkey=sx_bf[j])
            tt(em, "vector", sx_bf[j][:, 0:1], xn_last[:, j:j + 1],
               xn_bf[j][:, 0:1], ALU.subtract, wkey=sx_bf[j])
            copy(em, "vector", xn_last[:, j:j + 1], xn_bf[j][:, TT_ - 1:TT_],
                 wkey=xn_last)
        for j in range(8):
            stt(em, "vector", u_p[j][:],
                sx_bf[j][:], pvcol(IX, j), xn_bf[j][:], ALU.mult, ALU.add)
        for j in range(8):
            mm(em, P[2][0:96, 0:TT_], tm_w1[j][:, 0:96], u_p[j][:],
               start=(j == 0), stop=(j == 7))
            mm(em, P[3][0:64, 0:TT_], tm_w1[j][:, 96:160], u_p[j][:],
               start=(j == 0), stop=(j == 7))
        for f in range(3):
            act(em, tm_f[f][:], P[2][32 * f:32 * (f + 1), 0:TT_], AF.Tanh,
                wkey=tm_f[f])
        for f in range(2):
            act(em, tm_f[3 + f][:], P[3][32 * f:32 * (f + 1), 0:TT_], AF.Tanh,
                wkey=tm_f[3 + f])

        def emit_path(f_idx, maa_idx, dest):
            for j in range(8):
                pm = P[j % 2]
                mm(em, pm[:, 0:TT_], tm_w2[f_idx][:, 128 * j:128 * (j + 1)],
                   tm_f[f_idx][:])
                tf = tmpf[j % 2]
                ts(em, "vector", tf[:], pm[:, 0:TT_], pvcol(maa_idx, j), None,
                   ALU.add, wkey=tf)
                tt(em, "vector", tf[:], tf[:], sx_bf[j][:], ALU.mult, wkey=tf)
                tt(em, "vector", dest[j][:], tf[:], xn_bf[j][:], ALU.add)

        dbg(em, ctx, "sx0", sx_bf[0][:], 128, TT_)
        dbg(em, ctx, "tmf0", tm_f[0][:], 32, TT_)
        dbg(em, ctx, "tmf4", tm_f[4][:], 32, TT_)
        emit_path(0, IW, u_p)
        dbg(em, ctx, "uw0", u_p[0][:], 128, TT_)
        for j in range(8):
            mm(em, P[4][0:64, 0:TT_], td_w1[j][:, 0:64], u_p[j][:],
               start=(j == 0), stop=(j == 7))
        act(em, tdh[:], P[4][0:64, 0:TT_], AF.Tanh, wkey=tdh)
        for j in range(8):
            pm = P[j % 2]
            mm(em, pm[:, 0:TT_], td_w2[:, 128 * j:128 * (j + 1)], tdh[:])
            act(em, exp_w[:], pm[:, 0:TT_], AF.Exp, bias=pvcol(ITD, j), wkey=exp_w)
            ts(em, "vector", negew[:], exp_w[:], -1.0, None, ALU.mult, wkey=negew)
            scan(em, c_rel[j][:], negew[:], ctx.zeros[:, 0:TT_], 0.0,
                 ALU.add, ALU.add)
            for c in range(NCH - 1, 0, -1):
                cs = c * L
                ts(em, "vector", c_rel[j][:, cs:cs + L], c_rel[j][:, cs:cs + L],
                   c_rel[j][:, cs - 1:cs], None, ALU.subtract, wkey=c_rel[j])
            tf = tmpf[j % 2]
            tt(em, "vector", tf[:], c_rel[j][:], exp_w[:], ALU.add, wkey=tf)
            act(em, E1[j][:], tf[:], AF.Exp, wkey=E1[j])
            ts(em, "vector", tf[:], c_rel[j][:], -1.0, 70.0, ALU.mult, ALU.min,
               wkey=tf)
            act(em, E2[j][:], tf[:], AF.Exp, wkey=E2[j])
            em.op("scalar", lambda e, j=j: e.activation(
                E4h[2 * j][:, 0:NCH], c_rel[j][0:64, L - 1:TT_:L], AF.Exp),
                reads=[c_rel[j]], writes=[E4h[2 * j]])
            em.op("scalar", lambda e, j=j: e.activation(
                E4h[2 * j + 1][:, 0:NCH], c_rel[j][64:128, L - 1:TT_:L], AF.Exp),
                reads=[c_rel[j]], writes=[E4h[2 * j + 1]])

        dbg(em, ctx, "expw7", exp_w[:], 128, TT_)
        dbg(em, ctx, "crel0", c_rel[0][:], 128, TT_)
        dbg(em, ctx, "E1_0", E1[0][:], 128, TT_)
        dbg(em, ctx, "E2_0", E2[0][:], 128, TT_)
        dbg(em, ctx, "E4_0", E4h[0][:], 64, NCH)
        emit_path(1, IK, u_p)

        def k_consumer(j, pp):
            for half in range(2):
                h = 2 * j + half
                sl = slice(64 * half, 64 * (half + 1))
                tt(em, "vector", k_t[h][:], pp[sl, 0:TT_], E1[j][sl, :] if False
                   else E2[j][sl, :], ALU.mult)
                ts(em, "vector", ku[h][:], pp[sl, 0:TT_], pvcol(ITF, j, sl), None,
                   ALU.mult)
        proj_std("Wk", k_consumer)
        for c in range(NCH):
            for h in range(16):
                j, half = h // 2, h % 2
                kt_ = kh_tmp[h % 2]
                ts(em, "vector", kt_[:], k_t[h][:, c * L:(c + 1) * L],
                   E4h[h][:, c:c + 1], None, ALU.mult, wkey=kt_)
                transp(em, P7[0:64, 64 * h:64 * (h + 1)], kt_[:], ctx.eye[0:64, 0:64])
            copy(em, "vector", khT[c][:], P7[0:64, 0:1024], wkey=khT[c])

        dbg(em, ctx, "kt0", k_t[0][:], 64, TT_)
        dbg(em, ctx, "ku0", ku[0][:], 64, TT_)
        dbg(em, ctx, "khT0", khT[0][:], 64, 1024)
        emit_path(2, IV, u_p)

        def v_consumer(s, nh, pp):
            for half in range(2):
                c = 2 * s + half
                copy(em, "vector", v_tm[c][:, 512 * nh:512 * (nh + 1)],
                     pp[64 * half:64 * (half + 1), 0:512], wkey=v_tm[c])
        proj_flip("Wv", u_p, v_consumer)

        dbg(em, ctx, "vtm0", v_tm[0][:], 64, 1024)
        emit_path(3, IR, u_p)

        def r_consumer(j, pp):
            for half in range(2):
                h = 2 * j + half
                sl = slice(64 * half, 64 * (half + 1))
                tt(em, "vector", r_t[h][:], pp[sl, 0:TT_], E1[j][sl, :], ALU.mult)
                copy(em, "vector", r_raw[h][:], pp[sl, 0:TT_])
        proj_std("Wr", r_consumer)

        dbg(em, ctx, "rt0", r_t[0][:], 64, TT_)
        dbg(em, ctx, "rr0", r_raw[0][:], 64, TT_)
        emit_path(4, IG, u_g)

        def g_consumer(s, nh, pp):
            for half in range(2):
                c = 2 * s + half
                act(em, g_tm[c][:, 512 * nh:512 * (nh + 1)],
                    pp[64 * half:64 * (half + 1), 0:512], AF.Silu, wkey=g_tm[c])
        proj_flip("Wg", u_g, g_consumer)

        pA, pA2, pY, pS = P[0], P[1], P[2], P[3]
        for c in range(NCH):
            cs = c * L
            for h in range(16):
                j, half = h // 2, h % 2
                ia = h % 2
                mm(em, pA[0:64, 0:64], k_t[h][:, cs:cs + L], r_t[h][:, cs:cs + L])
                mm(em, pA2[0:64, 0:64], ku[h][:, cs:cs + L], r_raw[h][:, cs:cs + L])
                tt(em, "vector", At[ia][:], pA[0:64, 0:64], maskU[:], ALU.mult)
                tt(em, "vector", At2[ia][:], pA2[0:64, 0:64], ctx.eye[0:64, 0:64],
                   ALU.mult)
                tt(em, "vector", At[ia][:], At[ia][:], At2[ia][:], ALU.add)
                mm(em, pY[0:64, 0:64], At[ia][:], v_tm[c][:, 64 * h:64 * (h + 1)],
                   start=True, stop=False)
                mm(em, pY[0:64, 0:64], r_t[h][:, cs:cs + L], S_b[h][:],
                   start=False, stop=True)
                act(em, y_sb[:, 64 * h:64 * (h + 1)], pY[0:64, 0:64], AF.Identity,
                    scale=0.125, wkey=y_sb)
                mm(em, pS[0:64, 0:64], khT[c][:, 64 * h:64 * (h + 1)],
                   v_tm[c][:, 64 * h:64 * (h + 1)])
                stt(em, "vector", S_f[h][:], S_f[h][:],
                    E4h[h][:, c:c + 1], pS[0:64, 0:64],
                    ALU.mult, ALU.add)
                copy(em, "vector", S_b[h][:], S_f[h][:])
            act(em, gn_sq[:], y_sb[:], AF.Square, wkey=gn_sq)
            for h in range(16):
                reduce(em, gn_mu[:, h:h + 1], y_sb[:, 64 * h:64 * (h + 1)], ALU.add,
                       wkey=gn_mu)
                reduce(em, gn_s2[:, h:h + 1], gn_sq[:, 64 * h:64 * (h + 1)], ALU.add,
                       wkey=gn_s2)
            ts(em, "vector", gn_mu[:], gn_mu[:], 1.0 / 64, None, ALU.mult)
            ts(em, "vector", gn_s2[:], gn_s2[:], 1.0 / 64, None, ALU.mult)
            tt(em, "vector", gn_rs[:], gn_mu[:], gn_mu[:], ALU.mult)
            tt(em, "vector", gn_rs[:], gn_s2[:], gn_rs[:], ALU.subtract)
            act(em, gn_rs[:], gn_rs[:], AF.Sqrt, bias=ctx.eps5[0:64, 0:1], wkey=gn_rs)
            recip(em, gn_rs[:], gn_rs[:])
            for h in range(16):
                ts(em, "vector", y_sb[:, 64 * h:64 * (h + 1)],
                   y_sb[:, 64 * h:64 * (h + 1)], gn_mu[:, h:h + 1],
                   gn_rs[:, h:h + 1], ALU.subtract, ALU.mult, wkey=y_sb)
            if c == 0:
                dbg(em, ctx, "ysb0", y_sb[:], 64, 1024)
            tt(em, "vector", y2[c][:], y_sb[:], lnw_b[:], ALU.mult)
            tt(em, "vector", y2[c][:], y2[c][:], lnb_b[:], ALU.add)
            tt(em, "vector", y2[c][:], y2[c][:], g_tm[c][:], ALU.mult)

        dbg(em, ctx, "gtm0", g_tm[0][:], 64, 1024)
        dbg(em, ctx, "y2_0", y2[0][:], 64, 1024)
        for j in range(8):
            for c in range(NCH):
                transp(em, P7[0:128, 64 * c:64 * (c + 1)],
                       y2[c][:, 128 * j:128 * (j + 1)], ctx.eye[0:64, 0:64])
            copy(em, "vector", y_fm[j][:], P7[0:128, 0:TT_], wkey=y_fm[j])
        for pj in range(4):
            for kk in range(8):
                wb = wbuf[kk % 4]
                em.dma("gpsimd", out=wb[:],
                       in_=inp["Wo"][128 * kk:128 * (kk + 1), 256 * pj:256 * (pj + 1)],
                       reads=[wb], writes=[wb])
                mm(em, P[4][:, 0:TT_], wb[:, 0:128], y_fm[kk][:],
                   start=(kk == 0), stop=(kk == 7))
                mm(em, P[5][:, 0:TT_], wb[:, 128:256], y_fm[kk][:],
                   start=(kk == 0), stop=(kk == 7))
            for half in range(2):
                j = 2 * pj + half
                tt(em, "vector", xt[j][:], P[4 + half][:, 0:TT_], xt[j][:], ALU.add,
                   wkey=xt[j])
        if ctx.fused:
            for j in range(8):
                copy(em, "vector", ctx.X[j][:, t0:t0 + TT_], xt[j][:], wkey=ctx.X[j])
        else:
            for j in range(8):
                em.dma("gpsimd", out=ctx.y_out[128 * j:128 * (j + 1), t0:t0 + TT_],
                       in_=xt[j][:], reads=[xt[j]], writes=[f"OUTY{j}"])

    em.barrier()


def build_phase2(nc, em, ctx, E, inp):
    T = ctx.T
    NTW = T // 512
    NE = 160

    pg = [E(nc.psum_tensor(f"p2g{i}", [128, 512], F32)) for i in range(4)]
    pz = [E(nc.psum_tensor(f"p2z{i}", [128, 512], F32)) for i in range(2)]
    pxtra = E(nc.psum_tensor("p2xtra", [128, 512], F32))
    prow = pxtra
    pbc = pxtra


    eye32_d = nc.inline_tensor(np.eye(128).astype(np.float32), "eye_f32_d")
    eye_f = E(nc.sbuf_tensor("eye_f", [128, 128], F32))
    em.dma("sync", out=eye_f[:], in_=eye32_d.ap(), writes=[eye_f])

    rw = []
    for c in range(8):
        t = E(nc.sbuf_tensor(f"p2rw{c}", [128, 160], F32))
        em.dma("sync", out=t[:], in_=inp["router_wt"][128 * c:128 * (c + 1), :],
               writes=[t])
        rw.append(t)
    ebias = E(nc.sbuf_tensor("p2eb", [128, 2], F32))
    em.dma("sync", out=ebias[:], in_=inp["ebias"], writes=[ebias])

    xn_bf = [E(nc.sbuf_tensor(f"p2xn{j}", [128, T], BF16)) for j in range(8)]
    _sp = getattr(ctx, "moe_sparse", False)
    RNK = [E(nc.sbuf_tensor(f"p2rnk{i}", [128, 160], BF16)) for i in range(T // 128)] if _sp else None
    CTM = [E(nc.sbuf_tensor(f"p2ctm{i}", [128, 160], BF16)) for i in range(T // 128)] if _sp else None
    comb_lo = E(nc.sbuf_tensor("p2clo", [128, T], BF16))
    comb_hi = E(nc.sbuf_tensor("p2chi", [32, T], BF16))
    mcol = E(nc.sbuf_tensor("p2mcol", [128, 1], F32))
    tsum = E(nc.sbuf_tensor("p2tsum", [128, 1], F32))
    ctm = E(nc.sbuf_tensor("p2ctm", [128, 160], BF16))
    sq2 = [E(nc.sbuf_tensor(f"p2sq{i}", [128, 512], BF16)) for i in range(2)]
    tmpf = [E(nc.sbuf_tensor(f"p2tmp{i}", [128, 512], F32)) for i in range(2)]
    lnrow = [E(nc.sbuf_tensor(f"p2row{i}", [1, 512], F32)) for i in range(4)]
    h_bf = [E(nc.sbuf_tensor(f"p2h{i}", [128, 512], BF16)) for i in range(4)]
    xnf = [E(nc.sbuf_tensor(f"p2xnf{i}", [128, 512], F32)) for i in range(2)]
    rowb = E(nc.sbuf_tensor("p2rowb", [1, 512], BF16))
    NSET = 1 if getattr(ctx, "moe_sparse", False) else 2
    wg_bf = [[E(nc.sbuf_tensor(f"p2wg{b}_{c}", [128, 512], BF16)) for c in range(8)]
             for b in range(NSET)]
    wd_bf = [[E(nc.sbuf_tensor(f"p2wd{b}_{c}", [128, 1024], BF16)) for c in range(4)]
             for b in range(NSET)]
    from contextlib import ExitStack as _ES
    _pre = _ES()
    E2 = _pre.enter_context
    pbf = E2(nc.psum_tensor("p2bf", [128, 1024], BF16))
    probs_lo = E2(nc.sbuf_tensor("p2plo", [128, T], F32))
    probs_hi = E2(nc.sbuf_tensor("p2phi", [32, T], F32))
    ptm = [E2(nc.sbuf_tensor(f"p2ptm{i}", [128, 160], F32)) for i in range(T // 128)]
    work = E2(nc.sbuf_tensor("p2work", [128, 160], F32))
    sel = E2(nc.sbuf_tensor("p2sel", [128, 160], F32))
    msk = E2(nc.sbuf_tensor("p2msk", [128, 160], F32))

    for w in range(NTW):
        t0 = w * 512
        for j in range(8):
            copy(em, "vector", xn_bf[j][:, t0:t0 + 512], ctx.X[j][:, t0:t0 + 512],
                 wkey=xn_bf[j])
            act(em, sq2[j % 2][:], xn_bf[j][:, t0:t0 + 512], AF.Square,
                wkey=sq2[j % 2])
            mm(em, pg[0][0:1, 0:512], ctx.ones_col_bf[:], xn_bf[j][:, t0:t0 + 512],
               start=(j == 0), stop=(j == 7))
            mm(em, pg[1][0:1, 0:512], ctx.ones_col_bf[:], sq2[j % 2][:],
               start=(j == 0), stop=(j == 7))
        mean, msq, var, rstd = lnrow
        ts(em, "vector", mean[:], pg[0][0:1, 0:512], 1.0 / 1024, None, ALU.mult)
        ts(em, "vector", msq[:], pg[1][0:1, 0:512], 1.0 / 1024, None, ALU.mult)
        tt(em, "vector", var[:], mean[:], mean[:], ALU.mult)
        tt(em, "vector", var[:], msq[:], var[:], ALU.subtract)
        act(em, rstd[:], var[:], AF.Sqrt, bias=ctx.eps5[0:1, 0:1], wkey=rstd)
        recip(em, rstd[:], rstd[:])
        mm(em, pg[2][:, 0:512], ctx.ones_f[:], mean[:])
        mm(em, pg[3][:, 0:512], ctx.ones_f[:], rstd[:])
        for j in range(8):
            tf = tmpf[j % 2]
            xf = xnf[j % 2]
            tt(em, "vector", tf[:], ctx.X[j][:, t0:t0 + 512], pg[2][:, 0:512],
               ALU.subtract, wkey=tf)
            tt(em, "vector", xf[:], tf[:], pg[3][:, 0:512], ALU.mult, wkey=xf)
            copy(em, "vector", xn_bf[j][:, t0:t0 + 512], xf[:], wkey=xn_bf[j])
            mm(em, pg[0][:, 0:512], rw[j][:, 0:128], xf[:],
               start=(j == 0), stop=(j == 7))
            mm(em, pg[1][0:32, 0:512], rw[j][:, 128:160], xf[:],
               start=(j == 0), stop=(j == 7))
        act(em, probs_lo[:, t0:t0 + 512], pg[0][:, 0:512], AF.Sigmoid,
            bias=ebias[:, 0:1], wkey=probs_lo)
        act(em, probs_hi[:, t0:t0 + 512], pg[1][0:32, 0:512], AF.Sigmoid,
            bias=ebias[0:32, 1:2], wkey=probs_hi)

    for i in range(T // 128):
        c0 = i * 128
        transp(em, pg[0][:, 0:128], probs_lo[:, c0:c0 + 128], eye_f[:])
        transp(em, pg[0][:, 128:160], probs_hi[:, c0:c0 + 128], eye_f[0:32, 0:32])
        copy(em, "vector", ptm[i][:, 0:128], pg[0][0:128, 0:128], wkey=ptm[i])
        copy(em, "vector", ptm[i][:, 128:160], pg[0][0:128, 128:160], wkey=ptm[i])
    for i in range(T // 128):
        copy(em, "vector", work[:], ptm[i][:], wkey=work)
        memset(em, "vector", sel[:], 0.0, wkey=sel)
        if _sp:
            memset(em, "vector", RNK[i][:], 0.0, wkey=RNK[i])
        for it8 in range(8):
            reduce(em, mcol[:], work[:], ALU.max, wkey=mcol)
            ts(em, "vector", msk[:], work[:], mcol[:], None, ALU.is_ge, wkey=msk)
            tt(em, "vector", sel[:], sel[:], msk[:], ALU.add)
            if _sp:
                stt(em, "vector", RNK[i][:], msk[:], float(it8 + 1), RNK[i][:],
                    ALU.mult, ALU.add, wkey=RNK[i])
            tt(em, "vector", msk[:], msk[:], work[:], ALU.mult)
            tt(em, "vector", work[:], work[:], msk[:], ALU.subtract)
        tt(em, "vector", sel[:], sel[:], ptm[i][:], ALU.mult)
        reduce(em, tsum[:], sel[:], ALU.add, wkey=tsum)
        ts(em, "vector", tsum[:], tsum[:], 1e-9, None, ALU.add, wkey=tsum)
        recip(em, tsum[:], tsum[:])
        if _sp:
            ts(em, "vector", CTM[i][:], sel[:], tsum[:], 2.5, ALU.mult, ALU.mult,
               wkey=CTM[i])
            copy(em, "vector", ctm[:], CTM[i][:], wkey=ctm)
        else:
            ts(em, "vector", ctm[:], sel[:], tsum[:], 2.5, ALU.mult, ALU.mult,
               wkey=ctm)
        transp(em, pbf[0:128, 0:128], ctm[:, 0:128], ctx.eye[:])
        transp(em, pbf[0:32, 128:256], ctm[:, 128:160], ctx.eye[:])
        copy(em, "vector", comb_lo[:, i * 128:(i + 1) * 128], pbf[0:128, 0:128],
             wkey=comb_lo)
        copy(em, "vector", comb_hi[:, i * 128:(i + 1) * 128], pbf[0:32, 128:256],
             wkey=comb_hi)

    _pre.close()
    em.barrier()
    h16 = [E(nc.sbuf_tensor(f"p2h16_{i}", [128, 512], BF16)) for i in range(8)]
    for c in range(8):
        em.dma("gpsimd", out=wg_bf[0][c][:],
               in_=inp["wgs_t"][128 * c:128 * (c + 1), :],
               reads=[wg_bf[0][c]], writes=[wg_bf[0][c]])
    for c in range(4):
        em.dma("gpsimd", out=wd_bf[0][c][:],
               in_=inp["wds_t"][128 * c:128 * (c + 1), :],
               reads=[wd_bf[0][c]], writes=[wd_bf[0][c]])
    for w in range(NTW):
        t0 = w * 512
        for mi in range(4):
            for kk in range(8):
                mm(em, pg[mi][:, 0:512], wg_bf[0][kk][:, 128 * mi:128 * (mi + 1)],
                   xn_bf[kk][:, t0:t0 + 512], start=(kk == 0), stop=(kk == 7))
            act(em, h_bf[mi][:], pg[mi][:, 0:512], AF.Silu, wkey=h_bf[mi])
        for j in range(8):
            for kk in range(4):
                mm(em, pz[j % 2][:, 0:512], wd_bf[0][kk][:, 128 * j:128 * (j + 1)],
                   h_bf[kk][:], start=(kk == 0), stop=(kk == 3))
            tt(em, "vector", ctx.X[j][:, t0:t0 + 512], pz[j % 2][:, 0:512],
               ctx.X[j][:, t0:t0 + 512], ALU.add, wkey=ctx.X[j])

    if getattr(ctx, "moe_sparse", False):
        _moe_sparse(nc, em, ctx, E, inp, xn_bf, CTM, RNK, pg, pz, prow, pbf, eye_f,
                    tmpf, lnrow)
        em.barrier()
        return

    def load_expert(e, b):
        for c in range(8):
            em.dma("gpsimd", out=wg_bf[b][c][:],
                   in_=inp["wg_t"][e, 128 * c:128 * (c + 1), :],
                   reads=[wg_bf[b][c]], writes=[wg_bf[b][c]])
        for c in range(4):
            em.dma("gpsimd", out=wd_bf[b][c][:],
                   in_=inp["wd_t"][e, 128 * c:128 * (c + 1), :],
                   reads=[wd_bf[b][c]], writes=[wd_bf[b][c]])

    cwr = [E(nc.sbuf_tensor(f"p2cwr{i}", [128, 512], BF16)) for i in range(4)]
    rowr = [E(nc.sbuf_tensor(f"p2rowr{i}", [1, 512], BF16)) for i in range(2)]
    pbc2 = E(nc.psum_tensor("p2bc2", [128, 512], F32))
    pgb = [pg[0], pg[1]]
    pzb = [pz[0], pz[1], pg[2], pg[3]]
    UN = [(e, w) for e in range(NE) for w in range(NTW)]

    def row_prep(k):
        e, w = UN[k]
        t0 = w * 512
        if e < 128:
            mm(em, prow[0:1, 0:512], ctx.eye[:, e:e + 1], comb_lo[:, t0:t0 + 512])
        else:
            mm(em, prow[0:1, 0:512], ctx.eye[0:32, e - 128:e - 127],
               comb_hi[:, t0:t0 + 512])
        scopy(em, rowr[k % 2][:], prow[0:1, 0:512], wkey=rowr[k % 2])

    def bcast_prep(k):
        mm(em, pbc2[:, 0:512], ctx.ones_row_bf[:], rowr[k % 2][:])
        scopy(em, cwr[k % 4][:], pbc2[:, 0:512], wkey=cwr[k % 4])

    def emit_gate(k):
        e, w = UN[k]
        b = e % 2
        t0 = w * 512
        hp = (k % 2) * 4

        def g(mi):
            for kk in range(8):
                mm(em, pgb[mi % 2][:, 0:512],
                   wg_bf[b][kk][:, 128 * mi:128 * (mi + 1)],
                   xn_bf[kk][:, t0:t0 + 512], start=(kk == 0), stop=(kk == 7))

        def s(mi):
            act(em, h16[hp + mi][:], pgb[mi % 2][:, 0:512], AF.Silu,
                wkey=h16[hp + mi])
            tt(em, "vector", h16[hp + mi][:], h16[hp + mi][:], cwr[k % 4][:],
               ALU.mult)
        g(0); g(1); s(0); g(2); s(1); g(3); s(2); s(3)

    def emit_down(k):
        e, w = UN[k]
        b = e % 2
        t0 = w * 512
        hp = (k % 2) * 4
        for j in range(8):
            for kk in range(4):
                mm(em, pzb[j % 4][:, 0:512], wd_bf[b][kk][:, 128 * j:128 * (j + 1)],
                   h16[hp + kk][:], start=(kk == 0), stop=(kk == 3))
            tt(em, "vector", ctx.X[j][:, t0:t0 + 512], pzb[j % 4][:, 0:512],
               ctx.X[j][:, t0:t0 + 512], ALU.add, wkey=ctx.X[j])

    load_expert(0, 0)
    for k in range(min(3, len(UN))):
        row_prep(k)
        if k < 2:
            bcast_prep(k)
    for k in range(len(UN)):
        e, w = UN[k]
        emit_gate(k)
        if k + 3 < len(UN):
            row_prep(k + 3)
        if 2 <= k + 2 < len(UN):
            bcast_prep(k + 2)
        if k > 0:
            emit_down(k - 1)
        if w == 0 and e + 1 < NE:
            load_expert(e + 1, 1 - (e % 2))
    emit_down(len(UN) - 1)
def _moe_sparse(nc, em, ctx, E, inp, xn_bf, CTM, RNK, pg, pz, prow, pbf, eye_f,
                tmpf, lnrow):
    from concourse.expressions import RuntimeValue
    from concourse.bass_core_rust import IndirectOffsetOnAxis
    T = ctx.T
    NI = T // 128
    NB = T * 8 // 128 + 160
    BPC = (NB + 127) // 128
    U32 = mybir.dt.uint32
    I32 = mybir.dt.int32

    xn_tm_d = nc.dram_tensor("sp_xntm", [T, 1024], BF16, kind="Internal").ap()
    TAB = nc.dram_tensor("sp_tab", [NB * 128, 4], F32, kind="Internal").ap()
    EBD = nc.dram_tensor("sp_ebd", [BPC * 128], I32, kind="Internal").ap()
    Zacc = nc.dram_tensor("sp_zacc", [T + 128, 1024], F32, kind="Internal").ap()

    tabinit = np.zeros((NB * 128, 4), np.float32)
    tabinit[:, 1] = T + (np.arange(NB * 128) % 128)
    tabinit_d = nc.inline_tensor(tabinit, "sp_tabinit_d")
    tri_d = nc.inline_tensor(
        np.tril(np.ones((128, 128), np.float32), -1).T.astype(BF16NP), "sp_tri_d")
    iot_d = nc.inline_tensor(np.arange(128, dtype=np.float32)[:, None], "sp_iot_d")
    bcol_np = 128.0 * (np.arange(128)[:, None] + 128 * np.arange(BPC)[None, :])
    bcol_d = nc.inline_tensor(bcol_np.astype(np.float32), "sp_bcol_d")

    TRI = E(nc.sbuf_tensor("spTRI", [128, 128], BF16))
    em.dma("sync", out=TRI[:], in_=tri_d.ap(), writes=[TRI])
    IOT = E(nc.sbuf_tensor("spIOT", [128, 1], F32))
    em.dma("sync", out=IOT[:], in_=iot_d.ap(), writes=[IOT])
    BCOL = E(nc.sbuf_tensor("spBCOL", [128, BPC], F32))
    em.dma("sync", out=BCOL[:], in_=bcol_d.ap(), writes=[BCOL])
    em.dma("sync", out=TAB, in_=tabinit_d.ap(), writes=["TAB"])

    selb = E(nc.sbuf_tensor("spselb", [128, 160], BF16))
    cnts = E(nc.sbuf_tensor("spcnts", [1, 160 * NI], F32))
    offs = E(nc.sbuf_tensor("spoffs", [1, 160 * NI], F32))
    rowt = [E(nc.sbuf_tensor(f"sprow{i}", [1, 160], F32)) for i in range(4)]
    possb = E(nc.sbuf_tensor("sppos", [128, 160], F32))
    Dsb = E(nc.sbuf_tensor("spD", [128, 160], F32))
    mskb = E(nc.sbuf_tensor("spmsk", [128, 160], F32))
    t160 = E(nc.sbuf_tensor("spt160", [128, 160], F32))
    tokf = E(nc.sbuf_tensor("sptok", [128, 1], F32))
    fcol = E(nc.sbuf_tensor("spfc", [128, 1], F32))
    cwcol = E(nc.sbuf_tensor("spcw", [128, 1], F32))
    val4 = E(nc.sbuf_tensor("spval", [128, 4], F32))
    ufd = E(nc.sbuf_tensor("spufd", [128, 1], U32))
    ebf = E(nc.sbuf_tensor("spebf", [128, BPC], F32))
    ebi = E(nc.sbuf_tensor("spebi", [128, BPC], I32))
    ebrow = E(nc.sbuf_tensor("spebr", [1, BPC * 128], I32))
    xtm_st = [E(nc.sbuf_tensor(f"spxst{i}", [128, 1024], BF16)) for i in range(2)]
    wg_sb = [E(nc.sbuf_tensor(f"spwg{i}", [128, 4096], BF16)) for i in range(2)]
    wd_sb = [E(nc.sbuf_tensor(f"spwd{i}", [128, 4096], BF16)) for i in range(2)]
    xg_tm = [E(nc.sbuf_tensor(f"spxg{i}", [128, 1024], BF16)) for i in range(2)]
    xg_fm = [E(nc.sbuf_tensor(f"spxf{i}", [128, 1024], BF16)) for i in range(2)]
    hb = [E(nc.sbuf_tensor(f"sphb{i}", [128, 512], BF16)) for i in range(2)]
    z_bf = [E(nc.sbuf_tensor(f"spzb{i}", [128, 1024], BF16)) for i in range(2)]
    tab4 = [E(nc.sbuf_tensor(f"sptab{i}", [128, 4], F32)) for i in range(2)]
    utok = [E(nc.sbuf_tensor(f"sput{i}", [128, 1], U32)) for i in range(2)]
    uzd = [E(nc.sbuf_tensor(f"spuz{i}", [128, 1], U32)) for i in range(2)]

    wgflat = inp["wg_t"].rearrange("e c i -> (e c) i")
    wdflat = inp["wd_t"].rearrange("e i c -> (e i) c")

    for i in range(NI):
        st = xtm_st[i % 2]
        for j in range(8):
            transp(em, pbf[:, 128 * j:128 * (j + 1)],
                   xn_bf[j][:, 128 * i:128 * (i + 1)], ctx.eye[:])
        copy(em, "vector", st[:], pbf[0:128, 0:1024], wkey=st)
        em.dma("gpsimd", out=xn_tm_d[128 * i:128 * (i + 1), :], in_=st[:],
               reads=[st], writes_accum=["XNTM"])

    for i in range(NI):
        ts(em, "vector", selb[:], RNK[i][:], 0.5, None, ALU.is_ge, wkey=selb)
        mm(em, prow[0:1, 0:160], ctx.ones_col_bf[:], selb[:])
        copy(em, "vector", cnts[0:1, 160 * i:160 * (i + 1)], prow[0:1, 0:160],
             wkey=cnts)
    memset(em, "vector", offs[0:1, 0:160], 0.0, wkey=offs)
    for i in range(1, NI):
        tt(em, "vector", offs[0:1, 160 * i:160 * (i + 1)],
           offs[0:1, 160 * (i - 1):160 * i], cnts[0:1, 160 * (i - 1):160 * i],
           ALU.add, wkey=offs)
    tot, m1, padd, bstart = rowt
    tt(em, "vector", tot[:], offs[0:1, 160 * (NI - 1):160 * NI],
       cnts[0:1, 160 * (NI - 1):160 * NI], ALU.add, wkey=tot)
    ts(em, "vector", m1[:], tot[:], 128.0, None, ALU.mod, wkey=m1)
    ts(em, "vector", m1[:], m1[:], -1.0, 128.0, ALU.mult, ALU.add, wkey=m1)
    ts(em, "vector", m1[:], m1[:], 128.0, None, ALU.mod, wkey=m1)
    tt(em, "vector", padd[:], tot[:], m1[:], ALU.add, wkey=padd)
    scan(em, m1[:], padd[:], ctx.zeros[0:1, 0:160], 0.0, ALU.add, ALU.add)
    memset(em, "vector", bstart[0:1, 0:1], 0.0, wkey=bstart)
    copy(em, "vector", bstart[0:1, 1:160], m1[0:1, 0:159], wkey=bstart)

    mm(em, pg[0][:, 0:160], ctx.ones_f[:], bstart[:])
    for t in range(BPC):
        ts(em, "vector", t160[:], pg[0][:, 0:160], BCOL[:, t:t + 1], None,
           ALU.is_le, wkey=t160)
        reduce(em, ebf[:, t:t + 1], t160[:], ALU.add, wkey=ebf)
    ts(em, "vector", ebf[:], ebf[:], -1.0, None, ALU.add, wkey=ebf)
    copy(em, "vector", ebi[:], ebf[:], wkey=ebi)
    em.dma("gpsimd", out=EBD.rearrange("(t p) -> p t", p=128), in_=ebi[:],
           reads=[ebi], writes=["EBD"], allow_slow_non_contiguous=True)
    em.dma("sync", out=ebrow[0:1, :], in_=EBD[None, :], reads=["EBD"],
           writes=[ebrow])

    for i in range(NI):
        ts(em, "vector", selb[:], RNK[i][:], 0.5, None, ALU.is_ge, wkey=selb)
        mm(em, pg[1][:, 0:160], TRI[:], selb[:])
        copy(em, "vector", possb[:], pg[1][:, 0:160], wkey=possb)
        tt(em, "vector", rowt[1][:], bstart[:], offs[0:1, 160 * i:160 * (i + 1)],
           ALU.add, wkey=rowt[1])
        mm(em, pg[2][:, 0:160], ctx.ones_f[:], rowt[1][:])
        tt(em, "vector", Dsb[:], possb[:], pg[2][:, 0:160], ALU.add, wkey=Dsb)
        ts(em, "vector", tokf[:], IOT[:], float(128 * i), None, ALU.add, wkey=tokf)
        for r in range(8):
            ts(em, "vector", mskb[:], RNK[i][:], float(r + 1), None, ALU.is_equal,
               wkey=mskb)
            tt(em, "vector", t160[:], Dsb[:], mskb[:], ALU.mult, wkey=t160)
            reduce(em, fcol[:], t160[:], ALU.add, wkey=fcol)
            tt(em, "vector", t160[:], CTM[i][:], mskb[:], ALU.mult, wkey=t160)
            reduce(em, cwcol[:], t160[:], ALU.add, wkey=cwcol)
            copy(em, "vector", val4[:, 0:1], tokf[:], wkey=val4)
            copy(em, "vector", val4[:, 1:2], tokf[:], wkey=val4)
            copy(em, "vector", val4[:, 2:3], cwcol[:], wkey=val4)
            copy(em, "vector", ufd[:], fcol[:], wkey=ufd)
            em.op("gpsimd", lambda e: e.indirect_dma_start(
                out=TAB, out_offset=IndirectOffsetOnAxis(ufd[:], 0),
                in_=val4[:], in_offset=None,
                bounds_check=NB * 128 - 1, oob_is_err=False),
                reads=[val4, ufd, "TAB"], writes_accum=["TAB"], dma=True)

    for rr in range((T + 128) // 128):
        for hh in range(2):
            em.dma("gpsimd", out=Zacc[128 * rr:128 * (rr + 1), 512 * hh:512 * (hh + 1)],
                   in_=ctx.zeros[:, 0:512], writes_accum=["Zacc"])

    rgA = nc.gpsimd.alloc_register("sp_rga")
    rgB = nc.gpsimd.alloc_register("sp_rgb")
    for b in range(NB):
        p = b % 2
        ph = pg[p]
        pzh = (pz[0], pz[1]) if p == 0 else (pg[2], pg[3])
        t4 = tab4[p]
        em.dma("sync", out=t4[:], in_=TAB[128 * b:128 * (b + 1), :],
               reads=["TAB", t4], writes=[t4])
        copy(em, "vector", utok[p][:], t4[:, 0:1], wkey=utok[p])
        copy(em, "vector", uzd[p][:], t4[:, 1:2], wkey=uzd[p])
        em.op("gpsimd", lambda e, b=b: e.reg_load(rgA, ebrow[0:1, b:b + 1]),
              reads=[ebrow])
        em.op("gpsimd", lambda e: e.reg_mul(rgA, rgA, 1024))
        em.op("gpsimd", lambda e, w=wg_sb[p]: e.dma_start(
            out=w[:].rearrange("p (c i) -> p c i", c=8),
            in_=wgflat[bass.ds(RuntimeValue(rgA, min_val=0, max_val=159 * 1024), 1024),
                       :].rearrange("(c p) i -> p c i", p=128)),
            reads=[wg_sb[p]], writes=[wg_sb[p]], dma=True)
        em.op("gpsimd", lambda e, b=b: e.reg_load(rgB, ebrow[0:1, b:b + 1]),
              reads=[ebrow])
        em.op("gpsimd", lambda e: e.reg_mul(rgB, rgB, 512))
        em.op("gpsimd", lambda e, w=wd_sb[p]: e.dma_start(
            out=w[:].rearrange("p (c i) -> p c i", c=4),
            in_=wdflat[bass.ds(RuntimeValue(rgB, min_val=0, max_val=159 * 512), 512),
                       :].rearrange("(ci p) c -> p ci c", p=128)),
            reads=[wd_sb[p]], writes=[wd_sb[p]], dma=True)
        em.op("gpsimd", lambda e, p=p: e.indirect_dma_start(
            out=xg_tm[p][:], out_offset=None, in_=xn_tm_d,
            in_offset=IndirectOffsetOnAxis(utok[p][:], 0)),
            reads=["XNTM", utok[p], xg_tm[p]], writes=[xg_tm[p]], dma=True)
        for c in range(8):
            transp(em, pbf[:, 128 * c:128 * (c + 1)],
                   xg_tm[p][:, 128 * c:128 * (c + 1)], ctx.eye[:])
        copy(em, "vector", xg_fm[p][:], pbf[0:128, 0:1024], wkey=xg_fm[p])
        for c in range(8):
            for m in range(4):
                mm(em, ph[:, 128 * m:128 * (m + 1)],
                   wg_sb[p][:, 512 * c + 128 * m:512 * c + 128 * (m + 1)],
                   xg_fm[p][:, 128 * c:128 * (c + 1)],
                   start=(c == 0), stop=(c == 7))
        act(em, hb[p][:], ph[:, 0:512], AF.Silu, wkey=hb[p])
        for h in range(2):
            for ci in range(4):
                mm(em, pzh[h][:, 0:512], hb[p][:, 128 * ci:128 * (ci + 1)],
                   wd_sb[p][:, 1024 * ci + 512 * h:1024 * ci + 512 * (h + 1)],
                   start=(ci == 0), stop=(ci == 3))
            ts(em, "vector", z_bf[p][:, 512 * h:512 * (h + 1)], pzh[h][:, 0:512],
               t4[:, 2:3], None, ALU.mult, wkey=z_bf[p])
        _mode = os.environ.get("SPMODE", "add")
        if _mode == "add":
            em.op("gpsimd", lambda e, p=p: e.indirect_dma_start(
                out=Zacc, out_offset=IndirectOffsetOnAxis(uzd[p][:], 0),
                in_=z_bf[p][:], in_offset=None,
                bounds_check=T + 127, oob_is_err=True, compute_op=ALU.add),
                reads=[z_bf[p], uzd[p], "ZRMW"], writes=["ZRMW"],
                writes_accum=["Zacc"], dma=True)
        else:
            em.op("gpsimd", lambda e, p=p: e.indirect_dma_start(
                out=Zacc, out_offset=IndirectOffsetOnAxis(uzd[p][:], 0),
                in_=z_bf[p][:], in_offset=None,
                bounds_check=T + 127, oob_is_err=False),
                reads=[z_bf[p], uzd[p], "ZRMW"], writes=["ZRMW"],
                writes_accum=["Zacc"], dma=True)

    for i in range(NI):
        st = xtm_st[i % 2]
        em.dma("gpsimd", out=st[:], in_=Zacc[128 * i:128 * (i + 1), :],
               reads=["Zacc", "ZRMW", st], writes=[st])
        for j in range(8):
            transp(em, pbf[:, 128 * j:128 * (j + 1)], st[:, 128 * j:128 * (j + 1)],
                   ctx.eye[:])
        for j in range(8):
            tt(em, "vector", ctx.X[j][:, 128 * i:128 * (i + 1)],
               pbf[:, 128 * j:128 * (j + 1)], ctx.X[j][:, 128 * i:128 * (i + 1)],
               ALU.add, wkey=ctx.X[j])

def prep_phase2_inputs(inputs):
    d = {}
    d["router_wt"] = np.ascontiguousarray(inputs["L0_ffn_router_w"].T, np.float32)
    eb = np.zeros((128, 2), np.float32)
    eb[:, 0] = inputs["L0_ffn_expert_bias"][0:128]
    eb[0:32, 1] = inputs["L0_ffn_expert_bias"][128:160]
    d["ebias"] = eb
    d["wg_t"] = np.transpose(inputs["L0_ffn_w_gate"], (0, 2, 1))
    d["wd_t"] = np.transpose(inputs["L0_ffn_w_down"], (0, 2, 1))
    d["wgs_t"] = np.ascontiguousarray(inputs["L0_ffn_w_gate_shared"].T, np.float32)
    d["wds_t"] = np.ascontiguousarray(inputs["L0_ffn_w_down_shared"].T, np.float32)
    return d


PHASE2_TENSORS = [
    ("router_wt", [1024, 160]), ("ebias", [128, 2]),
    ("wg_t", [160, 1024, 512]), ("wd_t", [160, 512, 1024]),
    ("wgs_t", [1024, 512]), ("wds_t", [512, 1024]),
]


def build_nc_phase2(T=2048, debug=False, sparse=False):
    from contextlib import ExitStack
    nc = bass.Bass("TRN2", target_bir_lowering=False)
    x_in = nc.dram_tensor("x_fm", [1024, T], F32, kind="ExternalInput").ap()
    y_out = nc.dram_tensor("y_fm", [1024, T], F32, kind="ExternalOutput").ap()
    inp = {nm: nc.dram_tensor(nm, shp, F32, kind="ExternalInput").ap()
           for nm, shp in PHASE2_TENSORS}
    em = Em(nc)
    ctx = Ctx()
    ctx.fused = False
    ctx.debug = debug
    ctx.moe_sparse = sparse
    ctx.dbg_done = set()
    with ExitStack() as st:
        E = st.enter_context
        if debug:
            ctx.dbg_stage = E(nc.sbuf_tensor("dbgstage", [128, 1024], F32))
        ctx.X = alloc_stream(nc, E, T)
        setup_common(nc, em, ctx, E, T)
        load_stream(em, ctx, x_in)
        em.barrier()
        build_phase2(nc, em, ctx, E, inp)
        store_stream(em, ctx, y_out)
        em.build()
    print("phase2: ops=%d" % len(em.ops))
    return nc


def emit_ln_win(nc, em, ctx, t0, tw, src_of, dst_of, psums, eps_col,
                gamma=None, beta=None, lnrow=None, sq2=None, tmpf=None):
    pA, pB, pC, pD = psums
    for j in range(8):
        s = src_of(j)
        sq = sq2[j % 2]
        if s.tensor.dtype == F32:
            copy(em, "vector", sq[:, 0:tw], s, wkey=sq)
            mm(em, pA[0:1, 0:tw], ctx.ones_col_bf[:], sq[:, 0:tw],
               start=(j == 0), stop=(j == 7))
            act(em, sq[:, 0:tw], sq[:, 0:tw], AF.Square, wkey=sq)
        else:
            mm(em, pA[0:1, 0:tw], ctx.ones_col_bf[:], s, start=(j == 0), stop=(j == 7))
            act(em, sq[:, 0:tw], s, AF.Square, wkey=sq)
        mm(em, pB[0:1, 0:tw], ctx.ones_col_bf[:], sq[:, 0:tw],
           start=(j == 0), stop=(j == 7))
    mean, msq, var, rstd = lnrow
    ts(em, "vector", mean[:, 0:tw], pA[0:1, 0:tw], 1.0 / 1024, None, ALU.mult,
       wkey=mean)
    ts(em, "vector", msq[:, 0:tw], pB[0:1, 0:tw], 1.0 / 1024, None, ALU.mult,
       wkey=msq)
    tt(em, "vector", var[:, 0:tw], mean[:, 0:tw], mean[:, 0:tw], ALU.mult, wkey=var)
    tt(em, "vector", var[:, 0:tw], msq[:, 0:tw], var[:, 0:tw], ALU.subtract,
       wkey=var)
    act(em, rstd[:, 0:tw], var[:, 0:tw], AF.Sqrt, bias=eps_col, wkey=rstd)
    recip(em, rstd[:, 0:tw], rstd[:, 0:tw])
    mm(em, pC[:, 0:tw], ctx.ones_f[:], mean[:, 0:tw])
    mm(em, pD[:, 0:tw], ctx.ones_f[:], rstd[:, 0:tw])
    for j in range(8):
        tf = tmpf[j % 2]
        tt(em, "vector", tf[:, 0:tw], src_of(j), pC[:, 0:tw], ALU.subtract, wkey=tf)
        if gamma is None:
            dst_of(j, lambda out, tfv=tf: tt(em, "vector", out, tfv[:, 0:tw],
                                             pD[:, 0:tw], ALU.mult, wkey=None))
        else:
            tt(em, "vector", tf[:, 0:tw], tf[:, 0:tw], pD[:, 0:tw], ALU.mult,
               wkey=tf)
            dst_of(j, lambda out, tfv=tf, j=j: ts(em, "vector", out, tfv[:, 0:tw],
                                                  gamma(j), beta(j), ALU.mult,
                                                  ALU.add, wkey=None))


def stream_proj(nc, em, ctx, inp_ap, rhs_of, psums, consumer, wbuf, nK=8,
                TT_=512, npair=4):
    for pj in range(npair):
        for kk in range(nK):
            wb = wbuf[kk % len(wbuf)]
            em.dma("gpsimd", out=wb[:, 0:256],
                   in_=inp_ap[128 * kk:128 * (kk + 1), 256 * pj:256 * (pj + 1)],
                   reads=[wb], writes=[wb])
            mm(em, psums[0][:, 0:TT_], wb[:, 0:128], rhs_of(kk),
               start=(kk == 0), stop=(kk == nK - 1))
            mm(em, psums[1][:, 0:TT_], wb[:, 128:256], rhs_of(kk),
               start=(kk == 0), stop=(kk == nK - 1))
        consumer(2 * pj, psums[0])
        consumer(2 * pj + 1, psums[1])


def build_phase3(nc, em, ctx, E, inp, stop_at=None):
    T = ctx.T
    NTW = T // 512
    NI = T // 128

    pp = [E(nc.psum_tensor(f"p3p{i}", [128, 512], F32)) for i in range(7)]
    pbf = E(nc.psum_tensor("p3bf", [128, 1024], BF16))

    S1 = [E(nc.sbuf_tensor(f"p3S1_{j}", [128, T], BF16)) for j in range(8)]
    S2 = [E(nc.sbuf_tensor(f"p3S2_{j}", [128, T], BF16)) for j in range(8)]
    NI0 = T // 128
    S3 = [E(nc.sbuf_tensor(f"p3S3_{j}", [128, 1024], BF16)) for j in range(max(NI0, 8))]
    S4 = [E(nc.sbuf_tensor(f"p3S4_{j}", [128, T], BF16)) for j in range(8)]
    S5 = [E(nc.sbuf_tensor(f"p3S5_{j}", [128, 1024], BF16))
          for j in range(max(NI0, 16))]
    xt = [E(nc.sbuf_tensor(f"p3x{j}", [128, 512], F32)) for j in range(8)]
    s2b = lambda j: S3[j][:, 0:512]
    sq2 = [E(nc.sbuf_tensor(f"p3sq{i}", [128, 512], BF16)) for i in range(2)]
    tmpf = [E(nc.sbuf_tensor(f"p3tmp{i}", [128, 512], F32)) for i in range(2)]
    lnrow = [E(nc.sbuf_tensor(f"p3row{i}", [1, 512], F32)) for i in range(4)]
    wbuf = [E(nc.sbuf_tensor(f"p3wb{i}", [128, 256], BF16)) for i in range(4)]
    wbig3 = [E(nc.sbuf_tensor(f"p3wB{i}", [128, 512], BF16)) for i in range(3)]
    pv3 = E(nc.sbuf_tensor("p3pv", [128, 40], F32))
    em.dma("sync", out=pv3[:], in_=inp["pvec3"], writes=[pv3])
    IG1, IB1, IG2, IB2, IB2b = range(5)

    def pvc(idx, j):
        return pv3[:, idx * 8 + j:idx * 8 + j + 1]

    b1c = E(nc.sbuf_tensor("p3b1c", [128, 32], F32))
    em.dma("sync", out=b1c[:], in_=inp["b1cols"], writes=[b1c])
    projT2 = E(nc.sbuf_tensor("p3projT", [128, 64], BF16))
    pstage = E(nc.sbuf_tensor("p3pstage", [128, 64], F32))
    em.dma("sync", out=pstage[0:64, :], in_=inp["projT"], reads=[pstage],
           writes=[pstage])
    DN = 1.0 / np.sqrt(np.sqrt(64.0))
    ts(em, "vector", projT2[0:64, :], pstage[0:64, :], DN, None, ALU.mult,
       wkey=projT2)
    copy(em, "vector", projT2[64:128, :], projT2[0:64, :], wkey=projT2)
    bd = np.zeros((1024, 16), np.float32)
    for h in range(16):
        bd[64 * h:64 * (h + 1), h] = 1.0
    bd_d = nc.inline_tensor(bd.astype(BF16NP), "p3bd_d")
    BDH = [E(nc.sbuf_tensor(f"p3bd{c}", [128, 16], BF16)) for c in range(8)]
    for c in range(8):
        em.dma("sync", out=BDH[c][:], in_=bd_d.ap()[128 * c:128 * (c + 1), :],
               writes=[BDH[c]])
    diag_q = [E(nc.sbuf_tensor(f"p3dq{i}", [128, 16], F32)) for i in range(NI)]
    diag_k = [E(nc.sbuf_tensor(f"p3dk{i}", [128, 16], F32)) for i in range(NI)]
    ndmx = E(nc.sbuf_tensor("p3ndmx", [128, 16], F32))
    kv_sb = [E(nc.sbuf_tensor(f"p3kv{p}", [128, 65], BF16)) for p in range(8)]
    rc = E(nc.sbuf_tensor("p3rc", [128, 1], F32))
    mxcol = E(nc.sbuf_tensor("p3mx", [128, 1], F32))
    mxrow = E(nc.sbuf_tensor("p3mxr", [1, 16], F32))
    mxk_col = E(nc.sbuf_tensor("p3mxk", [128, 1], F32))
    cc_in = nc.dram_tensor("p3cc_in", [1, 16], F32)
    cc_out = nc.dram_tensor("p3cc_out", [1, 16], F32)

    em.barrier()

    for w in range(NTW):
        t0 = w * 512
        for j in range(8):
            em.dma("sync", out=xt[j][:], in_=ctx.x_in[128 * j:128 * (j + 1),
                                                      t0:t0 + 512],
                   reads=[xt[j]], writes=[xt[j]])
        emit_ln_win(nc, em, ctx, t0, 512, lambda j: xt[j][:],
                    lambda j, emitfn: emitfn(S1[j][:, t0:t0 + 512]),
                    (pp[0], pp[1], pp[2], pp[3]), ctx.eps5[0:1, 0:1],
                    lnrow=lnrow, sq2=sq2, tmpf=tmpf)

    if stop_at == 'ln':
        em.barrier()
        return
    def qk_path(wname, dst_tm, diag_dst, is_q):
        for w in range(NTW):
            t0 = w * 512
            stream_proj(nc, em, ctx, inp[wname],
                        lambda kk, t0=t0: S1[kk][:, t0:t0 + 512], (pp[0], pp[1]),
                        lambda j, ps, t0=t0: copy(em, "vector",
                                                  S2[j][:, t0:t0 + 512],
                                                  ps[:, 0:512], wkey=S2[j]),
                        wbuf)
        if stop_at == 'qk_mm':
            return
        for w in range(NTW):
            t0 = w * 512
            for j in range(8):
                act(em, sq2[j % 2][:], S2[j][:, t0:t0 + 512], AF.Square,
                    wkey=sq2[j % 2])
                mm(em, pp[4][0:16, 0:512], BDH[j][:], sq2[j % 2][:],
                   start=(j == 0), stop=(j == 7))
            for s in range(4):
                em.op("scalar", lambda e, s=s, w=w: e.activation(
                    tmpf[0][0:16, 128 * s:128 * (s + 1)],
                    pp[4][0:16, 128 * s:128 * (s + 1)], AF.Identity,
                    scale=float(DN * DN / 2.0)),
                    reads=[pp[4]], writes=[tmpf[0]])
            copy(em, "vector", sq2[0][0:16, 0:512], tmpf[0][0:16, 0:512],
                 wkey=sq2[0])
            for s in range(4):
                i = 4 * w + s
                transp(em, pbf[:, 0:16], sq2[0][0:16, 128 * s:128 * (s + 1)],
                       ctx.eye[0:16, 0:16])
                copy(em, "vector", diag_dst[i][:], pbf[0:128, 0:16],
                     wkey=diag_dst[i])
        if stop_at == 'qk_diag':
            return
        for i in range(NI):
            tc = i * 128
            for hh in range(2):
                for h4 in range(8):
                    h = 8 * hh + h4
                    par = h4 % 2
                    mm(em, pp[2 * hh + par][:, 64 * (h4 // 2):64 * (h4 // 2 + 1)],
                       S2[h // 2][64 * par:64 * par + 64, tc:tc + 128],
                       projT2[64 * par:64 * par + 64, :])
            for hh in range(2):
                for par in range(2):
                    dv = dst_tm(i).rearrange("p (g c) -> p g c", c=64)
                    dst = dv[:, 8 * hh + par:8 * hh + 8:2, :]
                    copy(em, "vector", dst, pp[2 * hh + par][:, 0:256], wkey=None)

    if stop_at == 'none2':
        em.barrier()
        return
    qdt = lambda i: S3[i][:, :]
    if stop_at == 'none3':
        em.barrier()
        return
    qk_path("Wq", qdt, diag_q, True)
    if stop_at == 'qproj':
        em.barrier()
        return
    for i in range(NI):
        dst = qdt(i)
        for h in range(16):
            reduce(em, mxcol[:], dst[:, 64 * h:64 * (h + 1)], ALU.max, wkey=mxcol)
            ts(em, "vector", ndmx[:, h:h + 1], diag_q[i][:, h:h + 1], mxcol[:],
               -1.0, ALU.add, ALU.mult, wkey=ndmx)
        for h in range(16):
            act(em, dst[:, 64 * h:64 * (h + 1)], dst[:, 64 * h:64 * (h + 1)],
                AF.Exp, bias=ndmx[:, h:h + 1], wkey=None)
        ts(em, "vector", dst[:], dst[:], 0.125, 0.125e-6, ALU.mult, ALU.add,
           wkey=None)
    if stop_at == 'qf':
        em.barrier()
        return
    for i in range(NI):
        src = qdt(i)
        for j in range(8):
            transp(em, pbf[:, 0:128], src[:, 128 * j:128 * (j + 1)], ctx.eye[:])
            copy(em, "vector", S4[j][:, 128 * i:128 * (i + 1)], pbf[0:128, 0:128],
                 wkey=S4[j])
    if stop_at == 'qftr':
        em.barrier()
        return
    qk_path("Wk", qdt, diag_k, False)
    if stop_at == 'kproj':
        em.barrier()
        return
    memset(em, "vector", mxk_col[:], -1e30, wkey=mxk_col)
    for i in range(NI):
        reduce(em, mxcol[:], qdt(i)[:], ALU.max, wkey=mxcol)
        tt(em, "vector", mxk_col[:], mxk_col[:], mxcol[:], ALU.max, wkey=mxk_col)
    copy(em, "vector", sq2[0][0:128, 0:1], mxk_col[:], wkey=sq2[0])
    transp(em, pbf[0:1, 0:128], sq2[0][0:128, 0:1], ctx.eye[:])
    copy(em, "vector", sq2[0][0:1, 0:128], pbf[0:1, 0:128], wkey=sq2[0])
    reduce(em, mxrow[0:1, 0:1], sq2[0][0:1, 0:128], ALU.max, wkey=mxrow)
    ts(em, "vector", mxrow[0:1, 0:16], ctx.zeros[0:1, 0:16], mxrow[0:1, 0:1],
       None, ALU.add, wkey=mxrow)
    em.dma("gpsimd", out=cc_in.ap(), in_=mxrow[0:1, 0:16], reads=[mxrow],
           writes=["p3ccin"])
    em.op("gpsimd", lambda e: e.collective_compute(
        "AllReduce", ALU.max, replica_groups=[list(range(ctx.ncores))],
        ins=[cc_in.ap().opt()], outs=[cc_out.ap().opt()]),
        reads=["p3ccin"], writes=["p3ccout"])
    em.dma("sync", out=mxrow[0:1, 0:16], in_=cc_out.ap(), reads=["p3ccout", mxrow],
           writes=[mxrow])
    mm(em, pp[6][:, 0:1], ctx.ones_f[:], mxrow[0:1, 0:1])
    copy(em, "vector", mxk_col[:], pp[6][:, 0:1], wkey=mxk_col)
    if stop_at == 'kmax':
        em.barrier()
        return
    for i in range(NI):
        dst = qdt(i)
        for h in range(16):
            ts(em, "vector", ndmx[:, h:h + 1], diag_k[i][:, h:h + 1], mxk_col[:],
               -1.0, ALU.add, ALU.mult, wkey=ndmx)
        for h in range(16):
            act(em, dst[:, 64 * h:64 * (h + 1)], dst[:, 64 * h:64 * (h + 1)],
                AF.Exp, bias=ndmx[:, h:h + 1], wkey=None)
        ts(em, "vector", dst[:], dst[:], 0.125, 0.125e-6, ALU.mult, ALU.add,
           wkey=None)
    if stop_at == 'kf':
        em.barrier()
        return
    for w in range(NTW):
        t0 = w * 512
        for nh in range(2):
            for kk in range(8):
                wb = wbig3[kk % 3]
                em.dma("gpsimd", out=wb[:], in_=inp["Wv"][
                    128 * kk:128 * (kk + 1), 512 * nh:512 * (nh + 1)],
                    reads=[wb], writes=[wb])
                for s in range(4):
                    mm(em, pp[s][:, 0:512],
                       S1[kk][:, t0 + 128 * s:t0 + 128 * (s + 1)], wb[:],
                       start=(kk == 0), stop=(kk == 7))
            for s in range(4):
                i = 4 * w + s
                copy(em, "vector", S5[i][:, 512 * nh:512 * (nh + 1)],
                     pp[s][:, 0:512], wkey=S5[i])
    if stop_at == 'vproj':
        em.barrier()
        return
    vt = lambda i: S5[i][:, :]
    for h in range(16):
        pr = h % 2
        pkv = pp[pr]
        pden = pp[2 + pr]
        for i in range(NI):
            mm(em, pkv[0:64, 0:64], qdt(i)[:, 64 * h:64 * (h + 1)],
               vt(i)[:, 64 * h:64 * (h + 1)], start=(i == 0), stop=(i == NI - 1))
            mm(em, pden[0:64, 0:1], qdt(i)[:, 64 * h:64 * (h + 1)],
               ctx.ones_col_bf[:], start=(i == 0), stop=(i == NI - 1))
        sl = slice(64 * pr, 64 * (pr + 1))
        copy(em, "vector", kv_sb[h // 2][sl, 0:64], pkv[0:64, 0:64],
             wkey=kv_sb[h // 2])
        copy(em, "vector", kv_sb[h // 2][sl, 64:65], pden[0:64, 0:1],
             wkey=kv_sb[h // 2])
    if stop_at == 'kv':
        em.barrier()
        return
    for i in range(NI):
        dst = qdt(i)
        for h in range(16):
            pr = h % 2
            sl = slice(64 * pr, 64 * (pr + 1))
            pn = pp[4 + pr]
            mm(em, pn[:, 0:65], S4[h // 2][sl, 128 * i:128 * (i + 1)],
               kv_sb[h // 2][sl, 0:65])
            recip(em, rc[:], pn[:, 64:65], wkey=rc)
            ts(em, "vector", dst[:, 64 * h:64 * (h + 1)], pn[:, 0:64], rc[:],
               None, ALU.mult, wkey=None)
    if stop_at == 'num':
        em.barrier()
        return
    for i in range(NI):
        src = qdt(i)
        for j in range(8):
            transp(em, pbf[:, 0:128], src[:, 128 * j:128 * (j + 1)], ctx.eye[:])
            copy(em, "vector", S4[j][:, 128 * i:128 * (i + 1)], pbf[0:128, 0:128],
                 wkey=S4[j])
    if stop_at == 'attntr':
        em.barrier()
        return
    for w in range(NTW):
        t0 = w * 512
        stream_proj(nc, em, ctx, inp["Wo"],
                    lambda kk, t0=t0: S4[kk][:, t0:t0 + 512], (pp[0], pp[1]),
                    lambda j, ps, t0=t0: tt(em, "vector", S1[j][:, t0:t0 + 512],
                                            ps[:, 0:512], S1[j][:, t0:t0 + 512],
                                            ALU.add, wkey=S1[j]),
                    wbuf)
    if stop_at == 'wo':
        em.barrier()
        return
    for w in range(NTW):
        t0 = w * 512
        emit_ln_win(nc, em, ctx, t0, 512, lambda j: S1[j][:, t0:t0 + 512],
                    lambda j, emitfn: emitfn(S2[j][:, t0:t0 + 512]),
                    (pp[0], pp[1], pp[2], pp[3]), ctx.eps6[0:1, 0:1],
                    gamma=lambda j: pvc(IG1, j), beta=lambda j: pvc(IB1, j),
                    lnrow=lnrow, sq2=sq2, tmpf=tmpf)
    if stop_at == 'ln1':
        em.barrier()
        return
    helu = lambda m: S5[m // 2][:, 512 * (m % 2):512 * (m % 2 + 1)]
    for w in range(NTW):
        t0 = w * 512
        for mp in range(16):
            for kk in range(8):
                wb = wbuf[kk % 4]
                em.dma("gpsimd", out=wb[:, 0:256],
                       in_=inp["W1"][128 * kk:128 * (kk + 1),
                                     256 * mp:256 * (mp + 1)],
                       reads=[wb], writes=[wb])
                mm(em, pp[0][:, 0:512], wb[:, 0:128], S2[kk][:, t0:t0 + 512],
                   start=(kk == 0), stop=(kk == 7))
                mm(em, pp[1][:, 0:512], wb[:, 128:256], S2[kk][:, t0:t0 + 512],
                   start=(kk == 0), stop=(kk == 7))
            for half in range(2):
                m = 2 * mp + half
                ps = pp[half]
                tf = tmpf[half]
                ts(em, "vector", tf[:], ps[:, 0:512], b1c[:, m:m + 1], 0.0,
                   ALU.add, ALU.min, wkey=tf)
                act(em, tf[:], tf[:], AF.Exp, wkey=tf)
                ts(em, "vector", xt[half][:], ps[:, 0:512], b1c[:, m:m + 1], 0.0,
                   ALU.add, ALU.max, wkey=xt[half])
                stt(em, "vector", helu(m)[:], tf[:], -1.0, xt[half][:], ALU.add,
                    ALU.add, wkey=None)
        for pj in range(4):
            for kk in range(32):
                wb = wbuf[kk % 4]
                em.dma("gpsimd", out=wb[:, 0:256],
                       in_=inp["W2"][128 * kk:128 * (kk + 1),
                                     256 * pj:256 * (pj + 1)],
                       reads=[wb], writes=[wb])
                mm(em, pp[2][:, 0:512], wb[:, 0:128], helu(kk)[:],
                   start=(kk == 0), stop=(kk == 31))
                mm(em, pp[3][:, 0:512], wb[:, 128:256], helu(kk)[:],
                   start=(kk == 0), stop=(kk == 31))
            for half in range(2):
                j = 2 * pj + half
                stt(em, "vector", s2b(j), pp[2 + half][:, 0:512],
                    pvc(IB2b, j), S2[j][:, t0:t0 + 512], ALU.add, ALU.add,
                    wkey=None)
        for j in range(8):
            em.dma("sync", out=xt[j][:], in_=ctx.x_in[128 * j:128 * (j + 1),
                                                      t0:t0 + 512],
                   reads=[xt[j]], writes=[xt[j]])
        def dst3(j, emitfn):
            tf2 = sq2[0][:, 0:512]
            emitfn(tf2)
            tt(em, "vector", xt[j][:], tf2, xt[j][:], ALU.add, wkey=xt[j])
            em.dma("gpsimd", out=ctx.y_out[128 * j:128 * (j + 1), t0:t0 + 512],
                   in_=xt[j][:], reads=[xt[j]], writes=[f"P3OUT{j}"])
        emit_ln_win(nc, em, ctx, t0, 512, lambda j: s2b(j),
                    dst3, (pp[0], pp[1], pp[2], pp[3]), ctx.eps6[0:1, 0:1],
                    gamma=lambda j: pvc(IG2, j), beta=lambda j: pvc(IB2, j),
                    lnrow=lnrow, sq2=sq2, tmpf=tmpf)
    em.barrier()


def prep_phase3_inputs(inputs):
    d = {}
    d["Wq"] = np.ascontiguousarray(inputs["L1_attn_Wq"].reshape(1024, 1024),
                                   np.float32)
    d["Wk"] = np.ascontiguousarray(inputs["L1_attn_Wk"].reshape(1024, 1024),
                                   np.float32)
    d["Wv"] = np.ascontiguousarray(inputs["L1_attn_Wv"].reshape(1024, 1024),
                                   np.float32)
    d["Wo"] = np.ascontiguousarray(inputs["L1_attn_Wo"].reshape(1024, 1024),
                                   np.float32)
    d["projT"] = np.ascontiguousarray(inputs["L1_attn_proj"].T, np.float32)
    d["W1"] = np.ascontiguousarray(inputs["L1_attn_W1"], np.float32)
    d["W2"] = np.ascontiguousarray(inputs["L1_attn_W2"], np.float32)
    d["b1cols"] = np.ascontiguousarray(inputs["L1_attn_b1"].reshape(32, 128).T,
                                       np.float32)
    d["pvec3"] = _pack_pvec([
        inputs["L1_attn_ln1_g"], inputs["L1_attn_ln1_b"],
        inputs["L1_attn_ln2_g"], inputs["L1_attn_ln2_b"], inputs["L1_attn_b2"]])
    return d


PHASE3_TENSORS = [
    ("Wq", [1024, 1024]), ("Wk", [1024, 1024]), ("Wv", [1024, 1024]),
    ("Wo", [1024, 1024]), ("projT", [64, 64]), ("W1", [1024, 4096]),
    ("W2", [4096, 1024]), ("b1cols", [128, 32]), ("pvec3", [128, 40]),
]


def build_nc_phase3(T=2048, ncores=8, debug=False, stop_at=None):
    from contextlib import ExitStack
    nc = bass.Bass("TRN2", target_bir_lowering=False)
    x_in = nc.dram_tensor("x_fm", [1024, T], F32, kind="ExternalInput").ap()
    y_out = nc.dram_tensor("y_fm", [1024, T], F32, kind="ExternalOutput").ap()
    inp = {nm: nc.dram_tensor(nm, shp, F32, kind="ExternalInput").ap()
           for nm, shp in PHASE3_TENSORS}
    em = Em(nc)
    ctx = Ctx()
    ctx.fused = False
    ctx.ncores = ncores
    ctx.debug = debug
    ctx.dbg_done = set()
    ctx.x_in = x_in
    ctx.y_out = y_out
    with ExitStack() as st:
        E = st.enter_context
        if debug:
            ctx.dbg_stage = E(nc.sbuf_tensor("dbgstage", [128, 1024], F32))
        setup_common(nc, em, ctx, E, T)
        build_phase3(nc, em, ctx, E, inp, stop_at=stop_at)
        em.build()
    print("phase3: ops=%d" % len(em.ops))
    return nc


def _p4_consts(T):
    DH, WIN = 64, 512
    half = np.arange(0, DH, 2, dtype=np.float64)
    invf = 1.0 / 10000.0 ** (half / DH)
    t = np.arange(T, dtype=np.float64)
    fr = t[:, None] * invf[None, :]
    fr = np.concatenate([fr, fr], axis=-1)
    base = (half + 0.4 * DH) / (1.4 * DH)
    power = (t - T // 2) / float(WIN // 2)
    sc = base[None, :] ** power[:, None]
    sc = np.concatenate([sc, sc], axis=-1)
    cos, sin = np.cos(fr), np.sin(fr)
    Aq = (cos * sc).T; Bq = (sin * sc).T
    Ak = (cos / sc).T; Bk = (sin / sc).T
    pack = lambda a: np.ascontiguousarray(np.tile(a, (2, 1))).astype(BF16NP)
    RT = np.zeros((64, 64), np.float32)
    for do in range(32):
        RT[do + 32, do] = -1.0
        RT[do, do + 32] = 1.0
    RT2 = np.zeros((128, 128), np.float32)
    RT2[0:64, 0:64] = RT; RT2[64:128, 64:128] = RT
    M = np.zeros((4, 128, 1024), np.float32)
    for qs in range(4):
        for p in range(128):
            i = qs * 128 + p
            M[qs, p, i:i + 513] = 1.0
    M0 = np.zeros((4, 128, 512), np.float32)
    for qs in range(4):
        for p in range(128):
            M0[qs, p, 0:qs * 128 + p + 1] = 1.0
    G8 = np.zeros((8, 512), np.float32)
    for h in range(8):
        G8[h, 64 * h:64 * (h + 1)] = 1.0
    BD8 = np.zeros((512, 8), np.float32)
    for h in range(8):
        BD8[64 * h:64 * (h + 1), h] = 1.0
    return (pack(Aq), pack(Bq), pack(Ak), pack(Bk), RT2.astype(BF16NP),
            M.astype(BF16NP), M0.astype(BF16NP), G8.astype(BF16NP),
            BD8.astype(BF16NP))


def build_phase4(nc, em, ctx, E, inp):
    T = ctx.T
    NTW = T // 512
    NI = T // 128
    NW = max(1, T // 512)

    pp = [E(nc.psum_tensor(f"p4p{i}", [128, 512], F32)) for i in range(7)]
    pbf = E(nc.psum_tensor("p4bf", [128, 1024], BF16))

    Aq_np, Bq_np, Ak_np, Bk_np, RT2_np, M_np, M0_np, G8_np, BD8_np = _p4_consts(T)
    def const(nm, arr):
        d = nc.inline_tensor(arr, nm + "_d")
        t = E(nc.sbuf_tensor(nm, list(arr.shape), BF16))
        em.dma("sync", out=t[:], in_=d.ap(), writes=[t])
        return t
    Aq = const("p4Aq", Aq_np); Bq = const("p4Bq", Bq_np)
    Ak = const("p4Ak", Ak_np); Bk = const("p4Bk", Bk_np)
    RT2 = const("p4RT", RT2_np)
    G8 = const("p4G8", G8_np)
    bd8_d = nc.inline_tensor(BD8_np, "p4BD_d")
    BD8 = [E(nc.sbuf_tensor(f"p4BD{c}", [128, 8], BF16)) for c in range(4)]
    for c in range(4):
        em.dma("sync", out=BD8[c][:], in_=bd8_d.ap()[128 * c:128 * (c + 1), :],
               writes=[BD8[c]])
    Mm = [const(f"p4M{qs}", M_np[qs]) for qs in range(4)]
    Mm0 = [const(f"p4M0{qs}", M0_np[qs]) for qs in range(4)]

    pv4 = E(nc.sbuf_tensor("p4pv", [128, 64], F32))
    em.dma("sync", out=pv4[:], in_=inp["pvec4"], writes=[pv4])

    def pvc(l, idx, j):
        base = l * 32
        if idx < 3:
            return pv4[:, base + idx * 8 + j:base + idx * 8 + j + 1]
        return pv4[:, base + 24 + (idx - 3) * 4 + j:base + 24 + (idx - 3) * 4 + j + 1]

    Z = [E(nc.sbuf_tensor(f"p4Z{j}", [128, T], BF16)) for j in range(8)]
    hs4 = [E(nc.sbuf_tensor(f"p4hs{j}", [128, T], BF16)) for j in range(4)]
    q_fm = [E(nc.sbuf_tensor(f"p4q{j}", [128, T], BF16)) for j in range(4)]
    k_fm = [E(nc.sbuf_tensor(f"p4k{j}", [128, T], BF16)) for j in range(4)]
    v_tm = [E(nc.sbuf_tensor(f"p4v{i}", [128, 512], BF16)) for i in range(NI)]
    o_fm = [E(nc.sbuf_tensor(f"p4o{j}", [128, max(T, 1024)], BF16)) for j in range(4)]
    helu = [E(nc.sbuf_tensor(f"p4he{m}", [128, 512], BF16)) for m in range(22)]
    attn_sb = [E(nc.sbuf_tensor(f"p4at{i}", [128, 1024], BF16)) for i in range(2)]
    attnT = [E(nc.sbuf_tensor(f"p4aT{i}", [128, 1024], BF16)) for i in range(2)]
    rnb = E(nc.sbuf_tensor("p4rn", [8, 512], BF16))
    rnf = E(nc.sbuf_tensor("p4rnf", [8, 512], F32))
    tb = [E(nc.sbuf_tensor(f"p4tb{i}", [128, 512], BF16)) for i in range(2)]
    mxc = E(nc.sbuf_tensor("p4mx", [128, 1], F32))
    mx2 = E(nc.sbuf_tensor("p4mx2", [128, 1], F32))
    rc = E(nc.sbuf_tensor("p4rc", [128, 1], F32))
    sq2 = [E(nc.sbuf_tensor(f"p4sq{i}", [128, 512], BF16)) for i in range(2)]
    tmpf = [E(nc.sbuf_tensor(f"p4tmp{i}", [128, 512], F32)) for i in range(2)]
    lnrow = [E(nc.sbuf_tensor(f"p4row{i}", [1, 512], F32)) for i in range(4)]
    xt = [E(nc.sbuf_tensor(f"p4x{j}", [128, 512], BF16)) for j in range(8)]
    wbig = [E(nc.sbuf_tensor(f"p4wB{i}", [128, 512], BF16)) for i in range(3)]
    wbuf = [E(nc.sbuf_tensor(f"p4wb{i}", [128, 256], BF16)) for i in range(4)]

    em.barrier()

    for w in range(NTW):
        t0 = w * 512
        for j in range(8):
            em.dma("gpsimd", out=xt[j][:], in_=ctx.x_in[128 * j:128 * (j + 1),
                                                        t0:t0 + 512],
                   reads=[xt[j]], writes=[xt[j]])
        emit_ln_win(nc, em, ctx, t0, 512, lambda j: xt[j][:],
                    lambda j, emitfn: emitfn(Z[j][:, t0:t0 + 512]),
                    (pp[0], pp[1], pp[2], pp[3]), ctx.eps5[0:1, 0:1],
                    lnrow=lnrow, sq2=sq2, tmpf=tmpf)

    for l in range(2):
        for j in range(4):
            copy(em, "vector", hs4[j][:, 1:T], Z[j + 4][:, 0:T - 1], wkey=hs4[j])
            memset(em, "vector", hs4[j][:, 0:1], 0.0, wkey=hs4[j])
        hsrc = lambda j: Z[j] if j < 4 else hs4[j - 4]
        for w in range(NTW):
            t0 = w * 512
            for pj in range(4):
                dst = q_fm[2 * pj] if pj < 2 else k_fm[2 * (pj - 2)]
                dst2 = q_fm[2 * pj + 1] if pj < 2 else k_fm[2 * (pj - 2) + 1]
                for kk in range(8):
                    wb = wbuf[kk % 4]
                    em.dma("gpsimd", out=wb[:, 0:256],
                           in_=inp["wqkv"][l, 128 * kk:128 * (kk + 1),
                                           256 * pj:256 * (pj + 1)],
                           reads=[wb], writes=[wb])
                    mm(em, pp[0][:, 0:512], wb[:, 0:128], hsrc(kk)[:, t0:t0 + 512],
                       start=(kk == 0), stop=(kk == 7))
                    mm(em, pp[1][:, 0:512], wb[:, 128:256], hsrc(kk)[:, t0:t0 + 512],
                       start=(kk == 0), stop=(kk == 7))
                copy(em, "vector", dst[:, t0:t0 + 512], pp[0][:, 0:512], wkey=dst)
                copy(em, "vector", dst2[:, t0:t0 + 512], pp[1][:, 0:512], wkey=dst2)
            for kk in range(8):
                wb = wbig[kk % 3]
                em.dma("gpsimd", out=wb[:],
                       in_=inp["wqkv"][l, 128 * kk:128 * (kk + 1), 1024:1536],
                       reads=[wb], writes=[wb])
                for s in range(4):
                    mm(em, pp[2 + s][:, 0:512],
                       hsrc(kk)[:, t0 + 128 * s:t0 + 128 * (s + 1)], wb[:],
                       start=(kk == 0), stop=(kk == 7))
            for s in range(4):
                copy(em, "vector", v_tm[4 * w + s][:], pp[2 + s][:, 0:512],
                     wkey=v_tm[4 * w + s])
        for (fm, Ac, Bc, isq) in ((q_fm, Aq, Bq, True), (k_fm, Ak, Bk, False)):
            for w in range(NTW):
                t0 = w * 512
                for j in range(4):
                    act(em, sq2[j % 2][:], fm[j][:, t0:t0 + 512], AF.Square,
                        wkey=sq2[j % 2])
                    mm(em, pp[0][0:8, 0:512], BD8[j][:],
                       sq2[j % 2][:], start=(j == 0), stop=(j == 3))
                act(em, rnf[:, 0:512], pp[0][0:8, 0:512], AF.Sqrt, wkey=rnf)
                ts(em, "vector", rnf[:, 0:512], rnf[:, 0:512], 1e-12, None,
                   ALU.max, wkey=rnf)
                recip(em, rnf[:, 0:512], rnf[:, 0:512])
                copy(em, "vector", rnb[:, 0:512], rnf[:, 0:512], wkey=rnb)
                for j in range(4):
                    mm(em, pp[1][:, 0:512], G8[:, 128 * j:128 * (j + 1)],
                       rnb[:, 0:512])
                    tt(em, "vector", fm[j][:, t0:t0 + 512], fm[j][:, t0:t0 + 512],
                       pp[1][:, 0:512], ALU.mult, wkey=fm[j])
                    qsc = pvc(l, 3 if isq else 4, j)
                    ts(em, "vector", fm[j][:, t0:t0 + 512], fm[j][:, t0:t0 + 512],
                       qsc, 8.0 if isq else 1.0, ALU.mult, ALU.mult, wkey=fm[j])
                    mm(em, pp[2][:, 0:512], RT2[:], fm[j][:, t0:t0 + 512])
                    tt(em, "vector", tb[0][:], fm[j][:, t0:t0 + 512],
                       Ac[:, t0:t0 + 512], ALU.mult, wkey=tb[0])
                    tt(em, "vector", tb[1][:], pp[2][:, 0:512],
                       Bc[:, t0:t0 + 512], ALU.mult, wkey=tb[1])
                    tt(em, "vector", fm[j][:, t0:t0 + 512], tb[0][:], tb[1][:],
                       ALU.add, wkey=fm[j])
        for h in range(8):
            hj, par = h // 2, h % 2
            sl = slice(64 * par, 64 * (par + 1))
            for w in range(NW):
                kb = (w - 1) * 4
                kcol0 = (w - 1) * 512
                nk = 512 if w == 0 else 1024
                for qs in range(4):
                    qc0 = w * 512 + qs * 128
                    ia = qs % 2
                    if w == 0:
                        mm(em, pp[0][:, 0:512], q_fm[hj][sl, qc0:qc0 + 128],
                           k_fm[hj][sl, 0:512])
                        reduce(em, mxc[:], pp[0][:, 0:512], ALU.max, wkey=mxc)
                        ts(em, "vector", mxc[:], mxc[:], -1.0, None, ALU.mult,
                           wkey=mxc)
                        act(em, attn_sb[ia][:, 0:512], pp[0][:, 0:512], AF.Exp,
                            bias=mxc[:], wkey=attn_sb[ia])
                        tt(em, "vector", attn_sb[ia][:, 0:512],
                           attn_sb[ia][:, 0:512], Mm0[qs][:], ALU.mult,
                           wkey=attn_sb[ia])
                        reduce(em, rc[:], attn_sb[ia][:, 0:512], ALU.add, wkey=rc)
                        recip(em, rc[:], rc[:])
                        ts(em, "vector", attn_sb[ia][:, 0:512],
                           attn_sb[ia][:, 0:512], rc[:], None, ALU.mult,
                           wkey=attn_sb[ia])
                        nblk = 4
                    else:
                        mm(em, pp[0][:, 0:512], q_fm[hj][sl, qc0:qc0 + 128],
                           k_fm[hj][sl, kcol0:kcol0 + 512])
                        mm(em, pp[1][:, 0:512], q_fm[hj][sl, qc0:qc0 + 128],
                           k_fm[hj][sl, kcol0 + 512:kcol0 + 1024])
                        reduce(em, mxc[:], pp[0][:, 0:512], ALU.max, wkey=mxc)
                        reduce(em, mx2[:], pp[1][:, 0:512], ALU.max, wkey=mx2)
                        tt(em, "vector", mxc[:], mxc[:], mx2[:], ALU.max, wkey=mxc)
                        ts(em, "vector", mxc[:], mxc[:], -1.0, None, ALU.mult,
                           wkey=mxc)
                        act(em, attn_sb[ia][:, 0:512], pp[0][:, 0:512], AF.Exp,
                            bias=mxc[:], wkey=attn_sb[ia])
                        act(em, attn_sb[ia][:, 512:1024], pp[1][:, 0:512], AF.Exp,
                            bias=mxc[:], wkey=attn_sb[ia])
                        tt(em, "vector", attn_sb[ia][:], attn_sb[ia][:], Mm[qs][:],
                           ALU.mult, wkey=attn_sb[ia])
                        reduce(em, rc[:], attn_sb[ia][:], ALU.add, wkey=rc)
                        recip(em, rc[:], rc[:])
                        ts(em, "vector", attn_sb[ia][:], attn_sb[ia][:], rc[:],
                           None, ALU.mult, wkey=attn_sb[ia])
                        nblk = 8
                    for b in range(nblk):
                        transp(em, pbf[:, 128 * b:128 * (b + 1)],
                               attn_sb[ia][:, 128 * b:128 * (b + 1)], ctx.eye[:])
                    copy(em, "vector", attnT[ia][:, 0:128 * nblk],
                         pbf[:, 0:128 * nblk], wkey=attnT[ia])
                    pav = pp[4 + (qs % 2)]
                    for b in range(nblk):
                        vi = (kb + b) if w > 0 else b
                        mm(em, pav[0:64, 0:128], v_tm[vi][:, 64 * h:64 * (h + 1)],
                           attnT[ia][:, 128 * b:128 * (b + 1)],
                           start=(b == 0), stop=(b == nblk - 1))
                    copy(em, "vector", o_fm[hj][sl, qc0:qc0 + 128],
                         pav[0:64, 0:128], wkey=o_fm[hj])
        for w in range(NTW):
            t0 = w * 512
            for pj in range(4):
                for kk in range(4):
                    wb = wbuf[kk % 4]
                    em.dma("gpsimd", out=wb[:, 0:256],
                           in_=inp["wo"][l, 128 * kk:128 * (kk + 1),
                                         256 * pj:256 * (pj + 1)],
                           reads=[wb], writes=[wb])
                    mm(em, pp[0][:, 0:512], wb[:, 0:128], o_fm[kk][:, t0:t0 + 512],
                       start=(kk == 0), stop=(kk == 3))
                    mm(em, pp[1][:, 0:512], wb[:, 128:256], o_fm[kk][:, t0:t0 + 512],
                       start=(kk == 0), stop=(kk == 3))
                for half in range(2):
                    j = 2 * pj + half
                    tt(em, "vector", Z[j][:, t0:t0 + 512],
                       pp[half][:, 0:512], Z[j][:, t0:t0 + 512], ALU.add,
                       wkey=Z[j])
        for j in range(4):
            copy(em, "vector", hs4[j][:, 1:T], Z[j + 4][:, 0:T - 1], wkey=hs4[j])
            memset(em, "vector", hs4[j][:, 0:1], 0.0, wkey=hs4[j])
        hffn = lambda j: o_fm[j // 2][:, 512 * (j % 2):512 * (j % 2) + 512]
        for w in range(NTW):
            t0 = w * 512
            emit_ln_win(nc, em, ctx, t0, 512, lambda j: hsrc(j)[:, t0:t0 + 512],
                        lambda j, emitfn: emitfn(hffn(j)),
                        (pp[0], pp[1], pp[2], pp[3]), ctx.eps5[0:1, 0:1],
                        gamma=lambda j: pvc(l, 0, j), beta=lambda j: pvc(l, 1, j),
                        lnrow=lnrow, sq2=sq2, tmpf=tmpf)
            for mp in range(22):
                for kk in range(8):
                    wb = wbuf[kk % 4]
                    em.dma("gpsimd", out=wb[:, 0:256],
                           in_=inp["w1p"][l, 128 * kk:128 * (kk + 1),
                                          256 * mp:256 * (mp + 1)],
                           reads=[wb], writes=[wb])
                    mm(em, pp[0][:, 0:512], wb[:, 0:128], hffn(kk),
                       start=(kk == 0), stop=(kk == 7))
                    mm(em, pp[1][:, 0:512], wb[:, 128:256], hffn(kk),
                       start=(kk == 0), stop=(kk == 7))
                for half in range(2):
                    m = 2 * mp + half
                    if m < 22:
                        ts(em, "vector", helu[m][:], pp[half][:, 0:512],
                           inp_b1col(em, ctx, l, m), None, ALU.add, wkey=helu[m])
                    else:
                        act(em, sq2[0][:], pp[half][:, 0:512], AF.Gelu,
                            bias=inp_b1col(em, ctx, l, m), wkey=sq2[0])
                        tt(em, "vector", helu[m - 22][:], helu[m - 22][:],
                           sq2[0][:], ALU.mult, wkey=helu[m - 22])
            for pj in range(4):
                for kk in range(22):
                    wb = wbuf[kk % 4]
                    em.dma("gpsimd", out=wb[:, 0:256],
                           in_=inp["w2p"][l, 128 * kk:128 * (kk + 1),
                                          256 * pj:256 * (pj + 1)],
                           reads=[wb], writes=[wb])
                    mm(em, pp[2][:, 0:512], wb[:, 0:128], helu[kk][:],
                       start=(kk == 0), stop=(kk == 21))
                    mm(em, pp[3][:, 0:512], wb[:, 128:256], helu[kk][:],
                       start=(kk == 0), stop=(kk == 21))
                for half in range(2):
                    j = 2 * pj + half
                    stt(em, "vector", Z[j][:, t0:t0 + 512],
                        pp[2 + half][:, 0:512], pvc(l, 2, j),
                        Z[j][:, t0:t0 + 512], ALU.add, ALU.add, wkey=Z[j])
    if ctx.fused:
        for j in range(8):
            tt(em, "vector", ctx.X[j][:], Z[j][:], ctx.X[j][:], ALU.add,
               wkey=ctx.X[j])
    else:
        for w in range(NTW):
            t0 = w * 512
            for j in range(8):
                em.dma("gpsimd", out=xt[j][:], in_=ctx.x_in[128 * j:128 * (j + 1),
                                                            t0:t0 + 512],
                       reads=[xt[j]], writes=[xt[j]])
                tt(em, "vector", xt[j][:], Z[j][:, t0:t0 + 512], xt[j][:],
                   ALU.add, wkey=xt[j])
                em.dma("gpsimd", out=ctx.y_out[128 * j:128 * (j + 1), t0:t0 + 512],
                       in_=xt[j][:], reads=[xt[j]], writes=[f"P4OUT{j}"])
    em.barrier()


def inp_b1col(em, ctx, l, m):
    return ctx.p4b1c[l][:, m:m + 1]


def prep_phase4_inputs(inputs):
    d = {}
    d["wqkv"] = np.ascontiguousarray(inputs["L1_ffn_wqkv"], np.float32)
    d["wo"] = np.ascontiguousarray(inputs["L1_ffn_wo"], np.float32)
    w1 = inputs["L1_ffn_w1"]
    w1p = np.zeros((2, 1024, 5632), np.float32)
    w1p[:, :, 0:2730] = w1[:, :, 0:2730]
    w1p[:, :, 2816:2816 + 2730] = w1[:, :, 2730:5460]
    d["w1p"] = w1p
    w2 = inputs["L1_ffn_w2"]
    w2p = np.zeros((2, 2816, 1024), np.float32)
    w2p[:, 0:2730, :] = w2
    d["w2p"] = w2p
    b1 = inputs["L1_ffn_b1"]
    b1p = np.zeros((2, 5632), np.float32)
    b1p[:, 0:2730] = b1[:, 0:2730]
    b1p[:, 2816:2816 + 2730] = b1[:, 2730:5460]
    d["b1cols4"] = np.ascontiguousarray(
        np.stack([b1p[l].reshape(44, 128).T for l in range(2)]), np.float32)
    cols = []
    for l in range(2):
        sub = _pack_pvec([inputs["L1_ffn_ln_g"][l], inputs["L1_ffn_ln_b"][l],
                          inputs["L1_ffn_b2"][l]])
        qs = np.tile(inputs["L1_ffn_q_scale"][l], 8).reshape(4, 128).T
        ks = np.tile(inputs["L1_ffn_k_scale"][l], 8).reshape(4, 128).T
        cols.append(np.concatenate([sub, qs, ks], axis=1))
    d["pvec4"] = np.ascontiguousarray(np.concatenate(cols, axis=1), np.float32)
    return d


PHASE4_TENSORS = [
    ("wqkv", [2, 1024, 1536]), ("wo", [2, 512, 1024]), ("w1p", [2, 1024, 5632]),
    ("w2p", [2, 2816, 1024]), ("b1cols4", [2, 128, 44]), ("pvec4", [128, 64]),
]


def build_nc_phase4(T=2048, debug=False):
    from contextlib import ExitStack
    nc = bass.Bass("TRN2", target_bir_lowering=False)
    x_in = nc.dram_tensor("x_fm", [1024, T], F32, kind="ExternalInput").ap()
    y_out = nc.dram_tensor("y_fm", [1024, T], F32, kind="ExternalOutput").ap()
    inp = {nm: nc.dram_tensor(nm, shp, F32, kind="ExternalInput").ap()
           for nm, shp in PHASE4_TENSORS}
    em = Em(nc)
    ctx = Ctx()
    ctx.fused = False
    ctx.debug = debug
    ctx.dbg_done = set()
    ctx.x_in = x_in
    ctx.y_out = y_out
    with ExitStack() as st:
        E = st.enter_context
        if debug:
            ctx.dbg_stage = E(nc.sbuf_tensor("dbgstage", [128, 1024], F32))
        setup_common(nc, em, ctx, E, T)
        ctx.p4b1c = [E(nc.sbuf_tensor(f"p4b1c{l}", [128, 44], F32)) for l in range(2)]
        for l in range(2):
            em.dma("sync", out=ctx.p4b1c[l][:], in_=inp["b1cols4"][l],
                   writes=[ctx.p4b1c[l]])
        build_phase4(nc, em, ctx, E, inp)
        em.build()
    print("phase4: ops=%d" % len(em.ops))
    return nc


def _pack_pvec(vectors):
    cols = [np.ascontiguousarray(v.reshape(8, 128).T) for v in vectors]
    return np.ascontiguousarray(np.concatenate(cols, axis=1).astype(np.float32))


def prep_phase1_inputs(inputs):
    d = {}
    d["Wr"] = np.ascontiguousarray(inputs["L0_attn_Wr"], np.float32)
    d["Wk"] = np.ascontiguousarray(inputs["L0_attn_Wk"], np.float32)
    d["Wv"] = np.ascontiguousarray(inputs["L0_attn_Wv"], np.float32)
    d["Wg"] = np.ascontiguousarray(inputs["L0_attn_Wg"], np.float32)
    d["Wo"] = np.ascontiguousarray(inputs["L0_attn_Wo"], np.float32)
    d["tm_w1"] = np.ascontiguousarray(inputs["L0_attn_tm_w1"], np.float32)
    d["tm_w2"] = np.ascontiguousarray(inputs["L0_attn_tm_w2"], np.float32)
    d["td_w1"] = np.ascontiguousarray(inputs["L0_attn_td_w1"], np.float32)
    d["td_w2"] = np.ascontiguousarray(inputs["L0_attn_td_w2"], np.float32)
    d["pvec"] = _pack_pvec([
        inputs["L0_attn_x_maa"], inputs["L0_attn_w_maa"], inputs["L0_attn_k_maa"],
        inputs["L0_attn_v_maa"], inputs["L0_attn_r_maa"], inputs["L0_attn_g_maa"],
        inputs["L0_attn_time_decay"].reshape(-1), inputs["L0_attn_time_first"].reshape(-1),
        inputs["L0_attn_ln_w"], inputs["L0_attn_ln_b"]])
    d["ln_w"] = np.ascontiguousarray(inputs["L0_attn_ln_w"], np.float32)
    d["ln_b"] = np.ascontiguousarray(inputs["L0_attn_ln_b"], np.float32)
    return d


PHASE1_TENSORS = [
    ("Wr", [1024, 1024]), ("Wk", [1024, 1024]), ("Wv", [1024, 1024]),
    ("Wg", [1024, 1024]), ("Wo", [1024, 1024]), ("tm_w1", [1024, 160]),
    ("tm_w2", [5, 32, 1024]), ("td_w1", [1024, 64]), ("td_w2", [64, 1024]),
    ("pvec", [128, 80]), ("ln_w", [1024]), ("ln_b", [1024]),
]


def build_nc_phase1(T=2048, debug=False):
    from contextlib import ExitStack
    nc = bass.Bass("TRN2", target_bir_lowering=False)
    x_in = nc.dram_tensor("x_fm", [1024, T], F32, kind="ExternalInput").ap()
    y_out = nc.dram_tensor("y_fm", [1024, T], F32, kind="ExternalOutput").ap()
    inp = {nm: nc.dram_tensor(nm, shp, F32, kind="ExternalInput").ap()
           for nm, shp in PHASE1_TENSORS}
    em = Em(nc)
    ctx = Ctx()
    ctx.fused = False
    ctx.debug = debug
    ctx.dbg_done = set()
    ctx.x_in = x_in
    ctx.y_out = y_out
    with ExitStack() as st:
        E = st.enter_context
        if debug:
            ctx.dbg_stage = E(nc.sbuf_tensor("dbgstage", [128, 1024], F32))
        setup_common(nc, em, ctx, E, T)
        build_phase1(nc, em, ctx, E, inp)
        em.build()
    print("phase1: ops=%d sbuf_left=%d psum_left=%d" %
          (len(em.ops), nc.sbuf_bytes_remaining, nc.psum_banks_remaining))
    return nc


def build_nc_fused(T=2048, ncores=8):
    from contextlib import ExitStack
    nc = bass.Bass("TRN2", target_bir_lowering=False)
    x_in = nc.dram_tensor("x_fm", [1024, T], F32, kind="ExternalInput").ap()
    y_out = nc.dram_tensor("y_fm", [1024, T], F32, kind="ExternalOutput").ap()
    xs = [nc.dram_tensor(f"xs{i}", [1024, T], F32, kind="Internal").ap()
          for i in range(3)]
    tens = [PHASE1_TENSORS, PHASE2_TENSORS, PHASE3_TENSORS, PHASE4_TENSORS]
    inps = []
    for ph in range(4):
        inps.append({nm: nc.dram_tensor(f"p{ph + 1}_{nm}", shp, F32,
                                        kind="ExternalInput").ap()
                     for nm, shp in tens[ph]})
    em = Em(nc)
    ctx = Ctx()
    ctx.ncores = ncores
    ctx.moe_sparse = False
    ctx.fused = False
    ctx.debug = False
    ctx.dbg_done = set()
    with ExitStack() as outer:
        E0 = outer.enter_context
        setup_common(nc, em, ctx, E0, T)
        em.barrier()
        ctx.x_in, ctx.y_out = x_in, xs[0]
        with ExitStack() as s:
            build_phase1(nc, em, ctx, s.enter_context, inps[0])
        with ExitStack() as s:
            E = s.enter_context
            ctx.X = alloc_stream(nc, E, T)
            load_stream(em, ctx, xs[0])
            em.barrier()
            build_phase2(nc, em, ctx, E, inps[1])
            for j in range(8):
                em.dma("gpsimd", out=xs[1][128 * j:128 * (j + 1), :],
                       in_=ctx.X[j][:], reads=[ctx.X[j]], writes=[f"XS1_{j}"])
            em.barrier()
        ctx.x_in, ctx.y_out = xs[1], xs[2]
        with ExitStack() as s:
            build_phase3(nc, em, ctx, s.enter_context, inps[2])
        ctx.x_in, ctx.y_out = xs[2], y_out
        with ExitStack() as s:
            E = s.enter_context
            ctx.p4b1c = [E(nc.sbuf_tensor(f"p4b1c{l}", [128, 44], F32))
                         for l in range(2)]
            for l in range(2):
                em.dma("sync", out=ctx.p4b1c[l][:], in_=inps[3]["b1cols4"][l],
                       writes=[ctx.p4b1c[l]])
            build_phase4(nc, em, ctx, E, inps[3])
        em.build()
    print("fused: ops=%d" % len(em.ops))
    return nc


def kernel(**inputs):
    T = 2048
    x = np.asarray(inputs["L0_attn_x"], np.float32)
    preps = [prep_phase1_inputs(inputs), prep_phase2_inputs(inputs),
             prep_phase3_inputs(inputs), prep_phase4_inputs(inputs)]
    nc = build_nc_fused(T, NCORE)
    base = {}
    for ph in range(4):
        for k, v in preps[ph].items():
            base[f"p{ph + 1}_{k}"] = v
    in_maps = []
    for c in range(NCORE):
        m = dict(base)
        m["x_fm"] = np.ascontiguousarray(x[c].T)
        in_maps.append(m)
    res = run_bass_kernel_spmd(nc, in_maps, core_ids=list(range(NCORE)))
    out = np.stack([np.ascontiguousarray(np.asarray(r["y_fm"]).T)
                    for r in res.results])
    return out.astype(np.float32)
